# version 11
# baseline (speedup 1.0000x reference)
import numpy as np
import ml_dtypes
import concourse.bass as bass
import concourse.mybir as mybir
from concourse.bass import IndirectOffsetOnAxis
from concourse.bass_utils import run_bass_kernel_spmd

F32 = mybir.dt.float32
BF16 = mybir.dt.bfloat16
I32 = mybir.dt.int32
AF = mybir.ActivationFunctionType
ALU = mybir.AluOpType
AX = mybir.AxisListType

NCORES = 8
BPC = 2
T = 2048
C = 1024
NHEAD = 16
HD = 64
NEXP = 8
CAP = 320
FFH = 4096
L1I = 3072
TPC = BPC * T
ROUND_C = 1.5 * 2.0 ** 23



def build_expert_ffn():
    nc = bass.Bass("TRN2", target_bir_lowering=False, debug=False,
                   num_devices=NCORES, name="expert_ffn")
    NS = 16 * CAP
    NSC = 512
    HT = FFH // 128

    HALF = NS // 2

    buf_in = nc.dram_tensor("buf", [NS, C], BF16, kind="ExternalInput").ap()
    w1_in = nc.dram_tensor("w1", [C, FFH], F32, kind="ExternalInput").ap()
    b1_in = nc.dram_tensor("b1c", [128, HT], F32, kind="ExternalInput").ap()
    w2_in = nc.dram_tensor("w2", [FFH, C], F32, kind="ExternalInput").ap()
    b2_in = nc.dram_tensor("b2", [1, C], F32, kind="ExternalInput").ap()
    y_out = nc.dram_tensor("y", [NS, C], BF16, kind="ExternalOutput").ap()
    ones1 = nc.inline_tensor(np.ones((1, 128), ml_dtypes.bfloat16), "ones1").ap()

    A = nc.alloc_sbuf_tensor
    bufT = [A(f"bufT{i}", [128, HALF], BF16).ap() for i in range(8)]
    w1sb = [A(f"w1sb{i}", [128, FFH], BF16).ap() for i in range(8)]
    w2sb = [A(f"w2sb{i}", [128, C], BF16).ap() for i in range(HT)]
    b1sb = A("b1sb", [128, HT], F32).ap()
    b2row = A("b2row", [1, C], BF16).ap()
    onesb = A("onesb", [1, 128], BF16).ap()
    hsb = [A(f"hsb{i}", [128, NSC], BF16).ap() for i in range(HT)]
    ysb = [A(f"ysb{i}", [128, C], BF16).ap() for i in range(2)]

    ld = nc.alloc_semaphore("ld")
    ldg = nc.alloc_semaphore("ldg")
    xb = nc.alloc_semaphore("xb")
    mm = nc.alloc_semaphore("mm")
    evh = nc.alloc_semaphore("evh")
    evy = nc.alloc_semaphore("evy")
    st0 = nc.alloc_semaphore("st0")
    st1 = nc.alloc_semaphore("st1")

    pb_h = [nc.alloc_psum_tensor(f"pb_h{i}", [128, NSC], F32).ap() for i in range(4)]
    pb_y = [nc.alloc_psum_tensor(f"pb_y{i}", [128, 512], F32).ap() for i in range(2)]

    NLOADS = 2 + 8 + 1 + HT
    nsc_h = HALF // NSC

    with nc.Block() as block:
        @block.sync
        def _(sync):
            sync.dma_start(out=onesb, in_=ones1).then_inc(ld, 16)
            n_y = 0
            for half in range(2):
                if half == 1:
                    sync.wait_ge(mm, 5 * 40)
                for i in range(8):
                    sync.dma_start(
                        out=bufT[i],
                        in_=buf_in[half * HALF:(half + 1) * HALF, 128 * i:128 * (i + 1)],
                        transpose=True).then_inc(xb, 16)
                for hsc in range(nsc_h):
                    sc = half * nsc_h + hsc
                    for stile in range(4):
                        n_y += 1
                        sync.wait_ge(evy, sc * 8 + (stile + 1) * 2)
                        k = sc * 4 + stile
                        sync.dma_start(
                            out=y_out[NSC * sc + 128 * stile: NSC * sc + 128 * (stile + 1), :],
                            in_=ysb[k % 2]).then_inc(st0 if k % 2 == 0 else st1, 16)
            sync.wait_ge(st0, 16 * (n_y // 2))
            sync.wait_ge(st1, 16 * (n_y // 2))

        @block.gpsimd
        def _(g):
            g.dma_start(out=b2row, in_=b2_in).then_inc(ldg, 16)
            for i in range(8):
                g.dma_start(out=w1sb[i], in_=w1_in[128 * i:128 * (i + 1), :]) \
                    .then_inc(ldg, 16)
            g.dma_start(out=b1sb, in_=b1_in).then_inc(ldg, 16)
            for i in range(HT):
                g.dma_start(out=w2sb[i], in_=w2_in[128 * i:128 * (i + 1), :]) \
                    .then_inc(ldg, 16)

        @block.tensor
        def _(t):
            t.wait_ge(ld, 16)
            t.wait_ge(ldg, 16 * (NLOADS - 1))
            n_mm = 0
            n_evac = 0
            for half in range(2):
                t.wait_ge(xb, 16 * 8 * (half + 1))
                for sc in range(nsc_h):
                    for ht in range(HT):
                        kh = (half * nsc_h + sc) * HT + ht
                        if kh >= 4:
                            t.wait_ge(evh, kh - 3)
                        for ci in range(8):
                            ins = t.matmul(pb_h[ht % 4],
                                           w1sb[ci][:, 128 * ht:128 * (ht + 1)],
                                           bufT[ci][:, NSC * sc:NSC * (sc + 1)],
                                           start=(ci == 0), stop=(ci == 7))
                            if ci == 7:
                                ins.then_inc(mm, 1)
                        n_mm += 1
                        n_evac += 1
                    t.wait_ge(evh, (half * nsc_h + sc + 1) * HT)
                    for stile in range(4):
                        for c2 in range(2):
                            j = (half * nsc_h + sc) * 8 + stile * 2 + c2
                            if j >= 2:
                                t.wait_ge(evy, j - 1)
                            for hc in range(HT):
                                t.matmul(pb_y[c2],
                                         hsb[hc][:, 128 * stile:128 * (stile + 1)],
                                         w2sb[hc][:, 512 * c2:512 * (c2 + 1)],
                                         start=(hc == 0), stop=False)
                            t.matmul(pb_y[c2], onesb,
                                     b2row[:, 512 * c2:512 * (c2 + 1)],
                                     start=False, stop=True).then_inc(mm, 1)
                            n_mm += 1
                            n_evac += 1

        @block.scalar
        def _(s):
            n_mm = 0
            for sc in range(2 * nsc_h):
                for ht in range(HT):
                    n_mm += 1
                    s.wait_ge(mm, n_mm)
                    s.activation(hsb[ht], pb_h[ht % 4], AF.Relu,
                                 bias=b1sb[:, ht:ht + 1]).then_inc(evh, 1)
                n_mm += 8

        @block.vector
        def _(v):
            n_mm = 0
            n_y = 0
            for sc in range(2 * nsc_h):
                n_mm += HT
                for stile in range(4):
                    for c2 in range(2):
                        n_mm += 1
                        v.wait_ge(mm, n_mm)
                        if n_y >= 2:
                            v.wait_ge(st0 if n_y % 2 == 0 else st1,
                                      16 * (n_y // 2))
                        v.tensor_copy(
                            ysb[(sc * 4 + stile) % 2][:, 512 * c2:512 * (c2 + 1)],
                            pb_y[c2]).then_inc(evy, 1)
                    n_y += 1

    return nc


_PROG_CACHE = {}


def _get_prog(name, builder):
    if name not in _PROG_CACHE:
        _PROG_CACHE[name] = builder()
    return _PROG_CACHE[name]


def kernel(**inputs):
    raise NotImplementedError("assembled incrementally")




def _l0_host_tables():
    import math
    dim = C // 2
    pos = np.arange(dim, dtype=np.float64)
    freqs = (1.0 / 10000.0 ** (pos / dim))
    active = int(np.sum(freqs * 2 * math.pi * T >= 1.0))
    active = min(active, dim - C // 4)
    f = freqs[:active]
    t = np.arange(T, dtype=np.float64)
    phi = np.zeros((8 * 64, T), np.float32)
    psi = np.zeros((8 * 64, T), np.float32)
    cmax = np.zeros(16, np.float32)
    d = np.arange(-(T - 1), T, dtype=np.float64)
    for h in range(8):
        fh = f[32 * h:32 * (h + 1)]
        th = (np.cos(d[:, None] * fh) + np.sin(d[:, None] * fh)).sum(1)
        cmax[h] = th.max()
        ang = t[:, None] * fh[None, :]
        cos, sin = np.cos(ang), np.sin(ang)
        phi[64 * h:64 * h + 32] = cos.T
        phi[64 * h + 32:64 * h + 64] = sin.T
        psi[64 * h:64 * h + 32] = (cos - sin).T
        psi[64 * h + 32:64 * h + 64] = (cos + sin).T
    return phi, psi, cmax


BIG = 999999.0
L1_MAXPH = 99
L1_SCOREMODE = 2


def build_launch1():
    nc = bass.Bass("TRN2", target_bir_lowering=False, debug=False,
                   num_devices=NCORES, name="launch1")
    NT = T // 128

    x_in = nc.dram_tensor("x", [BPC, T, C], F32, kind="ExternalInput").ap()
    wqkvT_in = nc.dram_tensor("wqkvT", [C, 3 * C], F32, kind="ExternalInput").ap()
    bqkvc_in = nc.dram_tensor("bqkvc", [128, 24], F32, kind="ExternalInput").ap()
    bvrow_in = nc.dram_tensor("bvrow", [1, C], F32, kind="ExternalInput").ap()
    wprojT_in = nc.dram_tensor("wprojT", [C, C], F32, kind="ExternalInput").ap()
    bprow_in = nc.dram_tensor("bprow", [1, C], F32, kind="ExternalInput").ap()
    phiT_in = nc.dram_tensor("phiT", [512, T], BF16, kind="ExternalInput").ap()
    psiT_in = nc.dram_tensor("psiT", [512, T], BF16, kind="ExternalInput").ap()
    cmaxneg_in = nc.dram_tensor("cmaxneg", [128, 16], F32, kind="ExternalInput").ap()
    wr_in = nc.dram_tensor("wr", [C, 8], F32, kind="ExternalInput").ap()
    posind_in = nc.dram_tensor("posind", [8, 64], F32, kind="ExternalInput").ap()

    resid2_out = nc.dram_tensor("resid2", [BPC, T, C], F32, kind="ExternalOutput").ap()
    buf_out = nc.dram_tensor("bufpart", [NEXP * BPC * CAP, C], BF16,
                             kind="ExternalOutput").ap()
    gidx_out = nc.dram_tensor("gidx", [BPC, 2, T], I32, kind="ExternalOutput").ap()
    gate_out = nc.dram_tensor("gates", [BPC, 2, T], F32, kind="ExternalOutput").ap()

    lnscr2b = [nc.dram_tensor(f"lnscr{i}", [T, C], BF16, kind="Internal").ap()
               for i in range(2)]
    pcflip = nc.dram_tensor("pcflip", [1, 8], F32, kind="Internal").ap()

    tri = np.triu(np.ones((128, 128)), 1).astype(ml_dtypes.bfloat16)
    tri_in = nc.inline_tensor(tri, "tri").ap()
    allone_in = nc.inline_tensor(np.ones((128, 128), ml_dtypes.bfloat16), "allone").ap()
    pp = np.arange(128)[:, None]
    comp = np.zeros((128, 896), np.float32)
    comp[:, 512:] = 1.0
    comp[:, 384:512] = (pp <= np.arange(128)[None, :])
    comp_in = nc.inline_tensor(comp.astype(ml_dtypes.bfloat16), "maskcomp").ap()
    iota8w_in = nc.inline_tensor(
        np.tile(np.arange(8, dtype=np.float32), (128, T // 128)), "iota8w").ap()
    ones64_in = nc.inline_tensor(np.ones((1, 64), np.float32), "ones64c").ap()
    ones128_in = nc.inline_tensor(np.ones((1, 128), np.float32), "ones128c").ap()

    A = nc.alloc_sbuf_tensor
    xlnT = [A(f"xlnT{i}", [128, T], BF16).ap() for i in range(8)]
    v1 = [A(f"v1_{i}", [128, NHEAD * 65], BF16).ap() for i in range(NT)]
    catT = [A(f"catT{i}", [128, T], BF16).ap() for i in range(8)]
    wqkv = [A(f"wqkv{i}", [128, 3 * C], BF16).ap() for i in range(8)]
    wproj = [A(f"wproj{i}", [128, C], BF16).ap() for i in range(8)]
    wrsb = [A(f"wrsb{i}", [128, 8], BF16).ap() for i in range(8)]
    qt = A("qt", [128, T], BF16).ap()
    ktl = A("ktl", [128, T], BF16).ap()
    expb = [A(f"expb{i}", [128, 1024], BF16).ap() for i in range(2)]
    sphi2 = A("sphi", [128, T], BF16).ap()
    spsi2 = A("spsi", [128, T], BF16).ap()
    brow = sphi2.bitcast(F32)[0:1, 0:512]
    sln2b = A("sln2b", [128, C], BF16).ap()
    bqkv8 = A("bqkv8", [128, 24], F32).ap()
    vbias = A("vbias", [128, C], BF16).ap()
    pbias = A("pbias", [128, C], BF16).ap()
    cmaxs = A("cmaxs", [128, 16], F32).ap()
    sbcast = A("sbcast", [64, 512], F32).ap()
    srecip = sbcast[0:1, :]
    sones64 = A("sones64", [1, 64], F32).ap()
    sones128 = A("sones128", [1, 128], F32).ap()
    sx = A("sx", [128, C], F32).ap()
    sx2 = sx
    sres = A("sres", [128, C], F32).ap()
    sq_t = sres
    lncols = A("lncols", [128, 8], F32).ap()
    L = {k: lncols[:, i:i + 1]
         for i, k in enumerate(["ssum", "ssumsq", "smean", "sm2", "svar",
                                "ssqrt", "srstd"])}
    sln = A("sln", [128, C], BF16).ap()
    sln2 = sx.bitcast(BF16)[:, 0:C]
    strie = A("strie", [128, 128], BF16).ap()
    sallone = A("sallone", [128, 128], BF16).ap()
    scomp = A("scomp", [128, 896], BF16).ap()
    smask = [scomp[:, 384 - 128 * r:384 - 128 * r + 512] for r in range(4)]
    NT8 = (T // 128) * 8
    rlog = A("rlog", [128, NT8], F32).ap()
    rxs = A("rxs", [128, NT8], F32).ap()
    rmsk = A("rmsk", [128, NT8], F32).ap()
    roh0 = A("roh0", [128, NT8], F32).ap()
    roh1 = A("roh1", [128, NT8], F32).ap()
    rohs = A("rohs", [128, NT8], BF16).ap()
    rcum = A("rcum", [128, NT8], F32).ap()
    riota = A("riota", [128, NT8], F32).ap()
    NTc = T // 128
    rcolsA = A("rcolsA", [128, 15 * NTc], F32).ap()
    _c = lambda i: rcolsA[:, i * NTc:(i + 1) * NTc]
    rm1, rm2, re1, re2, rz_, rp1, rp2 = (_c(0), _c(1), _c(2), _c(3), _c(4),
                                         _c(5), _c(6))
    rpos = [_c(7), _c(8)]
    rkeep = [_c(9), _c(10)]
    rgate = [_c(11), _c(12)]
    ridxf = [_c(13), _c(14)]
    ridx = [A(f"ridx{k}", [128, NTc], I32).ap() for k in range(2)]
    spind = A("spind", [8, 64], F32).ap()
    spc = A("spc", [8, 1], F32).ap()
    spcr = A("spcr", [1, 8], F32).ap()

    P = nc.alloc_psum_tensor
    pbig = [P(f"pbig{i}", [128, 1024], F32).ap() for i in range(2)]
    pb_v = [pbig[0][:, 0:512], pbig[1][:, 0:512]]
    pb_s = [pbig[0][:, 512:1024], pbig[1][:, 512:1024]]
    pb_av_t = P("pb_av", [128, 512], F32).ap()
    pb_bc = P("pb_bc", [128, 512], F32).ap()
    pb_o2 = [P(f"pb_o2{i}", [128, 512], F32).ap() for i in range(2)]
    pb_o = pb_o2
    pb_avs = [pb_av_t, pb_o2[1]]

    s_ld = nc.alloc_semaphore("s_ld")
    s_ldg = nc.alloc_semaphore("s_ldg")
    s_mm = nc.alloc_semaphore("s_mm")
    s_sc = nc.alloc_semaphore("s_sc")
    s_ve = nc.alloc_semaphore("s_ve")
    s_st = nc.alloc_semaphore("s_st")
    s_sg = nc.alloc_semaphore("s_sg")
    lp5 = [nc.alloc_semaphore("lp5_0"), nc.alloc_semaphore("lp5_1")]
    sgp = [nc.alloc_semaphore("sgp_0"), nc.alloc_semaphore("sgp_1")]
    SEMS = {"ld": s_ld, "ldg": s_ldg, "mm": s_mm, "sc": s_sc, "ve": s_ve,
            "st": s_st, "sg": s_sg, "l50": lp5[0], "l51": lp5[1],
            "g50": sgp[0], "g51": sgp[1]}
    PRODUCER = {"ld": "sync", "st": "sync", "ldg": "gpsimd", "sg": "gpsimd",
                "mm": "tensor", "sc": "scalar", "ve": "vector",
                "l50": "sync", "l51": "sync", "g50": "gpsimd", "g51": "gpsimd"}

    def emit(eng, e):
        n = dict(ld=0, ldg=0, mm=0, sc=0, ve=0, st=0, sg=0, l50=0, l51=0,
                 g50=0, g51=0)
        breg = None
        if eng == "gpsimd":
            breg = e.alloc_register("bcreg")
            e.reg_mov(breg, NEXP * BPC * CAP - 1)

        def LD(out, in_, **k):
            n["ld"] += 16
            if eng == "sync":
                e.dma_start(out=out, in_=in_, **k).then_inc(s_ld, 16)

        def LDG(out, in_, **k):
            n["ldg"] += 16
            if eng == "gpsimd":
                e.dma_start(out=out, in_=in_, **k).then_inc(s_ldg, 16)

        def ST(out, in_, **k):
            n["st"] += 16
            if eng == "sync":
                e.dma_start(out=out, in_=in_, **k).then_inc(s_st, 16)

        def W(which, tgt, val=None):
            if eng == tgt:
                e.wait_ge(SEMS[which], n[which] if val is None else val)

        def MM(fn):
            n["mm"] += 1
            if eng == "tensor":
                fn().then_inc(s_mm, 1)

        def MME(fn):
            if eng == "tensor":
                fn()

        def SC(fn):
            n["sc"] += 1
            if eng == "scalar":
                fn().then_inc(s_sc, 1)

        def SCE(fn):
            if eng == "scalar":
                fn()

        def VE(fn):
            n["ve"] += 1
            if eng == "vector":
                fn().then_inc(s_ve, 1)

        def VEE(fn):
            if eng == "vector":
                fn()

        def SG(fn):
            n["sg"] += 16
            if eng == "gpsimd":
                fn().then_inc(s_sg, 16)

        def VEc(fn):
            W("ve", "vector")
            VE(fn)

        def SCc(fn):
            W("sc", "scalar")
            SC(fn)

        def BAR(*engines):
            for tgt in engines:
                for k, sem in SEMS.items():
                    if PRODUCER[k] != tgt and n[k] > 0:
                        if eng == tgt:
                            e.wait_ge(sem, n[k])

        LD(strie, tri_in)
        LD(sallone, allone_in)
        LD(scomp, comp_in)
        LD(riota, iota8w_in)
        LD(sones64, ones64_in)
        LD(sones128, ones128_in)
        LD(cmaxs, cmaxneg_in)
        LD(bqkv8, bqkvc_in)
        for i in range(8):
            LDG(wqkv[i], wqkvT_in[128 * i:128 * (i + 1), :])
            LDG(wproj[i], wprojT_in[128 * i:128 * (i + 1), :])
            LDG(wrsb[i], wr_in[128 * i:128 * (i + 1), :])
        W("ld", "vector")
        VE(lambda: e.memset(expb[0], 0.0))
        W("ve", "gpsimd")
        for zi in range(NEXP * BPC * CAP // 128):
            LDG(buf_out[128 * zi:128 * (zi + 1), :], expb[0])
        LD(spind, posind_in)
        W("ld", "vector")
        VE(lambda: e.reduce_sum(spc, spind, AX.X))
        W("ve", "sync")
        ST(pcflip.rearrange("a b -> b a"), spc)
        W("st", "sync")
        LD(spcr, pcflip)
        W("ld", "tensor")
        MM(lambda: e.matmul(pb_bc[:, 0:8], sones128, spcr, start=True, stop=True))
        W("mm", "scalar")
        W("ld", "scalar")
        SC(lambda: e.copy(cmaxs[:, 8:16], pb_bc[:, 0:8]))
        for dst, row_in in ((vbias, bvrow_in), (pbias, bprow_in)):
            for cc in range(2):
                W("sc", "sync")
                LD(brow, row_in[:, 512 * cc:512 * (cc + 1)])
                W("ld", "tensor")
                W("sc", "tensor")
                MM(lambda: e.matmul(pb_bc, sones128, brow, start=True, stop=True))
                W("mm", "scalar")
                SC(lambda dst=dst, cc=cc: e.copy(dst[:, 512 * cc:512 * (cc + 1)],
                                                 pb_bc))

        def PH_ph0(b, lnscr):
            BAR("sync", "scalar", "vector")
            for tt in range(NT):
                W("ve", "sync")
                LD(sx, x_in[b, 128 * tt:128 * (tt + 1), :])
                W("ld", "scalar")
                W("ve", "scalar")
                SC(lambda: e.activation(sq_t, sx, AF.Square, accum_out=L["ssumsq"]))
                W("ld", "vector")
                VE(lambda: e.reduce_sum(L["ssum"], sx, AX.X))
                W("sc", "vector")
                VEc(lambda: e.tensor_scalar(L["smean"], L["ssum"], 1.0 / C, None, ALU.mult))
                VEc(lambda: e.tensor_scalar(L["svar"], L["ssumsq"], 1.0 / C, None, ALU.mult))
                VEc(lambda: e.tensor_mul(L["sm2"], L["smean"], L["smean"]))
                VEc(lambda: e.tensor_sub(L["svar"], L["svar"], L["sm2"]))
                VEc(lambda: e.tensor_scalar(L["svar"], L["svar"], 1e-5, None, ALU.add))
                W("ve", "scalar")
                SC(lambda: e.activation(L["ssqrt"], L["svar"], AF.Sqrt))
                W("sc", "vector")
                VEc(lambda: e.reciprocal(L["srstd"], L["ssqrt"]))
                W("st", "vector")
                VEc(lambda: e.tensor_scalar(sln, sx, L["smean"], L["srstd"],
                                           ALU.subtract, ALU.mult))
                W("ve", "sync")
                ST(lnscr[128 * tt:128 * (tt + 1), :], sln)
        def PH_ph1(b, lnscr):
            W("st", "sync")
            W("mm", "sync")
            for ci in range(8):
                LD(xlnT[ci], lnscr[:, 128 * ci:128 * (ci + 1)], transpose=True)
            W("ld", "sync")
        def PH_ph2a(b, lnscr):
            BAR("tensor", "vector")
            for tt in range(NT):
                VE(lambda tt=tt: e.memset(
                    v1[tt].rearrange("p (h w) -> p h w", w=65)[:, :, 64:65], 1.0))
            for tt in range(NT):
                for vc in range(2):
                    pb = pb_v[vc]
                    if tt > 0:
                        W("ve", "tensor", val=n["ve"] - 1)
                    for ci in range(7):
                        MME(lambda pb=pb, ci=ci, tt=tt, vc=vc: e.matmul(
                            pb, xlnT[ci][:, 128 * tt:128 * (tt + 1)],
                            wqkv[ci][:, 2 * C + 512 * vc:2 * C + 512 * (vc + 1)],
                            start=(ci == 0), stop=False))
                    MM(lambda pb=pb, tt=tt, vc=vc: e.matmul(
                        pb, xlnT[7][:, 128 * tt:128 * (tt + 1)],
                        wqkv[7][:, 2 * C + 512 * vc:2 * C + 512 * (vc + 1)],
                        start=False, stop=True))
                    W("mm", "vector")
                    VE(lambda tt=tt, vc=vc, pb=pb: e.tensor_add(
                        v1[tt].rearrange("p (h w) -> p h w", w=65)
                        [:, 8 * vc:8 * (vc + 1), 0:64],
                        pb.rearrange("p (h w) -> p h w", w=64),
                        vbias[:, 512 * vc:512 * (vc + 1)]
                        .rearrange("p (h w) -> p h w", w=64)))
        def PH_ph2b(b, lnscr):
            proj_hist = []
            for hp in range(NHEAD // 2 if L1_MAXPH >= 2 else 0):
                h0 = 2 * hp
                for which, dst in ((0, qt), (1, ktl)):
                    for tc in range(4):
                        pb = pb_o[tc % 2]
                        if which == 0 and tc == 0:
                            W("ve", "tensor")
                        if len(proj_hist) >= 2:
                            W("ve", "tensor", val=proj_hist[-2])
                        for ci in range(7):
                            MME(lambda pb=pb, ci=ci, h0=h0, tc=tc, which=which:
                                e.matmul(
                                    pb,
                                    wqkv[ci][:, C * which + 64 * h0:
                                             C * which + 64 * (h0 + 2)],
                                    xlnT[ci][:, 512 * tc:512 * (tc + 1)],
                                    start=(ci == 0), stop=False))
                        MM(lambda pb=pb, h0=h0, tc=tc, which=which: e.matmul(
                            pb,
                            wqkv[7][:, C * which + 64 * h0:C * which + 64 * (h0 + 2)],
                            xlnT[7][:, 512 * tc:512 * (tc + 1)],
                            start=False, stop=True))
                        W("mm", "vector")
                        VE(lambda dst=dst, tc=tc, pb=pb, h0=h0, which=which:
                           e.tensor_scalar(dst[:, 512 * tc:512 * (tc + 1)], pb,
                                           bqkv8[:, 8 * which + h0 // 2:
                                                 8 * which + h0 // 2 + 1],
                                           0.125 if which == 0 else 1.0,
                                           ALU.add, ALU.mult))
                        proj_hist.append(n["ve"])
                if h0 < 8 and L1_SCOREMODE >= 2:
                    W("mm", "sync")
                    LD(sphi2, phiT_in[64 * h0:64 * (h0 + 2), :])
                    LD(spsi2, psiT_in[64 * h0:64 * (h0 + 2), :])
                for hh in range(2 if L1_SCOREMODE > 0 else 0):
                    h = h0 + hh
                    ext = (h < 8) and L1_SCOREMODE >= 2
                    prow = 64 * hh
                    for qc in range(4):
                        pb_av = pb_avs[qc % 2]
                        nk = 4 * qc + 4
                        groups = [list(range(g, min(g + 2, nk)))
                                  for g in range(0, nk, 2)]
                        gi = 0
                        for grp in groups:
                            pbs = pbig[gi % 2]
                            if gi >= 2:
                                W("sc", "tensor", val=n["sc"] - 1)
                            else:
                                W("sc", "tensor")
                            W("ld", "tensor")
                            W("ve", "tensor", val=proj_hist[-1])
                            for j, kti in enumerate(grp):
                                is_last = (j == len(grp) - 1)
                                fn1 = (lambda pbs=pbs, j=j, kti=kti, prow=prow,
                                       qc=qc: e.matmul(
                                           pbs[:, 512 * j:512 * (j + 1)],
                                           ktl[prow:prow + 64,
                                               128 * kti:128 * (kti + 1)],
                                           qt[prow:prow + 64,
                                              512 * qc:512 * (qc + 1)],
                                           start=True, stop=not ext))
                                fn2 = (lambda pbs=pbs, j=j, kti=kti, qc=qc,
                                       prow=prow: e.matmul(
                                           pbs[:, 512 * j:512 * (j + 1)],
                                           spsi2[prow:prow + 64,
                                                 128 * kti:128 * (kti + 1)],
                                           sphi2[prow:prow + 64,
                                                 512 * qc:512 * (qc + 1)],
                                           start=False, stop=True)) if ext else None
                                if not is_last:
                                    MME(fn1)
                                    if ext:
                                        MME(fn2)
                                else:
                                    if ext:
                                        MME(fn1)
                                        MM(fn2)
                                    else:
                                        MM(fn1)
                            W("mm", "scalar")
                            if gi >= 2:
                                W("mm", "scalar", val=n["mm"])
                            wid = 512 * len(grp)
                            if b == 0 and hp == 0 and hh == 0 and qc == 0 \
                                    and gi == 0:
                                W("ldg", "scalar")
                            SC(lambda gi=gi, pbs=pbs, h=h, wid=wid: e.activation(
                                expb[gi % 2][:, 0:wid], pbs[:, 0:wid], AF.Exp,
                                bias=cmaxs[:, h:h + 1]))
                            ndiag = sum(1 for kti in grp if kti >= 4 * qc)
                            if ndiag:
                                W("sc", "vector")
                                for j, kti in enumerate(grp):
                                    if kti >= 4 * qc:
                                        VE(lambda gi=gi, j=j, kti=kti, qc=qc:
                                           e.tensor_mul(
                                               expb[gi % 2][:, 512 * j:
                                                            512 * (j + 1)],
                                               expb[gi % 2][:, 512 * j:
                                                            512 * (j + 1)],
                                               smask[kti - 4 * qc]))
                                W("ve", "tensor")
                            else:
                                W("sc", "tensor")
                            for j, kti in enumerate(grp):
                                last = (kti == nk - 1)
                                fn = (lambda gi=gi, j=j, kti=kti, h=h, last=last:
                                      e.matmul(pb_av[0:65, :],
                                               v1[kti][:, 65 * h:65 * (h + 1)],
                                               expb[gi % 2][:, 512 * j:
                                                            512 * (j + 1)],
                                               start=(kti == 0), stop=last))
                                if last:
                                    MM(fn)
                                else:
                                    MME(fn)
                            gi += 1
                        W("mm", "vector")
                        VE(lambda: e.reciprocal(srecip, pb_av[64:65, :]))
                        W("ve", "tensor")
                        MM(lambda: e.matmul(pb_bc[0:64, :], sones64, srecip,
                                            start=True, stop=True))
                        W("mm", "scalar")
                        SC(lambda: e.copy(sbcast, pb_bc[0:64, :]))
                        W("sc", "vector")
                        VE(lambda h=h, qc=qc: e.tensor_mul(
                            catT[h // 2][64 * (h % 2):64 * (h % 2) + 64,
                                         512 * qc:512 * (qc + 1)],
                            pb_av[0:64, :], sbcast))
                        W("ve", "tensor")
        def PH_ph2c(b, lnscr):
            BAR("tensor", "sync", "vector")
            for tt in range(NT if L1_MAXPH >= 3 else 0):
                W("ve", "sync", val=n["ve"])
                LD(sx2, x_in[b, 128 * tt:128 * (tt + 1), :])
                for cc in range(2):
                    pb = pb_o[cc]
                    if tt > 0:
                        W("ve", "tensor", val=n["ve"] - 1)
                    for ci in range(7):
                        MME(lambda pb=pb, ci=ci, tt=tt, cc=cc: e.matmul(
                            pb, catT[ci][:, 128 * tt:128 * (tt + 1)],
                            wproj[ci][:, 512 * cc:512 * (cc + 1)],
                            start=(ci == 0), stop=False))
                    MM(lambda pb=pb, tt=tt, cc=cc: e.matmul(
                        pb, catT[7][:, 128 * tt:128 * (tt + 1)],
                        wproj[7][:, 512 * cc:512 * (cc + 1)],
                        start=False, stop=True))
                    W("mm", "vector")
                    W("ld", "vector")
                    VE(lambda cc=cc, pb=pb: e.tensor_add(
                        sres[:, 512 * cc:512 * (cc + 1)], pb,
                        pbias[:, 512 * cc:512 * (cc + 1)]))
                    VEc(lambda cc=cc: e.tensor_add(
                        sres[:, 512 * cc:512 * (cc + 1)],
                        sres[:, 512 * cc:512 * (cc + 1)],
                        sx2[:, 512 * cc:512 * (cc + 1)]))
                W("ve", "sync")
                ST(resid2_out[b, 128 * tt:128 * (tt + 1), :], sres)
                W("st", "vector")
        def PH_ph3(b, lnscr):
            BAR("sync", "scalar", "vector")
            for tt in range(NT):
                W("ve", "sync")
                LD(sx, resid2_out[b, 128 * tt:128 * (tt + 1), :])
                W("ld", "scalar")
                W("ve", "scalar")
                SC(lambda: e.activation(sq_t, sx, AF.Square, accum_out=L["ssumsq"]))
                W("ld", "vector")
                VE(lambda: e.reduce_sum(L["ssum"], sx, AX.X))
                W("sc", "vector")
                VEc(lambda: e.tensor_scalar(L["smean"], L["ssum"], 1.0 / C, None, ALU.mult))
                VEc(lambda: e.tensor_scalar(L["svar"], L["ssumsq"], 1.0 / C, None, ALU.mult))
                VEc(lambda: e.tensor_mul(L["sm2"], L["smean"], L["smean"]))
                VEc(lambda: e.tensor_sub(L["svar"], L["svar"], L["sm2"]))
                VEc(lambda: e.tensor_scalar(L["svar"], L["svar"], 1e-5, None, ALU.add))
                W("ve", "scalar")
                SC(lambda: e.activation(L["ssqrt"], L["svar"], AF.Sqrt))
                W("sc", "vector")
                VEc(lambda: e.reciprocal(L["srstd"], L["ssqrt"]))
                W("st", "vector")
                VEc(lambda: e.tensor_scalar(sln, sx, L["smean"], L["srstd"],
                                           ALU.subtract, ALU.mult))
                W("ve", "sync")
                ST(lnscr[128 * tt:128 * (tt + 1), :], sln)
        def PH_ph4(b, lnscr):
            W("st", "sync")
            W("mm", "sync")
            for ci in range(8):
                LD(xlnT[ci], lnscr[:, 128 * ci:128 * (ci + 1)], transpose=True)
            W("ld", "sync")
        def PH_ph5(b, lnscr):
            BAR("tensor", "vector", "scalar", "gpsimd", "sync")
            NTc = T // 128
            r3 = lambda ap: ap.rearrange("p (t e) -> p t e", e=8)
            col3 = lambda ap: ap.unsqueeze(2)
            bc8 = lambda ap: ap.unsqueeze(2).broadcast_to([128, NTc, 8])
            sc_hist = []
            for tt in range(NTc):
                pbr = pb_s[tt % 2]
                if tt >= 2:
                    W("sc", "tensor", val=sc_hist[-2])
                for ci in range(7):
                    MME(lambda ci=ci, tt=tt, pbr=pbr: e.matmul(
                        pbr[:, 0:8], xlnT[ci][:, 128 * tt:128 * (tt + 1)],
                        wrsb[ci], start=(ci == 0), stop=False))
                MM(lambda tt=tt, pbr=pbr: e.matmul(
                    pbr[:, 0:8], xlnT[7][:, 128 * tt:128 * (tt + 1)],
                    wrsb[7], start=False, stop=True))
                W("mm", "scalar")
                SC(lambda tt=tt, pbr=pbr: e.copy(rlog[:, 8 * tt:8 * (tt + 1)],
                                                 pbr[:, 0:8]))
                sc_hist.append(n["sc"])
            W("sc", "vector")
            VE(lambda: e.tensor_reduce(col3(rm1), r3(rlog), AX.X, ALU.max))
            VEc(lambda: e.tensor_tensor(r3(rmsk), r3(rlog), bc8(rm1),
                                        ALU.is_equal))
            VEc(lambda: e.scalar_tensor_tensor(rxs, rmsk, -BIG, rlog,
                                               ALU.mult, ALU.add))
            VEc(lambda: e.tensor_reduce(col3(rm2), r3(rxs), AX.X, ALU.max))
            VEc(lambda: e.tensor_scalar(rxs, riota, BIG, None, ALU.subtract))
            VEc(lambda: e.tensor_mul(rxs, rmsk, rxs))
            VEc(lambda: e.tensor_scalar(rxs, rxs, BIG, None, ALU.add))
            VEc(lambda: e.tensor_reduce(col3(re1), r3(rxs), AX.X, ALU.min))
            VEc(lambda: e.tensor_tensor(r3(rmsk), r3(rlog), bc8(rm2),
                                        ALU.is_equal))
            VEc(lambda: e.tensor_scalar(rxs, riota, BIG, None, ALU.subtract))
            VEc(lambda: e.tensor_mul(rxs, rmsk, rxs))
            VEc(lambda: e.tensor_scalar(rxs, rxs, BIG, None, ALU.add))
            VEc(lambda: e.tensor_reduce(col3(re2), r3(rxs), AX.X, ALU.min))
            VEc(lambda: e.tensor_tensor(r3(rxs), r3(rlog), bc8(rm1),
                                        ALU.subtract))
            W("ve", "scalar")
            SCc(lambda: e.activation(rxs, rxs, AF.Exp))
            W("sc", "vector")
            VEc(lambda: e.tensor_reduce(col3(rz_), r3(rxs), AX.X, ALU.add))
            VEc(lambda: e.reciprocal(rp1, rz_))
            VEc(lambda: e.tensor_sub(rm2, rm2, rm1))
            W("ve", "scalar")
            SCc(lambda: e.activation(rm2, rm2, AF.Exp))
            W("sc", "vector")
            VEc(lambda: e.tensor_mul(rp2, rm2, rp1))
            VEc(lambda: e.tensor_tensor(r3(roh0), r3(riota), bc8(re1),
                                        ALU.is_equal))
            VEc(lambda: e.tensor_tensor(r3(roh1), r3(riota), bc8(re2),
                                        ALU.is_equal))
            VEc(lambda: e.tensor_add(rohs, roh0, roh1))
            W("ve", "tensor")
            sc_hist = []
            for tt in range(NTc):
                pbr = pb_s[tt % 2]
                if len(sc_hist) >= 2:
                    W("sc", "tensor", val=sc_hist[-2])
                for s_ in range(tt):
                    MME(lambda s_=s_, pbr=pbr: e.matmul(
                        pbr[:, 0:8], sallone, rohs[:, 8 * s_:8 * (s_ + 1)],
                        start=(s_ == 0), stop=False))
                MM(lambda tt=tt, pbr=pbr: e.matmul(
                    pbr[:, 0:8], strie, rohs[:, 8 * tt:8 * (tt + 1)],
                    start=(tt == 0), stop=True))
                W("mm", "scalar")
                SC(lambda tt=tt, pbr=pbr: e.copy(rcum[:, 8 * tt:8 * (tt + 1)],
                                                 pbr[:, 0:8]))
                sc_hist.append(n["sc"])
            W("sc", "vector")
            for k in range(2):
                oh = (roh0, roh1)[k]
                ee = (re1, re2)[k]
                pp_ = (rp1, rp2)[k]
                VEc(lambda oh=oh: e.tensor_mul(rxs, oh, rcum))
                VEc(lambda k=k: e.tensor_reduce(col3(rpos[k]), r3(rxs), AX.X,
                                                ALU.add))
                VEc(lambda k=k: e.tensor_scalar(rkeep[k], rpos[k], float(CAP),
                                                None, ALU.is_lt))
                VEc(lambda k=k, ee=ee: e.scalar_tensor_tensor(
                    ridxf[k], ee, float(BPC * CAP), rpos[k], ALU.mult, ALU.add))
                VEc(lambda k=k, b=b: e.tensor_scalar(
                    ridxf[k], ridxf[k], float(b * CAP), None, ALU.add))
                VEc(lambda k=k: e.tensor_mul(ridxf[k], ridxf[k], rkeep[k]))
                VEc(lambda k=k: e.tensor_scalar(rm1, rkeep[k], -1.0, None,
                                                ALU.mult))
                VEc(lambda: e.tensor_scalar(rm1, rm1, 1.0, None, ALU.add))
                VEc(lambda k=k: e.scalar_tensor_tensor(
                    ridxf[k], rm1, BIG, ridxf[k], ALU.mult, ALU.add))
                W("sg", "vector")
                VEc(lambda k=k: e.tensor_copy(ridx[k], ridxf[k]))
                W("st", "vector")
                VEc(lambda k=k, pp_=pp_: e.tensor_mul(rgate[k], pp_, rkeep[k]))
            W("ldg", "gpsimd")
            for tt in range(NTc):
                p5 = tt % 2
                stage = (sln2, sln2b)[p5]
                gk = "g50" if p5 == 0 else "g51"
                lk = "l50" if p5 == 0 else "l51"
                if tt >= 2:
                    W(gk, "sync")
                W("ve", "sync", val=n["ve"])
                n[lk] += 16
                if eng == "sync":
                    e.dma_start(out=stage,
                                in_=lnscr[128 * tt:128 * (tt + 1), :]) \
                        .then_inc(lp5[p5], 16)
                W(lk, "gpsimd")
                if tt == 0:
                    W("ve", "gpsimd")
                for k in range(2):
                    n[gk] += 16
                    if eng == "gpsimd":
                        e.indirect_dma_start(
                            buf_out[:, :],
                            IndirectOffsetOnAxis(ridx[k][:, tt:tt + 1], 0),
                            stage, None, bounds_check=breg,
                            oob_is_err=False).then_inc(sgp[p5], 16)
            if eng == "gpsimd":
                e.wait_ge(sgp[0], n["g50"])
                e.wait_ge(sgp[1], n["g51"])
            W("ve", "gpsimd")
            for k in range(2):
                SG(lambda k=k, b=b: e.dma_start(
                    out=gidx_out[b, k, :].rearrange("(t p) -> p t", p=128),
                    in_=ridx[k], allow_slow_non_contiguous=True))
            W("ve", "sync")
            for k in range(2):
                ST(gate_out[b, k, :].rearrange("(t p) -> p t", p=128),
                   rgate[k], allow_slow_non_contiguous=True)

        PH_ph0(0, lnscr2b[0])
        PH_ph1(0, lnscr2b[0])
        PH_ph2a(0, lnscr2b[0])
        PH_ph2b(0, lnscr2b[0])
        PH_ph0(1, lnscr2b[1])
        PH_ph2c(0, lnscr2b[0])
        PH_ph3(0, lnscr2b[0])
        PH_ph4(0, lnscr2b[0])
        PH_ph5(0, lnscr2b[0])
        PH_ph1(1, lnscr2b[1])
        PH_ph2a(1, lnscr2b[1])
        PH_ph2b(1, lnscr2b[1])
        PH_ph2c(1, lnscr2b[1])
        PH_ph3(1, lnscr2b[1])
        PH_ph4(1, lnscr2b[1])
        PH_ph5(1, lnscr2b[1])

        BAR("sync", "gpsimd", "tensor", "scalar", "vector")
        return n

    with nc.Block() as block:
        @block.sync
        def _(e):
            emit("sync", e)

        @block.gpsimd
        def _(e):
            emit("gpsimd", e)

        @block.tensor
        def _(e):
            emit("tensor", e)

        @block.scalar
        def _(e):
            emit("scalar", e)

        @block.vector
        def _(e):
            emit("vector", e)

    return nc



def build_launch3():
    nc = bass.Bass("TRN2", target_bir_lowering=False, debug=False,
                   num_devices=NCORES, name="launch3")
    NT = T // 128
    NS = NEXP * BPC * CAP

    r2_in = nc.dram_tensor("resid2", [BPC, T, C], F32, kind="ExternalInput").ap()
    yall_in = nc.dram_tensor("yall", [NS, C], BF16, kind="ExternalInput").ap()
    gidx_in = nc.dram_tensor("gidx", [BPC, 2, T], I32, kind="ExternalInput").ap()
    gate_in = nc.dram_tensor("gates", [BPC, 2, T], F32, kind="ExternalInput").ap()
    w_in = {w: nc.dram_tensor("w" + w, [C, C], F32, kind="ExternalInput").ap()
            for w in "qkvo"}
    b_in = {w: nc.dram_tensor("b" + w, [1, C], F32, kind="ExternalInput").ap()
            for w in "qkvo"}

    r4_out = nc.dram_tensor("resid4", [BPC, T, C], F32, kind="ExternalOutput").ap()
    xln4_out = nc.dram_tensor("xln4", [BPC, T, C], BF16, kind="ExternalOutput").ap()
    amax_out = nc.dram_tensor("amax1", [1, 1], F32, kind="ExternalOutput").ap()

    r3scr = nc.dram_tensor("r3scr", [BPC, T, C], F32, kind="Internal").ap()
    lnscr = nc.dram_tensor("lnscr3", [T, C], BF16, kind="Internal").ap()
    qkvscr = nc.dram_tensor("qkvscr", [T, C], BF16, kind="Internal").ap()
    flipscr = nc.dram_tensor("flipscr", [1, 128], F32, kind="Internal").ap()
    ones128_in = nc.inline_tensor(np.ones((1, 128), np.float32), "ones128c3").ap()

    A = nc.alloc_sbuf_tensor
    xlnT = [A(f"xlnT{i}", [128, T], BF16).ap() for i in range(8)]
    qkvT = [A(f"qkvT{i}", [128, T], BF16).ap() for i in range(8)]
    wsb = {w: [A(f"w_{w}{i}", [128, C], BF16).ap() for i in range(8)] for w in "qkvo"}
    bbc = {w: A(f"bbc_{w}", [128, C], BF16).ap() for w in "qkvo"}
    brow = A("brow3", [1, 512], F32).ap()
    sones128 = A("sones128_3", [1, 128], F32).ap()
    sx = [A(f"sx3_{i}", [128, C], F32).ap() for i in range(2)]
    sres = [A(f"sres3_{i}", [128, C], F32).ap() for i in range(2)]
    qk = [A(f"qk3_{i}", [128, 2 * C], F32).ap() for i in range(2)]
    vf = [A(f"vf3_{i}", [128, C], F32).ap() for i in range(2)]
    coeff = [A(f"coeff3_{i}", [128, 16], F32).ap() for i in range(2)]
    qkvsb = [A(f"qkvsb{i}", [128, C], BF16).ap() for i in range(2)]
    y0 = [A(f"y0sb{i}", [128, C], BF16).ap() for i in range(2)]
    y1 = [A(f"y1sb{i}", [128, C], BF16).ap() for i in range(2)]
    gi0 = [A(f"gi0_{i}", [128, 1], I32).ap() for i in range(2)]
    gi1 = [A(f"gi1_{i}", [128, 1], I32).ap() for i in range(2)]
    gt0 = [A(f"gt0_{i}", [128, 1], F32).ap() for i in range(2)]
    gt1 = [A(f"gt1_{i}", [128, 1], F32).ap() for i in range(2)]
    lncols = [A(f"l3cols{i}", [128, 8], F32).ap() for i in range(2)]
    LNK = ["ssum", "ssumsq", "smean", "sm2", "svar", "ssqrt", "srstd"]
    L2 = [{k: lncols[i][:, j:j + 1] for j, k in enumerate(LNK)}
          for i in range(2)]
    sln = [A(f"sln3_{i}", [128, C], BF16).ap() for i in range(2)]
    samax = A("samax", [128, 1], F32).ap()
    stmp1 = A("stmp1_3", [128, 1], F32).ap()
    sflip = A("sflip", [1, 128], F32).ap()
    sam1 = A("sam1", [1, 1], F32).ap()

    P = nc.alloc_psum_tensor
    pb = [P(f"pb3_{i}", [128, 512], F32).ap() for i in range(4)]
    pb_bc = P("pb3_bc", [128, 512], F32).ap()

    s_ld = nc.alloc_semaphore("s_ld")
    s_ldg = nc.alloc_semaphore("s_ldg")
    s_mm = nc.alloc_semaphore("s_mm")
    s_sc = nc.alloc_semaphore("s_sc")
    s_ve = nc.alloc_semaphore("s_ve")
    s_st = nc.alloc_semaphore("s_st")
    s_sg = nc.alloc_semaphore("s_sg")
    sp3 = [nc.alloc_semaphore("sp3_0"), nc.alloc_semaphore("sp3_1")]
    SEMS = {"ld": s_ld, "ldg": s_ldg, "mm": s_mm, "sc": s_sc, "ve": s_ve,
            "st": s_st, "sg": s_sg, "sp0": sp3[0], "sp1": sp3[1]}
    PRODUCER = {"ld": "sync", "st": "sync", "ldg": "gpsimd", "sg": "gpsimd",
                "mm": "tensor", "sc": "scalar", "ve": "vector",
                "sp0": "sync", "sp1": "sync"}

    def emit(eng, e):
        n = dict(ld=0, ldg=0, mm=0, sc=0, ve=0, st=0, sg=0, sp0=0, sp1=0)
        breg = None
        if eng == "gpsimd":
            breg = e.alloc_register("bcreg3")
            e.reg_mov(breg, NS - 1)

        def STP(p, out, in_):
            key = "sp0" if p == 0 else "sp1"
            n[key] += 16
            if eng == "sync":
                e.dma_start(out=out, in_=in_).then_inc(sp3[p], 16)

        def LD(out, in_, **k):
            n["ld"] += 16
            if eng == "sync":
                e.dma_start(out=out, in_=in_, **k).then_inc(s_ld, 16)

        def LDG(out, in_, **k):
            n["ldg"] += 16
            if eng == "gpsimd":
                e.dma_start(out=out, in_=in_, **k).then_inc(s_ldg, 16)

        def ST(out, in_):
            n["st"] += 16
            if eng == "sync":
                e.dma_start(out=out, in_=in_).then_inc(s_st, 16)

        def W(which, tgt, val=None):
            if eng == tgt:
                e.wait_ge(SEMS[which], n[which] if val is None else val)

        def MM(fn):
            n["mm"] += 1
            if eng == "tensor":
                fn().then_inc(s_mm, 1)

        def MME(fn):
            if eng == "tensor":
                fn()

        def SC(fn):
            n["sc"] += 1
            if eng == "scalar":
                fn().then_inc(s_sc, 1)

        def VE(fn):
            n["ve"] += 1
            if eng == "vector":
                fn().then_inc(s_ve, 1)

        def SG(fn):
            n["sg"] += 16
            if eng == "gpsimd":
                fn().then_inc(s_sg, 16)

        def VEc(fn):
            W("ve", "vector")
            VE(fn)

        def SCc(fn):
            W("sc", "scalar")
            SC(fn)

        def BAR(*engines):
            for tgt in engines:
                for k, sem in SEMS.items():
                    if PRODUCER[k] != tgt and n[k] > 0:
                        if eng == tgt:
                            e.wait_ge(sem, n[k])

        def LN_CHAIN(xin, Lc, dst_bf, extra_amax=False):
            W("ld", "scalar")
            W("ve", "scalar")
            SC(lambda: e.activation(sq_scr, xin, AF.Square,
                                    accum_out=Lc["ssumsq"]))
            W("ld", "vector")
            VE(lambda: e.reduce_sum(Lc["ssum"], xin, AX.X))
            W("sc", "vector")
            VEc(lambda: e.tensor_scalar(Lc["smean"], Lc["ssum"], 1.0 / C, None,
                                        ALU.mult))
            VEc(lambda: e.tensor_scalar(Lc["svar"], Lc["ssumsq"], 1.0 / C, None,
                                        ALU.mult))
            VEc(lambda: e.tensor_mul(Lc["sm2"], Lc["smean"], Lc["smean"]))
            VEc(lambda: e.tensor_sub(Lc["svar"], Lc["svar"], Lc["sm2"]))
            VEc(lambda: e.tensor_scalar(Lc["svar"], Lc["svar"], 1e-5, None,
                                        ALU.add))
            W("ve", "scalar")
            SC(lambda: e.activation(Lc["ssqrt"], Lc["svar"], AF.Sqrt))
            W("sc", "vector")
            VEc(lambda: e.reciprocal(Lc["srstd"], Lc["ssqrt"]))
            W("sp0", "vector")
            W("sp1", "vector")
            VEc(lambda: e.tensor_scalar(dst_bf, xin, Lc["smean"], Lc["srstd"],
                                        ALU.subtract, ALU.mult))
            if extra_amax:
                VEc(lambda: e.tensor_reduce(stmp1, dst_bf, AX.X, ALU.max,
                                            apply_absolute_value=True))
                VEc(lambda: e.tensor_max(samax, samax, stmp1))

        sq_scr = qk[0][:, 0:C]

        LD(sones128, ones128_in)
        for w in "qkvo":
            for i in range(8):
                LDG(wsb[w][i], w_in[w][128 * i:128 * (i + 1), :])
        for w in "qkvo":
            for cc in range(2):
                W("sc", "sync")
                LD(brow, b_in[w][:, 512 * cc:512 * (cc + 1)])
                W("ld", "tensor")
                W("sc", "tensor")
                MM(lambda: e.matmul(pb_bc, sones128, brow, start=True, stop=True))
                W("mm", "scalar")
                SC(lambda w=w, cc=cc: e.copy(bbc[w][:, 512 * cc:512 * (cc + 1)],
                                             pb_bc))
        W("ld", "vector")
        VE(lambda: e.memset(samax, 0.0))

        for b in range(BPC):
            BAR("sync", "scalar", "vector", "gpsimd")
            ve_hist = []
            sg_hist = []
            for tt in range(NT):
                p = tt % 2
                if tt >= 2:
                    W("ve", "sync", val=ve_hist[-2])
                    W("sg", "sync", val=sg_hist[-2])
                    W("sp0" if p == 0 else "sp1", "sync")
                LD(sx[p], r2_in[b, 128 * tt:128 * (tt + 1), :])
                LD(gi0[p], gidx_in[b, 0, 128 * tt:128 * (tt + 1)]
                   .rearrange("(p one) -> p one", one=1))
                LD(gi1[p], gidx_in[b, 1, 128 * tt:128 * (tt + 1)]
                   .rearrange("(p one) -> p one", one=1))
                LD(gt0[p], gate_in[b, 0, 128 * tt:128 * (tt + 1)]
                   .rearrange("(p one) -> p one", one=1))
                LD(gt1[p], gate_in[b, 1, 128 * tt:128 * (tt + 1)]
                   .rearrange("(p one) -> p one", one=1))
                if tt >= 2:
                    W("sg", "vector", val=sg_hist[-2])
                VE(lambda p=p: e.memset(y0[p], 0.0))
                VE(lambda p=p: e.memset(y1[p], 0.0))
                W("ve", "gpsimd")
                W("ld", "gpsimd")
                SG(lambda p=p: e.indirect_dma_start(
                    y0[p], None, yall_in[:, :], IndirectOffsetOnAxis(gi0[p], 0),
                    bounds_check=breg, oob_is_err=False))
                SG(lambda p=p: e.indirect_dma_start(
                    y1[p], None, yall_in[:, :], IndirectOffsetOnAxis(gi1[p], 0),
                    bounds_check=breg, oob_is_err=False))
                sg_hist.append(n["sg"])
                W("ld", "vector")
                W("sg", "vector")
                VEc(lambda p=p: e.scalar_tensor_tensor(sx[p], y0[p], gt0[p],
                                                       sx[p], ALU.mult, ALU.add))
                VEc(lambda p=p: e.scalar_tensor_tensor(sx[p], y1[p], gt1[p],
                                                       sx[p], ALU.mult, ALU.add))
                ve_hist.append(n["ve"])
                W("ve", "sync")
                STP(p, r3scr[b, 128 * tt:128 * (tt + 1), :], sx[p])
            if eng == "sync":
                e.wait_ge(sp3[0], n["sp0"])
                e.wait_ge(sp3[1], n["sp1"])
            ve_hist = []
            for tt in range(NT):
                p = tt % 2
                if tt >= 2:
                    W("ve", "sync", val=ve_hist[-2])
                    W("sp0" if p == 0 else "sp1", "sync")
                LD(sx[p], r3scr[b, 128 * tt:128 * (tt + 1), :])
                LN_CHAIN(sx[p], L2[p], sln[p])
                ve_hist.append(n["ve"])
                W("ve", "sync")
                STP(p, lnscr[128 * tt:128 * (tt + 1), :], sln[p])

            W("st", "sync")
            if eng == "sync":
                e.wait_ge(sp3[0], n["sp0"])
                e.wait_ge(sp3[1], n["sp1"])
            W("mm", "sync")
            for ci in range(8):
                LD(xlnT[ci], lnscr[:, 128 * ci:128 * (ci + 1)], transpose=True)
            W("ld", "sync")

            BAR("tensor", "vector", "scalar")
            ve_hist = []
            for tt in range(NT):
                p = tt % 2
                dsts = [(qk[p][:, 0:C], "q"), (qk[p][:, C:2 * C], "k"),
                        (vf[p], "v")]
                for wi, (dstf, w) in enumerate(dsts):
                    for cc in range(2):
                        g = tt * 6 + wi * 2 + cc
                        bank = pb[g % 3]
                        if g >= 3:
                            W("ve", "tensor", val=n["ve"] - 2)
                        for ci in range(7):
                            MME(lambda bank=bank, w=w, ci=ci, tt=tt, cc=cc:
                                e.matmul(
                                    bank, xlnT[ci][:, 128 * tt:128 * (tt + 1)],
                                    wsb[w][ci][:, 512 * cc:512 * (cc + 1)],
                                    start=(ci == 0), stop=False))
                        MM(lambda bank=bank, w=w, tt=tt, cc=cc: e.matmul(
                            bank, xlnT[7][:, 128 * tt:128 * (tt + 1)],
                            wsb[w][7][:, 512 * cc:512 * (cc + 1)],
                            start=False, stop=True))
                        W("mm", "vector")
                        VE(lambda dstf=dstf, w=w, cc=cc, bank=bank: e.tensor_add(
                            dstf[:, 512 * cc:512 * (cc + 1)], bank,
                            bbc[w][:, 512 * cc:512 * (cc + 1)]))
                W("ve", "scalar")
                SCc(lambda p=p: e.activation(qk[p], qk[p], AF.Square))
                SCc(lambda p=p: e.activation(qk[p], qk[p], AF.Exp, scale=-0.5))
                W("sc", "vector")
                VEc(lambda p=p: e.tensor_mul(qk[p][:, 0:C], qk[p][:, 0:C],
                                             qk[p][:, C:2 * C]))
                VEc(lambda p=p: e.tensor_reduce(
                    coeff[p].unsqueeze(2),
                    qk[p][:, 0:C].rearrange("p (h w) -> p h w", w=64),
                    AX.X, ALU.add))
                W("st", "vector")
                VEc(lambda p=p: e.tensor_mul(
                    qkvsb[p].rearrange("p (h w) -> p h w", w=64),
                    vf[p].rearrange("p (h w) -> p h w", w=64),
                    coeff[p].unsqueeze(2).broadcast_to([128, 16, 64])))
                ve_hist.append(n["ve"])
                W("ve", "sync")
                ST(qkvscr[128 * tt:128 * (tt + 1), :], qkvsb[p])

            W("st", "sync")
            W("mm", "sync")
            for ci in range(8):
                LD(qkvT[ci], qkvscr[:, 128 * ci:128 * (ci + 1)], transpose=True)
            W("ld", "sync")

            BAR("tensor", "sync", "vector", "scalar")
            ve_hist = []
            for tt in range(NT):
                p = tt % 2
                if tt >= 2:
                    W("ve", "sync", val=ve_hist[-2])
                    W("sp0" if p == 0 else "sp1", "sync")
                LD(sx[p], r3scr[b, 128 * tt:128 * (tt + 1), :])
                for cc in range(2):
                    g = tt * 2 + cc
                    bank = pb[g % 3]
                    if g >= 3:
                        W("ve", "tensor", val=n["ve"] - 3)
                    for ci in range(7):
                        MME(lambda bank=bank, ci=ci, tt=tt, cc=cc: e.matmul(
                            bank, qkvT[ci][:, 128 * tt:128 * (tt + 1)],
                            wsb["o"][ci][:, 512 * cc:512 * (cc + 1)],
                            start=(ci == 0), stop=False))
                    MM(lambda bank=bank, tt=tt, cc=cc: e.matmul(
                        bank, qkvT[7][:, 128 * tt:128 * (tt + 1)],
                        wsb["o"][7][:, 512 * cc:512 * (cc + 1)],
                        start=False, stop=True))
                    W("mm", "vector")
                    W("ld", "vector")
                    VE(lambda cc=cc, bank=bank, p=p: e.tensor_add(
                        sres[p][:, 512 * cc:512 * (cc + 1)], bank,
                        bbc["o"][:, 512 * cc:512 * (cc + 1)]))
                    VEc(lambda cc=cc, p=p: e.tensor_add(
                        sres[p][:, 512 * cc:512 * (cc + 1)],
                        sres[p][:, 512 * cc:512 * (cc + 1)],
                        sx[p][:, 512 * cc:512 * (cc + 1)]))
                W("ve", "sync")
                STP(p, r4_out[b, 128 * tt:128 * (tt + 1), :], sres[p])
                W("ve", "scalar")
                W("sp0" if p == 0 else "sp1", "scalar")
                SC(lambda p=p: e.activation(sx[p], sres[p], AF.Square,
                                            accum_out=L2[p]["ssumsq"]))
                VEc(lambda p=p: e.reduce_sum(L2[p]["ssum"], sres[p], AX.X))
                W("sc", "vector")
                VEc(lambda p=p: e.tensor_scalar(L2[p]["smean"], L2[p]["ssum"],
                                                1.0 / C, None, ALU.mult))
                VEc(lambda p=p: e.tensor_scalar(L2[p]["svar"], L2[p]["ssumsq"],
                                                1.0 / C, None, ALU.mult))
                VEc(lambda p=p: e.tensor_mul(L2[p]["sm2"], L2[p]["smean"],
                                             L2[p]["smean"]))
                VEc(lambda p=p: e.tensor_sub(L2[p]["svar"], L2[p]["svar"],
                                             L2[p]["sm2"]))
                VEc(lambda p=p: e.tensor_scalar(L2[p]["svar"], L2[p]["svar"],
                                                1e-5, None, ALU.add))
                W("ve", "scalar")
                SC(lambda p=p: e.activation(L2[p]["ssqrt"], L2[p]["svar"],
                                            AF.Sqrt))
                W("sc", "vector")
                VEc(lambda p=p: e.reciprocal(L2[p]["srstd"], L2[p]["ssqrt"]))
                W("sp0" if p == 0 else "sp1", "vector")
                VEc(lambda p=p: e.tensor_scalar(sln[p], sres[p], L2[p]["smean"],
                                                L2[p]["srstd"],
                                                ALU.subtract, ALU.mult))
                VEc(lambda p=p: e.tensor_reduce(stmp1, sln[p], AX.X, ALU.max,
                                                apply_absolute_value=True))
                VEc(lambda: e.tensor_max(samax, samax, stmp1))
                ve_hist.append(n["ve"])
                W("ve", "sync")
                STP(p, xln4_out[b, 128 * tt:128 * (tt + 1), :], sln[p])

        BAR("sync", "vector")
        ST(flipscr.rearrange("a b -> b a"), samax)
        W("st", "sync")
        LD(sflip, flipscr)
        W("ld", "vector")
        VEc(lambda: e.tensor_reduce(sam1, sflip, AX.X, ALU.max))
        W("ve", "sync")
        ST(amax_out, sam1)
        BAR("sync", "gpsimd", "tensor", "scalar", "vector")
        return n

    with nc.Block() as block:
        @block.sync
        def _(e):
            emit("sync", e)

        @block.gpsimd
        def _(e):
            emit("gpsimd", e)

        @block.tensor
        def _(e):
            emit("tensor", e)

        @block.scalar
        def _(e):
            emit("scalar", e)

        @block.vector
        def _(e):
            emit("vector", e)

    return nc



def _qdq_emit_helpers(nc, e, eng, n, SEMS, PRODUCER):
    h = {}

    def LD(out, in_, **k):
        n["ld"] += 16
        if eng == "sync":
            e.dma_start(out=out, in_=in_, **k).then_inc(SEMS["ld"], 16)

    def LDG(out, in_, **k):
        n["ldg"] += 16
        if eng == "gpsimd":
            e.dma_start(out=out, in_=in_, **k).then_inc(SEMS["ldg"], 16)

    def ST(out, in_):
        n["st"] += 16
        if eng == "sync":
            e.dma_start(out=out, in_=in_).then_inc(SEMS["st"], 16)

    def STG(out, in_):
        n["sg"] += 16
        if eng == "gpsimd":
            e.dma_start(out=out, in_=in_).then_inc(SEMS["sg"], 16)

    def W(which, tgt, val=None):
        if eng == tgt:
            e.wait_ge(SEMS[which], n[which] if val is None else val)

    def MM(fn):
        n["mm"] += 1
        if eng == "tensor":
            fn().then_inc(SEMS["mm"], 1)

    def MME(fn):
        if eng == "tensor":
            fn()

    def SC(fn):
        n["sc"] += 1
        if eng == "scalar":
            fn().then_inc(SEMS["sc"], 1)

    def VE(fn):
        n["ve"] += 1
        if eng == "vector":
            fn().then_inc(SEMS["ve"], 1)

    def VEc(fn):
        W("ve", "vector")
        VE(fn)

    def SCc(fn):
        W("sc", "scalar")
        SC(fn)

    def BAR(*engines):
        for tgt in engines:
            for k, sem in SEMS.items():
                if PRODUCER[k] != tgt and n[k] > 0:
                    if eng == tgt:
                        e.wait_ge(sem, n[k])

    h.update(LD=LD, LDG=LDG, ST=ST, STG=STG, W=W, MM=MM, MME=MME, SC=SC,
             VE=VE, VEc=VEc, SCc=SCc, BAR=BAR)
    return h


def _mk_sems(nc):
    SEMS = {k: nc.alloc_semaphore("s_" + k)
            for k in ["ld", "ldg", "mm", "sc", "ve", "st", "sg"]}
    PRODUCER = {"ld": "sync", "st": "sync", "ldg": "gpsimd", "sg": "gpsimd",
                "mm": "tensor", "sc": "scalar", "ve": "vector"}
    return SEMS, PRODUCER


def _run_block(nc, emit):
    with nc.Block() as block:
        @block.sync
        def _(e):
            emit("sync", e)

        @block.gpsimd
        def _(e):
            emit("gpsimd", e)

        @block.tensor
        def _(e):
            emit("tensor", e)

        @block.scalar
        def _(e):
            emit("scalar", e)

        @block.vector
        def _(e):
            emit("vector", e)
    return nc


def build_launch4():
    nc = bass.Bass("TRN2", target_bir_lowering=False, debug=False,
                   num_devices=NCORES, name="launch4")
    NT = TPC // 128
    IT = L1I // 128

    xln4_in = nc.dram_tensor("xln4", [TPC, C], BF16, kind="ExternalInput").ap()
    am_in = nc.dram_tensor("amax1all", [1, 8], F32, kind="ExternalInput").ap()
    wup_in = nc.dram_tensor("wup", [L1I, C], F32, kind="ExternalInput").ap()
    wg_in = nc.dram_tensor("wgate", [L1I, C], F32, kind="ExternalInput").ap()
    up_out = nc.dram_tensor("up", [L1I, TPC], BF16, kind="ExternalOutput").ap()
    gate_out = nc.dram_tensor("gate", [L1I, TPC], BF16, kind="ExternalOutput").ap()
    am2_out = nc.dram_tensor("amax2", [1, 2], F32, kind="ExternalOutput").ap()

    xqscr = nc.dram_tensor("xqscr", [TPC, C], BF16, kind="Internal").ap()
    wdscr = nc.dram_tensor("wdscr", [2, L1I, C], BF16, kind="Internal").ap()
    flipscr = nc.dram_tensor("flip4", [2, 128], F32, kind="Internal").ap()
    ones128_in = nc.inline_tensor(np.ones((1, 128), np.float32), "ones4").ap()

    A = nc.alloc_sbuf_tensor
    sam = A("sam4", [1, 8], F32).ap()
    sm1 = A("sm1", [1, 1], F32).ap()
    ssc = A("ssc4", [1, 2], F32).ap()
    sscb = A("sscb4", [128, 2], F32).ap()
    sones = A("sones4", [1, 128], F32).ap()
    sx = A("sx4", [128, C], BF16).ap()
    sxq = A("sxq", [128, C], F32).ap()
    sxqb = A("sxqb", [128, C], BF16).ap()
    sw = A("sw4", [128, C], F32).ap()
    swa = A("swa", [128, 32], F32).ap()
    sws = A("sws", [128, 32], F32).ap()
    swr = A("swr", [128, 32], F32).ap()
    swq = A("swq", [128, C], F32).ap()
    swqb = A("swqb", [128, C], BF16).ap()
    xqT = [A(f"xqT{i}", [128, TPC], BF16).ap() for i in range(8)]
    wT = [A(f"wT4_{i}", [128, L1I], BF16).ap() for i in range(8)]
    outb = [A(f"outb4_{i}", [128, 512], BF16).ap() for i in range(2)]
    samax = [A(f"samax4_{i}", [128, 1], F32).ap() for i in range(2)]
    stmp1 = A("stmp14", [128, 1], F32).ap()
    sflip = A("sflip4", [1, 128], F32).ap()
    sam2 = A("sam2o", [1, 2], F32).ap()

    P = nc.alloc_psum_tensor
    pb = [P(f"pb4_{i}", [128, 512], F32).ap() for i in range(2)]
    pb_bc = P("pb4_bc", [128, 2], F32).ap()

    SEMS, PRODUCER = _mk_sems(nc)
    sp = [nc.alloc_semaphore("sp4_0"), nc.alloc_semaphore("sp4_1")]

    def emit(eng, e):
        n = dict(ld=0, ldg=0, mm=0, sc=0, ve=0, st=0, sg=0)
        spn = [0, 0]
        H = _qdq_emit_helpers(nc, e, eng, n, SEMS, PRODUCER)
        LD, LDG, ST, STG, W = H["LD"], H["LDG"], H["ST"], H["STG"], H["W"]
        MM, MME, SC, VE, VEc, SCc, BAR = (H["MM"], H["MME"], H["SC"], H["VE"],
                                          H["VEc"], H["SCc"], H["BAR"])

        def STP(p, out, in_):
            spn[p] += 16
            if eng == "sync":
                e.dma_start(out=out, in_=in_).then_inc(sp[p], 16)

        LD(sam, am_in)
        LD(sones, ones128_in)
        W("ld", "vector")
        VE(lambda: e.tensor_reduce(sm1, sam, AX.X, ALU.max))
        VEc(lambda: e.tensor_scalar(ssc[:, 1:2], sm1, 1.0 / 32767.0, None,
                                    ALU.mult))
        VEc(lambda: e.tensor_scalar(ssc[:, 1:2], ssc[:, 1:2], 1e-12, None,
                                    ALU.max))
        VEc(lambda: e.reciprocal(ssc[:, 0:1], ssc[:, 1:2]))
        W("ve", "tensor")
        MM(lambda: e.matmul(pb_bc, sones, ssc, start=True, stop=True))
        W("mm", "scalar")
        SC(lambda: e.copy(sscb, pb_bc))

        for tt in range(NT):
            W("ve", "sync")
            LD(sx, xln4_in[128 * tt:128 * (tt + 1), :])
            W("ld", "vector")
            W("sc", "vector")
            VEc(lambda: e.tensor_scalar(sxq, sx, sscb[:, 0:1], ROUND_C,
                                        ALU.mult, ALU.add))
            W("st", "vector")
            VEc(lambda: e.tensor_scalar(sxqb, sxq, ROUND_C, sscb[:, 1:2],
                                        ALU.subtract, ALU.mult))
            W("ve", "sync")
            ST(xqscr[128 * tt:128 * (tt + 1), :], sxqb)

        for wi, w_in_ in enumerate((wup_in, wg_in)):
            for it in range(IT):
                W("ve", "sync")
                LD(sw, w_in_[128 * it:128 * (it + 1), :])
                W("ld", "vector")
                VEc(lambda: e.tensor_reduce(
                    swa.rearrange("p (a one) -> p a one", one=1),
                    sw.rearrange("p (a w) -> p a w", w=32), AX.X, ALU.max,
                    apply_absolute_value=True))
                VEc(lambda: e.tensor_scalar(sws, swa, 1.0 / 7.0, None, ALU.mult))
                VEc(lambda: e.tensor_scalar(sws, sws, 1e-12, None, ALU.max))
                VEc(lambda: e.reciprocal(swr, sws))
                VEc(lambda: e.tensor_mul(
                    swq.rearrange("p (a w) -> p a w", w=32),
                    sw.rearrange("p (a w) -> p a w", w=32),
                    swr.rearrange("p (a one) -> p a one", one=1)
                    .broadcast_to([128, 32, 32])))
                VEc(lambda: e.tensor_scalar(swq, swq, ROUND_C, ROUND_C,
                                            ALU.add, ALU.subtract))
                W("st", "vector")
                VEc(lambda: e.tensor_mul(
                    swqb.rearrange("p (a w) -> p a w", w=32),
                    swq.rearrange("p (a w) -> p a w", w=32),
                    sws.rearrange("p (a one) -> p a one", one=1)
                    .broadcast_to([128, 32, 32])))
                W("ve", "sync")
                ST(wdscr[wi, 128 * it:128 * (it + 1), :], swqb)

        W("st", "sync")
        for ci in range(8):
            LD(xqT[ci], xqscr[:, 128 * ci:128 * (ci + 1)], transpose=True)
        W("ld", "sync")

        for wi in range(2):
            dst = (up_out, gate_out)[wi]
            W("mm", "sync")
            for ci in range(8):
                LD(wT[ci], wdscr[wi, :, 128 * ci:128 * (ci + 1)], transpose=True)
            W("ld", "sync")
            BAR("tensor", "vector")
            if eng == "vector":
                e.memset(samax[wi], 0.0)
            evac_hist = []
            for it in range(IT):
                for tc in range(TPC // 512):
                    g = it * (TPC // 512) + tc
                    bank = pb[g % 2]
                    if g >= 2:
                        W("sc", "tensor", val=evac_hist[-2])
                    for ci in range(7):
                        MME(lambda bank=bank, ci=ci, it=it, tc=tc: e.matmul(
                            bank, wT[ci][:, 128 * it:128 * (it + 1)],
                            xqT[ci][:, 512 * tc:512 * (tc + 1)],
                            start=(ci == 0), stop=False))
                    MM(lambda bank=bank, it=it, tc=tc: e.matmul(
                        bank, wT[7][:, 128 * it:128 * (it + 1)],
                        xqT[7][:, 512 * tc:512 * (tc + 1)],
                        start=False, stop=True))
                    W("mm", "scalar")
                    if g >= 2:
                        if eng == "scalar":
                            e.wait_ge(sp[g % 2], spn[g % 2])
                    SC(lambda g=g, bank=bank: e.copy(outb[g % 2], bank))
                    evac_hist.append(n["sc"])
                    W("sc", "vector")
                    VEc(lambda g=g: e.tensor_reduce(stmp1, outb[g % 2], AX.X,
                                                    ALU.max,
                                                    apply_absolute_value=True))
                    VEc(lambda wi=wi: e.tensor_max(samax[wi], samax[wi], stmp1))
                    W("sc", "sync")
                    W("ve", "sync")
                    STP(g % 2, dst[128 * it:128 * (it + 1),
                                   512 * tc:512 * (tc + 1)], outb[g % 2])
            if eng == "sync":
                e.wait_ge(sp[0], spn[0])
                e.wait_ge(sp[1], spn[1])
            W("ve", "sync")
            ST(flipscr[wi:wi + 1, :].rearrange("a b -> b a"), samax[wi])
        W("st", "sync")
        LD(sflip, flipscr[0:1, :])
        W("ld", "vector")
        VEc(lambda: e.tensor_reduce(sam2[:, 0:1], sflip, AX.X, ALU.max))
        W("ve", "sync")
        LD(sflip, flipscr[1:2, :])
        W("ld", "vector")
        VEc(lambda: e.tensor_reduce(sam2[:, 1:2], sflip, AX.X, ALU.max))
        W("ve", "sync")
        ST(am2_out, sam2)
        BAR("sync", "gpsimd", "tensor", "scalar", "vector")
        return n

    return _run_block(nc, emit)


def build_launch5():
    nc = bass.Bass("TRN2", target_bir_lowering=False, debug=False,
                   num_devices=NCORES, name="launch5")
    WID = TPC

    gate_in = nc.dram_tensor("gate", [L1I, TPC], BF16, kind="ExternalInput").ap()
    am_in = nc.dram_tensor("amax2all", [8, 2], F32, kind="ExternalInput").ap()
    g2_out = nc.dram_tensor("g2pre", [L1I, TPC], BF16, kind="ExternalOutput").ap()
    am3_out = nc.dram_tensor("amax3", [1, 1], F32, kind="ExternalOutput").ap()
    flipscr = nc.dram_tensor("flip5", [1, 128], F32, kind="Internal").ap()
    mflip = nc.dram_tensor("mflip5", [1, 8], F32, kind="Internal").ap()
    ones128_in = nc.inline_tensor(np.ones((1, 128), np.float32), "ones5").ap()

    A = nc.alloc_sbuf_tensor
    sam = A("sam5", [8, 2], F32).ap()
    sm = A("sm5", [8, 1], F32).ap()
    smf = A("smf5", [1, 8], F32).ap()
    ssc = A("ssc5", [1, 2], F32).ap()
    sscb = A("sscb5", [128, 2], F32).ap()
    sones = A("sones5", [1, 128], F32).ap()
    sg_ = [A(f"sg5_{i}", [128, WID], BF16).ap() for i in range(2)]
    sq = [A(f"sq5_{i}", [128, WID], F32).ap() for i in range(2)]
    sgq = [A(f"sgq5_{i}", [128, WID], F32).ap() for i in range(2)]
    ssig = [A(f"ssig5_{i}", [128, WID], F32).ap() for i in range(2)]
    so = [A(f"so5_{i}", [128, WID], BF16).ap() for i in range(2)]
    samax = A("samax5", [128, 1], F32).ap()
    stmp1 = A("stmp15", [128, 1], F32).ap()
    sflip = A("sflip5", [1, 128], F32).ap()
    sam3 = A("sam3o", [1, 1], F32).ap()

    P = nc.alloc_psum_tensor
    pb_bc = P("pb5_bc", [128, 2], F32).ap()

    SEMS, PRODUCER = _mk_sems(nc)
    lp = [nc.alloc_semaphore("lp0"), nc.alloc_semaphore("lp1")]
    sp = [nc.alloc_semaphore("sp0"), nc.alloc_semaphore("sp1")]

    def emit(eng, e):
        n = dict(ld=0, ldg=0, mm=0, sc=0, ve=0, st=0, sg=0)
        lpn = [0, 0]
        spn = [0, 0]
        H = _qdq_emit_helpers(nc, e, eng, n, SEMS, PRODUCER)
        LD, LDG, ST, STG, W = H["LD"], H["LDG"], H["ST"], H["STG"], H["W"]
        MM, MME, SC, VE, VEc, SCc, BAR = (H["MM"], H["MME"], H["SC"], H["VE"],
                                          H["VEc"], H["SCc"], H["BAR"])

        def LDP(p, out, in_):
            lpn[p] += 16
            if eng == "sync":
                e.dma_start(out=out, in_=in_).then_inc(lp[p], 16)

        def STP(p, out, in_):
            spn[p] += 16
            if eng == "sync":
                e.dma_start(out=out, in_=in_).then_inc(sp[p], 16)

        def WP(sems, cnt, p, tgt):
            if eng == tgt:
                e.wait_ge(sems[p], cnt[p])

        LD(sam, am_in)
        LD(sones, ones128_in)
        W("ld", "vector")
        VE(lambda: e.tensor_copy(sm, sam[:, 1:2]))
        W("ve", "sync")
        ST(mflip.rearrange("a b -> b a"), sm)
        W("st", "sync")
        LD(smf, mflip)
        W("ld", "vector")
        VEc(lambda: e.tensor_reduce(ssc[:, 1:2], smf, AX.X, ALU.max))
        VEc(lambda: e.tensor_scalar(ssc[:, 1:2], ssc[:, 1:2], 1.0 / 32767.0,
                                    None, ALU.mult))
        VEc(lambda: e.tensor_scalar(ssc[:, 1:2], ssc[:, 1:2], 1e-12, None,
                                    ALU.max))
        VEc(lambda: e.reciprocal(ssc[:, 0:1], ssc[:, 1:2]))
        W("ve", "tensor")
        MM(lambda: e.matmul(pb_bc, sones, ssc, start=True, stop=True))
        W("mm", "scalar")
        SC(lambda: e.copy(sscb, pb_bc))
        if eng == "vector":
            e.memset(samax, 0.0)

        ve_hist = []
        for it in range(L1I // 128):
            p = it % 2
            if it >= 2:
                W("ve", "sync", val=ve_hist[-2][0])
                WP(sp, spn, p, "sync")
            LDP(p, sg_[p], gate_in[128 * it:128 * (it + 1), :])
            WP(lp, lpn, p, "vector")
            W("sc", "vector")
            VEc(lambda p=p: e.tensor_scalar(sq[p], sg_[p], sscb[:, 0:1],
                                            ROUND_C, ALU.mult, ALU.add))
            VEc(lambda p=p: e.tensor_scalar(sgq[p], sq[p], ROUND_C, sscb[:, 1:2],
                                            ALU.subtract, ALU.mult))
            W("ve", "scalar")
            SCc(lambda p=p: e.activation(ssig[p], sgq[p], AF.Sigmoid))
            W("sc", "vector")
            VEc(lambda p=p: e.tensor_scalar(ssig[p], ssig[p], 65536.0, ROUND_C,
                                            ALU.mult, ALU.add))
            VEc(lambda p=p: e.tensor_scalar(ssig[p], ssig[p], ROUND_C, 65535.0,
                                            ALU.subtract, ALU.min))
            VEc(lambda p=p: e.scalar_tensor_tensor(
                so[p], ssig[p], 1.0 / 65536.0, sgq[p], ALU.mult, ALU.mult))
            VEc(lambda p=p: e.tensor_reduce(stmp1, so[p], AX.X, ALU.max,
                                            apply_absolute_value=True))
            VEc(lambda: e.tensor_max(samax, samax, stmp1))
            ve_hist.append((n["ve"], None))
            W("ve", "sync")
            STP(p, g2_out[128 * it:128 * (it + 1), :], so[p])
        W("ve", "sync")
        ST(flipscr.rearrange("a b -> b a"), samax)
        W("st", "sync")
        LD(sflip, flipscr)
        W("ld", "vector")
        VEc(lambda: e.tensor_reduce(sam3, sflip, AX.X, ALU.max))
        W("ve", "sync")
        ST(am3_out, sam3)
        if eng == "sync":
            e.wait_ge(sp[0], spn[0])
            e.wait_ge(sp[1], spn[1])
        BAR("sync", "gpsimd", "tensor", "scalar", "vector")
        return n

    return _run_block(nc, emit)


def build_launch6():
    nc = bass.Bass("TRN2", target_bir_lowering=False, debug=False,
                   num_devices=NCORES, name="launch6")
    WID = TPC
    up_in = nc.dram_tensor("up", [L1I, TPC], BF16, kind="ExternalInput").ap()
    g2_in = nc.dram_tensor("g2pre", [L1I, TPC], BF16, kind="ExternalInput").ap()
    am2_in = nc.dram_tensor("amax2all", [8, 2], F32, kind="ExternalInput").ap()
    am3_in = nc.dram_tensor("amax3all", [1, 8], F32, kind="ExternalInput").ap()
    t_out = nc.dram_tensor("tprod", [L1I, TPC], BF16, kind="ExternalOutput").ap()
    am4_out = nc.dram_tensor("amax4", [1, 1], F32, kind="ExternalOutput").ap()
    flipscr = nc.dram_tensor("flip6", [1, 128], F32, kind="Internal").ap()
    mflip = nc.dram_tensor("mflip6", [1, 8], F32, kind="Internal").ap()
    ones128_in = nc.inline_tensor(np.ones((1, 128), np.float32), "ones6").ap()

    A = nc.alloc_sbuf_tensor
    sam2 = A("sam26", [8, 2], F32).ap()
    sam3 = A("sam36", [1, 8], F32).ap()
    sm = A("sm6", [8, 1], F32).ap()
    smf = A("smf6", [1, 8], F32).ap()
    ssc = A("ssc6", [1, 4], F32).ap()
    sscb = A("sscb6", [128, 4], F32).ap()
    sones = A("sones6", [1, 128], F32).ap()
    sup = [A(f"sup6_{i}", [128, WID], BF16).ap() for i in range(2)]
    sg2 = [A(f"sg26_{i}", [128, WID], BF16).ap() for i in range(2)]
    sq = [A(f"sq6_{i}", [128, WID], F32).ap() for i in range(2)]
    suq = [A(f"suq6_{i}", [128, WID], F32).ap() for i in range(2)]
    sgq = [A(f"sgq6_{i}", [128, WID], F32).ap() for i in range(2)]
    so = [A(f"so6_{i}", [128, WID], BF16).ap() for i in range(2)]
    samax = A("samax6", [128, 1], F32).ap()
    stmp1 = A("stmp16", [128, 1], F32).ap()
    sflip = A("sflip6", [1, 128], F32).ap()
    sam4 = A("sam4o", [1, 1], F32).ap()

    P = nc.alloc_psum_tensor
    pb_bc = P("pb6_bc", [128, 4], F32).ap()

    SEMS, PRODUCER = _mk_sems(nc)
    lp = [nc.alloc_semaphore("lp0"), nc.alloc_semaphore("lp1")]
    sp = [nc.alloc_semaphore("sp0"), nc.alloc_semaphore("sp1")]

    def emit(eng, e):
        n = dict(ld=0, ldg=0, mm=0, sc=0, ve=0, st=0, sg=0)
        lpn = [0, 0]
        spn = [0, 0]
        H = _qdq_emit_helpers(nc, e, eng, n, SEMS, PRODUCER)
        LD, LDG, ST, STG, W = H["LD"], H["LDG"], H["ST"], H["STG"], H["W"]
        MM, MME, SC, VE, VEc, SCc, BAR = (H["MM"], H["MME"], H["SC"], H["VE"],
                                          H["VEc"], H["SCc"], H["BAR"])

        def LDP(p, out, in_):
            lpn[p] += 16
            if eng == "sync":
                e.dma_start(out=out, in_=in_).then_inc(lp[p], 16)

        def STP(p, out, in_):
            spn[p] += 16
            if eng == "sync":
                e.dma_start(out=out, in_=in_).then_inc(sp[p], 16)

        def WP(sems, cnt, p, tgt):
            if eng == tgt:
                e.wait_ge(sems[p], cnt[p])

        LD(sam2, am2_in)
        LD(sam3, am3_in)
        LD(sones, ones128_in)
        W("ld", "vector")
        VE(lambda: e.tensor_copy(sm, sam2[:, 0:1]))
        W("ve", "sync")
        ST(mflip.rearrange("a b -> b a"), sm)
        W("st", "sync")
        LD(smf, mflip)
        W("ld", "vector")
        VEc(lambda: e.tensor_reduce(ssc[:, 1:2], smf, AX.X, ALU.max))
        VEc(lambda: e.tensor_scalar(ssc[:, 1:2], ssc[:, 1:2], 1.0 / 32767.0,
                                    None, ALU.mult))
        VEc(lambda: e.tensor_scalar(ssc[:, 1:2], ssc[:, 1:2], 1e-12, None,
                                    ALU.max))
        VEc(lambda: e.reciprocal(ssc[:, 0:1], ssc[:, 1:2]))
        VEc(lambda: e.tensor_reduce(ssc[:, 3:4], sam3, AX.X, ALU.max))
        VEc(lambda: e.tensor_scalar(ssc[:, 3:4], ssc[:, 3:4], 1.0 / 32767.0,
                                    None, ALU.mult))
        VEc(lambda: e.tensor_scalar(ssc[:, 3:4], ssc[:, 3:4], 1e-12, None,
                                    ALU.max))
        VEc(lambda: e.reciprocal(ssc[:, 2:3], ssc[:, 3:4]))
        W("ve", "tensor")
        MM(lambda: e.matmul(pb_bc, sones, ssc, start=True, stop=True))
        W("mm", "scalar")
        SC(lambda: e.copy(sscb, pb_bc))
        if eng == "vector":
            e.memset(samax, 0.0)

        ve_hist = []
        for it in range(L1I // 128):
            p = it % 2
            if it >= 2:
                W("ve", "sync", val=ve_hist[-2])
                WP(sp, spn, p, "sync")
            LDP(p, sup[p], up_in[128 * it:128 * (it + 1), :])
            LDP(p, sg2[p], g2_in[128 * it:128 * (it + 1), :])
            WP(lp, lpn, p, "vector")
            W("sc", "vector")
            VEc(lambda p=p: e.tensor_scalar(sq[p], sup[p], sscb[:, 0:1],
                                            ROUND_C, ALU.mult, ALU.add))
            VEc(lambda p=p: e.tensor_scalar(suq[p], sq[p], ROUND_C, sscb[:, 1:2],
                                            ALU.subtract, ALU.mult))
            VEc(lambda p=p: e.tensor_scalar(sq[p], sg2[p], sscb[:, 2:3],
                                            ROUND_C, ALU.mult, ALU.add))
            VEc(lambda p=p: e.tensor_scalar(sgq[p], sq[p], ROUND_C, sscb[:, 3:4],
                                            ALU.subtract, ALU.mult))
            VEc(lambda p=p: e.tensor_mul(so[p], sgq[p], suq[p]))
            VEc(lambda p=p: e.tensor_reduce(stmp1, so[p], AX.X, ALU.max,
                                            apply_absolute_value=True))
            VEc(lambda: e.tensor_max(samax, samax, stmp1))
            ve_hist.append(n["ve"])
            W("ve", "sync")
            STP(p, t_out[128 * it:128 * (it + 1), :], so[p])
        W("ve", "sync")
        ST(flipscr.rearrange("a b -> b a"), samax)
        W("st", "sync")
        LD(sflip, flipscr)
        W("ld", "vector")
        VEc(lambda: e.tensor_reduce(sam4, sflip, AX.X, ALU.max))
        W("ve", "sync")
        ST(am4_out, sam4)
        if eng == "sync":
            e.wait_ge(sp[0], spn[0])
            e.wait_ge(sp[1], spn[1])
        BAR("sync", "gpsimd", "tensor", "scalar", "vector")
        return n

    return _run_block(nc, emit)


def build_launch7():
    nc = bass.Bass("TRN2", target_bir_lowering=False, debug=False,
                   num_devices=NCORES, name="launch7")
    IT = L1I // 128
    TB = 1024
    NB = TPC // TB

    t_in = nc.dram_tensor("tprod", [L1I, TPC], BF16, kind="ExternalInput").ap()
    am4_in = nc.dram_tensor("amax4all", [1, 8], F32, kind="ExternalInput").ap()
    wd_in = nc.dram_tensor("wdn", [C, L1I], F32, kind="ExternalInput").ap()
    r4_in = nc.dram_tensor("resid4", [TPC, C], F32, kind="ExternalInput").ap()
    out_out = nc.dram_tensor("out", [TPC, C], F32, kind="ExternalOutput").ap()
    wdscr = nc.dram_tensor("wdscr7", [C, L1I], BF16, kind="Internal").ap()
    ones128_in = nc.inline_tensor(np.ones((1, 128), np.float32), "ones7").ap()

    A = nc.alloc_sbuf_tensor
    sam = A("sam7", [1, 8], F32).ap()
    ssc = A("ssc7", [1, 2], F32).ap()
    sscb = A("sscb7", [128, 2], F32).ap()
    sones = A("sones7", [1, 128], F32).ap()
    sw = [A(f"sw7_{i}", [128, L1I], F32).ap() for i in range(2)]
    swa = A("swa7", [128, 96], F32).ap()
    sws = [A(f"sws7_{i}", [128, 96], F32).ap() for i in range(2)]
    swr = A("swr7", [128, 96], F32).ap()
    swq = A("swq7", [128, L1I], F32).ap()
    swqb = [A(f"swqb7_{i}", [128, L1I], BF16).ap() for i in range(2)]
    wT = [A(f"wT7_{i}", [128, C], BF16).ap() for i in range(IT)]
    stin = [A(f"stin7_{i}", [128, TB], BF16).ap() for i in range(2)]
    sq = [A(f"sq7_{i}", [128, TB], F32).ap() for i in range(2)]
    obuf = [A(f"obuf7_{i}", [128, TB], BF16).ap() for i in range(IT)]
    sres = [A(f"sres7_{i}", [128, C], F32).ap() for i in range(2)]
    souts = [A(f"souts{i}", [128, C], F32).ap() for i in range(2)]

    P = nc.alloc_psum_tensor
    pb = [P(f"pb7_{i}", [128, 512], F32).ap() for i in range(4)]
    pb_bc = P("pb7_bc", [128, 2], F32).ap()

    SEMS, PRODUCER = _mk_sems(nc)
    lp = [nc.alloc_semaphore("lp0"), nc.alloc_semaphore("lp1")]
    sp = [nc.alloc_semaphore("sp0"), nc.alloc_semaphore("sp1")]

    def emit(eng, e):
        n = dict(ld=0, ldg=0, mm=0, sc=0, ve=0, st=0, sg=0)
        lpn = [0, 0]
        spn = [0, 0]
        H = _qdq_emit_helpers(nc, e, eng, n, SEMS, PRODUCER)
        LD, LDG, ST, STG, W = H["LD"], H["LDG"], H["ST"], H["STG"], H["W"]
        MM, MME, SC, VE, VEc, SCc, BAR = (H["MM"], H["MME"], H["SC"], H["VE"],
                                          H["VEc"], H["SCc"], H["BAR"])

        def LDP(p, out, in_):
            lpn[p] += 16
            if eng == "sync":
                e.dma_start(out=out, in_=in_).then_inc(lp[p], 16)

        def STP(p, out, in_):
            spn[p] += 16
            if eng == "sync":
                e.dma_start(out=out, in_=in_).then_inc(sp[p], 16)

        def WP(sems, cnt, p, tgt, val=None):
            if eng == tgt:
                e.wait_ge(sems[p], cnt[p] if val is None else val)

        LD(sam, am4_in)
        LD(sones, ones128_in)
        W("ld", "vector")
        VE(lambda: e.tensor_reduce(ssc[:, 1:2], sam, AX.X, ALU.max))
        VEc(lambda: e.tensor_scalar(ssc[:, 1:2], ssc[:, 1:2], 1.0 / 32767.0,
                                    None, ALU.mult))
        VEc(lambda: e.tensor_scalar(ssc[:, 1:2], ssc[:, 1:2], 1e-12, None,
                                    ALU.max))
        VEc(lambda: e.reciprocal(ssc[:, 0:1], ssc[:, 1:2]))
        W("ve", "tensor")
        MM(lambda: e.matmul(pb_bc, sones, ssc, start=True, stop=True))
        W("mm", "scalar")
        SC(lambda: e.copy(sscb, pb_bc))

        for ct in range(C // 128):
            p = ct % 2
            if ct >= 2:
                WP(sp, spn, p, "sync")
            LDP(p, sw[p], wd_in[128 * ct:128 * (ct + 1), :])
            WP(lp, lpn, p, "vector")
            VEc(lambda p=p: e.tensor_reduce(
                swa.rearrange("p (a one) -> p a one", one=1),
                sw[p].rearrange("p (a w) -> p a w", w=32), AX.X, ALU.max,
                apply_absolute_value=True))
            VEc(lambda p=p: e.tensor_scalar(sws[p], swa, 1.0 / 7.0, None, ALU.mult))
            VEc(lambda p=p: e.tensor_scalar(sws[p], sws[p], 1e-12, None, ALU.max))
            VEc(lambda p=p: e.reciprocal(swr, sws[p]))
            VEc(lambda p=p: e.tensor_mul(
                swq.rearrange("p (a w) -> p a w", w=32),
                sw[p].rearrange("p (a w) -> p a w", w=32),
                swr.rearrange("p (a one) -> p a one", one=1)
                .broadcast_to([128, 96, 32])))
            VEc(lambda: e.tensor_scalar(swq, swq, ROUND_C, ROUND_C,
                                        ALU.add, ALU.subtract))
            VEc(lambda p=p: e.tensor_mul(
                swqb[p].rearrange("p (a w) -> p a w", w=32),
                swq.rearrange("p (a w) -> p a w", w=32),
                sws[p].rearrange("p (a one) -> p a one", one=1)
                .broadcast_to([128, 96, 32])))
            W("ve", "sync")
            STP(p, wdscr[128 * ct:128 * (ct + 1), :], swqb[p])
        if eng == "sync":
            e.wait_ge(sp[0], spn[0])
            e.wait_ge(sp[1], spn[1])
        for it in range(IT):
            LD(wT[it], wdscr[:, 128 * it:128 * (it + 1)], transpose=True)
        W("ld", "sync")

        BAR("tensor", "vector", "scalar", "sync")
        qdq_ve = []
        for tb in range(NB):
            for it in range(IT):
                p = it % 2
                g = tb * IT + it
                if g >= 2:
                    W("ve", "sync", val=qdq_ve[g - 2])
                LDP(p, stin[p], t_in[128 * it:128 * (it + 1),
                                     TB * tb:TB * (tb + 1)])
                WP(lp, lpn, p, "vector")
                if tb > 0 and it == 0:
                    W("mm", "vector")
                VEc(lambda p=p: e.tensor_scalar(sq[p], stin[p], sscb[:, 0:1],
                                                ROUND_C, ALU.mult, ALU.add))
                VEc(lambda p=p, it=it: e.tensor_scalar(
                    obuf[it], sq[p], ROUND_C, sscb[:, 1:2],
                    ALU.subtract, ALU.mult))
                qdq_ve.append(n["ve"])
            W("ve", "tensor")
            for ts_ in range(TB // 128):
                tt = tb * (TB // 128) + ts_
                pr = tt % 2
                W("ve", "sync", val=n["ve"])
                if tt >= 2:
                    WP(sp, spn, pr, "sync")
                LDP(pr, sres[pr], r4_in[128 * tt:128 * (tt + 1), :])
                for cc in range(2):
                    bank = pb[cc]
                    if tt * 2 + cc >= 2:
                        W("ve", "tensor", val=n["ve"] - 1)
                    for it in range(IT - 1):
                        MME(lambda bank=bank, it=it, ts_=ts_, cc=cc: e.matmul(
                            bank, obuf[it][:, 128 * ts_:128 * (ts_ + 1)],
                            wT[it][:, 512 * cc:512 * (cc + 1)],
                            start=(it == 0), stop=False))
                    MM(lambda bank=bank, ts_=ts_, cc=cc: e.matmul(
                        bank, obuf[IT - 1][:, 128 * ts_:128 * (ts_ + 1)],
                        wT[IT - 1][:, 512 * cc:512 * (cc + 1)],
                        start=False, stop=True))
                    W("mm", "vector")
                    WP(lp, lpn, pr, "vector")
                    WP(sp, spn, pr, "vector")
                    VE(lambda cc=cc, bank=bank, pr=pr: e.tensor_add(
                        souts[pr][:, 512 * cc:512 * (cc + 1)], bank,
                        sres[pr][:, 512 * cc:512 * (cc + 1)]))
                W("ve", "sync")
                STP(pr, out_out[128 * tt:128 * (tt + 1), :], souts[pr])
        if eng == "sync":
            e.wait_ge(sp[0], spn[0])
            e.wait_ge(sp[1], spn[1])
        BAR("sync", "gpsimd", "tensor", "scalar", "vector")
        return n

    return _run_block(nc, emit)



_BUILDERS = {
    "l1": build_launch1, "l2": build_expert_ffn, "l3": build_launch3,
    "l4": build_launch4, "l5": build_launch5, "l6": build_launch6,
    "l7": build_launch7,
}


def _prog(name):
    if name not in _PROG_CACHE:
        _PROG_CACHE[name] = _BUILDERS[name]()
    return _PROG_CACHE[name]


def kernel(L0_attn_x, L0_attn_w_qkv, L0_attn_b_qkv, L0_attn_w_proj,
           L0_attn_b_proj, L0_attn_pos_independent, L0_ffn_Wr, L0_ffn_W1,
           L0_ffn_b1, L0_ffn_W2, L0_ffn_b2, L1_attn_Wq, L1_attn_bq,
           L1_attn_Wk, L1_attn_bk, L1_attn_Wv, L1_attn_bv, L1_attn_Wo,
           L1_attn_bo, L1_ffn_w_gate, L1_ffn_w_up, L1_ffn_w_down):
    x = np.ascontiguousarray(np.asarray(L0_attn_x, dtype=np.float32))
    w_qkv = np.asarray(L0_attn_w_qkv, np.float32)
    b_qkv = np.asarray(L0_attn_b_qkv, np.float32)
    w_proj = np.asarray(L0_attn_w_proj, np.float32)
    b_proj = np.asarray(L0_attn_b_proj, np.float32)
    Wr = np.ascontiguousarray(np.asarray(L0_ffn_Wr, np.float32))
    W1 = np.asarray(L0_ffn_W1, np.float32)
    b1 = np.asarray(L0_ffn_b1, np.float32)
    W2 = np.asarray(L0_ffn_W2, np.float32)
    b2 = np.asarray(L0_ffn_b2, np.float32)

    phi, psi, cmax = _l0_host_tables()
    cmaxneg = np.zeros((128, 16), np.float32)
    cmaxneg[:, :8] = -cmax[:8]

    in1 = []
    for c in range(NCORES):
        in1.append({
            "x": x[BPC * c:BPC * (c + 1)],
            "wqkvT": np.ascontiguousarray(w_qkv.T),
            "bqkvc": np.ascontiguousarray(b_qkv.reshape(24, 128).T),
            "bvrow": np.ascontiguousarray(b_qkv[2 * C:3 * C][None, :]),
            "wprojT": np.ascontiguousarray(w_proj.T),
            "bprow": np.ascontiguousarray(b_proj[None, :]),
            "phiT": phi.astype(ml_dtypes.bfloat16),
            "psiT": psi.astype(ml_dtypes.bfloat16), "cmaxneg": cmaxneg,
            "wr": Wr,
            "posind": np.ascontiguousarray(
                np.asarray(L0_attn_pos_independent,
                           np.float32)[:256].reshape(8, 64)),
        })
    r1 = run_bass_kernel_spmd(_prog("l1"), in1, core_ids=list(range(NCORES)))
    r1 = r1.results

    in2 = []
    for e in range(NCORES):
        buf_e = np.concatenate(
            [r1[c]["bufpart"][e * BPC * CAP:(e + 1) * BPC * CAP]
             for c in range(NCORES)], axis=0)
        in2.append({
            "buf": buf_e,
            "w1": np.ascontiguousarray(W1[e]),
            "b1c": np.ascontiguousarray(b1[e].reshape(FFH // 128, 128).T),
            "w2": np.ascontiguousarray(W2[e]),
            "b2": np.ascontiguousarray(b2[e][None, :]),
        })
    r2 = run_bass_kernel_spmd(_prog("l2"), in2, core_ids=list(range(NCORES)))
    r2 = r2.results

    in3 = []
    for c in range(NCORES):
        yall = np.concatenate(
            [r2[e]["y"][BPC * CAP * c:BPC * CAP * (c + 1)]
             for e in range(NCORES)], axis=0)
        in3.append({
            "resid2": r1[c]["resid2"],
            "yall": yall,
            "gidx": r1[c]["gidx"],
            "gates": r1[c]["gates"],
            "wq": np.ascontiguousarray(np.asarray(L1_attn_Wq, np.float32)),
            "wk": np.ascontiguousarray(np.asarray(L1_attn_Wk, np.float32)),
            "wv": np.ascontiguousarray(np.asarray(L1_attn_Wv, np.float32)),
            "wo": np.ascontiguousarray(np.asarray(L1_attn_Wo, np.float32)),
            "bq": np.asarray(L1_attn_bq, np.float32)[None, :],
            "bk": np.asarray(L1_attn_bk, np.float32)[None, :],
            "bv": np.asarray(L1_attn_bv, np.float32)[None, :],
            "bo": np.asarray(L1_attn_bo, np.float32)[None, :],
        })
    r3 = run_bass_kernel_spmd(_prog("l3"), in3, core_ids=list(range(NCORES)))
    r3 = r3.results

    am1all = np.concatenate([r3[c]["amax1"] for c in range(NCORES)],
                            axis=1)
    in4 = []
    for c in range(NCORES):
        in4.append({
            "xln4": r3[c]["xln4"].reshape(TPC, C),
            "amax1all": am1all,
            "wup": np.ascontiguousarray(np.asarray(L1_ffn_w_up, np.float32)),
            "wgate": np.ascontiguousarray(np.asarray(L1_ffn_w_gate, np.float32)),
        })
    r4 = run_bass_kernel_spmd(_prog("l4"), in4, core_ids=list(range(NCORES)))
    r4 = r4.results

    am2all = np.concatenate([r4[c]["amax2"] for c in range(NCORES)],
                            axis=0)
    in5 = [{"gate": r4[c]["gate"], "amax2all": am2all} for c in range(NCORES)]
    r5 = run_bass_kernel_spmd(_prog("l5"), in5, core_ids=list(range(NCORES)))
    r5 = r5.results

    am3all = np.concatenate([r5[c]["amax3"] for c in range(NCORES)], axis=1)
    in6 = [{"up": r4[c]["up"], "g2pre": r5[c]["g2pre"], "amax2all": am2all,
            "amax3all": am3all} for c in range(NCORES)]
    r6 = run_bass_kernel_spmd(_prog("l6"), in6, core_ids=list(range(NCORES)))
    r6 = r6.results

    am4all = np.concatenate([r6[c]["amax4"] for c in range(NCORES)], axis=1)
    in7 = []
    for c in range(NCORES):
        in7.append({
            "tprod": r6[c]["tprod"],
            "amax4all": am4all,
            "wdn": np.ascontiguousarray(np.asarray(L1_ffn_w_down, np.float32)),
            "resid4": r3[c]["resid4"].reshape(TPC, C),
        })
    r7 = run_bass_kernel_spmd(_prog("l7"), in7, core_ids=list(range(NCORES)))
    r7 = r7.results

    out = np.concatenate(
        [r7[c]["out"].reshape(BPC, T, C) for c in range(NCORES)], axis=0)
    return out


# version 12
# speedup vs baseline: 1.0053x; 1.0053x over previous
import numpy as np
import ml_dtypes
import concourse.bass as bass
import concourse.mybir as mybir
from concourse.bass import IndirectOffsetOnAxis
from concourse.bass_utils import run_bass_kernel_spmd

F32 = mybir.dt.float32
BF16 = mybir.dt.bfloat16
I32 = mybir.dt.int32
AF = mybir.ActivationFunctionType
ALU = mybir.AluOpType
AX = mybir.AxisListType

NCORES = 8
BPC = 2
T = 2048
C = 1024
NHEAD = 16
HD = 64
NEXP = 8
CAP = 320
FFH = 4096
L1I = 3072
TPC = BPC * T
ROUND_C = 1.5 * 2.0 ** 23



def build_expert_ffn():
    nc = bass.Bass("TRN2", target_bir_lowering=False, debug=False,
                   num_devices=NCORES, name="expert_ffn")
    NS = 16 * CAP
    NSC = 512
    HT = FFH // 128

    HALF = NS // 2

    buf_in = nc.dram_tensor("buf", [NS, C], BF16, kind="ExternalInput").ap()
    w1_in = nc.dram_tensor("w1", [C, FFH], F32, kind="ExternalInput").ap()
    b1_in = nc.dram_tensor("b1c", [128, HT], F32, kind="ExternalInput").ap()
    w2_in = nc.dram_tensor("w2", [FFH, C], F32, kind="ExternalInput").ap()
    b2_in = nc.dram_tensor("b2", [1, C], F32, kind="ExternalInput").ap()
    y_out = nc.dram_tensor("y", [NS, C], BF16, kind="ExternalOutput").ap()
    ones1 = nc.inline_tensor(np.ones((1, 128), ml_dtypes.bfloat16), "ones1").ap()

    A = nc.alloc_sbuf_tensor
    bufT = [A(f"bufT{i}", [128, HALF], BF16).ap() for i in range(8)]
    w1sb = [A(f"w1sb{i}", [128, FFH], BF16).ap() for i in range(8)]
    w2sb = [A(f"w2sb{i}", [128, C], BF16).ap() for i in range(HT)]
    b1sb = A("b1sb", [128, HT], F32).ap()
    b2row = A("b2row", [1, C], BF16).ap()
    onesb = A("onesb", [1, 128], BF16).ap()
    hsb = [A(f"hsb{i}", [128, NSC], BF16).ap() for i in range(HT)]
    ysb = [A(f"ysb{i}", [128, C], BF16).ap() for i in range(2)]

    ld = nc.alloc_semaphore("ld")
    ldg = nc.alloc_semaphore("ldg")
    xb = nc.alloc_semaphore("xb")
    mm = nc.alloc_semaphore("mm")
    evh = nc.alloc_semaphore("evh")
    evy = nc.alloc_semaphore("evy")
    st0 = nc.alloc_semaphore("st0")
    st1 = nc.alloc_semaphore("st1")

    pb_h = [nc.alloc_psum_tensor(f"pb_h{i}", [128, NSC], F32).ap() for i in range(4)]
    pb_y = [nc.alloc_psum_tensor(f"pb_y{i}", [128, 512], F32).ap() for i in range(2)]

    NLOADS = 2 + 8 + 1 + HT
    nsc_h = HALF // NSC

    with nc.Block() as block:
        @block.sync
        def _(sync):
            sync.dma_start(out=onesb, in_=ones1).then_inc(ld, 16)
            n_y = 0
            for half in range(2):
                if half == 1:
                    sync.wait_ge(mm, 5 * 40)
                for i in range(8):
                    sync.dma_start(
                        out=bufT[i],
                        in_=buf_in[half * HALF:(half + 1) * HALF, 128 * i:128 * (i + 1)],
                        transpose=True).then_inc(xb, 16)
                for hsc in range(nsc_h):
                    sc = half * nsc_h + hsc
                    for stile in range(4):
                        n_y += 1
                        sync.wait_ge(evy, sc * 8 + (stile + 1) * 2)
                        k = sc * 4 + stile
                        sync.dma_start(
                            out=y_out[NSC * sc + 128 * stile: NSC * sc + 128 * (stile + 1), :],
                            in_=ysb[k % 2]).then_inc(st0 if k % 2 == 0 else st1, 16)
            sync.wait_ge(st0, 16 * (n_y // 2))
            sync.wait_ge(st1, 16 * (n_y // 2))

        @block.gpsimd
        def _(g):
            g.dma_start(out=b2row, in_=b2_in).then_inc(ldg, 16)
            for i in range(8):
                g.dma_start(out=w1sb[i], in_=w1_in[128 * i:128 * (i + 1), :]) \
                    .then_inc(ldg, 16)
            g.dma_start(out=b1sb, in_=b1_in).then_inc(ldg, 16)
            for i in range(HT):
                g.dma_start(out=w2sb[i], in_=w2_in[128 * i:128 * (i + 1), :]) \
                    .then_inc(ldg, 16)

        @block.tensor
        def _(t):
            t.wait_ge(ld, 16)
            t.wait_ge(ldg, 16 * (NLOADS - 1))
            n_mm = 0
            n_evac = 0
            for half in range(2):
                t.wait_ge(xb, 16 * 8 * (half + 1))
                for sc in range(nsc_h):
                    for ht in range(HT):
                        kh = (half * nsc_h + sc) * HT + ht
                        if kh >= 4:
                            t.wait_ge(evh, kh - 3)
                        for ci in range(8):
                            ins = t.matmul(pb_h[ht % 4],
                                           w1sb[ci][:, 128 * ht:128 * (ht + 1)],
                                           bufT[ci][:, NSC * sc:NSC * (sc + 1)],
                                           start=(ci == 0), stop=(ci == 7))
                            if ci == 7:
                                ins.then_inc(mm, 1)
                        n_mm += 1
                        n_evac += 1
                    t.wait_ge(evh, (half * nsc_h + sc + 1) * HT)
                    for stile in range(4):
                        for c2 in range(2):
                            j = (half * nsc_h + sc) * 8 + stile * 2 + c2
                            if j >= 2:
                                t.wait_ge(evy, j - 1)
                            for hc in range(HT):
                                t.matmul(pb_y[c2],
                                         hsb[hc][:, 128 * stile:128 * (stile + 1)],
                                         w2sb[hc][:, 512 * c2:512 * (c2 + 1)],
                                         start=(hc == 0), stop=False)
                            t.matmul(pb_y[c2], onesb,
                                     b2row[:, 512 * c2:512 * (c2 + 1)],
                                     start=False, stop=True).then_inc(mm, 1)
                            n_mm += 1
                            n_evac += 1

        @block.scalar
        def _(s):
            n_mm = 0
            for sc in range(2 * nsc_h):
                for ht in range(HT):
                    n_mm += 1
                    s.wait_ge(mm, n_mm)
                    s.activation(hsb[ht], pb_h[ht % 4], AF.Relu,
                                 bias=b1sb[:, ht:ht + 1]).then_inc(evh, 1)
                n_mm += 8

        @block.vector
        def _(v):
            n_mm = 0
            n_y = 0
            for sc in range(2 * nsc_h):
                n_mm += HT
                for stile in range(4):
                    for c2 in range(2):
                        n_mm += 1
                        v.wait_ge(mm, n_mm)
                        if n_y >= 2:
                            v.wait_ge(st0 if n_y % 2 == 0 else st1,
                                      16 * (n_y // 2))
                        v.tensor_copy(
                            ysb[(sc * 4 + stile) % 2][:, 512 * c2:512 * (c2 + 1)],
                            pb_y[c2]).then_inc(evy, 1)
                    n_y += 1

    return nc


_PROG_CACHE = {}


def _get_prog(name, builder):
    if name not in _PROG_CACHE:
        _PROG_CACHE[name] = builder()
    return _PROG_CACHE[name]


def kernel(**inputs):
    raise NotImplementedError("assembled incrementally")




def _l0_host_tables():
    import math
    dim = C // 2
    pos = np.arange(dim, dtype=np.float64)
    freqs = (1.0 / 10000.0 ** (pos / dim))
    active = int(np.sum(freqs * 2 * math.pi * T >= 1.0))
    active = min(active, dim - C // 4)
    f = freqs[:active]
    t = np.arange(T, dtype=np.float64)
    phi = np.zeros((8 * 64, T), np.float32)
    psi = np.zeros((8 * 64, T), np.float32)
    cmax = np.zeros(16, np.float32)
    d = np.arange(-(T - 1), T, dtype=np.float64)
    for h in range(8):
        fh = f[32 * h:32 * (h + 1)]
        th = (np.cos(d[:, None] * fh) + np.sin(d[:, None] * fh)).sum(1)
        cmax[h] = th.max()
        ang = t[:, None] * fh[None, :]
        cos, sin = np.cos(ang), np.sin(ang)
        phi[64 * h:64 * h + 32] = cos.T
        phi[64 * h + 32:64 * h + 64] = sin.T
        psi[64 * h:64 * h + 32] = (cos - sin).T
        psi[64 * h + 32:64 * h + 64] = (cos + sin).T
    return phi, psi, cmax


BIG = 999999.0
L1_MAXPH = 99
L1_SCOREMODE = 2


def build_launch1():
    nc = bass.Bass("TRN2", target_bir_lowering=False, debug=False,
                   num_devices=NCORES, name="launch1")
    NT = T // 128

    x_in = nc.dram_tensor("x", [BPC, T, C], F32, kind="ExternalInput").ap()
    wqkvT_in = nc.dram_tensor("wqkvT", [C, 3 * C], F32, kind="ExternalInput").ap()
    bqkvc_in = nc.dram_tensor("bqkvc", [128, 24], F32, kind="ExternalInput").ap()
    bvrow_in = nc.dram_tensor("bvrow", [1, C], F32, kind="ExternalInput").ap()
    wprojT_in = nc.dram_tensor("wprojT", [C, C], F32, kind="ExternalInput").ap()
    bprow_in = nc.dram_tensor("bprow", [1, C], F32, kind="ExternalInput").ap()
    phiT_in = nc.dram_tensor("phiT", [512, T], BF16, kind="ExternalInput").ap()
    psiT_in = nc.dram_tensor("psiT", [512, T], BF16, kind="ExternalInput").ap()
    cmaxneg_in = nc.dram_tensor("cmaxneg", [128, 16], F32, kind="ExternalInput").ap()
    wr_in = nc.dram_tensor("wr", [C, 8], F32, kind="ExternalInput").ap()
    posind_in = nc.dram_tensor("posind", [8, 64], F32, kind="ExternalInput").ap()

    resid2_out = nc.dram_tensor("resid2", [BPC, T, C], F32, kind="ExternalOutput").ap()
    buf_out = nc.dram_tensor("bufpart", [NEXP * BPC * CAP, C], BF16,
                             kind="ExternalOutput").ap()
    gidx_out = nc.dram_tensor("gidx", [BPC, 2, T], I32, kind="ExternalOutput").ap()
    gate_out = nc.dram_tensor("gates", [BPC, 2, T], F32, kind="ExternalOutput").ap()

    lnscr2b = [nc.dram_tensor(f"lnscr{i}", [T, C], BF16, kind="Internal").ap()
               for i in range(2)]
    pcflip = nc.dram_tensor("pcflip", [1, 8], F32, kind="Internal").ap()

    tri = np.triu(np.ones((128, 128)), 1).astype(ml_dtypes.bfloat16)
    tri_in = nc.inline_tensor(tri, "tri").ap()
    allone_in = nc.inline_tensor(np.ones((128, 128), ml_dtypes.bfloat16), "allone").ap()
    pp = np.arange(128)[:, None]
    comp = np.zeros((128, 896), np.float32)
    comp[:, 512:] = 1.0
    comp[:, 384:512] = (pp <= np.arange(128)[None, :])
    comp_in = nc.inline_tensor(comp.astype(ml_dtypes.bfloat16), "maskcomp").ap()
    iota8w_in = nc.inline_tensor(
        np.tile(np.arange(8, dtype=np.float32), (128, T // 128)), "iota8w").ap()
    ones64_in = nc.inline_tensor(np.ones((1, 64), np.float32), "ones64c").ap()
    ones128_in = nc.inline_tensor(np.ones((1, 128), np.float32), "ones128c").ap()

    A = nc.alloc_sbuf_tensor
    xlnT = [A(f"xlnT{i}", [128, T], BF16).ap() for i in range(8)]
    v1 = [A(f"v1_{i}", [128, NHEAD * 65], BF16).ap() for i in range(NT)]
    catT = [A(f"catT{i}", [128, T], BF16).ap() for i in range(8)]
    wqkv = [A(f"wqkv{i}", [128, 3 * C], BF16).ap() for i in range(8)]
    wproj = [A(f"wproj{i}", [128, C], BF16).ap() for i in range(8)]
    wrsb = [A(f"wrsb{i}", [128, 8], BF16).ap() for i in range(8)]
    qt = A("qt", [128, T], BF16).ap()
    ktl = A("ktl", [128, T], BF16).ap()
    expb = [A(f"expb{i}", [128, 1024], BF16).ap() for i in range(2)]
    sphi2 = A("sphi", [128, T], BF16).ap()
    spsi2 = A("spsi", [128, T], BF16).ap()
    brow = sphi2.bitcast(F32)[0:1, 0:512]
    sln2b = A("sln2b", [128, C], BF16).ap()
    bqkv8 = A("bqkv8", [128, 24], F32).ap()
    vbias = A("vbias", [128, C], BF16).ap()
    pbias = A("pbias", [128, C], BF16).ap()
    cmaxs = A("cmaxs", [128, 16], F32).ap()
    sbcast = A("sbcast", [64, 512], F32).ap()
    srecip = sbcast[0:1, :]
    sones64 = A("sones64", [1, 64], F32).ap()
    sones128 = A("sones128", [1, 128], F32).ap()
    sx = A("sx", [128, C], F32).ap()
    sx2 = sx
    sres = A("sres", [128, C], F32).ap()
    sq_t = sres
    lncols = A("lncols", [128, 8], F32).ap()
    L = {k: lncols[:, i:i + 1]
         for i, k in enumerate(["ssum", "ssumsq", "smean", "sm2", "svar",
                                "ssqrt", "srstd"])}
    sln = A("sln", [128, C], BF16).ap()
    sln2 = sx.bitcast(BF16)[:, 0:C]
    strie = A("strie", [128, 128], BF16).ap()
    sallone = A("sallone", [128, 128], BF16).ap()
    scomp = A("scomp", [128, 896], BF16).ap()
    smask = [scomp[:, 384 - 128 * r:384 - 128 * r + 512] for r in range(4)]
    NT8 = (T // 128) * 8
    rlog = A("rlog", [128, NT8], F32).ap()
    rxs = A("rxs", [128, NT8], F32).ap()
    rmsk = A("rmsk", [128, NT8], F32).ap()
    roh0 = A("roh0", [128, NT8], F32).ap()
    roh1 = A("roh1", [128, NT8], F32).ap()
    rohs = A("rohs", [128, NT8], BF16).ap()
    rcum = A("rcum", [128, NT8], F32).ap()
    riota = A("riota", [128, NT8], F32).ap()
    NTc = T // 128
    rcolsA = A("rcolsA", [128, 15 * NTc], F32).ap()
    _c = lambda i: rcolsA[:, i * NTc:(i + 1) * NTc]
    rm1, rm2, re1, re2, rz_, rp1, rp2 = (_c(0), _c(1), _c(2), _c(3), _c(4),
                                         _c(5), _c(6))
    rpos = [_c(7), _c(8)]
    rkeep = [_c(9), _c(10)]
    rgate = [_c(11), _c(12)]
    ridxf = [_c(13), _c(14)]
    ridx = [A(f"ridx{k}", [128, NTc], I32).ap() for k in range(2)]
    spind = A("spind", [8, 64], F32).ap()
    spc = A("spc", [8, 1], F32).ap()
    spcr = A("spcr", [1, 8], F32).ap()

    P = nc.alloc_psum_tensor
    pbig = [P(f"pbig{i}", [128, 1024], F32).ap() for i in range(2)]
    pb_v = [pbig[0][:, 0:512], pbig[1][:, 0:512]]
    pb_s = [pbig[0][:, 512:1024], pbig[1][:, 512:1024]]
    pb_av_t = P("pb_av", [128, 512], F32).ap()
    pb_bc = P("pb_bc", [128, 512], F32).ap()
    pb_o2 = [P(f"pb_o2{i}", [128, 512], F32).ap() for i in range(2)]
    pb_o = pb_o2
    pb_avs = [pb_av_t, pb_o2[1]]

    s_ld = nc.alloc_semaphore("s_ld")
    s_ldg = nc.alloc_semaphore("s_ldg")
    s_mm = nc.alloc_semaphore("s_mm")
    s_sc = nc.alloc_semaphore("s_sc")
    s_ve = nc.alloc_semaphore("s_ve")
    s_st = nc.alloc_semaphore("s_st")
    s_sg = nc.alloc_semaphore("s_sg")
    lp5 = [nc.alloc_semaphore("lp5_0"), nc.alloc_semaphore("lp5_1")]
    sgp = [nc.alloc_semaphore("sgp_0"), nc.alloc_semaphore("sgp_1")]
    SEMS = {"ld": s_ld, "ldg": s_ldg, "mm": s_mm, "sc": s_sc, "ve": s_ve,
            "st": s_st, "sg": s_sg, "l50": lp5[0], "l51": lp5[1],
            "g50": sgp[0], "g51": sgp[1]}
    PRODUCER = {"ld": "sync", "st": "sync", "ldg": "gpsimd", "sg": "gpsimd",
                "mm": "tensor", "sc": "scalar", "ve": "vector",
                "l50": "sync", "l51": "sync", "g50": "gpsimd", "g51": "gpsimd"}

    def emit(eng, e):
        n = dict(ld=0, ldg=0, mm=0, sc=0, ve=0, st=0, sg=0, l50=0, l51=0,
                 g50=0, g51=0)
        breg = None
        if eng == "gpsimd":
            breg = e.alloc_register("bcreg")
            e.reg_mov(breg, NEXP * BPC * CAP - 1)

        def LD(out, in_, **k):
            n["ld"] += 16
            if eng == "sync":
                e.dma_start(out=out, in_=in_, **k).then_inc(s_ld, 16)

        def LDG(out, in_, **k):
            n["ldg"] += 16
            if eng == "gpsimd":
                e.dma_start(out=out, in_=in_, **k).then_inc(s_ldg, 16)

        def ST(out, in_, **k):
            n["st"] += 16
            if eng == "sync":
                e.dma_start(out=out, in_=in_, **k).then_inc(s_st, 16)

        def W(which, tgt, val=None):
            if eng == tgt:
                e.wait_ge(SEMS[which], n[which] if val is None else val)

        def MM(fn):
            n["mm"] += 1
            if eng == "tensor":
                fn().then_inc(s_mm, 1)

        def MME(fn):
            if eng == "tensor":
                fn()

        def SC(fn):
            n["sc"] += 1
            if eng == "scalar":
                fn().then_inc(s_sc, 1)

        def SCE(fn):
            if eng == "scalar":
                fn()

        def VE(fn):
            n["ve"] += 1
            if eng == "vector":
                fn().then_inc(s_ve, 1)

        def VEE(fn):
            if eng == "vector":
                fn()

        def SG(fn):
            n["sg"] += 16
            if eng == "gpsimd":
                fn().then_inc(s_sg, 16)

        def VEc(fn):
            W("ve", "vector")
            VE(fn)

        def SCc(fn):
            W("sc", "scalar")
            SC(fn)

        def BAR(*engines):
            for tgt in engines:
                for k, sem in SEMS.items():
                    if PRODUCER[k] != tgt and n[k] > 0:
                        if eng == tgt:
                            e.wait_ge(sem, n[k])

        LD(strie, tri_in)
        LD(sallone, allone_in)
        LD(scomp, comp_in)
        LD(riota, iota8w_in)
        LD(sones64, ones64_in)
        LD(sones128, ones128_in)
        LD(cmaxs, cmaxneg_in)
        LD(bqkv8, bqkvc_in)
        for i in range(8):
            LDG(wqkv[i], wqkvT_in[128 * i:128 * (i + 1), :])
            LDG(wproj[i], wprojT_in[128 * i:128 * (i + 1), :])
            LDG(wrsb[i], wr_in[128 * i:128 * (i + 1), :])
        VE(lambda: e.memset(expb[0], 0.0))
        W("ve", "gpsimd")
        for zi in range(NEXP * BPC * CAP // 128):
            LDG(buf_out[128 * zi:128 * (zi + 1), :], expb[0])
        LD(spind, posind_in)
        W("ld", "vector")
        VE(lambda: e.reduce_sum(spc, spind, AX.X))
        W("ve", "sync")
        ST(pcflip.rearrange("a b -> b a"), spc)
        W("st", "sync")
        LD(spcr, pcflip)
        W("ld", "tensor")
        MM(lambda: e.matmul(pb_bc[:, 0:8], sones128, spcr, start=True, stop=True))
        W("mm", "scalar")
        W("ld", "scalar")
        SC(lambda: e.copy(cmaxs[:, 8:16], pb_bc[:, 0:8]))
        for dst, row_in in ((vbias, bvrow_in), (pbias, bprow_in)):
            for cc in range(2):
                W("sc", "sync")
                LD(brow, row_in[:, 512 * cc:512 * (cc + 1)])
                W("ld", "tensor")
                W("sc", "tensor")
                MM(lambda: e.matmul(pb_bc, sones128, brow, start=True, stop=True))
                W("mm", "scalar")
                SC(lambda dst=dst, cc=cc: e.copy(dst[:, 512 * cc:512 * (cc + 1)],
                                                 pb_bc))

        def PH_ph0(b, lnscr):
            BAR("sync", "scalar", "vector")
            for tt in range(NT):
                W("ve", "sync")
                LD(sx, x_in[b, 128 * tt:128 * (tt + 1), :])
                W("ld", "scalar")
                W("ve", "scalar")
                SC(lambda: e.activation(sq_t, sx, AF.Square, accum_out=L["ssumsq"]))
                W("ld", "vector")
                VE(lambda: e.reduce_sum(L["ssum"], sx, AX.X))
                W("sc", "vector")
                VEc(lambda: e.tensor_scalar(L["smean"], L["ssum"], 1.0 / C, None, ALU.mult))
                VEc(lambda: e.tensor_scalar(L["svar"], L["ssumsq"], 1.0 / C, None, ALU.mult))
                VEc(lambda: e.tensor_mul(L["sm2"], L["smean"], L["smean"]))
                VEc(lambda: e.tensor_sub(L["svar"], L["svar"], L["sm2"]))
                VEc(lambda: e.tensor_scalar(L["svar"], L["svar"], 1e-5, None, ALU.add))
                W("ve", "scalar")
                SC(lambda: e.activation(L["ssqrt"], L["svar"], AF.Sqrt))
                W("sc", "vector")
                VEc(lambda: e.reciprocal(L["srstd"], L["ssqrt"]))
                W("st", "vector")
                VEc(lambda: e.tensor_scalar(sln, sx, L["smean"], L["srstd"],
                                           ALU.subtract, ALU.mult))
                W("ve", "sync")
                ST(lnscr[128 * tt:128 * (tt + 1), :], sln)
        def PH_ph1(b, lnscr):
            W("st", "sync")
            W("mm", "sync")
            for ci in range(8):
                LD(xlnT[ci], lnscr[:, 128 * ci:128 * (ci + 1)], transpose=True)
            W("ld", "sync")
        def PH_ph2a(b, lnscr):
            BAR("tensor", "vector")
            for tt in range(NT):
                VE(lambda tt=tt: e.memset(
                    v1[tt].rearrange("p (h w) -> p h w", w=65)[:, :, 64:65], 1.0))
            for tt in range(NT):
                for vc in range(2):
                    pb = pb_v[vc]
                    if tt > 0:
                        W("ve", "tensor", val=n["ve"] - 1)
                    for ci in range(7):
                        MME(lambda pb=pb, ci=ci, tt=tt, vc=vc: e.matmul(
                            pb, xlnT[ci][:, 128 * tt:128 * (tt + 1)],
                            wqkv[ci][:, 2 * C + 512 * vc:2 * C + 512 * (vc + 1)],
                            start=(ci == 0), stop=False))
                    MM(lambda pb=pb, tt=tt, vc=vc: e.matmul(
                        pb, xlnT[7][:, 128 * tt:128 * (tt + 1)],
                        wqkv[7][:, 2 * C + 512 * vc:2 * C + 512 * (vc + 1)],
                        start=False, stop=True))
                    W("mm", "vector")
                    VE(lambda tt=tt, vc=vc, pb=pb: e.tensor_add(
                        v1[tt].rearrange("p (h w) -> p h w", w=65)
                        [:, 8 * vc:8 * (vc + 1), 0:64],
                        pb.rearrange("p (h w) -> p h w", w=64),
                        vbias[:, 512 * vc:512 * (vc + 1)]
                        .rearrange("p (h w) -> p h w", w=64)))
        def PH_ph2b(b, lnscr):
            proj_hist = []
            for hp in range(NHEAD // 2 if L1_MAXPH >= 2 else 0):
                h0 = 2 * hp
                for which, dst in ((0, qt), (1, ktl)):
                    for tc in range(4):
                        pb = pb_o[tc % 2]
                        if which == 0 and tc == 0:
                            W("ve", "tensor")
                        if len(proj_hist) >= 2:
                            W("ve", "tensor", val=proj_hist[-2])
                        for ci in range(7):
                            MME(lambda pb=pb, ci=ci, h0=h0, tc=tc, which=which:
                                e.matmul(
                                    pb,
                                    wqkv[ci][:, C * which + 64 * h0:
                                             C * which + 64 * (h0 + 2)],
                                    xlnT[ci][:, 512 * tc:512 * (tc + 1)],
                                    start=(ci == 0), stop=False))
                        MM(lambda pb=pb, h0=h0, tc=tc, which=which: e.matmul(
                            pb,
                            wqkv[7][:, C * which + 64 * h0:C * which + 64 * (h0 + 2)],
                            xlnT[7][:, 512 * tc:512 * (tc + 1)],
                            start=False, stop=True))
                        W("mm", "vector")
                        VE(lambda dst=dst, tc=tc, pb=pb, h0=h0, which=which:
                           e.tensor_scalar(dst[:, 512 * tc:512 * (tc + 1)], pb,
                                           bqkv8[:, 8 * which + h0 // 2:
                                                 8 * which + h0 // 2 + 1],
                                           0.125 if which == 0 else 1.0,
                                           ALU.add, ALU.mult))
                        proj_hist.append(n["ve"])
                if h0 < 8 and L1_SCOREMODE >= 2:
                    W("mm", "sync")
                    LD(sphi2, phiT_in[64 * h0:64 * (h0 + 2), :])
                    LD(spsi2, psiT_in[64 * h0:64 * (h0 + 2), :])
                for hh in range(2 if L1_SCOREMODE > 0 else 0):
                    h = h0 + hh
                    ext = (h < 8) and L1_SCOREMODE >= 2
                    prow = 64 * hh
                    for qc in range(4):
                        pb_av = pb_avs[qc % 2]
                        nk = 4 * qc + 4
                        groups = [list(range(g, min(g + 2, nk)))
                                  for g in range(0, nk, 2)]
                        gi = 0
                        for grp in groups:
                            pbs = pbig[gi % 2]
                            if gi >= 2:
                                W("sc", "tensor", val=n["sc"] - 1)
                            else:
                                W("sc", "tensor")
                            W("ld", "tensor")
                            W("ve", "tensor", val=proj_hist[-1])
                            for j, kti in enumerate(grp):
                                is_last = (j == len(grp) - 1)
                                fn1 = (lambda pbs=pbs, j=j, kti=kti, prow=prow,
                                       qc=qc: e.matmul(
                                           pbs[:, 512 * j:512 * (j + 1)],
                                           ktl[prow:prow + 64,
                                               128 * kti:128 * (kti + 1)],
                                           qt[prow:prow + 64,
                                              512 * qc:512 * (qc + 1)],
                                           start=True, stop=not ext))
                                fn2 = (lambda pbs=pbs, j=j, kti=kti, qc=qc,
                                       prow=prow: e.matmul(
                                           pbs[:, 512 * j:512 * (j + 1)],
                                           spsi2[prow:prow + 64,
                                                 128 * kti:128 * (kti + 1)],
                                           sphi2[prow:prow + 64,
                                                 512 * qc:512 * (qc + 1)],
                                           start=False, stop=True)) if ext else None
                                if not is_last:
                                    MME(fn1)
                                    if ext:
                                        MME(fn2)
                                else:
                                    if ext:
                                        MME(fn1)
                                        MM(fn2)
                                    else:
                                        MM(fn1)
                            W("mm", "scalar")
                            if gi >= 2:
                                W("mm", "scalar", val=n["mm"])
                            wid = 512 * len(grp)
                            if b == 0 and hp == 0 and hh == 0 and qc == 0 \
                                    and gi == 0:
                                W("ldg", "scalar")
                            SC(lambda gi=gi, pbs=pbs, h=h, wid=wid: e.activation(
                                expb[gi % 2][:, 0:wid], pbs[:, 0:wid], AF.Exp,
                                bias=cmaxs[:, h:h + 1]))
                            ndiag = sum(1 for kti in grp if kti >= 4 * qc)
                            if ndiag:
                                W("sc", "vector")
                                for j, kti in enumerate(grp):
                                    if kti >= 4 * qc:
                                        VE(lambda gi=gi, j=j, kti=kti, qc=qc:
                                           e.tensor_mul(
                                               expb[gi % 2][:, 512 * j:
                                                            512 * (j + 1)],
                                               expb[gi % 2][:, 512 * j:
                                                            512 * (j + 1)],
                                               smask[kti - 4 * qc]))
                                W("ve", "tensor")
                            else:
                                W("sc", "tensor")
                            for j, kti in enumerate(grp):
                                last = (kti == nk - 1)
                                fn = (lambda gi=gi, j=j, kti=kti, h=h, last=last:
                                      e.matmul(pb_av[0:65, :],
                                               v1[kti][:, 65 * h:65 * (h + 1)],
                                               expb[gi % 2][:, 512 * j:
                                                            512 * (j + 1)],
                                               start=(kti == 0), stop=last))
                                if last:
                                    MM(fn)
                                else:
                                    MME(fn)
                            gi += 1
                        W("mm", "vector")
                        VE(lambda: e.reciprocal(srecip, pb_av[64:65, :]))
                        W("ve", "tensor")
                        MM(lambda: e.matmul(pb_bc[0:64, :], sones64, srecip,
                                            start=True, stop=True))
                        W("mm", "scalar")
                        SC(lambda: e.copy(sbcast, pb_bc[0:64, :]))
                        W("sc", "vector")
                        VE(lambda h=h, qc=qc: e.tensor_mul(
                            catT[h // 2][64 * (h % 2):64 * (h % 2) + 64,
                                         512 * qc:512 * (qc + 1)],
                            pb_av[0:64, :], sbcast))
                        W("ve", "tensor")
        def PH_ph2c(b, lnscr):
            BAR("tensor", "sync", "vector")
            for tt in range(NT if L1_MAXPH >= 3 else 0):
                W("ve", "sync", val=n["ve"])
                LD(sx2, x_in[b, 128 * tt:128 * (tt + 1), :])
                for cc in range(2):
                    pb = pb_o[cc]
                    if tt > 0:
                        W("ve", "tensor", val=n["ve"] - 1)
                    for ci in range(7):
                        MME(lambda pb=pb, ci=ci, tt=tt, cc=cc: e.matmul(
                            pb, catT[ci][:, 128 * tt:128 * (tt + 1)],
                            wproj[ci][:, 512 * cc:512 * (cc + 1)],
                            start=(ci == 0), stop=False))
                    MM(lambda pb=pb, tt=tt, cc=cc: e.matmul(
                        pb, catT[7][:, 128 * tt:128 * (tt + 1)],
                        wproj[7][:, 512 * cc:512 * (cc + 1)],
                        start=False, stop=True))
                    W("mm", "vector")
                    W("ld", "vector")
                    VE(lambda cc=cc, pb=pb: e.tensor_add(
                        sres[:, 512 * cc:512 * (cc + 1)], pb,
                        pbias[:, 512 * cc:512 * (cc + 1)]))
                    VEc(lambda cc=cc: e.tensor_add(
                        sres[:, 512 * cc:512 * (cc + 1)],
                        sres[:, 512 * cc:512 * (cc + 1)],
                        sx2[:, 512 * cc:512 * (cc + 1)]))
                W("ve", "sync")
                ST(resid2_out[b, 128 * tt:128 * (tt + 1), :], sres)
                W("st", "vector")
        def PH_ph3(b, lnscr):
            BAR("sync", "scalar", "vector")
            for tt in range(NT):
                W("ve", "sync")
                LD(sx, resid2_out[b, 128 * tt:128 * (tt + 1), :])
                W("ld", "scalar")
                W("ve", "scalar")
                SC(lambda: e.activation(sq_t, sx, AF.Square, accum_out=L["ssumsq"]))
                W("ld", "vector")
                VE(lambda: e.reduce_sum(L["ssum"], sx, AX.X))
                W("sc", "vector")
                VEc(lambda: e.tensor_scalar(L["smean"], L["ssum"], 1.0 / C, None, ALU.mult))
                VEc(lambda: e.tensor_scalar(L["svar"], L["ssumsq"], 1.0 / C, None, ALU.mult))
                VEc(lambda: e.tensor_mul(L["sm2"], L["smean"], L["smean"]))
                VEc(lambda: e.tensor_sub(L["svar"], L["svar"], L["sm2"]))
                VEc(lambda: e.tensor_scalar(L["svar"], L["svar"], 1e-5, None, ALU.add))
                W("ve", "scalar")
                SC(lambda: e.activation(L["ssqrt"], L["svar"], AF.Sqrt))
                W("sc", "vector")
                VEc(lambda: e.reciprocal(L["srstd"], L["ssqrt"]))
                W("st", "vector")
                VEc(lambda: e.tensor_scalar(sln, sx, L["smean"], L["srstd"],
                                           ALU.subtract, ALU.mult))
                W("ve", "sync")
                ST(lnscr[128 * tt:128 * (tt + 1), :], sln)
        def PH_ph4(b, lnscr):
            W("st", "sync")
            W("mm", "sync")
            for ci in range(8):
                LD(xlnT[ci], lnscr[:, 128 * ci:128 * (ci + 1)], transpose=True)
            W("ld", "sync")
        def PH_ph5(b, lnscr):
            BAR("tensor", "vector", "scalar", "gpsimd", "sync")
            NTc = T // 128
            r3 = lambda ap: ap.rearrange("p (t e) -> p t e", e=8)
            col3 = lambda ap: ap.unsqueeze(2)
            bc8 = lambda ap: ap.unsqueeze(2).broadcast_to([128, NTc, 8])
            sc_hist = []
            for tt in range(NTc):
                pbr = pb_s[tt % 2]
                if tt >= 2:
                    W("sc", "tensor", val=sc_hist[-2])
                for ci in range(7):
                    MME(lambda ci=ci, tt=tt, pbr=pbr: e.matmul(
                        pbr[:, 0:8], xlnT[ci][:, 128 * tt:128 * (tt + 1)],
                        wrsb[ci], start=(ci == 0), stop=False))
                MM(lambda tt=tt, pbr=pbr: e.matmul(
                    pbr[:, 0:8], xlnT[7][:, 128 * tt:128 * (tt + 1)],
                    wrsb[7], start=False, stop=True))
                W("mm", "scalar")
                SC(lambda tt=tt, pbr=pbr: e.copy(rlog[:, 8 * tt:8 * (tt + 1)],
                                                 pbr[:, 0:8]))
                sc_hist.append(n["sc"])
            W("sc", "vector")
            VE(lambda: e.tensor_reduce(col3(rm1), r3(rlog), AX.X, ALU.max))
            VEc(lambda: e.tensor_tensor(r3(rmsk), r3(rlog), bc8(rm1),
                                        ALU.is_equal))
            VEc(lambda: e.scalar_tensor_tensor(rxs, rmsk, -BIG, rlog,
                                               ALU.mult, ALU.add))
            VEc(lambda: e.tensor_reduce(col3(rm2), r3(rxs), AX.X, ALU.max))
            VEc(lambda: e.tensor_scalar(rxs, riota, BIG, None, ALU.subtract))
            VEc(lambda: e.tensor_mul(rxs, rmsk, rxs))
            VEc(lambda: e.tensor_scalar(rxs, rxs, BIG, None, ALU.add))
            VEc(lambda: e.tensor_reduce(col3(re1), r3(rxs), AX.X, ALU.min))
            VEc(lambda: e.tensor_tensor(r3(rmsk), r3(rlog), bc8(rm2),
                                        ALU.is_equal))
            VEc(lambda: e.tensor_scalar(rxs, riota, BIG, None, ALU.subtract))
            VEc(lambda: e.tensor_mul(rxs, rmsk, rxs))
            VEc(lambda: e.tensor_scalar(rxs, rxs, BIG, None, ALU.add))
            VEc(lambda: e.tensor_reduce(col3(re2), r3(rxs), AX.X, ALU.min))
            VEc(lambda: e.tensor_tensor(r3(rxs), r3(rlog), bc8(rm1),
                                        ALU.subtract))
            W("ve", "scalar")
            SCc(lambda: e.activation(rxs, rxs, AF.Exp))
            W("sc", "vector")
            VEc(lambda: e.tensor_reduce(col3(rz_), r3(rxs), AX.X, ALU.add))
            VEc(lambda: e.reciprocal(rp1, rz_))
            VEc(lambda: e.tensor_sub(rm2, rm2, rm1))
            W("ve", "scalar")
            SCc(lambda: e.activation(rm2, rm2, AF.Exp))
            W("sc", "vector")
            VEc(lambda: e.tensor_mul(rp2, rm2, rp1))
            VEc(lambda: e.tensor_tensor(r3(roh0), r3(riota), bc8(re1),
                                        ALU.is_equal))
            VEc(lambda: e.tensor_tensor(r3(roh1), r3(riota), bc8(re2),
                                        ALU.is_equal))
            VEc(lambda: e.tensor_add(rohs, roh0, roh1))
            W("ve", "tensor")
            sc_hist = []
            for tt in range(NTc):
                pbr = pb_s[tt % 2]
                if len(sc_hist) >= 2:
                    W("sc", "tensor", val=sc_hist[-2])
                for s_ in range(tt):
                    MME(lambda s_=s_, pbr=pbr: e.matmul(
                        pbr[:, 0:8], sallone, rohs[:, 8 * s_:8 * (s_ + 1)],
                        start=(s_ == 0), stop=False))
                MM(lambda tt=tt, pbr=pbr: e.matmul(
                    pbr[:, 0:8], strie, rohs[:, 8 * tt:8 * (tt + 1)],
                    start=(tt == 0), stop=True))
                W("mm", "scalar")
                SC(lambda tt=tt, pbr=pbr: e.copy(rcum[:, 8 * tt:8 * (tt + 1)],
                                                 pbr[:, 0:8]))
                sc_hist.append(n["sc"])
            W("sc", "vector")
            for k in range(2):
                oh = (roh0, roh1)[k]
                ee = (re1, re2)[k]
                pp_ = (rp1, rp2)[k]
                VEc(lambda oh=oh: e.tensor_mul(rxs, oh, rcum))
                VEc(lambda k=k: e.tensor_reduce(col3(rpos[k]), r3(rxs), AX.X,
                                                ALU.add))
                VEc(lambda k=k: e.tensor_scalar(rkeep[k], rpos[k], float(CAP),
                                                None, ALU.is_lt))
                VEc(lambda k=k, ee=ee: e.scalar_tensor_tensor(
                    ridxf[k], ee, float(BPC * CAP), rpos[k], ALU.mult, ALU.add))
                VEc(lambda k=k, b=b: e.tensor_scalar(
                    ridxf[k], ridxf[k], float(b * CAP), None, ALU.add))
                VEc(lambda k=k: e.tensor_mul(ridxf[k], ridxf[k], rkeep[k]))
                VEc(lambda k=k: e.tensor_scalar(rm1, rkeep[k], -1.0, None,
                                                ALU.mult))
                VEc(lambda: e.tensor_scalar(rm1, rm1, 1.0, None, ALU.add))
                VEc(lambda k=k: e.scalar_tensor_tensor(
                    ridxf[k], rm1, BIG, ridxf[k], ALU.mult, ALU.add))
                W("sg", "vector")
                VEc(lambda k=k: e.tensor_copy(ridx[k], ridxf[k]))
                W("st", "vector")
                VEc(lambda k=k, pp_=pp_: e.tensor_mul(rgate[k], pp_, rkeep[k]))
            W("ldg", "gpsimd")
            for tt in range(NTc):
                p5 = tt % 2
                stage = (sln2, sln2b)[p5]
                gk = "g50" if p5 == 0 else "g51"
                lk = "l50" if p5 == 0 else "l51"
                if tt >= 2:
                    W(gk, "sync")
                W("ve", "sync", val=n["ve"])
                n[lk] += 16
                if eng == "sync":
                    e.dma_start(out=stage,
                                in_=lnscr[128 * tt:128 * (tt + 1), :]) \
                        .then_inc(lp5[p5], 16)
                W(lk, "gpsimd")
                if tt == 0:
                    W("ve", "gpsimd")
                for k in range(2):
                    n[gk] += 16
                    if eng == "gpsimd":
                        e.indirect_dma_start(
                            buf_out[:, :],
                            IndirectOffsetOnAxis(ridx[k][:, tt:tt + 1], 0),
                            stage, None, bounds_check=breg,
                            oob_is_err=False).then_inc(sgp[p5], 16)
            if eng == "gpsimd":
                e.wait_ge(sgp[0], n["g50"])
                e.wait_ge(sgp[1], n["g51"])
            W("ve", "gpsimd")
            for k in range(2):
                SG(lambda k=k, b=b: e.dma_start(
                    out=gidx_out[b, k, :].rearrange("(t p) -> p t", p=128),
                    in_=ridx[k], allow_slow_non_contiguous=True))
            W("ve", "sync")
            for k in range(2):
                ST(gate_out[b, k, :].rearrange("(t p) -> p t", p=128),
                   rgate[k], allow_slow_non_contiguous=True)

        PH_ph0(0, lnscr2b[0])
        PH_ph1(0, lnscr2b[0])
        PH_ph2a(0, lnscr2b[0])
        PH_ph2b(0, lnscr2b[0])
        PH_ph0(1, lnscr2b[1])
        PH_ph2c(0, lnscr2b[0])
        PH_ph3(0, lnscr2b[0])
        PH_ph4(0, lnscr2b[0])
        PH_ph5(0, lnscr2b[0])
        PH_ph1(1, lnscr2b[1])
        PH_ph2a(1, lnscr2b[1])
        PH_ph2b(1, lnscr2b[1])
        PH_ph2c(1, lnscr2b[1])
        PH_ph3(1, lnscr2b[1])
        PH_ph4(1, lnscr2b[1])
        PH_ph5(1, lnscr2b[1])

        BAR("sync", "gpsimd", "tensor", "scalar", "vector")
        return n

    with nc.Block() as block:
        @block.sync
        def _(e):
            emit("sync", e)

        @block.gpsimd
        def _(e):
            emit("gpsimd", e)

        @block.tensor
        def _(e):
            emit("tensor", e)

        @block.scalar
        def _(e):
            emit("scalar", e)

        @block.vector
        def _(e):
            emit("vector", e)

    return nc



def build_launch3():
    nc = bass.Bass("TRN2", target_bir_lowering=False, debug=False,
                   num_devices=NCORES, name="launch3")
    NT = T // 128
    NS = NEXP * BPC * CAP

    r2_in = nc.dram_tensor("resid2", [BPC, T, C], F32, kind="ExternalInput").ap()
    yall_in = nc.dram_tensor("yall", [NS, C], BF16, kind="ExternalInput").ap()
    gidx_in = nc.dram_tensor("gidx", [BPC, 2, T], I32, kind="ExternalInput").ap()
    gate_in = nc.dram_tensor("gates", [BPC, 2, T], F32, kind="ExternalInput").ap()
    w_in = {w: nc.dram_tensor("w" + w, [C, C], F32, kind="ExternalInput").ap()
            for w in "qkvo"}
    b_in = {w: nc.dram_tensor("b" + w, [1, C], F32, kind="ExternalInput").ap()
            for w in "qkvo"}

    r4_out = nc.dram_tensor("resid4", [BPC, T, C], F32, kind="ExternalOutput").ap()
    xln4_out = nc.dram_tensor("xln4", [BPC, T, C], BF16, kind="ExternalOutput").ap()
    amax_out = nc.dram_tensor("amax1", [1, 1], F32, kind="ExternalOutput").ap()

    r3scr = nc.dram_tensor("r3scr", [BPC, T, C], F32, kind="Internal").ap()
    lnscr = nc.dram_tensor("lnscr3", [T, C], BF16, kind="Internal").ap()
    qkvscr = nc.dram_tensor("qkvscr", [T, C], BF16, kind="Internal").ap()
    flipscr = nc.dram_tensor("flipscr", [1, 128], F32, kind="Internal").ap()
    ones128_in = nc.inline_tensor(np.ones((1, 128), np.float32), "ones128c3").ap()

    A = nc.alloc_sbuf_tensor
    xlnT = [A(f"xlnT{i}", [128, T], BF16).ap() for i in range(8)]
    qkvT = [A(f"qkvT{i}", [128, T], BF16).ap() for i in range(8)]
    wsb = {w: [A(f"w_{w}{i}", [128, C], BF16).ap() for i in range(8)] for w in "qkvo"}
    bbc = {w: A(f"bbc_{w}", [128, C], BF16).ap() for w in "qkvo"}
    brow = A("brow3", [1, 512], F32).ap()
    sones128 = A("sones128_3", [1, 128], F32).ap()
    sx = [A(f"sx3_{i}", [128, C], F32).ap() for i in range(2)]
    sres = [A(f"sres3_{i}", [128, C], F32).ap() for i in range(2)]
    qk = [A(f"qk3_{i}", [128, 2 * C], F32).ap() for i in range(2)]
    vf = [A(f"vf3_{i}", [128, C], F32).ap() for i in range(2)]
    coeff = [A(f"coeff3_{i}", [128, 16], F32).ap() for i in range(2)]
    qkvsb = [A(f"qkvsb{i}", [128, C], BF16).ap() for i in range(2)]
    y0 = [A(f"y0sb{i}", [128, C], BF16).ap() for i in range(2)]
    y1 = [A(f"y1sb{i}", [128, C], BF16).ap() for i in range(2)]
    gi0 = [A(f"gi0_{i}", [128, 1], I32).ap() for i in range(2)]
    gi1 = [A(f"gi1_{i}", [128, 1], I32).ap() for i in range(2)]
    gt0 = [A(f"gt0_{i}", [128, 1], F32).ap() for i in range(2)]
    gt1 = [A(f"gt1_{i}", [128, 1], F32).ap() for i in range(2)]
    lncols = [A(f"l3cols{i}", [128, 8], F32).ap() for i in range(2)]
    LNK = ["ssum", "ssumsq", "smean", "sm2", "svar", "ssqrt", "srstd"]
    L2 = [{k: lncols[i][:, j:j + 1] for j, k in enumerate(LNK)}
          for i in range(2)]
    sln = [A(f"sln3_{i}", [128, C], BF16).ap() for i in range(2)]
    samax = A("samax", [128, 1], F32).ap()
    stmp1 = A("stmp1_3", [128, 1], F32).ap()
    sflip = A("sflip", [1, 128], F32).ap()
    sam1 = A("sam1", [1, 1], F32).ap()

    P = nc.alloc_psum_tensor
    pb = [P(f"pb3_{i}", [128, 512], F32).ap() for i in range(4)]
    pb_bc = P("pb3_bc", [128, 512], F32).ap()

    s_ld = nc.alloc_semaphore("s_ld")
    s_ldg = nc.alloc_semaphore("s_ldg")
    s_mm = nc.alloc_semaphore("s_mm")
    s_sc = nc.alloc_semaphore("s_sc")
    s_ve = nc.alloc_semaphore("s_ve")
    s_st = nc.alloc_semaphore("s_st")
    s_sg = nc.alloc_semaphore("s_sg")
    sp3 = [nc.alloc_semaphore("sp3_0"), nc.alloc_semaphore("sp3_1")]
    SEMS = {"ld": s_ld, "ldg": s_ldg, "mm": s_mm, "sc": s_sc, "ve": s_ve,
            "st": s_st, "sg": s_sg, "sp0": sp3[0], "sp1": sp3[1]}
    PRODUCER = {"ld": "sync", "st": "sync", "ldg": "gpsimd", "sg": "gpsimd",
                "mm": "tensor", "sc": "scalar", "ve": "vector",
                "sp0": "sync", "sp1": "sync"}

    def emit(eng, e):
        n = dict(ld=0, ldg=0, mm=0, sc=0, ve=0, st=0, sg=0, sp0=0, sp1=0)
        breg = None
        if eng == "gpsimd":
            breg = e.alloc_register("bcreg3")
            e.reg_mov(breg, NS - 1)

        def STP(p, out, in_):
            key = "sp0" if p == 0 else "sp1"
            n[key] += 16
            if eng == "sync":
                e.dma_start(out=out, in_=in_).then_inc(sp3[p], 16)

        def LD(out, in_, **k):
            n["ld"] += 16
            if eng == "sync":
                e.dma_start(out=out, in_=in_, **k).then_inc(s_ld, 16)

        def LDG(out, in_, **k):
            n["ldg"] += 16
            if eng == "gpsimd":
                e.dma_start(out=out, in_=in_, **k).then_inc(s_ldg, 16)

        def ST(out, in_):
            n["st"] += 16
            if eng == "sync":
                e.dma_start(out=out, in_=in_).then_inc(s_st, 16)

        def W(which, tgt, val=None):
            if eng == tgt:
                e.wait_ge(SEMS[which], n[which] if val is None else val)

        def MM(fn):
            n["mm"] += 1
            if eng == "tensor":
                fn().then_inc(s_mm, 1)

        def MME(fn):
            if eng == "tensor":
                fn()

        def SC(fn):
            n["sc"] += 1
            if eng == "scalar":
                fn().then_inc(s_sc, 1)

        def VE(fn):
            n["ve"] += 1
            if eng == "vector":
                fn().then_inc(s_ve, 1)

        def SG(fn):
            n["sg"] += 16
            if eng == "gpsimd":
                fn().then_inc(s_sg, 16)

        def VEc(fn):
            W("ve", "vector")
            VE(fn)

        def SCc(fn):
            W("sc", "scalar")
            SC(fn)

        def BAR(*engines):
            for tgt in engines:
                for k, sem in SEMS.items():
                    if PRODUCER[k] != tgt and n[k] > 0:
                        if eng == tgt:
                            e.wait_ge(sem, n[k])

        def LN_CHAIN(xin, Lc, dst_bf, extra_amax=False):
            W("ld", "scalar")
            W("ve", "scalar")
            SC(lambda: e.activation(sq_scr, xin, AF.Square,
                                    accum_out=Lc["ssumsq"]))
            W("ld", "vector")
            VE(lambda: e.reduce_sum(Lc["ssum"], xin, AX.X))
            W("sc", "vector")
            VEc(lambda: e.tensor_scalar(Lc["smean"], Lc["ssum"], 1.0 / C, None,
                                        ALU.mult))
            VEc(lambda: e.tensor_scalar(Lc["svar"], Lc["ssumsq"], 1.0 / C, None,
                                        ALU.mult))
            VEc(lambda: e.tensor_mul(Lc["sm2"], Lc["smean"], Lc["smean"]))
            VEc(lambda: e.tensor_sub(Lc["svar"], Lc["svar"], Lc["sm2"]))
            VEc(lambda: e.tensor_scalar(Lc["svar"], Lc["svar"], 1e-5, None,
                                        ALU.add))
            W("ve", "scalar")
            SC(lambda: e.activation(Lc["ssqrt"], Lc["svar"], AF.Sqrt))
            W("sc", "vector")
            VEc(lambda: e.reciprocal(Lc["srstd"], Lc["ssqrt"]))
            W("sp0", "vector")
            W("sp1", "vector")
            VEc(lambda: e.tensor_scalar(dst_bf, xin, Lc["smean"], Lc["srstd"],
                                        ALU.subtract, ALU.mult))
            if extra_amax:
                VEc(lambda: e.tensor_reduce(stmp1, dst_bf, AX.X, ALU.max,
                                            apply_absolute_value=True))
                VEc(lambda: e.tensor_max(samax, samax, stmp1))

        sq_scr = qk[0][:, 0:C]

        LD(sones128, ones128_in)
        for w in "qkvo":
            for i in range(8):
                LDG(wsb[w][i], w_in[w][128 * i:128 * (i + 1), :])
        for w in "qkvo":
            for cc in range(2):
                W("sc", "sync")
                LD(brow, b_in[w][:, 512 * cc:512 * (cc + 1)])
                W("ld", "tensor")
                W("sc", "tensor")
                MM(lambda: e.matmul(pb_bc, sones128, brow, start=True, stop=True))
                W("mm", "scalar")
                SC(lambda w=w, cc=cc: e.copy(bbc[w][:, 512 * cc:512 * (cc + 1)],
                                             pb_bc))
        W("ld", "vector")
        VE(lambda: e.memset(samax, 0.0))

        for b in range(BPC):
            BAR("sync", "scalar", "vector", "gpsimd")
            ve_hist = []
            sg_hist = []
            for tt in range(NT):
                p = tt % 2
                if tt >= 2:
                    W("ve", "sync", val=ve_hist[-2])
                    W("sg", "sync", val=sg_hist[-2])
                    W("sp0" if p == 0 else "sp1", "sync")
                LD(sx[p], r2_in[b, 128 * tt:128 * (tt + 1), :])
                LD(gi0[p], gidx_in[b, 0, 128 * tt:128 * (tt + 1)]
                   .rearrange("(p one) -> p one", one=1))
                LD(gi1[p], gidx_in[b, 1, 128 * tt:128 * (tt + 1)]
                   .rearrange("(p one) -> p one", one=1))
                LD(gt0[p], gate_in[b, 0, 128 * tt:128 * (tt + 1)]
                   .rearrange("(p one) -> p one", one=1))
                LD(gt1[p], gate_in[b, 1, 128 * tt:128 * (tt + 1)]
                   .rearrange("(p one) -> p one", one=1))
                if tt >= 2:
                    W("sg", "vector", val=sg_hist[-2])
                VE(lambda p=p: e.memset(y0[p], 0.0))
                VE(lambda p=p: e.memset(y1[p], 0.0))
                W("ve", "gpsimd")
                W("ld", "gpsimd")
                SG(lambda p=p: e.indirect_dma_start(
                    y0[p], None, yall_in[:, :], IndirectOffsetOnAxis(gi0[p], 0),
                    bounds_check=breg, oob_is_err=False))
                SG(lambda p=p: e.indirect_dma_start(
                    y1[p], None, yall_in[:, :], IndirectOffsetOnAxis(gi1[p], 0),
                    bounds_check=breg, oob_is_err=False))
                sg_hist.append(n["sg"])
                W("ld", "vector")
                W("sg", "vector")
                VEc(lambda p=p: e.scalar_tensor_tensor(sx[p], y0[p], gt0[p],
                                                       sx[p], ALU.mult, ALU.add))
                VEc(lambda p=p: e.scalar_tensor_tensor(sx[p], y1[p], gt1[p],
                                                       sx[p], ALU.mult, ALU.add))
                ve_hist.append(n["ve"])
                W("ve", "sync")
                STP(p, r3scr[b, 128 * tt:128 * (tt + 1), :], sx[p])
            if eng == "sync":
                e.wait_ge(sp3[0], n["sp0"])
                e.wait_ge(sp3[1], n["sp1"])
            ve_hist = []
            for tt in range(NT):
                p = tt % 2
                if tt >= 2:
                    W("ve", "sync", val=ve_hist[-2])
                    W("sp0" if p == 0 else "sp1", "sync")
                LD(sx[p], r3scr[b, 128 * tt:128 * (tt + 1), :])
                LN_CHAIN(sx[p], L2[p], sln[p])
                ve_hist.append(n["ve"])
                W("ve", "sync")
                STP(p, lnscr[128 * tt:128 * (tt + 1), :], sln[p])

            W("st", "sync")
            if eng == "sync":
                e.wait_ge(sp3[0], n["sp0"])
                e.wait_ge(sp3[1], n["sp1"])
            W("mm", "sync")
            for ci in range(8):
                LD(xlnT[ci], lnscr[:, 128 * ci:128 * (ci + 1)], transpose=True)
            W("ld", "sync")

            BAR("tensor", "vector", "scalar")
            ve_hist = []
            for tt in range(NT):
                p = tt % 2
                dsts = [(qk[p][:, 0:C], "q"), (qk[p][:, C:2 * C], "k"),
                        (vf[p], "v")]
                for wi, (dstf, w) in enumerate(dsts):
                    for cc in range(2):
                        g = tt * 6 + wi * 2 + cc
                        bank = pb[g % 3]
                        if g >= 3:
                            W("ve", "tensor", val=n["ve"] - 2)
                        for ci in range(7):
                            MME(lambda bank=bank, w=w, ci=ci, tt=tt, cc=cc:
                                e.matmul(
                                    bank, xlnT[ci][:, 128 * tt:128 * (tt + 1)],
                                    wsb[w][ci][:, 512 * cc:512 * (cc + 1)],
                                    start=(ci == 0), stop=False))
                        MM(lambda bank=bank, w=w, tt=tt, cc=cc: e.matmul(
                            bank, xlnT[7][:, 128 * tt:128 * (tt + 1)],
                            wsb[w][7][:, 512 * cc:512 * (cc + 1)],
                            start=False, stop=True))
                        W("mm", "vector")
                        VE(lambda dstf=dstf, w=w, cc=cc, bank=bank: e.tensor_add(
                            dstf[:, 512 * cc:512 * (cc + 1)], bank,
                            bbc[w][:, 512 * cc:512 * (cc + 1)]))
                W("ve", "scalar")
                SCc(lambda p=p: e.activation(qk[p], qk[p], AF.Square))
                SCc(lambda p=p: e.activation(qk[p], qk[p], AF.Exp, scale=-0.5))
                W("sc", "vector")
                VEc(lambda p=p: e.tensor_mul(qk[p][:, 0:C], qk[p][:, 0:C],
                                             qk[p][:, C:2 * C]))
                VEc(lambda p=p: e.tensor_reduce(
                    coeff[p].unsqueeze(2),
                    qk[p][:, 0:C].rearrange("p (h w) -> p h w", w=64),
                    AX.X, ALU.add))
                W("st", "vector")
                VEc(lambda p=p: e.tensor_mul(
                    qkvsb[p].rearrange("p (h w) -> p h w", w=64),
                    vf[p].rearrange("p (h w) -> p h w", w=64),
                    coeff[p].unsqueeze(2).broadcast_to([128, 16, 64])))
                ve_hist.append(n["ve"])
                W("ve", "sync")
                ST(qkvscr[128 * tt:128 * (tt + 1), :], qkvsb[p])

            W("st", "sync")
            W("mm", "sync")
            for ci in range(8):
                LD(qkvT[ci], qkvscr[:, 128 * ci:128 * (ci + 1)], transpose=True)
            W("ld", "sync")

            BAR("tensor", "sync", "vector", "scalar")
            ve_hist = []
            for tt in range(NT):
                p = tt % 2
                if tt >= 2:
                    W("ve", "sync", val=ve_hist[-2])
                    W("sp0" if p == 0 else "sp1", "sync")
                LD(sx[p], r3scr[b, 128 * tt:128 * (tt + 1), :])
                for cc in range(2):
                    g = tt * 2 + cc
                    bank = pb[g % 3]
                    if g >= 3:
                        W("ve", "tensor", val=n["ve"] - 3)
                    for ci in range(7):
                        MME(lambda bank=bank, ci=ci, tt=tt, cc=cc: e.matmul(
                            bank, qkvT[ci][:, 128 * tt:128 * (tt + 1)],
                            wsb["o"][ci][:, 512 * cc:512 * (cc + 1)],
                            start=(ci == 0), stop=False))
                    MM(lambda bank=bank, tt=tt, cc=cc: e.matmul(
                        bank, qkvT[7][:, 128 * tt:128 * (tt + 1)],
                        wsb["o"][7][:, 512 * cc:512 * (cc + 1)],
                        start=False, stop=True))
                    W("mm", "vector")
                    W("ld", "vector")
                    VE(lambda cc=cc, bank=bank, p=p: e.tensor_add(
                        sres[p][:, 512 * cc:512 * (cc + 1)], bank,
                        bbc["o"][:, 512 * cc:512 * (cc + 1)]))
                    VEc(lambda cc=cc, p=p: e.tensor_add(
                        sres[p][:, 512 * cc:512 * (cc + 1)],
                        sres[p][:, 512 * cc:512 * (cc + 1)],
                        sx[p][:, 512 * cc:512 * (cc + 1)]))
                W("ve", "sync")
                STP(p, r4_out[b, 128 * tt:128 * (tt + 1), :], sres[p])
                W("ve", "scalar")
                W("sp0" if p == 0 else "sp1", "scalar")
                SC(lambda p=p: e.activation(sx[p], sres[p], AF.Square,
                                            accum_out=L2[p]["ssumsq"]))
                VEc(lambda p=p: e.reduce_sum(L2[p]["ssum"], sres[p], AX.X))
                W("sc", "vector")
                VEc(lambda p=p: e.tensor_scalar(L2[p]["smean"], L2[p]["ssum"],
                                                1.0 / C, None, ALU.mult))
                VEc(lambda p=p: e.tensor_scalar(L2[p]["svar"], L2[p]["ssumsq"],
                                                1.0 / C, None, ALU.mult))
                VEc(lambda p=p: e.tensor_mul(L2[p]["sm2"], L2[p]["smean"],
                                             L2[p]["smean"]))
                VEc(lambda p=p: e.tensor_sub(L2[p]["svar"], L2[p]["svar"],
                                             L2[p]["sm2"]))
                VEc(lambda p=p: e.tensor_scalar(L2[p]["svar"], L2[p]["svar"],
                                                1e-5, None, ALU.add))
                W("ve", "scalar")
                SC(lambda p=p: e.activation(L2[p]["ssqrt"], L2[p]["svar"],
                                            AF.Sqrt))
                W("sc", "vector")
                VEc(lambda p=p: e.reciprocal(L2[p]["srstd"], L2[p]["ssqrt"]))
                W("sp0" if p == 0 else "sp1", "vector")
                VEc(lambda p=p: e.tensor_scalar(sln[p], sres[p], L2[p]["smean"],
                                                L2[p]["srstd"],
                                                ALU.subtract, ALU.mult))
                VEc(lambda p=p: e.tensor_reduce(stmp1, sln[p], AX.X, ALU.max,
                                                apply_absolute_value=True))
                VEc(lambda: e.tensor_max(samax, samax, stmp1))
                ve_hist.append(n["ve"])
                W("ve", "sync")
                STP(p, xln4_out[b, 128 * tt:128 * (tt + 1), :], sln[p])

        BAR("sync", "vector")
        ST(flipscr.rearrange("a b -> b a"), samax)
        W("st", "sync")
        LD(sflip, flipscr)
        W("ld", "vector")
        VEc(lambda: e.tensor_reduce(sam1, sflip, AX.X, ALU.max))
        W("ve", "sync")
        ST(amax_out, sam1)
        BAR("sync", "gpsimd", "tensor", "scalar", "vector")
        return n

    with nc.Block() as block:
        @block.sync
        def _(e):
            emit("sync", e)

        @block.gpsimd
        def _(e):
            emit("gpsimd", e)

        @block.tensor
        def _(e):
            emit("tensor", e)

        @block.scalar
        def _(e):
            emit("scalar", e)

        @block.vector
        def _(e):
            emit("vector", e)

    return nc



def _qdq_emit_helpers(nc, e, eng, n, SEMS, PRODUCER):
    h = {}

    def LD(out, in_, **k):
        n["ld"] += 16
        if eng == "sync":
            e.dma_start(out=out, in_=in_, **k).then_inc(SEMS["ld"], 16)

    def LDG(out, in_, **k):
        n["ldg"] += 16
        if eng == "gpsimd":
            e.dma_start(out=out, in_=in_, **k).then_inc(SEMS["ldg"], 16)

    def ST(out, in_):
        n["st"] += 16
        if eng == "sync":
            e.dma_start(out=out, in_=in_).then_inc(SEMS["st"], 16)

    def STG(out, in_):
        n["sg"] += 16
        if eng == "gpsimd":
            e.dma_start(out=out, in_=in_).then_inc(SEMS["sg"], 16)

    def W(which, tgt, val=None):
        if eng == tgt:
            e.wait_ge(SEMS[which], n[which] if val is None else val)

    def MM(fn):
        n["mm"] += 1
        if eng == "tensor":
            fn().then_inc(SEMS["mm"], 1)

    def MME(fn):
        if eng == "tensor":
            fn()

    def SC(fn):
        n["sc"] += 1
        if eng == "scalar":
            fn().then_inc(SEMS["sc"], 1)

    def VE(fn):
        n["ve"] += 1
        if eng == "vector":
            fn().then_inc(SEMS["ve"], 1)

    def VEc(fn):
        W("ve", "vector")
        VE(fn)

    def SCc(fn):
        W("sc", "scalar")
        SC(fn)

    def BAR(*engines):
        for tgt in engines:
            for k, sem in SEMS.items():
                if PRODUCER[k] != tgt and n[k] > 0:
                    if eng == tgt:
                        e.wait_ge(sem, n[k])

    h.update(LD=LD, LDG=LDG, ST=ST, STG=STG, W=W, MM=MM, MME=MME, SC=SC,
             VE=VE, VEc=VEc, SCc=SCc, BAR=BAR)
    return h


def _mk_sems(nc):
    SEMS = {k: nc.alloc_semaphore("s_" + k)
            for k in ["ld", "ldg", "mm", "sc", "ve", "st", "sg"]}
    PRODUCER = {"ld": "sync", "st": "sync", "ldg": "gpsimd", "sg": "gpsimd",
                "mm": "tensor", "sc": "scalar", "ve": "vector"}
    return SEMS, PRODUCER


def _run_block(nc, emit):
    with nc.Block() as block:
        @block.sync
        def _(e):
            emit("sync", e)

        @block.gpsimd
        def _(e):
            emit("gpsimd", e)

        @block.tensor
        def _(e):
            emit("tensor", e)

        @block.scalar
        def _(e):
            emit("scalar", e)

        @block.vector
        def _(e):
            emit("vector", e)
    return nc


def build_launch4():
    nc = bass.Bass("TRN2", target_bir_lowering=False, debug=False,
                   num_devices=NCORES, name="launch4")
    NT = TPC // 128
    IT = L1I // 128

    xln4_in = nc.dram_tensor("xln4", [TPC, C], BF16, kind="ExternalInput").ap()
    am_in = nc.dram_tensor("amax1all", [1, 8], F32, kind="ExternalInput").ap()
    wup_in = nc.dram_tensor("wup", [L1I, C], F32, kind="ExternalInput").ap()
    wg_in = nc.dram_tensor("wgate", [L1I, C], F32, kind="ExternalInput").ap()
    up_out = nc.dram_tensor("up", [L1I, TPC], BF16, kind="ExternalOutput").ap()
    gate_out = nc.dram_tensor("gate", [L1I, TPC], BF16, kind="ExternalOutput").ap()
    am2_out = nc.dram_tensor("amax2", [1, 2], F32, kind="ExternalOutput").ap()

    xqscr = nc.dram_tensor("xqscr", [TPC, C], BF16, kind="Internal").ap()
    wdscr = nc.dram_tensor("wdscr", [2, L1I, C], BF16, kind="Internal").ap()
    flipscr = nc.dram_tensor("flip4", [2, 128], F32, kind="Internal").ap()
    ones128_in = nc.inline_tensor(np.ones((1, 128), np.float32), "ones4").ap()

    A = nc.alloc_sbuf_tensor
    sam = A("sam4", [1, 8], F32).ap()
    sm1 = A("sm1", [1, 1], F32).ap()
    ssc = A("ssc4", [1, 2], F32).ap()
    sscb = A("sscb4", [128, 2], F32).ap()
    sones = A("sones4", [1, 128], F32).ap()
    sx = A("sx4", [128, C], BF16).ap()
    sxq = A("sxq", [128, C], F32).ap()
    sxqb = A("sxqb", [128, C], BF16).ap()
    sw = A("sw4", [128, C], F32).ap()
    swa = A("swa", [128, 32], F32).ap()
    sws = A("sws", [128, 32], F32).ap()
    swr = A("swr", [128, 32], F32).ap()
    swq = A("swq", [128, C], F32).ap()
    swqb = A("swqb", [128, C], BF16).ap()
    xqT = [A(f"xqT{i}", [128, TPC], BF16).ap() for i in range(8)]
    wT = [A(f"wT4_{i}", [128, L1I], BF16).ap() for i in range(8)]
    outb = [A(f"outb4_{i}", [128, 512], BF16).ap() for i in range(2)]
    samax = [A(f"samax4_{i}", [128, 1], F32).ap() for i in range(2)]
    stmp1 = A("stmp14", [128, 1], F32).ap()
    sflip = A("sflip4", [1, 128], F32).ap()
    sam2 = A("sam2o", [1, 2], F32).ap()

    P = nc.alloc_psum_tensor
    pb = [P(f"pb4_{i}", [128, 512], F32).ap() for i in range(2)]
    pb_bc = P("pb4_bc", [128, 2], F32).ap()

    SEMS, PRODUCER = _mk_sems(nc)
    sp = [nc.alloc_semaphore("sp4_0"), nc.alloc_semaphore("sp4_1")]

    def emit(eng, e):
        n = dict(ld=0, ldg=0, mm=0, sc=0, ve=0, st=0, sg=0)
        spn = [0, 0]
        H = _qdq_emit_helpers(nc, e, eng, n, SEMS, PRODUCER)
        LD, LDG, ST, STG, W = H["LD"], H["LDG"], H["ST"], H["STG"], H["W"]
        MM, MME, SC, VE, VEc, SCc, BAR = (H["MM"], H["MME"], H["SC"], H["VE"],
                                          H["VEc"], H["SCc"], H["BAR"])

        def STP(p, out, in_):
            spn[p] += 16
            if eng == "sync":
                e.dma_start(out=out, in_=in_).then_inc(sp[p], 16)

        LD(sam, am_in)
        LD(sones, ones128_in)
        W("ld", "vector")
        VE(lambda: e.tensor_reduce(sm1, sam, AX.X, ALU.max))
        VEc(lambda: e.tensor_scalar(ssc[:, 1:2], sm1, 1.0 / 32767.0, None,
                                    ALU.mult))
        VEc(lambda: e.tensor_scalar(ssc[:, 1:2], ssc[:, 1:2], 1e-12, None,
                                    ALU.max))
        VEc(lambda: e.reciprocal(ssc[:, 0:1], ssc[:, 1:2]))
        W("ve", "tensor")
        MM(lambda: e.matmul(pb_bc, sones, ssc, start=True, stop=True))
        W("mm", "scalar")
        SC(lambda: e.copy(sscb, pb_bc))

        for tt in range(NT):
            W("ve", "sync")
            LD(sx, xln4_in[128 * tt:128 * (tt + 1), :])
            W("ld", "vector")
            W("sc", "vector")
            VEc(lambda: e.tensor_scalar(sxq, sx, sscb[:, 0:1], ROUND_C,
                                        ALU.mult, ALU.add))
            W("st", "vector")
            VEc(lambda: e.tensor_scalar(sxqb, sxq, ROUND_C, sscb[:, 1:2],
                                        ALU.subtract, ALU.mult))
            W("ve", "sync")
            ST(xqscr[128 * tt:128 * (tt + 1), :], sxqb)

        for wi, w_in_ in enumerate((wup_in, wg_in)):
            for it in range(IT):
                W("ve", "sync")
                LD(sw, w_in_[128 * it:128 * (it + 1), :])
                W("ld", "vector")
                VEc(lambda: e.tensor_reduce(
                    swa.rearrange("p (a one) -> p a one", one=1),
                    sw.rearrange("p (a w) -> p a w", w=32), AX.X, ALU.max,
                    apply_absolute_value=True))
                VEc(lambda: e.tensor_scalar(sws, swa, 1.0 / 7.0, None, ALU.mult))
                VEc(lambda: e.tensor_scalar(sws, sws, 1e-12, None, ALU.max))
                VEc(lambda: e.reciprocal(swr, sws))
                VEc(lambda: e.tensor_mul(
                    swq.rearrange("p (a w) -> p a w", w=32),
                    sw.rearrange("p (a w) -> p a w", w=32),
                    swr.rearrange("p (a one) -> p a one", one=1)
                    .broadcast_to([128, 32, 32])))
                VEc(lambda: e.tensor_scalar(swq, swq, ROUND_C, ROUND_C,
                                            ALU.add, ALU.subtract))
                W("st", "vector")
                VEc(lambda: e.tensor_mul(
                    swqb.rearrange("p (a w) -> p a w", w=32),
                    swq.rearrange("p (a w) -> p a w", w=32),
                    sws.rearrange("p (a one) -> p a one", one=1)
                    .broadcast_to([128, 32, 32])))
                W("ve", "sync")
                ST(wdscr[wi, 128 * it:128 * (it + 1), :], swqb)

        W("st", "sync")
        for ci in range(8):
            LD(xqT[ci], xqscr[:, 128 * ci:128 * (ci + 1)], transpose=True)
        W("ld", "sync")

        for wi in range(2):
            dst = (up_out, gate_out)[wi]
            W("mm", "sync")
            for ci in range(8):
                LD(wT[ci], wdscr[wi, :, 128 * ci:128 * (ci + 1)], transpose=True)
            W("ld", "sync")
            BAR("tensor", "vector")
            if eng == "vector":
                e.memset(samax[wi], 0.0)
            evac_hist = []
            for it in range(IT):
                for tc in range(TPC // 512):
                    g = it * (TPC // 512) + tc
                    bank = pb[g % 2]
                    if g >= 2:
                        W("sc", "tensor", val=evac_hist[-2])
                    for ci in range(7):
                        MME(lambda bank=bank, ci=ci, it=it, tc=tc: e.matmul(
                            bank, wT[ci][:, 128 * it:128 * (it + 1)],
                            xqT[ci][:, 512 * tc:512 * (tc + 1)],
                            start=(ci == 0), stop=False))
                    MM(lambda bank=bank, it=it, tc=tc: e.matmul(
                        bank, wT[7][:, 128 * it:128 * (it + 1)],
                        xqT[7][:, 512 * tc:512 * (tc + 1)],
                        start=False, stop=True))
                    W("mm", "scalar")
                    if g >= 2:
                        if eng == "scalar":
                            e.wait_ge(sp[g % 2], spn[g % 2])
                    SC(lambda g=g, bank=bank: e.copy(outb[g % 2], bank))
                    evac_hist.append(n["sc"])
                    W("sc", "vector")
                    VEc(lambda g=g: e.tensor_reduce(stmp1, outb[g % 2], AX.X,
                                                    ALU.max,
                                                    apply_absolute_value=True))
                    VEc(lambda wi=wi: e.tensor_max(samax[wi], samax[wi], stmp1))
                    W("sc", "sync")
                    W("ve", "sync")
                    STP(g % 2, dst[128 * it:128 * (it + 1),
                                   512 * tc:512 * (tc + 1)], outb[g % 2])
            if eng == "sync":
                e.wait_ge(sp[0], spn[0])
                e.wait_ge(sp[1], spn[1])
            W("ve", "sync")
            ST(flipscr[wi:wi + 1, :].rearrange("a b -> b a"), samax[wi])
        W("st", "sync")
        LD(sflip, flipscr[0:1, :])
        W("ld", "vector")
        VEc(lambda: e.tensor_reduce(sam2[:, 0:1], sflip, AX.X, ALU.max))
        W("ve", "sync")
        LD(sflip, flipscr[1:2, :])
        W("ld", "vector")
        VEc(lambda: e.tensor_reduce(sam2[:, 1:2], sflip, AX.X, ALU.max))
        W("ve", "sync")
        ST(am2_out, sam2)
        BAR("sync", "gpsimd", "tensor", "scalar", "vector")
        return n

    return _run_block(nc, emit)


def build_launch5():
    nc = bass.Bass("TRN2", target_bir_lowering=False, debug=False,
                   num_devices=NCORES, name="launch5")
    WID = TPC

    gate_in = nc.dram_tensor("gate", [L1I, TPC], BF16, kind="ExternalInput").ap()
    am_in = nc.dram_tensor("amax2all", [8, 2], F32, kind="ExternalInput").ap()
    g2_out = nc.dram_tensor("g2pre", [L1I, TPC], BF16, kind="ExternalOutput").ap()
    am3_out = nc.dram_tensor("amax3", [1, 1], F32, kind="ExternalOutput").ap()
    flipscr = nc.dram_tensor("flip5", [1, 128], F32, kind="Internal").ap()
    mflip = nc.dram_tensor("mflip5", [1, 8], F32, kind="Internal").ap()
    ones128_in = nc.inline_tensor(np.ones((1, 128), np.float32), "ones5").ap()

    A = nc.alloc_sbuf_tensor
    sam = A("sam5", [8, 2], F32).ap()
    sm = A("sm5", [8, 1], F32).ap()
    smf = A("smf5", [1, 8], F32).ap()
    ssc = A("ssc5", [1, 2], F32).ap()
    sscb = A("sscb5", [128, 2], F32).ap()
    sones = A("sones5", [1, 128], F32).ap()
    sg_ = [A(f"sg5_{i}", [128, WID], BF16).ap() for i in range(2)]
    sq = [A(f"sq5_{i}", [128, WID], F32).ap() for i in range(2)]
    sgq = [A(f"sgq5_{i}", [128, WID], F32).ap() for i in range(2)]
    ssig = [A(f"ssig5_{i}", [128, WID], F32).ap() for i in range(2)]
    so = [A(f"so5_{i}", [128, WID], BF16).ap() for i in range(2)]
    samax = A("samax5", [128, 1], F32).ap()
    stmp1 = A("stmp15", [128, 1], F32).ap()
    sflip = A("sflip5", [1, 128], F32).ap()
    sam3 = A("sam3o", [1, 1], F32).ap()

    P = nc.alloc_psum_tensor
    pb_bc = P("pb5_bc", [128, 2], F32).ap()

    SEMS, PRODUCER = _mk_sems(nc)
    lp = [nc.alloc_semaphore("lp0"), nc.alloc_semaphore("lp1")]
    sp = [nc.alloc_semaphore("sp0"), nc.alloc_semaphore("sp1")]

    def emit(eng, e):
        n = dict(ld=0, ldg=0, mm=0, sc=0, ve=0, st=0, sg=0)
        lpn = [0, 0]
        spn = [0, 0]
        H = _qdq_emit_helpers(nc, e, eng, n, SEMS, PRODUCER)
        LD, LDG, ST, STG, W = H["LD"], H["LDG"], H["ST"], H["STG"], H["W"]
        MM, MME, SC, VE, VEc, SCc, BAR = (H["MM"], H["MME"], H["SC"], H["VE"],
                                          H["VEc"], H["SCc"], H["BAR"])

        def LDP(p, out, in_):
            lpn[p] += 16
            if eng == "sync":
                e.dma_start(out=out, in_=in_).then_inc(lp[p], 16)

        def STP(p, out, in_):
            spn[p] += 16
            if eng == "sync":
                e.dma_start(out=out, in_=in_).then_inc(sp[p], 16)

        def WP(sems, cnt, p, tgt):
            if eng == tgt:
                e.wait_ge(sems[p], cnt[p])

        LD(sam, am_in)
        LD(sones, ones128_in)
        W("ld", "vector")
        VE(lambda: e.tensor_copy(sm, sam[:, 1:2]))
        W("ve", "sync")
        ST(mflip.rearrange("a b -> b a"), sm)
        W("st", "sync")
        LD(smf, mflip)
        W("ld", "vector")
        VEc(lambda: e.tensor_reduce(ssc[:, 1:2], smf, AX.X, ALU.max))
        VEc(lambda: e.tensor_scalar(ssc[:, 1:2], ssc[:, 1:2], 1.0 / 32767.0,
                                    None, ALU.mult))
        VEc(lambda: e.tensor_scalar(ssc[:, 1:2], ssc[:, 1:2], 1e-12, None,
                                    ALU.max))
        VEc(lambda: e.reciprocal(ssc[:, 0:1], ssc[:, 1:2]))
        W("ve", "tensor")
        MM(lambda: e.matmul(pb_bc, sones, ssc, start=True, stop=True))
        W("mm", "scalar")
        SC(lambda: e.copy(sscb, pb_bc))
        if eng == "vector":
            e.memset(samax, 0.0)

        ve_hist = []
        for it in range(L1I // 128):
            p = it % 2
            if it >= 2:
                W("ve", "sync", val=ve_hist[-2][0])
                WP(sp, spn, p, "sync")
            LDP(p, sg_[p], gate_in[128 * it:128 * (it + 1), :])
            WP(lp, lpn, p, "vector")
            W("sc", "vector")
            VEc(lambda p=p: e.tensor_scalar(sq[p], sg_[p], sscb[:, 0:1],
                                            ROUND_C, ALU.mult, ALU.add))
            VEc(lambda p=p: e.tensor_scalar(sgq[p], sq[p], ROUND_C, sscb[:, 1:2],
                                            ALU.subtract, ALU.mult))
            W("ve", "scalar")
            SCc(lambda p=p: e.activation(ssig[p], sgq[p], AF.Sigmoid))
            W("sc", "vector")
            VEc(lambda p=p: e.tensor_scalar(ssig[p], ssig[p], 65536.0, ROUND_C,
                                            ALU.mult, ALU.add))
            VEc(lambda p=p: e.tensor_scalar(ssig[p], ssig[p], ROUND_C, 65535.0,
                                            ALU.subtract, ALU.min))
            VEc(lambda p=p: e.scalar_tensor_tensor(
                so[p], ssig[p], 1.0 / 65536.0, sgq[p], ALU.mult, ALU.mult))
            VEc(lambda p=p: e.tensor_reduce(stmp1, so[p], AX.X, ALU.max,
                                            apply_absolute_value=True))
            VEc(lambda: e.tensor_max(samax, samax, stmp1))
            ve_hist.append((n["ve"], None))
            W("ve", "sync")
            STP(p, g2_out[128 * it:128 * (it + 1), :], so[p])
        W("ve", "sync")
        ST(flipscr.rearrange("a b -> b a"), samax)
        W("st", "sync")
        LD(sflip, flipscr)
        W("ld", "vector")
        VEc(lambda: e.tensor_reduce(sam3, sflip, AX.X, ALU.max))
        W("ve", "sync")
        ST(am3_out, sam3)
        if eng == "sync":
            e.wait_ge(sp[0], spn[0])
            e.wait_ge(sp[1], spn[1])
        BAR("sync", "gpsimd", "tensor", "scalar", "vector")
        return n

    return _run_block(nc, emit)


def build_launch6():
    nc = bass.Bass("TRN2", target_bir_lowering=False, debug=False,
                   num_devices=NCORES, name="launch6")
    WID = TPC
    up_in = nc.dram_tensor("up", [L1I, TPC], BF16, kind="ExternalInput").ap()
    g2_in = nc.dram_tensor("g2pre", [L1I, TPC], BF16, kind="ExternalInput").ap()
    am2_in = nc.dram_tensor("amax2all", [8, 2], F32, kind="ExternalInput").ap()
    am3_in = nc.dram_tensor("amax3all", [1, 8], F32, kind="ExternalInput").ap()
    t_out = nc.dram_tensor("tprod", [L1I, TPC], BF16, kind="ExternalOutput").ap()
    am4_out = nc.dram_tensor("amax4", [1, 1], F32, kind="ExternalOutput").ap()
    flipscr = nc.dram_tensor("flip6", [1, 128], F32, kind="Internal").ap()
    mflip = nc.dram_tensor("mflip6", [1, 8], F32, kind="Internal").ap()
    ones128_in = nc.inline_tensor(np.ones((1, 128), np.float32), "ones6").ap()

    A = nc.alloc_sbuf_tensor
    sam2 = A("sam26", [8, 2], F32).ap()
    sam3 = A("sam36", [1, 8], F32).ap()
    sm = A("sm6", [8, 1], F32).ap()
    smf = A("smf6", [1, 8], F32).ap()
    ssc = A("ssc6", [1, 4], F32).ap()
    sscb = A("sscb6", [128, 4], F32).ap()
    sones = A("sones6", [1, 128], F32).ap()
    sup = [A(f"sup6_{i}", [128, WID], BF16).ap() for i in range(2)]
    sg2 = [A(f"sg26_{i}", [128, WID], BF16).ap() for i in range(2)]
    sq = [A(f"sq6_{i}", [128, WID], F32).ap() for i in range(2)]
    suq = [A(f"suq6_{i}", [128, WID], F32).ap() for i in range(2)]
    sgq = [A(f"sgq6_{i}", [128, WID], F32).ap() for i in range(2)]
    so = [A(f"so6_{i}", [128, WID], BF16).ap() for i in range(2)]
    samax = A("samax6", [128, 1], F32).ap()
    stmp1 = A("stmp16", [128, 1], F32).ap()
    sflip = A("sflip6", [1, 128], F32).ap()
    sam4 = A("sam4o", [1, 1], F32).ap()

    P = nc.alloc_psum_tensor
    pb_bc = P("pb6_bc", [128, 4], F32).ap()

    SEMS, PRODUCER = _mk_sems(nc)
    lp = [nc.alloc_semaphore("lp0"), nc.alloc_semaphore("lp1")]
    sp = [nc.alloc_semaphore("sp0"), nc.alloc_semaphore("sp1")]

    def emit(eng, e):
        n = dict(ld=0, ldg=0, mm=0, sc=0, ve=0, st=0, sg=0)
        lpn = [0, 0]
        spn = [0, 0]
        H = _qdq_emit_helpers(nc, e, eng, n, SEMS, PRODUCER)
        LD, LDG, ST, STG, W = H["LD"], H["LDG"], H["ST"], H["STG"], H["W"]
        MM, MME, SC, VE, VEc, SCc, BAR = (H["MM"], H["MME"], H["SC"], H["VE"],
                                          H["VEc"], H["SCc"], H["BAR"])

        def LDP(p, out, in_):
            lpn[p] += 16
            if eng == "sync":
                e.dma_start(out=out, in_=in_).then_inc(lp[p], 16)

        def STP(p, out, in_):
            spn[p] += 16
            if eng == "sync":
                e.dma_start(out=out, in_=in_).then_inc(sp[p], 16)

        def WP(sems, cnt, p, tgt):
            if eng == tgt:
                e.wait_ge(sems[p], cnt[p])

        LD(sam2, am2_in)
        LD(sam3, am3_in)
        LD(sones, ones128_in)
        W("ld", "vector")
        VE(lambda: e.tensor_copy(sm, sam2[:, 0:1]))
        W("ve", "sync")
        ST(mflip.rearrange("a b -> b a"), sm)
        W("st", "sync")
        LD(smf, mflip)
        W("ld", "vector")
        VEc(lambda: e.tensor_reduce(ssc[:, 1:2], smf, AX.X, ALU.max))
        VEc(lambda: e.tensor_scalar(ssc[:, 1:2], ssc[:, 1:2], 1.0 / 32767.0,
                                    None, ALU.mult))
        VEc(lambda: e.tensor_scalar(ssc[:, 1:2], ssc[:, 1:2], 1e-12, None,
                                    ALU.max))
        VEc(lambda: e.reciprocal(ssc[:, 0:1], ssc[:, 1:2]))
        VEc(lambda: e.tensor_reduce(ssc[:, 3:4], sam3, AX.X, ALU.max))
        VEc(lambda: e.tensor_scalar(ssc[:, 3:4], ssc[:, 3:4], 1.0 / 32767.0,
                                    None, ALU.mult))
        VEc(lambda: e.tensor_scalar(ssc[:, 3:4], ssc[:, 3:4], 1e-12, None,
                                    ALU.max))
        VEc(lambda: e.reciprocal(ssc[:, 2:3], ssc[:, 3:4]))
        W("ve", "tensor")
        MM(lambda: e.matmul(pb_bc, sones, ssc, start=True, stop=True))
        W("mm", "scalar")
        SC(lambda: e.copy(sscb, pb_bc))
        if eng == "vector":
            e.memset(samax, 0.0)

        ve_hist = []
        for it in range(L1I // 128):
            p = it % 2
            if it >= 2:
                W("ve", "sync", val=ve_hist[-2])
                WP(sp, spn, p, "sync")
            LDP(p, sup[p], up_in[128 * it:128 * (it + 1), :])
            LDP(p, sg2[p], g2_in[128 * it:128 * (it + 1), :])
            WP(lp, lpn, p, "vector")
            W("sc", "vector")
            VEc(lambda p=p: e.tensor_scalar(sq[p], sup[p], sscb[:, 0:1],
                                            ROUND_C, ALU.mult, ALU.add))
            VEc(lambda p=p: e.tensor_scalar(suq[p], sq[p], ROUND_C, sscb[:, 1:2],
                                            ALU.subtract, ALU.mult))
            VEc(lambda p=p: e.tensor_scalar(sq[p], sg2[p], sscb[:, 2:3],
                                            ROUND_C, ALU.mult, ALU.add))
            VEc(lambda p=p: e.tensor_scalar(sgq[p], sq[p], ROUND_C, sscb[:, 3:4],
                                            ALU.subtract, ALU.mult))
            VEc(lambda p=p: e.tensor_mul(so[p], sgq[p], suq[p]))
            VEc(lambda p=p: e.tensor_reduce(stmp1, so[p], AX.X, ALU.max,
                                            apply_absolute_value=True))
            VEc(lambda: e.tensor_max(samax, samax, stmp1))
            ve_hist.append(n["ve"])
            W("ve", "sync")
            STP(p, t_out[128 * it:128 * (it + 1), :], so[p])
        W("ve", "sync")
        ST(flipscr.rearrange("a b -> b a"), samax)
        W("st", "sync")
        LD(sflip, flipscr)
        W("ld", "vector")
        VEc(lambda: e.tensor_reduce(sam4, sflip, AX.X, ALU.max))
        W("ve", "sync")
        ST(am4_out, sam4)
        if eng == "sync":
            e.wait_ge(sp[0], spn[0])
            e.wait_ge(sp[1], spn[1])
        BAR("sync", "gpsimd", "tensor", "scalar", "vector")
        return n

    return _run_block(nc, emit)


def build_launch7():
    nc = bass.Bass("TRN2", target_bir_lowering=False, debug=False,
                   num_devices=NCORES, name="launch7")
    IT = L1I // 128
    TB = 1024
    NB = TPC // TB

    t_in = nc.dram_tensor("tprod", [L1I, TPC], BF16, kind="ExternalInput").ap()
    am4_in = nc.dram_tensor("amax4all", [1, 8], F32, kind="ExternalInput").ap()
    wd_in = nc.dram_tensor("wdn", [C, L1I], F32, kind="ExternalInput").ap()
    r4_in = nc.dram_tensor("resid4", [TPC, C], F32, kind="ExternalInput").ap()
    out_out = nc.dram_tensor("out", [TPC, C], F32, kind="ExternalOutput").ap()
    wdscr = nc.dram_tensor("wdscr7", [C, L1I], BF16, kind="Internal").ap()
    ones128_in = nc.inline_tensor(np.ones((1, 128), np.float32), "ones7").ap()

    A = nc.alloc_sbuf_tensor
    sam = A("sam7", [1, 8], F32).ap()
    ssc = A("ssc7", [1, 2], F32).ap()
    sscb = A("sscb7", [128, 2], F32).ap()
    sones = A("sones7", [1, 128], F32).ap()
    sw = [A(f"sw7_{i}", [128, L1I], F32).ap() for i in range(2)]
    swa = A("swa7", [128, 96], F32).ap()
    sws = [A(f"sws7_{i}", [128, 96], F32).ap() for i in range(2)]
    swr = A("swr7", [128, 96], F32).ap()
    swq = A("swq7", [128, L1I], F32).ap()
    swqb = [A(f"swqb7_{i}", [128, L1I], BF16).ap() for i in range(2)]
    wT = [A(f"wT7_{i}", [128, C], BF16).ap() for i in range(IT)]
    stin = [A(f"stin7_{i}", [128, TB], BF16).ap() for i in range(2)]
    sq = [A(f"sq7_{i}", [128, TB], F32).ap() for i in range(2)]
    obuf = [A(f"obuf7_{i}", [128, TB], BF16).ap() for i in range(IT)]
    sres = [A(f"sres7_{i}", [128, C], F32).ap() for i in range(2)]
    souts = [A(f"souts{i}", [128, C], F32).ap() for i in range(2)]

    P = nc.alloc_psum_tensor
    pb = [P(f"pb7_{i}", [128, 512], F32).ap() for i in range(4)]
    pb_bc = P("pb7_bc", [128, 2], F32).ap()

    SEMS, PRODUCER = _mk_sems(nc)
    lp = [nc.alloc_semaphore("lp0"), nc.alloc_semaphore("lp1")]
    sp = [nc.alloc_semaphore("sp0"), nc.alloc_semaphore("sp1")]

    def emit(eng, e):
        n = dict(ld=0, ldg=0, mm=0, sc=0, ve=0, st=0, sg=0)
        lpn = [0, 0]
        spn = [0, 0]
        H = _qdq_emit_helpers(nc, e, eng, n, SEMS, PRODUCER)
        LD, LDG, ST, STG, W = H["LD"], H["LDG"], H["ST"], H["STG"], H["W"]
        MM, MME, SC, VE, VEc, SCc, BAR = (H["MM"], H["MME"], H["SC"], H["VE"],
                                          H["VEc"], H["SCc"], H["BAR"])

        def LDP(p, out, in_):
            lpn[p] += 16
            if eng == "sync":
                e.dma_start(out=out, in_=in_).then_inc(lp[p], 16)

        def STP(p, out, in_):
            spn[p] += 16
            if eng == "sync":
                e.dma_start(out=out, in_=in_).then_inc(sp[p], 16)

        def WP(sems, cnt, p, tgt, val=None):
            if eng == tgt:
                e.wait_ge(sems[p], cnt[p] if val is None else val)

        LD(sam, am4_in)
        LD(sones, ones128_in)
        W("ld", "vector")
        VE(lambda: e.tensor_reduce(ssc[:, 1:2], sam, AX.X, ALU.max))
        VEc(lambda: e.tensor_scalar(ssc[:, 1:2], ssc[:, 1:2], 1.0 / 32767.0,
                                    None, ALU.mult))
        VEc(lambda: e.tensor_scalar(ssc[:, 1:2], ssc[:, 1:2], 1e-12, None,
                                    ALU.max))
        VEc(lambda: e.reciprocal(ssc[:, 0:1], ssc[:, 1:2]))
        W("ve", "tensor")
        MM(lambda: e.matmul(pb_bc, sones, ssc, start=True, stop=True))
        W("mm", "scalar")
        SC(lambda: e.copy(sscb, pb_bc))

        for ct in range(C // 128):
            p = ct % 2
            if ct >= 2:
                WP(sp, spn, p, "sync")
            LDP(p, sw[p], wd_in[128 * ct:128 * (ct + 1), :])
            WP(lp, lpn, p, "vector")
            VEc(lambda p=p: e.tensor_reduce(
                swa.rearrange("p (a one) -> p a one", one=1),
                sw[p].rearrange("p (a w) -> p a w", w=32), AX.X, ALU.max,
                apply_absolute_value=True))
            VEc(lambda p=p: e.tensor_scalar(sws[p], swa, 1.0 / 7.0, None, ALU.mult))
            VEc(lambda p=p: e.tensor_scalar(sws[p], sws[p], 1e-12, None, ALU.max))
            VEc(lambda p=p: e.reciprocal(swr, sws[p]))
            VEc(lambda p=p: e.tensor_mul(
                swq.rearrange("p (a w) -> p a w", w=32),
                sw[p].rearrange("p (a w) -> p a w", w=32),
                swr.rearrange("p (a one) -> p a one", one=1)
                .broadcast_to([128, 96, 32])))
            VEc(lambda: e.tensor_scalar(swq, swq, ROUND_C, ROUND_C,
                                        ALU.add, ALU.subtract))
            VEc(lambda p=p: e.tensor_mul(
                swqb[p].rearrange("p (a w) -> p a w", w=32),
                swq.rearrange("p (a w) -> p a w", w=32),
                sws[p].rearrange("p (a one) -> p a one", one=1)
                .broadcast_to([128, 96, 32])))
            W("ve", "sync")
            STP(p, wdscr[128 * ct:128 * (ct + 1), :], swqb[p])
        if eng == "sync":
            e.wait_ge(sp[0], spn[0])
            e.wait_ge(sp[1], spn[1])
        for it in range(IT):
            LD(wT[it], wdscr[:, 128 * it:128 * (it + 1)], transpose=True)
        W("ld", "sync")

        BAR("tensor", "vector", "scalar", "sync")
        qdq_ve = []
        for tb in range(NB):
            for it in range(IT):
                p = it % 2
                g = tb * IT + it
                if g >= 2:
                    W("ve", "sync", val=qdq_ve[g - 2])
                LDP(p, stin[p], t_in[128 * it:128 * (it + 1),
                                     TB * tb:TB * (tb + 1)])
                WP(lp, lpn, p, "vector")
                if tb > 0 and it == 0:
                    W("mm", "vector")
                VEc(lambda p=p: e.tensor_scalar(sq[p], stin[p], sscb[:, 0:1],
                                                ROUND_C, ALU.mult, ALU.add))
                VEc(lambda p=p, it=it: e.tensor_scalar(
                    obuf[it], sq[p], ROUND_C, sscb[:, 1:2],
                    ALU.subtract, ALU.mult))
                qdq_ve.append(n["ve"])
            W("ve", "tensor")
            for ts_ in range(TB // 128):
                tt = tb * (TB // 128) + ts_
                pr = tt % 2
                W("ve", "sync", val=n["ve"])
                if tt >= 2:
                    WP(sp, spn, pr, "sync")
                LDP(pr, sres[pr], r4_in[128 * tt:128 * (tt + 1), :])
                for cc in range(2):
                    bank = pb[cc]
                    if tt * 2 + cc >= 2:
                        W("ve", "tensor", val=n["ve"] - 1)
                    for it in range(IT - 1):
                        MME(lambda bank=bank, it=it, ts_=ts_, cc=cc: e.matmul(
                            bank, obuf[it][:, 128 * ts_:128 * (ts_ + 1)],
                            wT[it][:, 512 * cc:512 * (cc + 1)],
                            start=(it == 0), stop=False))
                    MM(lambda bank=bank, ts_=ts_, cc=cc: e.matmul(
                        bank, obuf[IT - 1][:, 128 * ts_:128 * (ts_ + 1)],
                        wT[IT - 1][:, 512 * cc:512 * (cc + 1)],
                        start=False, stop=True))
                    W("mm", "vector")
                    WP(lp, lpn, pr, "vector")
                    WP(sp, spn, pr, "vector")
                    VE(lambda cc=cc, bank=bank, pr=pr: e.tensor_add(
                        souts[pr][:, 512 * cc:512 * (cc + 1)], bank,
                        sres[pr][:, 512 * cc:512 * (cc + 1)]))
                W("ve", "sync")
                STP(pr, out_out[128 * tt:128 * (tt + 1), :], souts[pr])
        if eng == "sync":
            e.wait_ge(sp[0], spn[0])
            e.wait_ge(sp[1], spn[1])
        BAR("sync", "gpsimd", "tensor", "scalar", "vector")
        return n

    return _run_block(nc, emit)



_BUILDERS = {
    "l1": build_launch1, "l2": build_expert_ffn, "l3": build_launch3,
    "l4": build_launch4, "l5": build_launch5, "l6": build_launch6,
    "l7": build_launch7,
}


def _prog(name):
    if name not in _PROG_CACHE:
        _PROG_CACHE[name] = _BUILDERS[name]()
    return _PROG_CACHE[name]


def kernel(L0_attn_x, L0_attn_w_qkv, L0_attn_b_qkv, L0_attn_w_proj,
           L0_attn_b_proj, L0_attn_pos_independent, L0_ffn_Wr, L0_ffn_W1,
           L0_ffn_b1, L0_ffn_W2, L0_ffn_b2, L1_attn_Wq, L1_attn_bq,
           L1_attn_Wk, L1_attn_bk, L1_attn_Wv, L1_attn_bv, L1_attn_Wo,
           L1_attn_bo, L1_ffn_w_gate, L1_ffn_w_up, L1_ffn_w_down):
    x = np.ascontiguousarray(np.asarray(L0_attn_x, dtype=np.float32))
    w_qkv = np.asarray(L0_attn_w_qkv, np.float32)
    b_qkv = np.asarray(L0_attn_b_qkv, np.float32)
    w_proj = np.asarray(L0_attn_w_proj, np.float32)
    b_proj = np.asarray(L0_attn_b_proj, np.float32)
    Wr = np.ascontiguousarray(np.asarray(L0_ffn_Wr, np.float32))
    W1 = np.asarray(L0_ffn_W1, np.float32)
    b1 = np.asarray(L0_ffn_b1, np.float32)
    W2 = np.asarray(L0_ffn_W2, np.float32)
    b2 = np.asarray(L0_ffn_b2, np.float32)

    phi, psi, cmax = _l0_host_tables()
    cmaxneg = np.zeros((128, 16), np.float32)
    cmaxneg[:, :8] = -cmax[:8]

    in1 = []
    for c in range(NCORES):
        in1.append({
            "x": x[BPC * c:BPC * (c + 1)],
            "wqkvT": np.ascontiguousarray(w_qkv.T),
            "bqkvc": np.ascontiguousarray(b_qkv.reshape(24, 128).T),
            "bvrow": np.ascontiguousarray(b_qkv[2 * C:3 * C][None, :]),
            "wprojT": np.ascontiguousarray(w_proj.T),
            "bprow": np.ascontiguousarray(b_proj[None, :]),
            "phiT": phi.astype(ml_dtypes.bfloat16),
            "psiT": psi.astype(ml_dtypes.bfloat16), "cmaxneg": cmaxneg,
            "wr": Wr,
            "posind": np.ascontiguousarray(
                np.asarray(L0_attn_pos_independent,
                           np.float32)[:256].reshape(8, 64)),
        })
    r1 = run_bass_kernel_spmd(_prog("l1"), in1, core_ids=list(range(NCORES)))
    r1 = r1.results

    in2 = []
    for e in range(NCORES):
        buf_e = np.concatenate(
            [r1[c]["bufpart"][e * BPC * CAP:(e + 1) * BPC * CAP]
             for c in range(NCORES)], axis=0)
        in2.append({
            "buf": buf_e,
            "w1": np.ascontiguousarray(W1[e]),
            "b1c": np.ascontiguousarray(b1[e].reshape(FFH // 128, 128).T),
            "w2": np.ascontiguousarray(W2[e]),
            "b2": np.ascontiguousarray(b2[e][None, :]),
        })
    r2 = run_bass_kernel_spmd(_prog("l2"), in2, core_ids=list(range(NCORES)))
    r2 = r2.results

    in3 = []
    for c in range(NCORES):
        yall = np.concatenate(
            [r2[e]["y"][BPC * CAP * c:BPC * CAP * (c + 1)]
             for e in range(NCORES)], axis=0)
        in3.append({
            "resid2": r1[c]["resid2"],
            "yall": yall,
            "gidx": r1[c]["gidx"],
            "gates": r1[c]["gates"],
            "wq": np.ascontiguousarray(np.asarray(L1_attn_Wq, np.float32)),
            "wk": np.ascontiguousarray(np.asarray(L1_attn_Wk, np.float32)),
            "wv": np.ascontiguousarray(np.asarray(L1_attn_Wv, np.float32)),
            "wo": np.ascontiguousarray(np.asarray(L1_attn_Wo, np.float32)),
            "bq": np.asarray(L1_attn_bq, np.float32)[None, :],
            "bk": np.asarray(L1_attn_bk, np.float32)[None, :],
            "bv": np.asarray(L1_attn_bv, np.float32)[None, :],
            "bo": np.asarray(L1_attn_bo, np.float32)[None, :],
        })
    r3 = run_bass_kernel_spmd(_prog("l3"), in3, core_ids=list(range(NCORES)))
    r3 = r3.results

    am1all = np.concatenate([r3[c]["amax1"] for c in range(NCORES)],
                            axis=1)
    in4 = []
    for c in range(NCORES):
        in4.append({
            "xln4": r3[c]["xln4"].reshape(TPC, C),
            "amax1all": am1all,
            "wup": np.ascontiguousarray(np.asarray(L1_ffn_w_up, np.float32)),
            "wgate": np.ascontiguousarray(np.asarray(L1_ffn_w_gate, np.float32)),
        })
    r4 = run_bass_kernel_spmd(_prog("l4"), in4, core_ids=list(range(NCORES)))
    r4 = r4.results

    am2all = np.concatenate([r4[c]["amax2"] for c in range(NCORES)],
                            axis=0)
    in5 = [{"gate": r4[c]["gate"], "amax2all": am2all} for c in range(NCORES)]
    r5 = run_bass_kernel_spmd(_prog("l5"), in5, core_ids=list(range(NCORES)))
    r5 = r5.results

    am3all = np.concatenate([r5[c]["amax3"] for c in range(NCORES)], axis=1)
    in6 = [{"up": r4[c]["up"], "g2pre": r5[c]["g2pre"], "amax2all": am2all,
            "amax3all": am3all} for c in range(NCORES)]
    r6 = run_bass_kernel_spmd(_prog("l6"), in6, core_ids=list(range(NCORES)))
    r6 = r6.results

    am4all = np.concatenate([r6[c]["amax4"] for c in range(NCORES)], axis=1)
    in7 = []
    for c in range(NCORES):
        in7.append({
            "tprod": r6[c]["tprod"],
            "amax4all": am4all,
            "wdn": np.ascontiguousarray(np.asarray(L1_ffn_w_down, np.float32)),
            "resid4": r3[c]["resid4"].reshape(TPC, C),
        })
    r7 = run_bass_kernel_spmd(_prog("l7"), in7, core_ids=list(range(NCORES)))
    r7 = r7.results

    out = np.concatenate(
        [r7[c]["out"].reshape(BPC, T, C) for c in range(NCORES)], axis=0)
    return out


# version 14
# speedup vs baseline: 1.0956x; 1.0898x over previous
import numpy as np
import ml_dtypes
import concourse.bass as bass
import concourse.mybir as mybir
from concourse.bass import IndirectOffsetOnAxis
from concourse.bass_utils import run_bass_kernel_spmd

F32 = mybir.dt.float32
BF16 = mybir.dt.bfloat16
I32 = mybir.dt.int32
AF = mybir.ActivationFunctionType
ALU = mybir.AluOpType
AX = mybir.AxisListType

NCORES = 8
BPC = 2
T = 2048
C = 1024
NHEAD = 16
HD = 64
NEXP = 8
CAP = 320
FFH = 4096
L1I = 3072
TPC = BPC * T
ROUND_C = 1.5 * 2.0 ** 23



def build_expert_ffn():
    nc = bass.Bass("TRN2", target_bir_lowering=False, debug=False,
                   num_devices=NCORES, name="expert_ffn")
    NS = 16 * CAP
    NSC = 512
    HT = FFH // 128

    HALF = NS // 2

    buf_in = nc.dram_tensor("buf", [NS, C], BF16, kind="ExternalInput").ap()
    w1_in = nc.dram_tensor("w1", [C, FFH], F32, kind="ExternalInput").ap()
    b1_in = nc.dram_tensor("b1c", [128, HT], F32, kind="ExternalInput").ap()
    w2_in = nc.dram_tensor("w2", [FFH, C], F32, kind="ExternalInput").ap()
    b2_in = nc.dram_tensor("b2", [1, C], F32, kind="ExternalInput").ap()
    y_out = nc.dram_tensor("y", [NS, C], BF16, kind="ExternalOutput").ap()
    ones1 = nc.inline_tensor(np.ones((1, 128), ml_dtypes.bfloat16), "ones1").ap()

    A = nc.alloc_sbuf_tensor
    bufT = [A(f"bufT{i}", [128, HALF], BF16).ap() for i in range(8)]
    w1sb = [A(f"w1sb{i}", [128, FFH], BF16).ap() for i in range(8)]
    w2sb = [A(f"w2sb{i}", [128, C], BF16).ap() for i in range(HT)]
    b1sb = A("b1sb", [128, HT], F32).ap()
    b2row = A("b2row", [1, C], BF16).ap()
    onesb = A("onesb", [1, 128], BF16).ap()
    hsb = [A(f"hsb{i}", [128, NSC], BF16).ap() for i in range(HT)]
    ysb = [A(f"ysb{i}", [128, C], BF16).ap() for i in range(2)]

    ld = nc.alloc_semaphore("ld")
    ldg = nc.alloc_semaphore("ldg")
    xb = nc.alloc_semaphore("xb")
    mm = nc.alloc_semaphore("mm")
    evh = nc.alloc_semaphore("evh")
    evy = nc.alloc_semaphore("evy")
    st0 = nc.alloc_semaphore("st0")
    st1 = nc.alloc_semaphore("st1")

    pb_h = [nc.alloc_psum_tensor(f"pb_h{i}", [128, NSC], F32).ap() for i in range(4)]
    pb_y = [nc.alloc_psum_tensor(f"pb_y{i}", [128, 512], F32).ap() for i in range(2)]

    NLOADS = 2 + 8 + 1 + HT
    nsc_h = HALF // NSC

    with nc.Block() as block:
        @block.sync
        def _(sync):
            sync.dma_start(out=onesb, in_=ones1).then_inc(ld, 16)
            n_y = 0
            for half in range(2):
                if half == 1:
                    sync.wait_ge(mm, 5 * 40)
                for i in range(8):
                    sync.dma_start(
                        out=bufT[i],
                        in_=buf_in[half * HALF:(half + 1) * HALF, 128 * i:128 * (i + 1)],
                        transpose=True).then_inc(xb, 16)
                for hsc in range(nsc_h):
                    sc = half * nsc_h + hsc
                    for stile in range(4):
                        n_y += 1
                        sync.wait_ge(evy, sc * 8 + (stile + 1) * 2)
                        k = sc * 4 + stile
                        sync.dma_start(
                            out=y_out[NSC * sc + 128 * stile: NSC * sc + 128 * (stile + 1), :],
                            in_=ysb[k % 2]).then_inc(st0 if k % 2 == 0 else st1, 16)
            sync.wait_ge(st0, 16 * (n_y // 2))
            sync.wait_ge(st1, 16 * (n_y // 2))

        @block.gpsimd
        def _(g):
            g.dma_start(out=b2row, in_=b2_in).then_inc(ldg, 16)
            for i in range(8):
                g.dma_start(out=w1sb[i], in_=w1_in[128 * i:128 * (i + 1), :]) \
                    .then_inc(ldg, 16)
            g.dma_start(out=b1sb, in_=b1_in).then_inc(ldg, 16)
            for i in range(HT):
                g.dma_start(out=w2sb[i], in_=w2_in[128 * i:128 * (i + 1), :]) \
                    .then_inc(ldg, 16)

        @block.tensor
        def _(t):
            t.wait_ge(ld, 16)
            t.wait_ge(ldg, 16 * (NLOADS - 1))
            n_mm = 0
            n_evac = 0
            for half in range(2):
                t.wait_ge(xb, 16 * 8 * (half + 1))
                for sc in range(nsc_h):
                    for ht in range(HT):
                        kh = (half * nsc_h + sc) * HT + ht
                        if kh >= 4:
                            t.wait_ge(evh, kh - 3)
                        for ci in range(8):
                            ins = t.matmul(pb_h[ht % 4],
                                           w1sb[ci][:, 128 * ht:128 * (ht + 1)],
                                           bufT[ci][:, NSC * sc:NSC * (sc + 1)],
                                           start=(ci == 0), stop=(ci == 7))
                            if ci == 7:
                                ins.then_inc(mm, 1)
                        n_mm += 1
                        n_evac += 1
                    t.wait_ge(evh, (half * nsc_h + sc + 1) * HT)
                    for stile in range(4):
                        for c2 in range(2):
                            j = (half * nsc_h + sc) * 8 + stile * 2 + c2
                            if j >= 2:
                                t.wait_ge(evy, j - 1)
                            for hc in range(HT):
                                t.matmul(pb_y[c2],
                                         hsb[hc][:, 128 * stile:128 * (stile + 1)],
                                         w2sb[hc][:, 512 * c2:512 * (c2 + 1)],
                                         start=(hc == 0), stop=False)
                            t.matmul(pb_y[c2], onesb,
                                     b2row[:, 512 * c2:512 * (c2 + 1)],
                                     start=False, stop=True).then_inc(mm, 1)
                            n_mm += 1
                            n_evac += 1

        @block.scalar
        def _(s):
            n_mm = 0
            for sc in range(2 * nsc_h):
                for ht in range(HT):
                    n_mm += 1
                    s.wait_ge(mm, n_mm)
                    s.activation(hsb[ht], pb_h[ht % 4], AF.Relu,
                                 bias=b1sb[:, ht:ht + 1]).then_inc(evh, 1)
                n_mm += 8

        @block.vector
        def _(v):
            n_mm = 0
            n_y = 0
            for sc in range(2 * nsc_h):
                n_mm += HT
                for stile in range(4):
                    for c2 in range(2):
                        n_mm += 1
                        v.wait_ge(mm, n_mm)
                        if n_y >= 2:
                            v.wait_ge(st0 if n_y % 2 == 0 else st1,
                                      16 * (n_y // 2))
                        v.tensor_copy(
                            ysb[(sc * 4 + stile) % 2][:, 512 * c2:512 * (c2 + 1)],
                            pb_y[c2]).then_inc(evy, 1)
                    n_y += 1

    return nc


_PROG_CACHE = {}


def _get_prog(name, builder):
    if name not in _PROG_CACHE:
        _PROG_CACHE[name] = builder()
    return _PROG_CACHE[name]


def kernel(**inputs):
    raise NotImplementedError("assembled incrementally")




def _l0_host_tables():
    import math
    dim = C // 2
    pos = np.arange(dim, dtype=np.float64)
    freqs = (1.0 / 10000.0 ** (pos / dim))
    active = int(np.sum(freqs * 2 * math.pi * T >= 1.0))
    active = min(active, dim - C // 4)
    f = freqs[:active]
    t = np.arange(T, dtype=np.float64)
    phi = np.zeros((8 * 64, T), np.float32)
    psi = np.zeros((8 * 64, T), np.float32)
    cmax = np.zeros(16, np.float32)
    d = np.arange(-(T - 1), T, dtype=np.float64)
    for h in range(8):
        fh = f[32 * h:32 * (h + 1)]
        th = (np.cos(d[:, None] * fh) + np.sin(d[:, None] * fh)).sum(1)
        cmax[h] = th.max()
        ang = t[:, None] * fh[None, :]
        cos, sin = np.cos(ang), np.sin(ang)
        phi[64 * h:64 * h + 32] = cos.T
        phi[64 * h + 32:64 * h + 64] = sin.T
        psi[64 * h:64 * h + 32] = (cos - sin).T
        psi[64 * h + 32:64 * h + 64] = (cos + sin).T
    return phi, psi, cmax


BIG = 999999.0
L1_MAXPH = 99
L1_SCOREMODE = 2


def build_launch1():
    nc = bass.Bass("TRN2", target_bir_lowering=False, debug=False,
                   num_devices=NCORES, name="launch1")
    NT = T // 128

    x_in = nc.dram_tensor("x", [BPC, T, C], F32, kind="ExternalInput").ap()
    wqkvT_in = nc.dram_tensor("wqkvT", [C, 3 * C], F32, kind="ExternalInput").ap()
    bqkvc_in = nc.dram_tensor("bqkvc", [128, 24], F32, kind="ExternalInput").ap()
    bvrow_in = nc.dram_tensor("bvrow", [1, C], F32, kind="ExternalInput").ap()
    wprojT_in = nc.dram_tensor("wprojT", [C, C], F32, kind="ExternalInput").ap()
    bprow_in = nc.dram_tensor("bprow", [1, C], F32, kind="ExternalInput").ap()
    phiT_in = nc.dram_tensor("phiT", [512, T], BF16, kind="ExternalInput").ap()
    psiT_in = nc.dram_tensor("psiT", [512, T], BF16, kind="ExternalInput").ap()
    cmaxneg_in = nc.dram_tensor("cmaxneg", [128, 16], F32, kind="ExternalInput").ap()
    wr_in = nc.dram_tensor("wr", [C, 8], F32, kind="ExternalInput").ap()
    posind_in = nc.dram_tensor("posind", [8, 64], F32, kind="ExternalInput").ap()

    resid2_out = nc.dram_tensor("resid2", [BPC, T, C], F32, kind="ExternalOutput").ap()
    buf_out = nc.dram_tensor("bufpart", [NEXP * BPC * CAP, C], BF16,
                             kind="ExternalOutput").ap()
    gidx_out = nc.dram_tensor("gidx", [BPC, 2, T], I32, kind="ExternalOutput").ap()
    gate_out = nc.dram_tensor("gates", [BPC, 2, T], F32, kind="ExternalOutput").ap()

    lnscr2b = [nc.dram_tensor(f"lnscr{i}", [T, C], BF16, kind="Internal").ap()
               for i in range(2)]
    pcflip = nc.dram_tensor("pcflip", [1, 8], F32, kind="Internal").ap()

    tri = np.triu(np.ones((128, 128)), 1).astype(ml_dtypes.bfloat16)
    tri_in = nc.inline_tensor(tri, "tri").ap()
    allone_in = nc.inline_tensor(np.ones((128, 128), ml_dtypes.bfloat16), "allone").ap()
    pp = np.arange(128)[:, None]
    comp = np.zeros((128, 896), np.float32)
    comp[:, 512:] = 1.0
    comp[:, 384:512] = (pp <= np.arange(128)[None, :])
    comp_in = nc.inline_tensor(comp.astype(ml_dtypes.bfloat16), "maskcomp").ap()
    iota8w_in = nc.inline_tensor(
        np.tile(np.arange(8, dtype=np.float32), (128, T // 128)), "iota8w").ap()
    ones64_in = nc.inline_tensor(np.ones((1, 64), np.float32), "ones64c").ap()
    identb_in = nc.inline_tensor(np.eye(128, dtype=ml_dtypes.bfloat16),
                                 "identb").ap()
    ones128_in = nc.inline_tensor(np.ones((1, 128), np.float32), "ones128c").ap()

    A = nc.alloc_sbuf_tensor
    xlnT = [A(f"xlnT{i}", [128, T], BF16).ap() for i in range(8)]
    v1 = [A(f"v1_{i}", [128, NHEAD * 65], BF16).ap() for i in range(NT)]
    catT = [A(f"catT{i}", [128, T], BF16).ap() for i in range(8)]
    wqkv = [A(f"wqkv{i}", [128, 3 * C], BF16).ap() for i in range(8)]
    wproj = [A(f"wproj{i}", [128, C], BF16).ap() for i in range(8)]
    wrsb = [A(f"wrsb{i}", [128, 8], BF16).ap() for i in range(8)]
    qt = A("qt", [128, T], BF16).ap()
    ktl = A("ktl", [128, T], BF16).ap()
    expb = [A(f"expb{i}", [128, 1024], BF16).ap() for i in range(2)]
    sphi2 = A("sphi", [128, T], BF16).ap()
    spsi2 = A("spsi", [128, T], BF16).ap()
    brow = sphi2.bitcast(F32)[0:1, 0:512]
    sln2b = A("sln2b", [128, C], BF16).ap()
    bqkv8 = A("bqkv8", [128, 24], F32).ap()
    vbias = A("vbias", [128, C], BF16).ap()
    pbias = A("pbias", [128, C], BF16).ap()
    cmaxs = A("cmaxs", [128, 16], F32).ap()
    sbcast = A("sbcast", [64, 512], F32).ap()
    srecip = sbcast[0:1, :]
    sones64 = A("sones64", [1, 64], F32).ap()
    sones128 = A("sones128", [1, 128], F32).ap()
    sx = A("sx", [128, C], F32).ap()
    sx2 = sx
    sres = A("sres", [128, C], F32).ap()
    sq_t = sres
    lncols = A("lncols", [128, 8], F32).ap()
    L = {k: lncols[:, i:i + 1]
         for i, k in enumerate(["ssum", "ssumsq", "smean", "sm2", "svar",
                                "ssqrt", "srstd"])}
    sln = A("sln", [128, C], BF16).ap()
    sln2 = sx.bitcast(BF16)[:, 0:C]
    strie = A("strie", [128, 128], BF16).ap()
    sallone = A("sallone", [128, 128], BF16).ap()
    sident = A("sident", [128, 128], BF16).ap()
    scomp = A("scomp", [128, 896], BF16).ap()
    smask = [scomp[:, 384 - 128 * r:384 - 128 * r + 512] for r in range(4)]
    NT8 = (T // 128) * 8
    rlog = A("rlog", [128, NT8], F32).ap()
    rxs = A("rxs", [128, NT8], F32).ap()
    rmsk = A("rmsk", [128, NT8], F32).ap()
    roh0 = A("roh0", [128, NT8], F32).ap()
    roh1 = A("roh1", [128, NT8], F32).ap()
    rohs = A("rohs", [128, NT8], BF16).ap()
    rcum = A("rcum", [128, NT8], F32).ap()
    riota = A("riota", [128, NT8], F32).ap()
    NTc = T // 128
    rcolsA = A("rcolsA", [128, 15 * NTc], F32).ap()
    _c = lambda i: rcolsA[:, i * NTc:(i + 1) * NTc]
    rm1, rm2, re1, re2, rz_, rp1, rp2 = (_c(0), _c(1), _c(2), _c(3), _c(4),
                                         _c(5), _c(6))
    rpos = [_c(7), _c(8)]
    rkeep = [_c(9), _c(10)]
    rgate = [_c(11), _c(12)]
    ridxf = [_c(13), _c(14)]
    ridx = [A(f"ridx{k}", [128, NTc], I32).ap() for k in range(2)]
    spind = A("spind", [8, 64], F32).ap()
    spc = A("spc", [8, 1], F32).ap()
    spcr = A("spcr", [1, 8], F32).ap()

    P = nc.alloc_psum_tensor
    pbig = [P(f"pbig{i}", [128, 1024], F32).ap() for i in range(2)]
    pb_v = [pbig[0][:, 0:512], pbig[1][:, 0:512]]
    pb_s = [pbig[0][:, 512:1024], pbig[1][:, 512:1024]]
    pb_av_t = P("pb_av", [128, 512], F32).ap()
    pb_bc = P("pb_bc", [128, 512], F32).ap()
    pb_o2 = [P(f"pb_o2{i}", [128, 512], F32).ap() for i in range(2)]
    pb_o = pb_o2
    pb_avs = [pb_av_t, pb_o2[1]]

    s_ld = nc.alloc_semaphore("s_ld")
    s_ldg = nc.alloc_semaphore("s_ldg")
    s_mm = nc.alloc_semaphore("s_mm")
    s_sc = nc.alloc_semaphore("s_sc")
    s_ve = nc.alloc_semaphore("s_ve")
    s_st = nc.alloc_semaphore("s_st")
    s_sg = nc.alloc_semaphore("s_sg")
    lp5 = [nc.alloc_semaphore("lp5_0"), nc.alloc_semaphore("lp5_1")]
    sgp = [nc.alloc_semaphore("sgp_0"), nc.alloc_semaphore("sgp_1")]
    SEMS = {"ld": s_ld, "ldg": s_ldg, "mm": s_mm, "sc": s_sc, "ve": s_ve,
            "st": s_st, "sg": s_sg, "l50": lp5[0], "l51": lp5[1],
            "g50": sgp[0], "g51": sgp[1]}
    PRODUCER = {"ld": "sync", "st": "sync", "ldg": "gpsimd", "sg": "gpsimd",
                "mm": "tensor", "sc": "scalar", "ve": "vector",
                "l50": "sync", "l51": "sync", "g50": "gpsimd", "g51": "gpsimd"}

    def emit(eng, e):
        n = dict(ld=0, ldg=0, mm=0, sc=0, ve=0, st=0, sg=0, l50=0, l51=0,
                 g50=0, g51=0)
        breg = None
        if eng == "gpsimd":
            breg = e.alloc_register("bcreg")
            e.reg_mov(breg, NEXP * BPC * CAP - 1)

        def LD(out, in_, **k):
            n["ld"] += 16
            if eng == "sync":
                e.dma_start(out=out, in_=in_, **k).then_inc(s_ld, 16)

        def LDG(out, in_, **k):
            n["ldg"] += 16
            if eng == "gpsimd":
                e.dma_start(out=out, in_=in_, **k).then_inc(s_ldg, 16)

        def ST(out, in_, **k):
            n["st"] += 16
            if eng == "sync":
                e.dma_start(out=out, in_=in_, **k).then_inc(s_st, 16)

        def W(which, tgt, val=None):
            if eng == tgt:
                e.wait_ge(SEMS[which], n[which] if val is None else val)

        def MM(fn):
            n["mm"] += 1
            if eng == "tensor":
                fn().then_inc(s_mm, 1)

        def MME(fn):
            if eng == "tensor":
                fn()

        def SC(fn):
            n["sc"] += 1
            if eng == "scalar":
                fn().then_inc(s_sc, 1)

        def SCE(fn):
            if eng == "scalar":
                fn()

        def VE(fn):
            n["ve"] += 1
            if eng == "vector":
                fn().then_inc(s_ve, 1)

        def VEE(fn):
            if eng == "vector":
                fn()

        def SG(fn):
            n["sg"] += 16
            if eng == "gpsimd":
                fn().then_inc(s_sg, 16)

        def VEc(fn):
            W("ve", "vector")
            VE(fn)

        def SCc(fn):
            W("sc", "scalar")
            SC(fn)

        def BAR(*engines):
            for tgt in engines:
                for k, sem in SEMS.items():
                    if PRODUCER[k] != tgt and n[k] > 0:
                        if eng == tgt:
                            e.wait_ge(sem, n[k])

        LD(strie, tri_in)
        LD(sallone, allone_in)
        LD(sident, identb_in)
        LD(scomp, comp_in)
        LD(riota, iota8w_in)
        LD(sones64, ones64_in)
        LD(sones128, ones128_in)
        LD(cmaxs, cmaxneg_in)
        LD(bqkv8, bqkvc_in)
        for i in range(8):
            LDG(wqkv[i], wqkvT_in[128 * i:128 * (i + 1), :])
            LDG(wproj[i], wprojT_in[128 * i:128 * (i + 1), :])
            LDG(wrsb[i], wr_in[128 * i:128 * (i + 1), :])
        VE(lambda: e.memset(expb[0], 0.0))
        W("ve", "gpsimd")
        for zi in range(NEXP * BPC * CAP // 128):
            LDG(buf_out[128 * zi:128 * (zi + 1), :], expb[0])
        LD(spind, posind_in)
        W("ld", "vector")
        VE(lambda: e.reduce_sum(spc, spind, AX.X))
        W("ve", "sync")
        ST(pcflip.rearrange("a b -> b a"), spc)
        W("st", "sync")
        LD(spcr, pcflip)
        W("ld", "tensor")
        MM(lambda: e.matmul(pb_bc[:, 0:8], sones128, spcr, start=True, stop=True))
        W("mm", "scalar")
        W("ld", "scalar")
        SC(lambda: e.copy(cmaxs[:, 8:16], pb_bc[:, 0:8]))
        for dst, row_in in ((vbias, bvrow_in), (pbias, bprow_in)):
            for cc in range(2):
                W("sc", "sync")
                LD(brow, row_in[:, 512 * cc:512 * (cc + 1)])
                W("ld", "tensor")
                W("sc", "tensor")
                MM(lambda: e.matmul(pb_bc, sones128, brow, start=True, stop=True))
                W("mm", "scalar")
                SC(lambda dst=dst, cc=cc: e.copy(dst[:, 512 * cc:512 * (cc + 1)],
                                                 pb_bc))

        def PH_ph0(b, lnscr):
            BAR("sync", "scalar", "vector")
            for tt in range(NT):
                W("ve", "sync")
                LD(sx, x_in[b, 128 * tt:128 * (tt + 1), :])
                W("ld", "scalar")
                W("ve", "scalar")
                SC(lambda: e.activation(sq_t, sx, AF.Square, accum_out=L["ssumsq"]))
                W("ld", "vector")
                VE(lambda: e.reduce_sum(L["ssum"], sx, AX.X))
                W("sc", "vector")
                VEc(lambda: e.tensor_scalar(L["smean"], L["ssum"], 1.0 / C, None, ALU.mult))
                VEc(lambda: e.tensor_scalar(L["svar"], L["ssumsq"], 1.0 / C, None, ALU.mult))
                VEc(lambda: e.tensor_mul(L["sm2"], L["smean"], L["smean"]))
                VEc(lambda: e.tensor_sub(L["svar"], L["svar"], L["sm2"]))
                VEc(lambda: e.tensor_scalar(L["svar"], L["svar"], 1e-5, None, ALU.add))
                W("ve", "scalar")
                SC(lambda: e.activation(L["ssqrt"], L["svar"], AF.Sqrt))
                W("sc", "vector")
                VEc(lambda: e.reciprocal(L["srstd"], L["ssqrt"]))
                W("st", "vector")
                VEc(lambda: e.tensor_scalar(sln, sx, L["smean"], L["srstd"],
                                           ALU.subtract, ALU.mult))
                W("ve", "sync")
                ST(lnscr[128 * tt:128 * (tt + 1), :], sln)
        def PH_ph1(b, lnscr):
            W("st", "sync")
            W("mm", "sync")
            for ci in range(8):
                LD(xlnT[ci], lnscr[:, 128 * ci:128 * (ci + 1)], transpose=True)
            W("ld", "sync")
        def PH_ph2a(b, lnscr):
            BAR("tensor", "vector")
            for tt in range(NT):
                VE(lambda tt=tt: e.memset(
                    v1[tt].rearrange("p (h w) -> p h w", w=65)[:, :, 64:65], 1.0))
            for tt in range(NT):
                for vc in range(2):
                    pb = pb_v[vc]
                    if tt > 0:
                        W("ve", "tensor", val=n["ve"] - 1)
                    for ci in range(7):
                        MME(lambda pb=pb, ci=ci, tt=tt, vc=vc: e.matmul(
                            pb, xlnT[ci][:, 128 * tt:128 * (tt + 1)],
                            wqkv[ci][:, 2 * C + 512 * vc:2 * C + 512 * (vc + 1)],
                            start=(ci == 0), stop=False))
                    MM(lambda pb=pb, tt=tt, vc=vc: e.matmul(
                        pb, xlnT[7][:, 128 * tt:128 * (tt + 1)],
                        wqkv[7][:, 2 * C + 512 * vc:2 * C + 512 * (vc + 1)],
                        start=False, stop=True))
                    W("mm", "vector")
                    VE(lambda tt=tt, vc=vc, pb=pb: e.tensor_add(
                        v1[tt].rearrange("p (h w) -> p h w", w=65)
                        [:, 8 * vc:8 * (vc + 1), 0:64],
                        pb.rearrange("p (h w) -> p h w", w=64),
                        vbias[:, 512 * vc:512 * (vc + 1)]
                        .rearrange("p (h w) -> p h w", w=64)))
        def PH_ph2b(b, lnscr):
            proj_hist = []
            pending_norm = [None]
            last_norm_ve = [0]
            for hp in range(NHEAD // 2 if L1_MAXPH >= 2 else 0):
                h0 = 2 * hp
                if pending_norm[0] is not None:
                    pending_norm[0]()
                    pending_norm[0] = None
                for which, dst in ((0, qt), (1, ktl)):
                    for tc in range(4):
                        pb = pb_o[tc % 2]
                        if which == 0 and tc == 0:
                            W("ve", "tensor")
                        if len(proj_hist) >= 2:
                            W("ve", "tensor", val=proj_hist[-2])
                        for ci in range(7):
                            MME(lambda pb=pb, ci=ci, h0=h0, tc=tc, which=which:
                                e.matmul(
                                    pb,
                                    wqkv[ci][:, C * which + 64 * h0:
                                             C * which + 64 * (h0 + 2)],
                                    xlnT[ci][:, 512 * tc:512 * (tc + 1)],
                                    start=(ci == 0), stop=False))
                        MM(lambda pb=pb, h0=h0, tc=tc, which=which: e.matmul(
                            pb,
                            wqkv[7][:, C * which + 64 * h0:C * which + 64 * (h0 + 2)],
                            xlnT[7][:, 512 * tc:512 * (tc + 1)],
                            start=False, stop=True))
                        W("mm", "vector")
                        VE(lambda dst=dst, tc=tc, pb=pb, h0=h0, which=which:
                           e.tensor_scalar(dst[:, 512 * tc:512 * (tc + 1)], pb,
                                           bqkv8[:, 8 * which + h0 // 2:
                                                 8 * which + h0 // 2 + 1],
                                           0.125 if which == 0 else 1.0,
                                           ALU.add, ALU.mult))
                        proj_hist.append(n["ve"])
                if h0 < 8 and L1_SCOREMODE >= 2:
                    W("mm", "sync")
                    LD(sphi2, phiT_in[64 * h0:64 * (h0 + 2), :])
                    LD(spsi2, psiT_in[64 * h0:64 * (h0 + 2), :])
                for hh in range(2 if L1_SCOREMODE > 0 else 0):
                    h = h0 + hh
                    ext = (h < 8) and L1_SCOREMODE >= 2
                    prow = 64 * hh
                    for qc in range(4):
                        pb_av = pb_avs[qc % 2]
                        nk = 4 * qc + 4
                        groups = [list(range(g, min(g + 2, nk)))
                                  for g in range(0, nk, 2)]
                        exp_hist = {}
                        ve_hist = {}
                        has_mask = {}

                        def emit_av(g, groups=groups, nk=nk, pb_av=pb_av,
                                    exp_hist=exp_hist, ve_hist=ve_hist,
                                    has_mask=has_mask, h=h):
                            grp = groups[g]
                            if g == 0 and last_norm_ve[0]:
                                W("ve", "tensor", val=last_norm_ve[0])
                            if has_mask[g]:
                                W("ve", "tensor", val=ve_hist[g])
                            else:
                                W("sc", "tensor", val=exp_hist[g])
                            for j, kti in enumerate(grp):
                                last = (kti == nk - 1)
                                fn = (lambda g=g, j=j, kti=kti, h=h, last=last,
                                      pb_av=pb_av: e.matmul(
                                          pb_av[0:65, :],
                                          v1[kti][:, 65 * h:65 * (h + 1)],
                                          expb[g % 2][:, 512 * j:
                                                      512 * (j + 1)],
                                          start=(kti == 0), stop=last))
                                if last:
                                    MM(fn)
                                else:
                                    MME(fn)

                        for gi, grp in enumerate(groups):
                            pbs = pbig[gi % 2]
                            if gi >= 2:
                                W("sc", "tensor", val=exp_hist[gi - 2])
                            else:
                                W("sc", "tensor")
                            W("ld", "tensor")
                            W("ve", "tensor", val=proj_hist[-1])
                            for j, kti in enumerate(grp):
                                is_last = (j == len(grp) - 1)
                                fn1 = (lambda pbs=pbs, j=j, kti=kti, prow=prow,
                                       qc=qc: e.matmul(
                                           pbs[:, 512 * j:512 * (j + 1)],
                                           ktl[prow:prow + 64,
                                               128 * kti:128 * (kti + 1)],
                                           qt[prow:prow + 64,
                                              512 * qc:512 * (qc + 1)],
                                           start=True, stop=not ext))
                                fn2 = (lambda pbs=pbs, j=j, kti=kti, qc=qc,
                                       prow=prow: e.matmul(
                                           pbs[:, 512 * j:512 * (j + 1)],
                                           spsi2[prow:prow + 64,
                                                 128 * kti:128 * (kti + 1)],
                                           sphi2[prow:prow + 64,
                                                 512 * qc:512 * (qc + 1)],
                                           start=False, stop=True)) if ext else None
                                if not is_last:
                                    MME(fn1)
                                    if ext:
                                        MME(fn2)
                                else:
                                    if ext:
                                        MME(fn1)
                                        MM(fn2)
                                    else:
                                        MM(fn1)
                            W("mm", "scalar")
                            wid = 512 * len(grp)
                            if b == 0 and hp == 0 and hh == 0 and qc == 0 \
                                    and gi == 0:
                                W("ldg", "scalar")
                            SC(lambda gi=gi, pbs=pbs, h=h, wid=wid: e.activation(
                                expb[gi % 2][:, 0:wid], pbs[:, 0:wid], AF.Exp,
                                bias=cmaxs[:, h:h + 1]))
                            exp_hist[gi] = n["sc"]
                            ndiag = sum(1 for kti in grp if kti >= 4 * qc)
                            has_mask[gi] = ndiag > 0
                            if ndiag:
                                W("sc", "vector")
                                for j, kti in enumerate(grp):
                                    if kti >= 4 * qc:
                                        VE(lambda gi=gi, j=j, kti=kti, qc=qc:
                                           e.tensor_mul(
                                               expb[gi % 2][:, 512 * j:
                                                            512 * (j + 1)],
                                               expb[gi % 2][:, 512 * j:
                                                            512 * (j + 1)],
                                               smask[kti - 4 * qc]))
                                ve_hist[gi] = n["ve"]
                            if gi == 0 and pending_norm[0] is not None:
                                pending_norm[0]()
                                pending_norm[0] = None
                            if gi >= 1:
                                emit_av(gi - 1)
                        emit_av(len(groups) - 1)
                        av_mm_val = n["mm"]

                        def mk_norm(pb_av=pb_av, h=h, qc=qc,
                                    av_mm_val=av_mm_val):
                            def go():
                                W("mm", "vector", val=av_mm_val)
                                VE(lambda: e.reciprocal(srecip,
                                                        pb_av[64:65, :]))
                                W("ve", "tensor")
                                MM(lambda: e.matmul(pb_bc[0:64, :], sones64,
                                                    srecip,
                                                    start=True, stop=True))
                                W("mm", "scalar")
                                SC(lambda: e.copy(sbcast, pb_bc[0:64, :]))
                                W("sc", "vector")
                                VE(lambda: e.tensor_mul(
                                    catT[h // 2][64 * (h % 2):
                                                 64 * (h % 2) + 64,
                                                 512 * qc:512 * (qc + 1)],
                                    pb_av[0:64, :], sbcast))
                                last_norm_ve[0] = n["ve"]
                            return go
                        pending_norm[0] = mk_norm()
            if pending_norm[0] is not None:
                pending_norm[0]()
                pending_norm[0] = None
                W("ve", "tensor")
        def PH_ph2c(b, lnscr):
            BAR("tensor", "sync", "vector")
            for tt in range(NT if L1_MAXPH >= 3 else 0):
                W("ve", "sync", val=n["ve"])
                LD(sx2, x_in[b, 128 * tt:128 * (tt + 1), :])
                for cc in range(2):
                    pb = pb_o[cc]
                    if tt > 0:
                        W("ve", "tensor", val=n["ve"] - 1)
                    for ci in range(7):
                        MME(lambda pb=pb, ci=ci, tt=tt, cc=cc: e.matmul(
                            pb, catT[ci][:, 128 * tt:128 * (tt + 1)],
                            wproj[ci][:, 512 * cc:512 * (cc + 1)],
                            start=(ci == 0), stop=False))
                    MM(lambda pb=pb, tt=tt, cc=cc: e.matmul(
                        pb, catT[7][:, 128 * tt:128 * (tt + 1)],
                        wproj[7][:, 512 * cc:512 * (cc + 1)],
                        start=False, stop=True))
                    W("mm", "vector")
                    W("ld", "vector")
                    VE(lambda cc=cc, pb=pb: e.tensor_add(
                        sres[:, 512 * cc:512 * (cc + 1)], pb,
                        pbias[:, 512 * cc:512 * (cc + 1)]))
                    VEc(lambda cc=cc: e.tensor_add(
                        sres[:, 512 * cc:512 * (cc + 1)],
                        sres[:, 512 * cc:512 * (cc + 1)],
                        sx2[:, 512 * cc:512 * (cc + 1)]))
                W("ve", "sync")
                ST(resid2_out[b, 128 * tt:128 * (tt + 1), :], sres)
                W("st", "vector")
        def PH_ph3(b, lnscr):
            BAR("sync", "scalar", "vector")
            for tt in range(NT):
                W("ve", "sync")
                LD(sx, resid2_out[b, 128 * tt:128 * (tt + 1), :])
                W("ld", "scalar")
                W("ve", "scalar")
                SC(lambda: e.activation(sq_t, sx, AF.Square, accum_out=L["ssumsq"]))
                W("ld", "vector")
                VE(lambda: e.reduce_sum(L["ssum"], sx, AX.X))
                W("sc", "vector")
                VEc(lambda: e.tensor_scalar(L["smean"], L["ssum"], 1.0 / C, None, ALU.mult))
                VEc(lambda: e.tensor_scalar(L["svar"], L["ssumsq"], 1.0 / C, None, ALU.mult))
                VEc(lambda: e.tensor_mul(L["sm2"], L["smean"], L["smean"]))
                VEc(lambda: e.tensor_sub(L["svar"], L["svar"], L["sm2"]))
                VEc(lambda: e.tensor_scalar(L["svar"], L["svar"], 1e-5, None, ALU.add))
                W("ve", "scalar")
                SC(lambda: e.activation(L["ssqrt"], L["svar"], AF.Sqrt))
                W("sc", "vector")
                VEc(lambda: e.reciprocal(L["srstd"], L["ssqrt"]))
                W("st", "vector")
                VEc(lambda: e.tensor_scalar(sln, sx, L["smean"], L["srstd"],
                                           ALU.subtract, ALU.mult))
                W("ve", "sync")
                ST(lnscr[128 * tt:128 * (tt + 1), :], sln)
                trdsts = [pb_bc.bitcast(BF16)[:, 0:128],
                          pb_av_t.bitcast(BF16)[:, 0:128]]
                W("ve", "tensor")
                for ci in range(8):
                    trd = trdsts[ci % 2]
                    if ci >= 2:
                        W("ve", "tensor", val=n["ve"] - 1)
                    MM(lambda trd=trd, ci=ci: e.transpose(
                        trd, sln[:, 128 * ci:128 * (ci + 1)], sident))
                    W("mm", "vector")
                    VE(lambda trd=trd, ci=ci: e.tensor_copy(
                        expb[0][:, 128 * ci:128 * (ci + 1)], trd))
                W("ve", "tensor")
                for ci in range(7):
                    MME(lambda ci=ci, tt=tt: e.matmul(
                        pb_s[tt % 2][:, 0:8],
                        expb[0][:, 128 * ci:128 * (ci + 1)],
                        wrsb[ci], start=(ci == 0), stop=False))
                MM(lambda tt=tt: e.matmul(
                    pb_s[tt % 2][:, 0:8], expb[0][:, 896:1024],
                    wrsb[7], start=False, stop=True))
                W("mm", "scalar")
                SC(lambda tt=tt: e.copy(rlog[:, 8 * tt:8 * (tt + 1)],
                                        pb_s[tt % 2][:, 0:8]))
        def PH_ph4(b, lnscr):
            W("st", "sync")
            W("mm", "sync")
            for ci in range(8):
                LD(xlnT[ci], lnscr[:, 128 * ci:128 * (ci + 1)], transpose=True)
            W("ld", "sync")
        def PH_ph5(b, lnscr):
            BAR("tensor", "vector", "scalar", "gpsimd", "sync")
            NTc = T // 128
            r3 = lambda ap: ap.rearrange("p (t e) -> p t e", e=8)
            col3 = lambda ap: ap.unsqueeze(2)
            bc8 = lambda ap: ap.unsqueeze(2).broadcast_to([128, NTc, 8])
            W("sc", "vector")
            VE(lambda: e.tensor_reduce(col3(rm1), r3(rlog), AX.X, ALU.max))
            VEc(lambda: e.tensor_tensor(r3(rmsk), r3(rlog), bc8(rm1),
                                        ALU.is_equal))
            VEc(lambda: e.scalar_tensor_tensor(rxs, rmsk, -BIG, rlog,
                                               ALU.mult, ALU.add))
            VEc(lambda: e.tensor_reduce(col3(rm2), r3(rxs), AX.X, ALU.max))
            VEc(lambda: e.tensor_scalar(rxs, riota, BIG, None, ALU.subtract))
            VEc(lambda: e.tensor_mul(rxs, rmsk, rxs))
            VEc(lambda: e.tensor_scalar(rxs, rxs, BIG, None, ALU.add))
            VEc(lambda: e.tensor_reduce(col3(re1), r3(rxs), AX.X, ALU.min))
            VEc(lambda: e.tensor_tensor(r3(rmsk), r3(rlog), bc8(rm2),
                                        ALU.is_equal))
            VEc(lambda: e.tensor_scalar(rxs, riota, BIG, None, ALU.subtract))
            VEc(lambda: e.tensor_mul(rxs, rmsk, rxs))
            VEc(lambda: e.tensor_scalar(rxs, rxs, BIG, None, ALU.add))
            VEc(lambda: e.tensor_reduce(col3(re2), r3(rxs), AX.X, ALU.min))
            VEc(lambda: e.tensor_tensor(r3(rxs), r3(rlog), bc8(rm1),
                                        ALU.subtract))
            W("ve", "scalar")
            SCc(lambda: e.activation(rxs, rxs, AF.Exp))
            W("sc", "vector")
            VEc(lambda: e.tensor_reduce(col3(rz_), r3(rxs), AX.X, ALU.add))
            VEc(lambda: e.reciprocal(rp1, rz_))
            VEc(lambda: e.tensor_sub(rm2, rm2, rm1))
            W("ve", "scalar")
            SCc(lambda: e.activation(rm2, rm2, AF.Exp))
            W("sc", "vector")
            VEc(lambda: e.tensor_mul(rp2, rm2, rp1))
            VEc(lambda: e.tensor_tensor(r3(roh0), r3(riota), bc8(re1),
                                        ALU.is_equal))
            VEc(lambda: e.tensor_tensor(r3(roh1), r3(riota), bc8(re2),
                                        ALU.is_equal))
            VEc(lambda: e.tensor_add(rohs, roh0, roh1))
            W("ve", "tensor")
            sc_hist = []
            for tt in range(NTc):
                pbr = pb_s[tt % 2]
                if len(sc_hist) >= 2:
                    W("sc", "tensor", val=sc_hist[-2])
                for s_ in range(tt):
                    MME(lambda s_=s_, pbr=pbr: e.matmul(
                        pbr[:, 0:8], sallone, rohs[:, 8 * s_:8 * (s_ + 1)],
                        start=(s_ == 0), stop=False))
                MM(lambda tt=tt, pbr=pbr: e.matmul(
                    pbr[:, 0:8], strie, rohs[:, 8 * tt:8 * (tt + 1)],
                    start=(tt == 0), stop=True))
                W("mm", "scalar")
                SC(lambda tt=tt, pbr=pbr: e.copy(rcum[:, 8 * tt:8 * (tt + 1)],
                                                 pbr[:, 0:8]))
                sc_hist.append(n["sc"])
            W("sc", "vector")
            for k in range(2):
                oh = (roh0, roh1)[k]
                ee = (re1, re2)[k]
                pp_ = (rp1, rp2)[k]
                VEc(lambda oh=oh: e.tensor_mul(rxs, oh, rcum))
                VEc(lambda k=k: e.tensor_reduce(col3(rpos[k]), r3(rxs), AX.X,
                                                ALU.add))
                VEc(lambda k=k: e.tensor_scalar(rkeep[k], rpos[k], float(CAP),
                                                None, ALU.is_lt))
                VEc(lambda k=k, ee=ee: e.scalar_tensor_tensor(
                    ridxf[k], ee, float(BPC * CAP), rpos[k], ALU.mult, ALU.add))
                VEc(lambda k=k, b=b: e.tensor_scalar(
                    ridxf[k], ridxf[k], float(b * CAP), None, ALU.add))
                VEc(lambda k=k: e.tensor_mul(ridxf[k], ridxf[k], rkeep[k]))
                VEc(lambda k=k: e.tensor_scalar(rm1, rkeep[k], -1.0, None,
                                                ALU.mult))
                VEc(lambda: e.tensor_scalar(rm1, rm1, 1.0, None, ALU.add))
                VEc(lambda k=k: e.scalar_tensor_tensor(
                    ridxf[k], rm1, BIG, ridxf[k], ALU.mult, ALU.add))
                W("sg", "vector")
                VEc(lambda k=k: e.tensor_copy(ridx[k], ridxf[k]))
                W("st", "vector")
                VEc(lambda k=k, pp_=pp_: e.tensor_mul(rgate[k], pp_, rkeep[k]))
            W("ldg", "gpsimd")
            for tt in range(NTc):
                p5 = tt % 2
                stage = (sln2, sln2b)[p5]
                gk = "g50" if p5 == 0 else "g51"
                lk = "l50" if p5 == 0 else "l51"
                if tt >= 2:
                    W(gk, "sync")
                W("ve", "sync", val=n["ve"])
                n[lk] += 16
                if eng == "sync":
                    e.dma_start(out=stage,
                                in_=lnscr[128 * tt:128 * (tt + 1), :]) \
                        .then_inc(lp5[p5], 16)
                W(lk, "gpsimd")
                if tt == 0:
                    W("ve", "gpsimd")
                for k in range(2):
                    n[gk] += 16
                    if eng == "gpsimd":
                        e.indirect_dma_start(
                            buf_out[:, :],
                            IndirectOffsetOnAxis(ridx[k][:, tt:tt + 1], 0),
                            stage, None, bounds_check=breg,
                            oob_is_err=False).then_inc(sgp[p5], 16)
            if eng == "gpsimd":
                e.wait_ge(sgp[0], n["g50"])
                e.wait_ge(sgp[1], n["g51"])
            W("ve", "gpsimd")
            for k in range(2):
                SG(lambda k=k, b=b: e.dma_start(
                    out=gidx_out[b, k, :].rearrange("(t p) -> p t", p=128),
                    in_=ridx[k], allow_slow_non_contiguous=True))
            W("ve", "sync")
            for k in range(2):
                ST(gate_out[b, k, :].rearrange("(t p) -> p t", p=128),
                   rgate[k], allow_slow_non_contiguous=True)

        PH_ph0(0, lnscr2b[0])
        PH_ph1(0, lnscr2b[0])
        PH_ph2a(0, lnscr2b[0])
        PH_ph2b(0, lnscr2b[0])
        PH_ph0(1, lnscr2b[1])
        PH_ph2c(0, lnscr2b[0])
        PH_ph3(0, lnscr2b[0])
        PH_ph1(1, lnscr2b[1])
        PH_ph5(0, lnscr2b[0])
        PH_ph2a(1, lnscr2b[1])
        PH_ph2b(1, lnscr2b[1])
        PH_ph2c(1, lnscr2b[1])
        PH_ph3(1, lnscr2b[1])
        PH_ph5(1, lnscr2b[1])

        BAR("sync", "gpsimd", "tensor", "scalar", "vector")
        return n

    with nc.Block() as block:
        @block.sync
        def _(e):
            emit("sync", e)

        @block.gpsimd
        def _(e):
            emit("gpsimd", e)

        @block.tensor
        def _(e):
            emit("tensor", e)

        @block.scalar
        def _(e):
            emit("scalar", e)

        @block.vector
        def _(e):
            emit("vector", e)

    return nc



def build_launch3():
    nc = bass.Bass("TRN2", target_bir_lowering=False, debug=False,
                   num_devices=NCORES, name="launch3")
    NT = T // 128
    NS = NEXP * BPC * CAP

    r2_in = nc.dram_tensor("resid2", [BPC, T, C], F32, kind="ExternalInput").ap()
    yall_in = nc.dram_tensor("yall", [NS, C], BF16, kind="ExternalInput").ap()
    gidx_in = nc.dram_tensor("gidx", [BPC, 2, T], I32, kind="ExternalInput").ap()
    gate_in = nc.dram_tensor("gates", [BPC, 2, T], F32, kind="ExternalInput").ap()
    w_in = {w: nc.dram_tensor("w" + w, [C, C], F32, kind="ExternalInput").ap()
            for w in "qkvo"}
    b_in = {w: nc.dram_tensor("b" + w, [1, C], F32, kind="ExternalInput").ap()
            for w in "qkvo"}

    r4_out = nc.dram_tensor("resid4", [BPC, T, C], F32, kind="ExternalOutput").ap()
    xln4_out = nc.dram_tensor("xln4", [BPC, T, C], BF16, kind="ExternalOutput").ap()
    amax_out = nc.dram_tensor("amax1", [1, 1], F32, kind="ExternalOutput").ap()

    r3scr = nc.dram_tensor("r3scr", [BPC, T, C], F32, kind="Internal").ap()
    lnscr = nc.dram_tensor("lnscr3", [T, C], BF16, kind="Internal").ap()
    qkvscr = nc.dram_tensor("qkvscr", [T, C], BF16, kind="Internal").ap()
    flipscr = nc.dram_tensor("flipscr", [1, 128], F32, kind="Internal").ap()
    ones128_in = nc.inline_tensor(np.ones((1, 128), np.float32), "ones128c3").ap()

    A = nc.alloc_sbuf_tensor
    xlnT = [A(f"xlnT{i}", [128, T], BF16).ap() for i in range(8)]
    qkvT = [A(f"qkvT{i}", [128, T], BF16).ap() for i in range(8)]
    wsb = {w: [A(f"w_{w}{i}", [128, C], BF16).ap() for i in range(8)] for w in "qkvo"}
    bbc = {w: A(f"bbc_{w}", [128, C], BF16).ap() for w in "qkvo"}
    brow = A("brow3", [1, 512], F32).ap()
    sones128 = A("sones128_3", [1, 128], F32).ap()
    sx = [A(f"sx3_{i}", [128, C], F32).ap() for i in range(2)]
    sres = [A(f"sres3_{i}", [128, C], F32).ap() for i in range(2)]
    qk = [A(f"qk3_{i}", [128, 2 * C], F32).ap() for i in range(2)]
    vf = [A(f"vf3_{i}", [128, C], F32).ap() for i in range(2)]
    coeff = [A(f"coeff3_{i}", [128, 16], F32).ap() for i in range(2)]
    qkvsb = [A(f"qkvsb{i}", [128, C], BF16).ap() for i in range(2)]
    y0 = [A(f"y0sb{i}", [128, C], BF16).ap() for i in range(2)]
    y1 = [A(f"y1sb{i}", [128, C], BF16).ap() for i in range(2)]
    gi0 = [A(f"gi0_{i}", [128, 1], I32).ap() for i in range(2)]
    gi1 = [A(f"gi1_{i}", [128, 1], I32).ap() for i in range(2)]
    gt0 = [A(f"gt0_{i}", [128, 1], F32).ap() for i in range(2)]
    gt1 = [A(f"gt1_{i}", [128, 1], F32).ap() for i in range(2)]
    lncols = [A(f"l3cols{i}", [128, 8], F32).ap() for i in range(2)]
    LNK = ["ssum", "ssumsq", "smean", "sm2", "svar", "ssqrt", "srstd"]
    L2 = [{k: lncols[i][:, j:j + 1] for j, k in enumerate(LNK)}
          for i in range(2)]
    sln = [A(f"sln3_{i}", [128, C], BF16).ap() for i in range(2)]
    samax = A("samax", [128, 1], F32).ap()
    stmp1 = A("stmp1_3", [128, 1], F32).ap()
    sflip = A("sflip", [1, 128], F32).ap()
    sam1 = A("sam1", [1, 1], F32).ap()

    P = nc.alloc_psum_tensor
    pb = [P(f"pb3_{i}", [128, 512], F32).ap() for i in range(4)]
    pb_bc = P("pb3_bc", [128, 512], F32).ap()

    s_ld = nc.alloc_semaphore("s_ld")
    s_ldg = nc.alloc_semaphore("s_ldg")
    s_mm = nc.alloc_semaphore("s_mm")
    s_sc = nc.alloc_semaphore("s_sc")
    s_ve = nc.alloc_semaphore("s_ve")
    s_st = nc.alloc_semaphore("s_st")
    s_sg = nc.alloc_semaphore("s_sg")
    sp3 = [nc.alloc_semaphore("sp3_0"), nc.alloc_semaphore("sp3_1")]
    SEMS = {"ld": s_ld, "ldg": s_ldg, "mm": s_mm, "sc": s_sc, "ve": s_ve,
            "st": s_st, "sg": s_sg, "sp0": sp3[0], "sp1": sp3[1]}
    PRODUCER = {"ld": "sync", "st": "sync", "ldg": "gpsimd", "sg": "gpsimd",
                "mm": "tensor", "sc": "scalar", "ve": "vector",
                "sp0": "sync", "sp1": "sync"}

    def emit(eng, e):
        n = dict(ld=0, ldg=0, mm=0, sc=0, ve=0, st=0, sg=0, sp0=0, sp1=0)
        breg = None
        if eng == "gpsimd":
            breg = e.alloc_register("bcreg3")
            e.reg_mov(breg, NS - 1)

        def STP(p, out, in_):
            key = "sp0" if p == 0 else "sp1"
            n[key] += 16
            if eng == "sync":
                e.dma_start(out=out, in_=in_).then_inc(sp3[p], 16)

        def LD(out, in_, **k):
            n["ld"] += 16
            if eng == "sync":
                e.dma_start(out=out, in_=in_, **k).then_inc(s_ld, 16)

        def LDG(out, in_, **k):
            n["ldg"] += 16
            if eng == "gpsimd":
                e.dma_start(out=out, in_=in_, **k).then_inc(s_ldg, 16)

        def ST(out, in_):
            n["st"] += 16
            if eng == "sync":
                e.dma_start(out=out, in_=in_).then_inc(s_st, 16)

        def W(which, tgt, val=None):
            if eng == tgt:
                e.wait_ge(SEMS[which], n[which] if val is None else val)

        def MM(fn):
            n["mm"] += 1
            if eng == "tensor":
                fn().then_inc(s_mm, 1)

        def MME(fn):
            if eng == "tensor":
                fn()

        def SC(fn):
            n["sc"] += 1
            if eng == "scalar":
                fn().then_inc(s_sc, 1)

        def VE(fn):
            n["ve"] += 1
            if eng == "vector":
                fn().then_inc(s_ve, 1)

        def SG(fn):
            n["sg"] += 16
            if eng == "gpsimd":
                fn().then_inc(s_sg, 16)

        def VEc(fn):
            W("ve", "vector")
            VE(fn)

        def SCc(fn):
            W("sc", "scalar")
            SC(fn)

        def BAR(*engines):
            for tgt in engines:
                for k, sem in SEMS.items():
                    if PRODUCER[k] != tgt and n[k] > 0:
                        if eng == tgt:
                            e.wait_ge(sem, n[k])

        def LN_CHAIN(xin, Lc, dst_bf, extra_amax=False):
            W("ld", "scalar")
            W("ve", "scalar")
            SC(lambda: e.activation(sq_scr, xin, AF.Square,
                                    accum_out=Lc["ssumsq"]))
            W("ld", "vector")
            VE(lambda: e.reduce_sum(Lc["ssum"], xin, AX.X))
            W("sc", "vector")
            VEc(lambda: e.tensor_scalar(Lc["smean"], Lc["ssum"], 1.0 / C, None,
                                        ALU.mult))
            VEc(lambda: e.tensor_scalar(Lc["svar"], Lc["ssumsq"], 1.0 / C, None,
                                        ALU.mult))
            VEc(lambda: e.tensor_mul(Lc["sm2"], Lc["smean"], Lc["smean"]))
            VEc(lambda: e.tensor_sub(Lc["svar"], Lc["svar"], Lc["sm2"]))
            VEc(lambda: e.tensor_scalar(Lc["svar"], Lc["svar"], 1e-5, None,
                                        ALU.add))
            W("ve", "scalar")
            SC(lambda: e.activation(Lc["ssqrt"], Lc["svar"], AF.Sqrt))
            W("sc", "vector")
            VEc(lambda: e.reciprocal(Lc["srstd"], Lc["ssqrt"]))
            W("sp0", "vector")
            W("sp1", "vector")
            VEc(lambda: e.tensor_scalar(dst_bf, xin, Lc["smean"], Lc["srstd"],
                                        ALU.subtract, ALU.mult))
            if extra_amax:
                VEc(lambda: e.tensor_reduce(stmp1, dst_bf, AX.X, ALU.max,
                                            apply_absolute_value=True))
                VEc(lambda: e.tensor_max(samax, samax, stmp1))

        sq_scr = qk[0][:, 0:C]

        LD(sones128, ones128_in)
        for w in "qkvo":
            for i in range(8):
                LDG(wsb[w][i], w_in[w][128 * i:128 * (i + 1), :])
        for w in "qkvo":
            for cc in range(2):
                W("sc", "sync")
                LD(brow, b_in[w][:, 512 * cc:512 * (cc + 1)])
                W("ld", "tensor")
                W("sc", "tensor")
                MM(lambda: e.matmul(pb_bc, sones128, brow, start=True, stop=True))
                W("mm", "scalar")
                SC(lambda w=w, cc=cc: e.copy(bbc[w][:, 512 * cc:512 * (cc + 1)],
                                             pb_bc))
        W("ld", "vector")
        VE(lambda: e.memset(samax, 0.0))

        for b in range(BPC):
            BAR("sync", "scalar", "vector", "gpsimd")
            ve_hist = []
            sg_hist = []
            for tt in range(NT):
                p = tt % 2
                if tt >= 2:
                    W("ve", "sync", val=ve_hist[-2])
                    W("sg", "sync", val=sg_hist[-2])
                    W("sp0" if p == 0 else "sp1", "sync")
                LD(sx[p], r2_in[b, 128 * tt:128 * (tt + 1), :])
                LD(gi0[p], gidx_in[b, 0, 128 * tt:128 * (tt + 1)]
                   .rearrange("(p one) -> p one", one=1))
                LD(gi1[p], gidx_in[b, 1, 128 * tt:128 * (tt + 1)]
                   .rearrange("(p one) -> p one", one=1))
                LD(gt0[p], gate_in[b, 0, 128 * tt:128 * (tt + 1)]
                   .rearrange("(p one) -> p one", one=1))
                LD(gt1[p], gate_in[b, 1, 128 * tt:128 * (tt + 1)]
                   .rearrange("(p one) -> p one", one=1))
                if tt >= 2:
                    W("sg", "vector", val=sg_hist[-2])
                VE(lambda p=p: e.memset(y0[p], 0.0))
                VE(lambda p=p: e.memset(y1[p], 0.0))
                W("ve", "gpsimd")
                W("ld", "gpsimd")
                SG(lambda p=p: e.indirect_dma_start(
                    y0[p], None, yall_in[:, :], IndirectOffsetOnAxis(gi0[p], 0),
                    bounds_check=breg, oob_is_err=False))
                SG(lambda p=p: e.indirect_dma_start(
                    y1[p], None, yall_in[:, :], IndirectOffsetOnAxis(gi1[p], 0),
                    bounds_check=breg, oob_is_err=False))
                sg_hist.append(n["sg"])
                W("ld", "vector")
                W("sg", "vector")
                VEc(lambda p=p: e.scalar_tensor_tensor(sx[p], y0[p], gt0[p],
                                                       sx[p], ALU.mult, ALU.add))
                VEc(lambda p=p: e.scalar_tensor_tensor(sx[p], y1[p], gt1[p],
                                                       sx[p], ALU.mult, ALU.add))
                ve_hist.append(n["ve"])
                W("ve", "sync")
                STP(p, r3scr[b, 128 * tt:128 * (tt + 1), :], sx[p])
            if eng == "sync":
                e.wait_ge(sp3[0], n["sp0"])
                e.wait_ge(sp3[1], n["sp1"])
            ve_hist = []
            for tt in range(NT):
                p = tt % 2
                if tt >= 2:
                    W("ve", "sync", val=ve_hist[-2])
                    W("sp0" if p == 0 else "sp1", "sync")
                LD(sx[p], r3scr[b, 128 * tt:128 * (tt + 1), :])
                LN_CHAIN(sx[p], L2[p], sln[p])
                ve_hist.append(n["ve"])
                W("ve", "sync")
                STP(p, lnscr[128 * tt:128 * (tt + 1), :], sln[p])

            W("st", "sync")
            if eng == "sync":
                e.wait_ge(sp3[0], n["sp0"])
                e.wait_ge(sp3[1], n["sp1"])
            W("mm", "sync")
            for ci in range(8):
                LD(xlnT[ci], lnscr[:, 128 * ci:128 * (ci + 1)], transpose=True)
            W("ld", "sync")

            BAR("tensor", "vector", "scalar")
            ve_hist = []
            for tt in range(NT):
                p = tt % 2
                dsts = [(qk[p][:, 0:C], "q"), (qk[p][:, C:2 * C], "k"),
                        (vf[p], "v")]
                for wi, (dstf, w) in enumerate(dsts):
                    for cc in range(2):
                        g = tt * 6 + wi * 2 + cc
                        bank = pb[g % 3]
                        if g >= 3:
                            W("ve", "tensor", val=n["ve"] - 2)
                        for ci in range(7):
                            MME(lambda bank=bank, w=w, ci=ci, tt=tt, cc=cc:
                                e.matmul(
                                    bank, xlnT[ci][:, 128 * tt:128 * (tt + 1)],
                                    wsb[w][ci][:, 512 * cc:512 * (cc + 1)],
                                    start=(ci == 0), stop=False))
                        MM(lambda bank=bank, w=w, tt=tt, cc=cc: e.matmul(
                            bank, xlnT[7][:, 128 * tt:128 * (tt + 1)],
                            wsb[w][7][:, 512 * cc:512 * (cc + 1)],
                            start=False, stop=True))
                        W("mm", "vector")
                        VE(lambda dstf=dstf, w=w, cc=cc, bank=bank: e.tensor_add(
                            dstf[:, 512 * cc:512 * (cc + 1)], bank,
                            bbc[w][:, 512 * cc:512 * (cc + 1)]))
                W("ve", "scalar")
                SCc(lambda p=p: e.activation(qk[p], qk[p], AF.Square))
                SCc(lambda p=p: e.activation(qk[p], qk[p], AF.Exp, scale=-0.5))
                W("sc", "vector")
                VEc(lambda p=p: e.tensor_mul(qk[p][:, 0:C], qk[p][:, 0:C],
                                             qk[p][:, C:2 * C]))
                VEc(lambda p=p: e.tensor_reduce(
                    coeff[p].unsqueeze(2),
                    qk[p][:, 0:C].rearrange("p (h w) -> p h w", w=64),
                    AX.X, ALU.add))
                W("st", "vector")
                VEc(lambda p=p: e.tensor_mul(
                    qkvsb[p].rearrange("p (h w) -> p h w", w=64),
                    vf[p].rearrange("p (h w) -> p h w", w=64),
                    coeff[p].unsqueeze(2).broadcast_to([128, 16, 64])))
                ve_hist.append(n["ve"])
                W("ve", "sync")
                ST(qkvscr[128 * tt:128 * (tt + 1), :], qkvsb[p])

            W("st", "sync")
            W("mm", "sync")
            for ci in range(8):
                LD(qkvT[ci], qkvscr[:, 128 * ci:128 * (ci + 1)], transpose=True)
            W("ld", "sync")

            BAR("tensor", "sync", "vector", "scalar")
            ve_hist = []
            for tt in range(NT):
                p = tt % 2
                if tt >= 2:
                    W("ve", "sync", val=ve_hist[-2])
                    W("sp0" if p == 0 else "sp1", "sync")
                LD(sx[p], r3scr[b, 128 * tt:128 * (tt + 1), :])
                for cc in range(2):
                    g = tt * 2 + cc
                    bank = pb[g % 3]
                    if g >= 3:
                        W("ve", "tensor", val=n["ve"] - 3)
                    for ci in range(7):
                        MME(lambda bank=bank, ci=ci, tt=tt, cc=cc: e.matmul(
                            bank, qkvT[ci][:, 128 * tt:128 * (tt + 1)],
                            wsb["o"][ci][:, 512 * cc:512 * (cc + 1)],
                            start=(ci == 0), stop=False))
                    MM(lambda bank=bank, tt=tt, cc=cc: e.matmul(
                        bank, qkvT[7][:, 128 * tt:128 * (tt + 1)],
                        wsb["o"][7][:, 512 * cc:512 * (cc + 1)],
                        start=False, stop=True))
                    W("mm", "vector")
                    W("ld", "vector")
                    VE(lambda cc=cc, bank=bank, p=p: e.tensor_add(
                        sres[p][:, 512 * cc:512 * (cc + 1)], bank,
                        bbc["o"][:, 512 * cc:512 * (cc + 1)]))
                    VEc(lambda cc=cc, p=p: e.tensor_add(
                        sres[p][:, 512 * cc:512 * (cc + 1)],
                        sres[p][:, 512 * cc:512 * (cc + 1)],
                        sx[p][:, 512 * cc:512 * (cc + 1)]))
                W("ve", "sync")
                STP(p, r4_out[b, 128 * tt:128 * (tt + 1), :], sres[p])
                W("ve", "scalar")
                W("sp0" if p == 0 else "sp1", "scalar")
                SC(lambda p=p: e.activation(sx[p], sres[p], AF.Square,
                                            accum_out=L2[p]["ssumsq"]))
                VEc(lambda p=p: e.reduce_sum(L2[p]["ssum"], sres[p], AX.X))
                W("sc", "vector")
                VEc(lambda p=p: e.tensor_scalar(L2[p]["smean"], L2[p]["ssum"],
                                                1.0 / C, None, ALU.mult))
                VEc(lambda p=p: e.tensor_scalar(L2[p]["svar"], L2[p]["ssumsq"],
                                                1.0 / C, None, ALU.mult))
                VEc(lambda p=p: e.tensor_mul(L2[p]["sm2"], L2[p]["smean"],
                                             L2[p]["smean"]))
                VEc(lambda p=p: e.tensor_sub(L2[p]["svar"], L2[p]["svar"],
                                             L2[p]["sm2"]))
                VEc(lambda p=p: e.tensor_scalar(L2[p]["svar"], L2[p]["svar"],
                                                1e-5, None, ALU.add))
                W("ve", "scalar")
                SC(lambda p=p: e.activation(L2[p]["ssqrt"], L2[p]["svar"],
                                            AF.Sqrt))
                W("sc", "vector")
                VEc(lambda p=p: e.reciprocal(L2[p]["srstd"], L2[p]["ssqrt"]))
                W("sp0" if p == 0 else "sp1", "vector")
                VEc(lambda p=p: e.tensor_scalar(sln[p], sres[p], L2[p]["smean"],
                                                L2[p]["srstd"],
                                                ALU.subtract, ALU.mult))
                VEc(lambda p=p: e.tensor_reduce(stmp1, sln[p], AX.X, ALU.max,
                                                apply_absolute_value=True))
                VEc(lambda: e.tensor_max(samax, samax, stmp1))
                ve_hist.append(n["ve"])
                W("ve", "sync")
                STP(p, xln4_out[b, 128 * tt:128 * (tt + 1), :], sln[p])

        BAR("sync", "vector")
        ST(flipscr.rearrange("a b -> b a"), samax)
        W("st", "sync")
        LD(sflip, flipscr)
        W("ld", "vector")
        VEc(lambda: e.tensor_reduce(sam1, sflip, AX.X, ALU.max))
        W("ve", "sync")
        ST(amax_out, sam1)
        BAR("sync", "gpsimd", "tensor", "scalar", "vector")
        return n

    with nc.Block() as block:
        @block.sync
        def _(e):
            emit("sync", e)

        @block.gpsimd
        def _(e):
            emit("gpsimd", e)

        @block.tensor
        def _(e):
            emit("tensor", e)

        @block.scalar
        def _(e):
            emit("scalar", e)

        @block.vector
        def _(e):
            emit("vector", e)

    return nc



def _qdq_emit_helpers(nc, e, eng, n, SEMS, PRODUCER):
    h = {}

    def LD(out, in_, **k):
        n["ld"] += 16
        if eng == "sync":
            e.dma_start(out=out, in_=in_, **k).then_inc(SEMS["ld"], 16)

    def LDG(out, in_, **k):
        n["ldg"] += 16
        if eng == "gpsimd":
            e.dma_start(out=out, in_=in_, **k).then_inc(SEMS["ldg"], 16)

    def ST(out, in_):
        n["st"] += 16
        if eng == "sync":
            e.dma_start(out=out, in_=in_).then_inc(SEMS["st"], 16)

    def STG(out, in_):
        n["sg"] += 16
        if eng == "gpsimd":
            e.dma_start(out=out, in_=in_).then_inc(SEMS["sg"], 16)

    def W(which, tgt, val=None):
        if eng == tgt:
            e.wait_ge(SEMS[which], n[which] if val is None else val)

    def MM(fn):
        n["mm"] += 1
        if eng == "tensor":
            fn().then_inc(SEMS["mm"], 1)

    def MME(fn):
        if eng == "tensor":
            fn()

    def SC(fn):
        n["sc"] += 1
        if eng == "scalar":
            fn().then_inc(SEMS["sc"], 1)

    def VE(fn):
        n["ve"] += 1
        if eng == "vector":
            fn().then_inc(SEMS["ve"], 1)

    def VEc(fn):
        W("ve", "vector")
        VE(fn)

    def SCc(fn):
        W("sc", "scalar")
        SC(fn)

    def BAR(*engines):
        for tgt in engines:
            for k, sem in SEMS.items():
                if PRODUCER[k] != tgt and n[k] > 0:
                    if eng == tgt:
                        e.wait_ge(sem, n[k])

    h.update(LD=LD, LDG=LDG, ST=ST, STG=STG, W=W, MM=MM, MME=MME, SC=SC,
             VE=VE, VEc=VEc, SCc=SCc, BAR=BAR)
    return h


def _mk_sems(nc):
    SEMS = {k: nc.alloc_semaphore("s_" + k)
            for k in ["ld", "ldg", "mm", "sc", "ve", "st", "sg"]}
    PRODUCER = {"ld": "sync", "st": "sync", "ldg": "gpsimd", "sg": "gpsimd",
                "mm": "tensor", "sc": "scalar", "ve": "vector"}
    return SEMS, PRODUCER


def _run_block(nc, emit):
    with nc.Block() as block:
        @block.sync
        def _(e):
            emit("sync", e)

        @block.gpsimd
        def _(e):
            emit("gpsimd", e)

        @block.tensor
        def _(e):
            emit("tensor", e)

        @block.scalar
        def _(e):
            emit("scalar", e)

        @block.vector
        def _(e):
            emit("vector", e)
    return nc


def build_launch4():
    nc = bass.Bass("TRN2", target_bir_lowering=False, debug=False,
                   num_devices=NCORES, name="launch4")
    NT = TPC // 128
    IT = L1I // 128

    xln4_in = nc.dram_tensor("xln4", [TPC, C], BF16, kind="ExternalInput").ap()
    am_in = nc.dram_tensor("amax1all", [1, 8], F32, kind="ExternalInput").ap()
    wup_in = nc.dram_tensor("wup", [L1I, C], F32, kind="ExternalInput").ap()
    wg_in = nc.dram_tensor("wgate", [L1I, C], F32, kind="ExternalInput").ap()
    up_out = nc.dram_tensor("up", [L1I, TPC], BF16, kind="ExternalOutput").ap()
    gate_out = nc.dram_tensor("gate", [L1I, TPC], BF16, kind="ExternalOutput").ap()
    am2_out = nc.dram_tensor("amax2", [1, 2], F32, kind="ExternalOutput").ap()

    xqscr = nc.dram_tensor("xqscr", [TPC, C], BF16, kind="Internal").ap()
    wdscr = nc.dram_tensor("wdscr", [2, L1I, C], BF16, kind="Internal").ap()
    flipscr = nc.dram_tensor("flip4", [2, 128], F32, kind="Internal").ap()
    ones128_in = nc.inline_tensor(np.ones((1, 128), np.float32), "ones4").ap()

    A = nc.alloc_sbuf_tensor
    sam = A("sam4", [1, 8], F32).ap()
    sm1 = A("sm1", [1, 1], F32).ap()
    ssc = A("ssc4", [1, 2], F32).ap()
    sscb = A("sscb4", [128, 2], F32).ap()
    sones = A("sones4", [1, 128], F32).ap()
    sx = A("sx4", [128, C], BF16).ap()
    sxq = A("sxq", [128, C], F32).ap()
    sxqb = A("sxqb", [128, C], BF16).ap()
    sw = A("sw4", [128, C], F32).ap()
    swa = A("swa", [128, 32], F32).ap()
    sws = A("sws", [128, 32], F32).ap()
    swr = A("swr", [128, 32], F32).ap()
    swq = A("swq", [128, C], F32).ap()
    swqb = A("swqb", [128, C], BF16).ap()
    xqT = [A(f"xqT{i}", [128, TPC], BF16).ap() for i in range(8)]
    wT = [A(f"wT4_{i}", [128, L1I], BF16).ap() for i in range(8)]
    outb = [A(f"outb4_{i}", [128, 512], BF16).ap() for i in range(2)]
    samax = [A(f"samax4_{i}", [128, 1], F32).ap() for i in range(2)]
    stmp1 = A("stmp14", [128, 1], F32).ap()
    sflip = A("sflip4", [1, 128], F32).ap()
    sam2 = A("sam2o", [1, 2], F32).ap()

    P = nc.alloc_psum_tensor
    pb = [P(f"pb4_{i}", [128, 512], F32).ap() for i in range(2)]
    pb_bc = P("pb4_bc", [128, 2], F32).ap()

    SEMS, PRODUCER = _mk_sems(nc)
    sp = [nc.alloc_semaphore("sp4_0"), nc.alloc_semaphore("sp4_1")]

    def emit(eng, e):
        n = dict(ld=0, ldg=0, mm=0, sc=0, ve=0, st=0, sg=0)
        spn = [0, 0]
        H = _qdq_emit_helpers(nc, e, eng, n, SEMS, PRODUCER)
        LD, LDG, ST, STG, W = H["LD"], H["LDG"], H["ST"], H["STG"], H["W"]
        MM, MME, SC, VE, VEc, SCc, BAR = (H["MM"], H["MME"], H["SC"], H["VE"],
                                          H["VEc"], H["SCc"], H["BAR"])

        def STP(p, out, in_):
            spn[p] += 16
            if eng == "sync":
                e.dma_start(out=out, in_=in_).then_inc(sp[p], 16)

        LD(sam, am_in)
        LD(sones, ones128_in)
        W("ld", "vector")
        VE(lambda: e.tensor_reduce(sm1, sam, AX.X, ALU.max))
        VEc(lambda: e.tensor_scalar(ssc[:, 1:2], sm1, 1.0 / 32767.0, None,
                                    ALU.mult))
        VEc(lambda: e.tensor_scalar(ssc[:, 1:2], ssc[:, 1:2], 1e-12, None,
                                    ALU.max))
        VEc(lambda: e.reciprocal(ssc[:, 0:1], ssc[:, 1:2]))
        W("ve", "tensor")
        MM(lambda: e.matmul(pb_bc, sones, ssc, start=True, stop=True))
        W("mm", "scalar")
        SC(lambda: e.copy(sscb, pb_bc))

        for tt in range(NT):
            W("ve", "sync")
            LD(sx, xln4_in[128 * tt:128 * (tt + 1), :])
            W("ld", "vector")
            W("sc", "vector")
            VEc(lambda: e.tensor_scalar(sxq, sx, sscb[:, 0:1], ROUND_C,
                                        ALU.mult, ALU.add))
            W("st", "vector")
            VEc(lambda: e.tensor_scalar(sxqb, sxq, ROUND_C, sscb[:, 1:2],
                                        ALU.subtract, ALU.mult))
            W("ve", "sync")
            ST(xqscr[128 * tt:128 * (tt + 1), :], sxqb)

        for wi, w_in_ in enumerate((wup_in, wg_in)):
            for it in range(IT):
                W("ve", "sync")
                LD(sw, w_in_[128 * it:128 * (it + 1), :])
                W("ld", "vector")
                VEc(lambda: e.tensor_reduce(
                    swa.rearrange("p (a one) -> p a one", one=1),
                    sw.rearrange("p (a w) -> p a w", w=32), AX.X, ALU.max,
                    apply_absolute_value=True))
                VEc(lambda: e.tensor_scalar(sws, swa, 1.0 / 7.0, None, ALU.mult))
                VEc(lambda: e.tensor_scalar(sws, sws, 1e-12, None, ALU.max))
                VEc(lambda: e.reciprocal(swr, sws))
                VEc(lambda: e.tensor_mul(
                    swq.rearrange("p (a w) -> p a w", w=32),
                    sw.rearrange("p (a w) -> p a w", w=32),
                    swr.rearrange("p (a one) -> p a one", one=1)
                    .broadcast_to([128, 32, 32])))
                VEc(lambda: e.tensor_scalar(swq, swq, ROUND_C, ROUND_C,
                                            ALU.add, ALU.subtract))
                W("st", "vector")
                VEc(lambda: e.tensor_mul(
                    swqb.rearrange("p (a w) -> p a w", w=32),
                    swq.rearrange("p (a w) -> p a w", w=32),
                    sws.rearrange("p (a one) -> p a one", one=1)
                    .broadcast_to([128, 32, 32])))
                W("ve", "sync")
                ST(wdscr[wi, 128 * it:128 * (it + 1), :], swqb)

        W("st", "sync")
        for ci in range(8):
            LD(xqT[ci], xqscr[:, 128 * ci:128 * (ci + 1)], transpose=True)
        W("ld", "sync")

        for wi in range(2):
            dst = (up_out, gate_out)[wi]
            W("mm", "sync")
            for ci in range(8):
                LD(wT[ci], wdscr[wi, :, 128 * ci:128 * (ci + 1)], transpose=True)
            W("ld", "sync")
            BAR("tensor", "vector")
            if eng == "vector":
                e.memset(samax[wi], 0.0)
            evac_hist = []
            for it in range(IT):
                for tc in range(TPC // 512):
                    g = it * (TPC // 512) + tc
                    bank = pb[g % 2]
                    if g >= 2:
                        W("sc", "tensor", val=evac_hist[-2])
                    for ci in range(7):
                        MME(lambda bank=bank, ci=ci, it=it, tc=tc: e.matmul(
                            bank, wT[ci][:, 128 * it:128 * (it + 1)],
                            xqT[ci][:, 512 * tc:512 * (tc + 1)],
                            start=(ci == 0), stop=False))
                    MM(lambda bank=bank, it=it, tc=tc: e.matmul(
                        bank, wT[7][:, 128 * it:128 * (it + 1)],
                        xqT[7][:, 512 * tc:512 * (tc + 1)],
                        start=False, stop=True))
                    W("mm", "scalar")
                    if g >= 2:
                        if eng == "scalar":
                            e.wait_ge(sp[g % 2], spn[g % 2])
                    SC(lambda g=g, bank=bank: e.copy(outb[g % 2], bank))
                    evac_hist.append(n["sc"])
                    W("sc", "vector")
                    VEc(lambda g=g: e.tensor_reduce(stmp1, outb[g % 2], AX.X,
                                                    ALU.max,
                                                    apply_absolute_value=True))
                    VEc(lambda wi=wi: e.tensor_max(samax[wi], samax[wi], stmp1))
                    W("sc", "sync")
                    W("ve", "sync")
                    STP(g % 2, dst[128 * it:128 * (it + 1),
                                   512 * tc:512 * (tc + 1)], outb[g % 2])
            if eng == "sync":
                e.wait_ge(sp[0], spn[0])
                e.wait_ge(sp[1], spn[1])
            W("ve", "sync")
            ST(flipscr[wi:wi + 1, :].rearrange("a b -> b a"), samax[wi])
        W("st", "sync")
        LD(sflip, flipscr[0:1, :])
        W("ld", "vector")
        VEc(lambda: e.tensor_reduce(sam2[:, 0:1], sflip, AX.X, ALU.max))
        W("ve", "sync")
        LD(sflip, flipscr[1:2, :])
        W("ld", "vector")
        VEc(lambda: e.tensor_reduce(sam2[:, 1:2], sflip, AX.X, ALU.max))
        W("ve", "sync")
        ST(am2_out, sam2)
        BAR("sync", "gpsimd", "tensor", "scalar", "vector")
        return n

    return _run_block(nc, emit)


def build_launch5():
    nc = bass.Bass("TRN2", target_bir_lowering=False, debug=False,
                   num_devices=NCORES, name="launch5")
    WID = TPC

    gate_in = nc.dram_tensor("gate", [L1I, TPC], BF16, kind="ExternalInput").ap()
    am_in = nc.dram_tensor("amax2all", [8, 2], F32, kind="ExternalInput").ap()
    g2_out = nc.dram_tensor("g2pre", [L1I, TPC], BF16, kind="ExternalOutput").ap()
    am3_out = nc.dram_tensor("amax3", [1, 1], F32, kind="ExternalOutput").ap()
    flipscr = nc.dram_tensor("flip5", [1, 128], F32, kind="Internal").ap()
    mflip = nc.dram_tensor("mflip5", [1, 8], F32, kind="Internal").ap()
    ones128_in = nc.inline_tensor(np.ones((1, 128), np.float32), "ones5").ap()

    A = nc.alloc_sbuf_tensor
    sam = A("sam5", [8, 2], F32).ap()
    sm = A("sm5", [8, 1], F32).ap()
    smf = A("smf5", [1, 8], F32).ap()
    ssc = A("ssc5", [1, 2], F32).ap()
    sscb = A("sscb5", [128, 2], F32).ap()
    sones = A("sones5", [1, 128], F32).ap()
    sg_ = [A(f"sg5_{i}", [128, WID], BF16).ap() for i in range(2)]
    sq = [A(f"sq5_{i}", [128, WID], F32).ap() for i in range(2)]
    sgq = [A(f"sgq5_{i}", [128, WID], F32).ap() for i in range(2)]
    ssig = [A(f"ssig5_{i}", [128, WID], F32).ap() for i in range(2)]
    so = [A(f"so5_{i}", [128, WID], BF16).ap() for i in range(2)]
    samax = A("samax5", [128, 1], F32).ap()
    stmp1 = A("stmp15", [128, 1], F32).ap()
    sflip = A("sflip5", [1, 128], F32).ap()
    sam3 = A("sam3o", [1, 1], F32).ap()

    P = nc.alloc_psum_tensor
    pb_bc = P("pb5_bc", [128, 2], F32).ap()

    SEMS, PRODUCER = _mk_sems(nc)
    lp = [nc.alloc_semaphore("lp0"), nc.alloc_semaphore("lp1")]
    sp = [nc.alloc_semaphore("sp0"), nc.alloc_semaphore("sp1")]

    def emit(eng, e):
        n = dict(ld=0, ldg=0, mm=0, sc=0, ve=0, st=0, sg=0)
        lpn = [0, 0]
        spn = [0, 0]
        H = _qdq_emit_helpers(nc, e, eng, n, SEMS, PRODUCER)
        LD, LDG, ST, STG, W = H["LD"], H["LDG"], H["ST"], H["STG"], H["W"]
        MM, MME, SC, VE, VEc, SCc, BAR = (H["MM"], H["MME"], H["SC"], H["VE"],
                                          H["VEc"], H["SCc"], H["BAR"])

        def LDP(p, out, in_):
            lpn[p] += 16
            if eng == "sync":
                e.dma_start(out=out, in_=in_).then_inc(lp[p], 16)

        def STP(p, out, in_):
            spn[p] += 16
            if eng == "sync":
                e.dma_start(out=out, in_=in_).then_inc(sp[p], 16)

        def WP(sems, cnt, p, tgt):
            if eng == tgt:
                e.wait_ge(sems[p], cnt[p])

        LD(sam, am_in)
        LD(sones, ones128_in)
        W("ld", "vector")
        VE(lambda: e.tensor_copy(sm, sam[:, 1:2]))
        W("ve", "sync")
        ST(mflip.rearrange("a b -> b a"), sm)
        W("st", "sync")
        LD(smf, mflip)
        W("ld", "vector")
        VEc(lambda: e.tensor_reduce(ssc[:, 1:2], smf, AX.X, ALU.max))
        VEc(lambda: e.tensor_scalar(ssc[:, 1:2], ssc[:, 1:2], 1.0 / 32767.0,
                                    None, ALU.mult))
        VEc(lambda: e.tensor_scalar(ssc[:, 1:2], ssc[:, 1:2], 1e-12, None,
                                    ALU.max))
        VEc(lambda: e.reciprocal(ssc[:, 0:1], ssc[:, 1:2]))
        W("ve", "tensor")
        MM(lambda: e.matmul(pb_bc, sones, ssc, start=True, stop=True))
        W("mm", "scalar")
        SC(lambda: e.copy(sscb, pb_bc))
        if eng == "vector":
            e.memset(samax, 0.0)

        ve_hist = []
        for it in range(L1I // 128):
            p = it % 2
            if it >= 2:
                W("ve", "sync", val=ve_hist[-2][0])
                WP(sp, spn, p, "sync")
            LDP(p, sg_[p], gate_in[128 * it:128 * (it + 1), :])
            WP(lp, lpn, p, "vector")
            W("sc", "vector")
            VEc(lambda p=p: e.tensor_scalar(sq[p], sg_[p], sscb[:, 0:1],
                                            ROUND_C, ALU.mult, ALU.add))
            VEc(lambda p=p: e.tensor_scalar(sgq[p], sq[p], ROUND_C, sscb[:, 1:2],
                                            ALU.subtract, ALU.mult))
            W("ve", "scalar")
            SCc(lambda p=p: e.activation(ssig[p], sgq[p], AF.Sigmoid))
            W("sc", "vector")
            VEc(lambda p=p: e.tensor_scalar(ssig[p], ssig[p], 65536.0, ROUND_C,
                                            ALU.mult, ALU.add))
            VEc(lambda p=p: e.tensor_scalar(ssig[p], ssig[p], ROUND_C, 65535.0,
                                            ALU.subtract, ALU.min))
            VEc(lambda p=p: e.scalar_tensor_tensor(
                so[p], ssig[p], 1.0 / 65536.0, sgq[p], ALU.mult, ALU.mult))
            VEc(lambda p=p: e.tensor_reduce(stmp1, so[p], AX.X, ALU.max,
                                            apply_absolute_value=True))
            VEc(lambda: e.tensor_max(samax, samax, stmp1))
            ve_hist.append((n["ve"], None))
            W("ve", "sync")
            STP(p, g2_out[128 * it:128 * (it + 1), :], so[p])
        W("ve", "sync")
        ST(flipscr.rearrange("a b -> b a"), samax)
        W("st", "sync")
        LD(sflip, flipscr)
        W("ld", "vector")
        VEc(lambda: e.tensor_reduce(sam3, sflip, AX.X, ALU.max))
        W("ve", "sync")
        ST(am3_out, sam3)
        if eng == "sync":
            e.wait_ge(sp[0], spn[0])
            e.wait_ge(sp[1], spn[1])
        BAR("sync", "gpsimd", "tensor", "scalar", "vector")
        return n

    return _run_block(nc, emit)


def build_launch6():
    nc = bass.Bass("TRN2", target_bir_lowering=False, debug=False,
                   num_devices=NCORES, name="launch6")
    WID = TPC
    up_in = nc.dram_tensor("up", [L1I, TPC], BF16, kind="ExternalInput").ap()
    g2_in = nc.dram_tensor("g2pre", [L1I, TPC], BF16, kind="ExternalInput").ap()
    am2_in = nc.dram_tensor("amax2all", [8, 2], F32, kind="ExternalInput").ap()
    am3_in = nc.dram_tensor("amax3all", [1, 8], F32, kind="ExternalInput").ap()
    t_out = nc.dram_tensor("tprod", [L1I, TPC], BF16, kind="ExternalOutput").ap()
    am4_out = nc.dram_tensor("amax4", [1, 1], F32, kind="ExternalOutput").ap()
    flipscr = nc.dram_tensor("flip6", [1, 128], F32, kind="Internal").ap()
    mflip = nc.dram_tensor("mflip6", [1, 8], F32, kind="Internal").ap()
    ones128_in = nc.inline_tensor(np.ones((1, 128), np.float32), "ones6").ap()

    A = nc.alloc_sbuf_tensor
    sam2 = A("sam26", [8, 2], F32).ap()
    sam3 = A("sam36", [1, 8], F32).ap()
    sm = A("sm6", [8, 1], F32).ap()
    smf = A("smf6", [1, 8], F32).ap()
    ssc = A("ssc6", [1, 4], F32).ap()
    sscb = A("sscb6", [128, 4], F32).ap()
    sones = A("sones6", [1, 128], F32).ap()
    sup = [A(f"sup6_{i}", [128, WID], BF16).ap() for i in range(2)]
    sg2 = [A(f"sg26_{i}", [128, WID], BF16).ap() for i in range(2)]
    sq = [A(f"sq6_{i}", [128, WID], F32).ap() for i in range(2)]
    suq = [A(f"suq6_{i}", [128, WID], F32).ap() for i in range(2)]
    sgq = [A(f"sgq6_{i}", [128, WID], F32).ap() for i in range(2)]
    so = [A(f"so6_{i}", [128, WID], BF16).ap() for i in range(2)]
    samax = A("samax6", [128, 1], F32).ap()
    stmp1 = A("stmp16", [128, 1], F32).ap()
    sflip = A("sflip6", [1, 128], F32).ap()
    sam4 = A("sam4o", [1, 1], F32).ap()

    P = nc.alloc_psum_tensor
    pb_bc = P("pb6_bc", [128, 4], F32).ap()

    SEMS, PRODUCER = _mk_sems(nc)
    lp = [nc.alloc_semaphore("lp0"), nc.alloc_semaphore("lp1")]
    sp = [nc.alloc_semaphore("sp0"), nc.alloc_semaphore("sp1")]

    def emit(eng, e):
        n = dict(ld=0, ldg=0, mm=0, sc=0, ve=0, st=0, sg=0)
        lpn = [0, 0]
        spn = [0, 0]
        H = _qdq_emit_helpers(nc, e, eng, n, SEMS, PRODUCER)
        LD, LDG, ST, STG, W = H["LD"], H["LDG"], H["ST"], H["STG"], H["W"]
        MM, MME, SC, VE, VEc, SCc, BAR = (H["MM"], H["MME"], H["SC"], H["VE"],
                                          H["VEc"], H["SCc"], H["BAR"])

        def LDP(p, out, in_):
            lpn[p] += 16
            if eng == "sync":
                e.dma_start(out=out, in_=in_).then_inc(lp[p], 16)

        def STP(p, out, in_):
            spn[p] += 16
            if eng == "sync":
                e.dma_start(out=out, in_=in_).then_inc(sp[p], 16)

        def WP(sems, cnt, p, tgt):
            if eng == tgt:
                e.wait_ge(sems[p], cnt[p])

        LD(sam2, am2_in)
        LD(sam3, am3_in)
        LD(sones, ones128_in)
        W("ld", "vector")
        VE(lambda: e.tensor_copy(sm, sam2[:, 0:1]))
        W("ve", "sync")
        ST(mflip.rearrange("a b -> b a"), sm)
        W("st", "sync")
        LD(smf, mflip)
        W("ld", "vector")
        VEc(lambda: e.tensor_reduce(ssc[:, 1:2], smf, AX.X, ALU.max))
        VEc(lambda: e.tensor_scalar(ssc[:, 1:2], ssc[:, 1:2], 1.0 / 32767.0,
                                    None, ALU.mult))
        VEc(lambda: e.tensor_scalar(ssc[:, 1:2], ssc[:, 1:2], 1e-12, None,
                                    ALU.max))
        VEc(lambda: e.reciprocal(ssc[:, 0:1], ssc[:, 1:2]))
        VEc(lambda: e.tensor_reduce(ssc[:, 3:4], sam3, AX.X, ALU.max))
        VEc(lambda: e.tensor_scalar(ssc[:, 3:4], ssc[:, 3:4], 1.0 / 32767.0,
                                    None, ALU.mult))
        VEc(lambda: e.tensor_scalar(ssc[:, 3:4], ssc[:, 3:4], 1e-12, None,
                                    ALU.max))
        VEc(lambda: e.reciprocal(ssc[:, 2:3], ssc[:, 3:4]))
        W("ve", "tensor")
        MM(lambda: e.matmul(pb_bc, sones, ssc, start=True, stop=True))
        W("mm", "scalar")
        SC(lambda: e.copy(sscb, pb_bc))
        if eng == "vector":
            e.memset(samax, 0.0)

        ve_hist = []
        for it in range(L1I // 128):
            p = it % 2
            if it >= 2:
                W("ve", "sync", val=ve_hist[-2])
                WP(sp, spn, p, "sync")
            LDP(p, sup[p], up_in[128 * it:128 * (it + 1), :])
            LDP(p, sg2[p], g2_in[128 * it:128 * (it + 1), :])
            WP(lp, lpn, p, "vector")
            W("sc", "vector")
            VEc(lambda p=p: e.tensor_scalar(sq[p], sup[p], sscb[:, 0:1],
                                            ROUND_C, ALU.mult, ALU.add))
            VEc(lambda p=p: e.tensor_scalar(suq[p], sq[p], ROUND_C, sscb[:, 1:2],
                                            ALU.subtract, ALU.mult))
            VEc(lambda p=p: e.tensor_scalar(sq[p], sg2[p], sscb[:, 2:3],
                                            ROUND_C, ALU.mult, ALU.add))
            VEc(lambda p=p: e.tensor_scalar(sgq[p], sq[p], ROUND_C, sscb[:, 3:4],
                                            ALU.subtract, ALU.mult))
            VEc(lambda p=p: e.tensor_mul(so[p], sgq[p], suq[p]))
            VEc(lambda p=p: e.tensor_reduce(stmp1, so[p], AX.X, ALU.max,
                                            apply_absolute_value=True))
            VEc(lambda: e.tensor_max(samax, samax, stmp1))
            ve_hist.append(n["ve"])
            W("ve", "sync")
            STP(p, t_out[128 * it:128 * (it + 1), :], so[p])
        W("ve", "sync")
        ST(flipscr.rearrange("a b -> b a"), samax)
        W("st", "sync")
        LD(sflip, flipscr)
        W("ld", "vector")
        VEc(lambda: e.tensor_reduce(sam4, sflip, AX.X, ALU.max))
        W("ve", "sync")
        ST(am4_out, sam4)
        if eng == "sync":
            e.wait_ge(sp[0], spn[0])
            e.wait_ge(sp[1], spn[1])
        BAR("sync", "gpsimd", "tensor", "scalar", "vector")
        return n

    return _run_block(nc, emit)


def build_launch7():
    nc = bass.Bass("TRN2", target_bir_lowering=False, debug=False,
                   num_devices=NCORES, name="launch7")
    IT = L1I // 128
    TB = 1024
    NB = TPC // TB

    t_in = nc.dram_tensor("tprod", [L1I, TPC], BF16, kind="ExternalInput").ap()
    am4_in = nc.dram_tensor("amax4all", [1, 8], F32, kind="ExternalInput").ap()
    wd_in = nc.dram_tensor("wdn", [C, L1I], F32, kind="ExternalInput").ap()
    r4_in = nc.dram_tensor("resid4", [TPC, C], F32, kind="ExternalInput").ap()
    out_out = nc.dram_tensor("out", [TPC, C], F32, kind="ExternalOutput").ap()
    wdscr = nc.dram_tensor("wdscr7", [C, L1I], BF16, kind="Internal").ap()
    ones128_in = nc.inline_tensor(np.ones((1, 128), np.float32), "ones7").ap()

    A = nc.alloc_sbuf_tensor
    sam = A("sam7", [1, 8], F32).ap()
    ssc = A("ssc7", [1, 2], F32).ap()
    sscb = A("sscb7", [128, 2], F32).ap()
    sones = A("sones7", [1, 128], F32).ap()
    sw = [A(f"sw7_{i}", [128, L1I], F32).ap() for i in range(2)]
    swa = A("swa7", [128, 96], F32).ap()
    sws = [A(f"sws7_{i}", [128, 96], F32).ap() for i in range(2)]
    swr = A("swr7", [128, 96], F32).ap()
    swq = A("swq7", [128, L1I], F32).ap()
    swqb = [A(f"swqb7_{i}", [128, L1I], BF16).ap() for i in range(2)]
    wT = [A(f"wT7_{i}", [128, C], BF16).ap() for i in range(IT)]
    stin = [A(f"stin7_{i}", [128, TB], BF16).ap() for i in range(2)]
    sq = [A(f"sq7_{i}", [128, TB], F32).ap() for i in range(2)]
    obuf = [A(f"obuf7_{i}", [128, TB], BF16).ap() for i in range(IT)]
    sres = [A(f"sres7_{i}", [128, C], F32).ap() for i in range(2)]
    souts = [A(f"souts{i}", [128, C], F32).ap() for i in range(2)]

    P = nc.alloc_psum_tensor
    pb = [P(f"pb7_{i}", [128, 512], F32).ap() for i in range(4)]
    pb_bc = P("pb7_bc", [128, 2], F32).ap()

    SEMS, PRODUCER = _mk_sems(nc)
    lp = [nc.alloc_semaphore("lp0"), nc.alloc_semaphore("lp1")]
    sp = [nc.alloc_semaphore("sp0"), nc.alloc_semaphore("sp1")]

    def emit(eng, e):
        n = dict(ld=0, ldg=0, mm=0, sc=0, ve=0, st=0, sg=0)
        lpn = [0, 0]
        spn = [0, 0]
        H = _qdq_emit_helpers(nc, e, eng, n, SEMS, PRODUCER)
        LD, LDG, ST, STG, W = H["LD"], H["LDG"], H["ST"], H["STG"], H["W"]
        MM, MME, SC, VE, VEc, SCc, BAR = (H["MM"], H["MME"], H["SC"], H["VE"],
                                          H["VEc"], H["SCc"], H["BAR"])

        def LDP(p, out, in_):
            lpn[p] += 16
            if eng == "sync":
                e.dma_start(out=out, in_=in_).then_inc(lp[p], 16)

        def STP(p, out, in_):
            spn[p] += 16
            if eng == "sync":
                e.dma_start(out=out, in_=in_).then_inc(sp[p], 16)

        def WP(sems, cnt, p, tgt, val=None):
            if eng == tgt:
                e.wait_ge(sems[p], cnt[p] if val is None else val)

        LD(sam, am4_in)
        LD(sones, ones128_in)
        W("ld", "vector")
        VE(lambda: e.tensor_reduce(ssc[:, 1:2], sam, AX.X, ALU.max))
        VEc(lambda: e.tensor_scalar(ssc[:, 1:2], ssc[:, 1:2], 1.0 / 32767.0,
                                    None, ALU.mult))
        VEc(lambda: e.tensor_scalar(ssc[:, 1:2], ssc[:, 1:2], 1e-12, None,
                                    ALU.max))
        VEc(lambda: e.reciprocal(ssc[:, 0:1], ssc[:, 1:2]))
        W("ve", "tensor")
        MM(lambda: e.matmul(pb_bc, sones, ssc, start=True, stop=True))
        W("mm", "scalar")
        SC(lambda: e.copy(sscb, pb_bc))

        for ct in range(C // 128):
            p = ct % 2
            if ct >= 2:
                WP(sp, spn, p, "sync")
            LDP(p, sw[p], wd_in[128 * ct:128 * (ct + 1), :])
            WP(lp, lpn, p, "vector")
            VEc(lambda p=p: e.tensor_reduce(
                swa.rearrange("p (a one) -> p a one", one=1),
                sw[p].rearrange("p (a w) -> p a w", w=32), AX.X, ALU.max,
                apply_absolute_value=True))
            VEc(lambda p=p: e.tensor_scalar(sws[p], swa, 1.0 / 7.0, None, ALU.mult))
            VEc(lambda p=p: e.tensor_scalar(sws[p], sws[p], 1e-12, None, ALU.max))
            VEc(lambda p=p: e.reciprocal(swr, sws[p]))
            VEc(lambda p=p: e.tensor_mul(
                swq.rearrange("p (a w) -> p a w", w=32),
                sw[p].rearrange("p (a w) -> p a w", w=32),
                swr.rearrange("p (a one) -> p a one", one=1)
                .broadcast_to([128, 96, 32])))
            VEc(lambda: e.tensor_scalar(swq, swq, ROUND_C, ROUND_C,
                                        ALU.add, ALU.subtract))
            VEc(lambda p=p: e.tensor_mul(
                swqb[p].rearrange("p (a w) -> p a w", w=32),
                swq.rearrange("p (a w) -> p a w", w=32),
                sws[p].rearrange("p (a one) -> p a one", one=1)
                .broadcast_to([128, 96, 32])))
            W("ve", "sync")
            STP(p, wdscr[128 * ct:128 * (ct + 1), :], swqb[p])
        if eng == "sync":
            e.wait_ge(sp[0], spn[0])
            e.wait_ge(sp[1], spn[1])
        for it in range(IT):
            LD(wT[it], wdscr[:, 128 * it:128 * (it + 1)], transpose=True)
        W("ld", "sync")

        BAR("tensor", "vector", "scalar", "sync")
        qdq_ve = []
        for tb in range(NB):
            for it in range(IT):
                p = it % 2
                g = tb * IT + it
                if g >= 2:
                    W("ve", "sync", val=qdq_ve[g - 2])
                LDP(p, stin[p], t_in[128 * it:128 * (it + 1),
                                     TB * tb:TB * (tb + 1)])
                WP(lp, lpn, p, "vector")
                if tb > 0 and it == 0:
                    W("mm", "vector")
                VEc(lambda p=p: e.tensor_scalar(sq[p], stin[p], sscb[:, 0:1],
                                                ROUND_C, ALU.mult, ALU.add))
                VEc(lambda p=p, it=it: e.tensor_scalar(
                    obuf[it], sq[p], ROUND_C, sscb[:, 1:2],
                    ALU.subtract, ALU.mult))
                qdq_ve.append(n["ve"])
            W("ve", "tensor")
            for ts_ in range(TB // 128):
                tt = tb * (TB // 128) + ts_
                pr = tt % 2
                W("ve", "sync", val=n["ve"])
                if tt >= 2:
                    WP(sp, spn, pr, "sync")
                LDP(pr, sres[pr], r4_in[128 * tt:128 * (tt + 1), :])
                for cc in range(2):
                    bank = pb[cc]
                    if tt * 2 + cc >= 2:
                        W("ve", "tensor", val=n["ve"] - 1)
                    for it in range(IT - 1):
                        MME(lambda bank=bank, it=it, ts_=ts_, cc=cc: e.matmul(
                            bank, obuf[it][:, 128 * ts_:128 * (ts_ + 1)],
                            wT[it][:, 512 * cc:512 * (cc + 1)],
                            start=(it == 0), stop=False))
                    MM(lambda bank=bank, ts_=ts_, cc=cc: e.matmul(
                        bank, obuf[IT - 1][:, 128 * ts_:128 * (ts_ + 1)],
                        wT[IT - 1][:, 512 * cc:512 * (cc + 1)],
                        start=False, stop=True))
                    W("mm", "vector")
                    WP(lp, lpn, pr, "vector")
                    WP(sp, spn, pr, "vector")
                    VE(lambda cc=cc, bank=bank, pr=pr: e.tensor_add(
                        souts[pr][:, 512 * cc:512 * (cc + 1)], bank,
                        sres[pr][:, 512 * cc:512 * (cc + 1)]))
                W("ve", "sync")
                STP(pr, out_out[128 * tt:128 * (tt + 1), :], souts[pr])
        if eng == "sync":
            e.wait_ge(sp[0], spn[0])
            e.wait_ge(sp[1], spn[1])
        BAR("sync", "gpsimd", "tensor", "scalar", "vector")
        return n

    return _run_block(nc, emit)



_BUILDERS = {
    "l1": build_launch1, "l2": build_expert_ffn, "l3": build_launch3,
    "l4": build_launch4, "l5": build_launch5, "l6": build_launch6,
    "l7": build_launch7,
}


def _prog(name):
    if name not in _PROG_CACHE:
        _PROG_CACHE[name] = _BUILDERS[name]()
    return _PROG_CACHE[name]


def kernel(L0_attn_x, L0_attn_w_qkv, L0_attn_b_qkv, L0_attn_w_proj,
           L0_attn_b_proj, L0_attn_pos_independent, L0_ffn_Wr, L0_ffn_W1,
           L0_ffn_b1, L0_ffn_W2, L0_ffn_b2, L1_attn_Wq, L1_attn_bq,
           L1_attn_Wk, L1_attn_bk, L1_attn_Wv, L1_attn_bv, L1_attn_Wo,
           L1_attn_bo, L1_ffn_w_gate, L1_ffn_w_up, L1_ffn_w_down):
    x = np.ascontiguousarray(np.asarray(L0_attn_x, dtype=np.float32))
    w_qkv = np.asarray(L0_attn_w_qkv, np.float32)
    b_qkv = np.asarray(L0_attn_b_qkv, np.float32)
    w_proj = np.asarray(L0_attn_w_proj, np.float32)
    b_proj = np.asarray(L0_attn_b_proj, np.float32)
    Wr = np.ascontiguousarray(np.asarray(L0_ffn_Wr, np.float32))
    W1 = np.asarray(L0_ffn_W1, np.float32)
    b1 = np.asarray(L0_ffn_b1, np.float32)
    W2 = np.asarray(L0_ffn_W2, np.float32)
    b2 = np.asarray(L0_ffn_b2, np.float32)

    phi, psi, cmax = _l0_host_tables()
    cmaxneg = np.zeros((128, 16), np.float32)
    cmaxneg[:, :8] = -cmax[:8]

    in1 = []
    for c in range(NCORES):
        in1.append({
            "x": x[BPC * c:BPC * (c + 1)],
            "wqkvT": np.ascontiguousarray(w_qkv.T),
            "bqkvc": np.ascontiguousarray(b_qkv.reshape(24, 128).T),
            "bvrow": np.ascontiguousarray(b_qkv[2 * C:3 * C][None, :]),
            "wprojT": np.ascontiguousarray(w_proj.T),
            "bprow": np.ascontiguousarray(b_proj[None, :]),
            "phiT": phi.astype(ml_dtypes.bfloat16),
            "psiT": psi.astype(ml_dtypes.bfloat16), "cmaxneg": cmaxneg,
            "wr": Wr,
            "posind": np.ascontiguousarray(
                np.asarray(L0_attn_pos_independent,
                           np.float32)[:256].reshape(8, 64)),
        })
    r1 = run_bass_kernel_spmd(_prog("l1"), in1, core_ids=list(range(NCORES)))
    r1 = r1.results

    in2 = []
    for e in range(NCORES):
        buf_e = np.concatenate(
            [r1[c]["bufpart"][e * BPC * CAP:(e + 1) * BPC * CAP]
             for c in range(NCORES)], axis=0)
        in2.append({
            "buf": buf_e,
            "w1": np.ascontiguousarray(W1[e]),
            "b1c": np.ascontiguousarray(b1[e].reshape(FFH // 128, 128).T),
            "w2": np.ascontiguousarray(W2[e]),
            "b2": np.ascontiguousarray(b2[e][None, :]),
        })
    r2 = run_bass_kernel_spmd(_prog("l2"), in2, core_ids=list(range(NCORES)))
    r2 = r2.results

    in3 = []
    for c in range(NCORES):
        yall = np.concatenate(
            [r2[e]["y"][BPC * CAP * c:BPC * CAP * (c + 1)]
             for e in range(NCORES)], axis=0)
        in3.append({
            "resid2": r1[c]["resid2"],
            "yall": yall,
            "gidx": r1[c]["gidx"],
            "gates": r1[c]["gates"],
            "wq": np.ascontiguousarray(np.asarray(L1_attn_Wq, np.float32)),
            "wk": np.ascontiguousarray(np.asarray(L1_attn_Wk, np.float32)),
            "wv": np.ascontiguousarray(np.asarray(L1_attn_Wv, np.float32)),
            "wo": np.ascontiguousarray(np.asarray(L1_attn_Wo, np.float32)),
            "bq": np.asarray(L1_attn_bq, np.float32)[None, :],
            "bk": np.asarray(L1_attn_bk, np.float32)[None, :],
            "bv": np.asarray(L1_attn_bv, np.float32)[None, :],
            "bo": np.asarray(L1_attn_bo, np.float32)[None, :],
        })
    r3 = run_bass_kernel_spmd(_prog("l3"), in3, core_ids=list(range(NCORES)))
    r3 = r3.results

    am1all = np.concatenate([r3[c]["amax1"] for c in range(NCORES)],
                            axis=1)
    in4 = []
    for c in range(NCORES):
        in4.append({
            "xln4": r3[c]["xln4"].reshape(TPC, C),
            "amax1all": am1all,
            "wup": np.ascontiguousarray(np.asarray(L1_ffn_w_up, np.float32)),
            "wgate": np.ascontiguousarray(np.asarray(L1_ffn_w_gate, np.float32)),
        })
    r4 = run_bass_kernel_spmd(_prog("l4"), in4, core_ids=list(range(NCORES)))
    r4 = r4.results

    am2all = np.concatenate([r4[c]["amax2"] for c in range(NCORES)],
                            axis=0)
    in5 = [{"gate": r4[c]["gate"], "amax2all": am2all} for c in range(NCORES)]
    r5 = run_bass_kernel_spmd(_prog("l5"), in5, core_ids=list(range(NCORES)))
    r5 = r5.results

    am3all = np.concatenate([r5[c]["amax3"] for c in range(NCORES)], axis=1)
    in6 = [{"up": r4[c]["up"], "g2pre": r5[c]["g2pre"], "amax2all": am2all,
            "amax3all": am3all} for c in range(NCORES)]
    r6 = run_bass_kernel_spmd(_prog("l6"), in6, core_ids=list(range(NCORES)))
    r6 = r6.results

    am4all = np.concatenate([r6[c]["amax4"] for c in range(NCORES)], axis=1)
    in7 = []
    for c in range(NCORES):
        in7.append({
            "tprod": r6[c]["tprod"],
            "amax4all": am4all,
            "wdn": np.ascontiguousarray(np.asarray(L1_ffn_w_down, np.float32)),
            "resid4": r3[c]["resid4"].reshape(TPC, C),
        })
    r7 = run_bass_kernel_spmd(_prog("l7"), in7, core_ids=list(range(NCORES)))
    r7 = r7.results

    out = np.concatenate(
        [r7[c]["out"].reshape(BPC, T, C) for c in range(NCORES)], axis=0)
    return out


# version 16
# speedup vs baseline: 1.1067x; 1.0102x over previous
import numpy as np
import ml_dtypes
import concourse.bass as bass
import concourse.mybir as mybir
from concourse.bass import IndirectOffsetOnAxis
from concourse.bass_utils import run_bass_kernel_spmd

F32 = mybir.dt.float32
BF16 = mybir.dt.bfloat16
I32 = mybir.dt.int32
AF = mybir.ActivationFunctionType
ALU = mybir.AluOpType
AX = mybir.AxisListType

NCORES = 8
BPC = 2
T = 2048
C = 1024
NHEAD = 16
HD = 64
NEXP = 8
CAP = 320
FFH = 4096
L1I = 3072
TPC = BPC * T
ROUND_C = 1.5 * 2.0 ** 23



def build_expert_ffn():
    nc = bass.Bass("TRN2", target_bir_lowering=False, debug=False,
                   num_devices=NCORES, name="expert_ffn")
    NS = 16 * CAP
    NSC = 512
    HT = FFH // 128

    HALF = NS // 2

    buf_in = nc.dram_tensor("buf", [NS, C], BF16, kind="ExternalInput").ap()
    w1_in = nc.dram_tensor("w1", [C, FFH], F32, kind="ExternalInput").ap()
    b1_in = nc.dram_tensor("b1c", [128, HT], F32, kind="ExternalInput").ap()
    w2_in = nc.dram_tensor("w2", [FFH, C], F32, kind="ExternalInput").ap()
    b2_in = nc.dram_tensor("b2", [1, C], F32, kind="ExternalInput").ap()
    y_out = nc.dram_tensor("y", [NS, C], BF16, kind="ExternalOutput").ap()
    ones1 = nc.inline_tensor(np.ones((1, 128), ml_dtypes.bfloat16), "ones1").ap()

    A = nc.alloc_sbuf_tensor
    bufT = [A(f"bufT{i}", [128, HALF], BF16).ap() for i in range(8)]
    w1sb = [A(f"w1sb{i}", [128, FFH], BF16).ap() for i in range(8)]
    w2sb = [A(f"w2sb{i}", [128, C], BF16).ap() for i in range(HT)]
    b1sb = A("b1sb", [128, HT], F32).ap()
    b2row = A("b2row", [1, C], BF16).ap()
    onesb = A("onesb", [1, 128], BF16).ap()
    hsb = [A(f"hsb{i}", [128, NSC], BF16).ap() for i in range(HT)]
    ysb = [A(f"ysb{i}", [128, C], BF16).ap() for i in range(2)]

    ld = nc.alloc_semaphore("ld")
    ldg = nc.alloc_semaphore("ldg")
    xb = nc.alloc_semaphore("xb")
    mm = nc.alloc_semaphore("mm")
    evh = nc.alloc_semaphore("evh")
    evy = nc.alloc_semaphore("evy")
    st0 = nc.alloc_semaphore("st0")
    st1 = nc.alloc_semaphore("st1")

    pb_h = [nc.alloc_psum_tensor(f"pb_h{i}", [128, NSC], F32).ap() for i in range(4)]
    pb_y = [nc.alloc_psum_tensor(f"pb_y{i}", [128, 512], F32).ap() for i in range(2)]

    NLOADS = 2 + 8 + 1 + HT
    nsc_h = HALF // NSC

    with nc.Block() as block:
        @block.sync
        def _(sync):
            sync.dma_start(out=onesb, in_=ones1).then_inc(ld, 16)
            n_y = 0
            for half in range(2):
                if half == 1:
                    sync.wait_ge(mm, 5 * 40)
                for i in range(8):
                    sync.dma_start(
                        out=bufT[i],
                        in_=buf_in[half * HALF:(half + 1) * HALF, 128 * i:128 * (i + 1)],
                        transpose=True).then_inc(xb, 16)
                for hsc in range(nsc_h):
                    sc = half * nsc_h + hsc
                    for stile in range(4):
                        n_y += 1
                        sync.wait_ge(evy, sc * 8 + (stile + 1) * 2)
                        k = sc * 4 + stile
                        sync.dma_start(
                            out=y_out[NSC * sc + 128 * stile: NSC * sc + 128 * (stile + 1), :],
                            in_=ysb[k % 2]).then_inc(st0 if k % 2 == 0 else st1, 16)
            sync.wait_ge(st0, 16 * (n_y // 2))
            sync.wait_ge(st1, 16 * (n_y // 2))

        @block.gpsimd
        def _(g):
            g.dma_start(out=b2row, in_=b2_in).then_inc(ldg, 16)
            for i in range(8):
                g.dma_start(out=w1sb[i], in_=w1_in[128 * i:128 * (i + 1), :]) \
                    .then_inc(ldg, 16)
            g.dma_start(out=b1sb, in_=b1_in).then_inc(ldg, 16)
            for i in range(HT):
                g.dma_start(out=w2sb[i], in_=w2_in[128 * i:128 * (i + 1), :]) \
                    .then_inc(ldg, 16)

        @block.tensor
        def _(t):
            t.wait_ge(ld, 16)
            t.wait_ge(ldg, 16 * (NLOADS - 1))
            n_mm = 0
            n_evac = 0
            for half in range(2):
                t.wait_ge(xb, 16 * 8 * (half + 1))
                for sc in range(nsc_h):
                    for ht in range(HT):
                        kh = (half * nsc_h + sc) * HT + ht
                        if kh >= 4:
                            t.wait_ge(evh, kh - 3)
                        for ci in range(8):
                            ins = t.matmul(pb_h[ht % 4],
                                           w1sb[ci][:, 128 * ht:128 * (ht + 1)],
                                           bufT[ci][:, NSC * sc:NSC * (sc + 1)],
                                           start=(ci == 0), stop=(ci == 7))
                            if ci == 7:
                                ins.then_inc(mm, 1)
                        n_mm += 1
                        n_evac += 1
                    t.wait_ge(evh, (half * nsc_h + sc + 1) * HT)
                    for stile in range(4):
                        for c2 in range(2):
                            j = (half * nsc_h + sc) * 8 + stile * 2 + c2
                            if j >= 2:
                                t.wait_ge(evy, j - 1)
                            for hc in range(HT):
                                t.matmul(pb_y[c2],
                                         hsb[hc][:, 128 * stile:128 * (stile + 1)],
                                         w2sb[hc][:, 512 * c2:512 * (c2 + 1)],
                                         start=(hc == 0), stop=False)
                            t.matmul(pb_y[c2], onesb,
                                     b2row[:, 512 * c2:512 * (c2 + 1)],
                                     start=False, stop=True).then_inc(mm, 1)
                            n_mm += 1
                            n_evac += 1

        @block.scalar
        def _(s):
            n_mm = 0
            for sc in range(2 * nsc_h):
                for ht in range(HT):
                    n_mm += 1
                    s.wait_ge(mm, n_mm)
                    s.activation(hsb[ht], pb_h[ht % 4], AF.Relu,
                                 bias=b1sb[:, ht:ht + 1]).then_inc(evh, 1)
                n_mm += 8

        @block.vector
        def _(v):
            n_mm = 0
            n_y = 0
            for sc in range(2 * nsc_h):
                n_mm += HT
                for stile in range(4):
                    for c2 in range(2):
                        n_mm += 1
                        v.wait_ge(mm, n_mm)
                        if n_y >= 2:
                            v.wait_ge(st0 if n_y % 2 == 0 else st1,
                                      16 * (n_y // 2))
                        v.tensor_copy(
                            ysb[(sc * 4 + stile) % 2][:, 512 * c2:512 * (c2 + 1)],
                            pb_y[c2]).then_inc(evy, 1)
                    n_y += 1

    return nc


_PROG_CACHE = {}


def _get_prog(name, builder):
    if name not in _PROG_CACHE:
        _PROG_CACHE[name] = builder()
    return _PROG_CACHE[name]


def kernel(**inputs):
    raise NotImplementedError("assembled incrementally")




def _l0_host_tables():
    import math
    dim = C // 2
    pos = np.arange(dim, dtype=np.float64)
    freqs = (1.0 / 10000.0 ** (pos / dim))
    active = int(np.sum(freqs * 2 * math.pi * T >= 1.0))
    active = min(active, dim - C // 4)
    f = freqs[:active]
    t = np.arange(T, dtype=np.float64)
    phi = np.zeros((8 * 64, T), np.float32)
    psi = np.zeros((8 * 64, T), np.float32)
    cmax = np.zeros(16, np.float32)
    d = np.arange(-(T - 1), T, dtype=np.float64)
    for h in range(8):
        fh = f[32 * h:32 * (h + 1)]
        th = (np.cos(d[:, None] * fh) + np.sin(d[:, None] * fh)).sum(1)
        cmax[h] = th.max()
        ang = t[:, None] * fh[None, :]
        cos, sin = np.cos(ang), np.sin(ang)
        phi[64 * h:64 * h + 32] = cos.T
        phi[64 * h + 32:64 * h + 64] = sin.T
        psi[64 * h:64 * h + 32] = (cos - sin).T
        psi[64 * h + 32:64 * h + 64] = (cos + sin).T
    return phi, psi, cmax


BIG = 999999.0
L1_MAXPH = 99
L1_SCOREMODE = 2


def build_launch1():
    nc = bass.Bass("TRN2", target_bir_lowering=False, debug=False,
                   num_devices=NCORES, name="launch1")
    NT = T // 128

    x_in = nc.dram_tensor("x", [BPC, T, C], F32, kind="ExternalInput").ap()
    wqkvT_in = nc.dram_tensor("wqkvT", [C, 3 * C], F32, kind="ExternalInput").ap()
    bqkvc_in = nc.dram_tensor("bqkvc", [128, 24], F32, kind="ExternalInput").ap()
    bvrow_in = nc.dram_tensor("bvrow", [1, C], F32, kind="ExternalInput").ap()
    wprojT_in = nc.dram_tensor("wprojT", [C, C], F32, kind="ExternalInput").ap()
    bprow_in = nc.dram_tensor("bprow", [1, C], F32, kind="ExternalInput").ap()
    phiT_in = nc.dram_tensor("phiT", [512, T], BF16, kind="ExternalInput").ap()
    psiT_in = nc.dram_tensor("psiT", [512, T], BF16, kind="ExternalInput").ap()
    cmaxneg_in = nc.dram_tensor("cmaxneg", [128, 16], F32, kind="ExternalInput").ap()
    wr_in = nc.dram_tensor("wr", [C, 8], F32, kind="ExternalInput").ap()
    posind_in = nc.dram_tensor("posind", [8, 64], F32, kind="ExternalInput").ap()

    resid2_out = nc.dram_tensor("resid2", [BPC, T, C], F32, kind="ExternalOutput").ap()
    buf_out = nc.dram_tensor("bufpart", [NEXP * BPC * CAP, C], BF16,
                             kind="ExternalOutput").ap()
    gidx_out = nc.dram_tensor("gidx", [BPC, 2, T], I32, kind="ExternalOutput").ap()
    gate_out = nc.dram_tensor("gates", [BPC, 2, T], F32, kind="ExternalOutput").ap()

    lnscr2b = [nc.dram_tensor(f"lnscr{i}", [T, C], BF16, kind="Internal").ap()
               for i in range(2)]
    pcflip = nc.dram_tensor("pcflip", [1, 8], F32, kind="Internal").ap()

    tri = np.triu(np.ones((128, 128)), 1).astype(ml_dtypes.bfloat16)
    tri_in = nc.inline_tensor(tri, "tri").ap()
    allone_in = nc.inline_tensor(np.ones((128, 128), ml_dtypes.bfloat16), "allone").ap()
    pp = np.arange(128)[:, None]
    comp = np.zeros((128, 896), np.float32)
    comp[:, 512:] = 1.0
    comp[:, 384:512] = (pp <= np.arange(128)[None, :])
    comp_in = nc.inline_tensor(comp.astype(ml_dtypes.bfloat16), "maskcomp").ap()
    iota8w_in = nc.inline_tensor(
        np.tile(np.arange(8, dtype=np.float32), (128, T // 128)), "iota8w").ap()
    ones64_in = nc.inline_tensor(np.ones((1, 64), np.float32), "ones64c").ap()
    identb_in = nc.inline_tensor(np.eye(128, dtype=ml_dtypes.bfloat16),
                                 "identb").ap()
    ones128_in = nc.inline_tensor(np.ones((1, 128), np.float32), "ones128c").ap()

    A = nc.alloc_sbuf_tensor
    xlnT = [A(f"xlnT{i}", [128, T], BF16).ap() for i in range(8)]
    v1 = [A(f"v1_{i}", [128, NHEAD * 65], BF16).ap() for i in range(NT)]
    catT = [A(f"catT{i}", [128, T], BF16).ap() for i in range(8)]
    wqkv = [A(f"wqkv{i}", [128, 3 * C], BF16).ap() for i in range(8)]
    wproj = [A(f"wproj{i}", [128, C], BF16).ap() for i in range(8)]
    wrsb = [A(f"wrsb{i}", [128, 8], BF16).ap() for i in range(8)]
    qt = A("qt", [128, T], BF16).ap()
    ktl = A("ktl", [128, T], BF16).ap()
    expb = [A(f"expb{i}", [128, 1024], BF16).ap() for i in range(2)]
    sphi2 = A("sphi", [128, T], BF16).ap()
    spsi2 = A("spsi", [128, T], BF16).ap()
    brow = sphi2.bitcast(F32)[0:1, 0:512]
    sln2b = A("sln2b", [128, C], BF16).ap()
    bqkv8 = A("bqkv8", [128, 24], F32).ap()
    vbias = A("vbias", [128, C], BF16).ap()
    pbias = A("pbias", [128, C], BF16).ap()
    cmaxs = A("cmaxs", [128, 16], F32).ap()
    sbcast = A("sbcast", [64, 512], F32).ap()
    srecip = sbcast[0:1, :]
    sones64 = A("sones64", [1, 64], F32).ap()
    sones128 = A("sones128", [1, 128], F32).ap()
    sx = A("sx", [128, C], F32).ap()
    sx2 = sx
    sres = A("sres", [128, C], F32).ap()
    sq_t = sres
    lncols = A("lncols", [128, 8], F32).ap()
    L = {k: lncols[:, i:i + 1]
         for i, k in enumerate(["ssum", "ssumsq", "smean", "sm2", "svar",
                                "ssqrt", "srstd"])}
    sln = A("sln", [128, C], BF16).ap()
    sln2 = sx.bitcast(BF16)[:, 0:C]
    strie = A("strie", [128, 128], BF16).ap()
    sallone = A("sallone", [128, 128], BF16).ap()
    sident = A("sident", [128, 128], BF16).ap()
    scomp = A("scomp", [128, 896], BF16).ap()
    smask = [scomp[:, 384 - 128 * r:384 - 128 * r + 512] for r in range(4)]
    NT8 = (T // 128) * 8
    rlog = A("rlog", [128, NT8], F32).ap()
    rxs = A("rxs", [128, NT8], F32).ap()
    rmsk = A("rmsk", [128, NT8], F32).ap()
    roh0 = A("roh0", [128, NT8], F32).ap()
    roh1 = A("roh1", [128, NT8], F32).ap()
    rohs = A("rohs", [128, NT8], BF16).ap()
    rcum = A("rcum", [128, NT8], F32).ap()
    riota = A("riota", [128, NT8], F32).ap()
    NTc = T // 128
    rcolsA = A("rcolsA", [128, 15 * NTc], F32).ap()
    _c = lambda i: rcolsA[:, i * NTc:(i + 1) * NTc]
    rm1, rm2, re1, re2, rz_, rp1, rp2 = (_c(0), _c(1), _c(2), _c(3), _c(4),
                                         _c(5), _c(6))
    rpos = [_c(7), _c(8)]
    rkeep = [_c(9), _c(10)]
    rgate = [_c(11), _c(12)]
    ridxf = [_c(13), _c(14)]
    ridx = [A(f"ridx{k}", [128, NTc], I32).ap() for k in range(2)]
    spind = A("spind", [8, 64], F32).ap()
    spc = A("spc", [8, 1], F32).ap()
    spcr = A("spcr", [1, 8], F32).ap()

    P = nc.alloc_psum_tensor
    pbig = [P(f"pbig{i}", [128, 1024], F32).ap() for i in range(2)]
    pb_v = [pbig[0][:, 0:512], pbig[1][:, 0:512]]
    pb_s = [pbig[0][:, 512:1024], pbig[1][:, 512:1024]]
    pb_av_t = P("pb_av", [128, 512], F32).ap()
    pb_bc = P("pb_bc", [128, 512], F32).ap()
    pb_o2 = [P(f"pb_o2{i}", [128, 512], F32).ap() for i in range(2)]
    pb_o = pb_o2
    pb_avs = [pb_av_t, pb_o2[1]]

    s_ld = nc.alloc_semaphore("s_ld")
    s_ldg = nc.alloc_semaphore("s_ldg")
    s_mm = nc.alloc_semaphore("s_mm")
    s_sc = nc.alloc_semaphore("s_sc")
    s_ve = nc.alloc_semaphore("s_ve")
    s_st = nc.alloc_semaphore("s_st")
    s_sg = nc.alloc_semaphore("s_sg")
    lp5 = [nc.alloc_semaphore("lp5_0"), nc.alloc_semaphore("lp5_1")]
    sgp = [nc.alloc_semaphore("sgp_0"), nc.alloc_semaphore("sgp_1")]
    SEMS = {"ld": s_ld, "ldg": s_ldg, "mm": s_mm, "sc": s_sc, "ve": s_ve,
            "st": s_st, "sg": s_sg, "l50": lp5[0], "l51": lp5[1],
            "g50": sgp[0], "g51": sgp[1]}
    PRODUCER = {"ld": "sync", "st": "sync", "ldg": "gpsimd", "sg": "gpsimd",
                "mm": "tensor", "sc": "scalar", "ve": "vector",
                "l50": "sync", "l51": "sync", "g50": "gpsimd", "g51": "gpsimd"}

    def emit(eng, e):
        n = dict(ld=0, ldg=0, mm=0, sc=0, ve=0, st=0, sg=0, l50=0, l51=0,
                 g50=0, g51=0)
        breg = None
        if eng == "gpsimd":
            breg = e.alloc_register("bcreg")
            e.reg_mov(breg, NEXP * BPC * CAP - 1)

        def LD(out, in_, **k):
            n["ld"] += 16
            if eng == "sync":
                e.dma_start(out=out, in_=in_, **k).then_inc(s_ld, 16)

        def LDG(out, in_, **k):
            n["ldg"] += 16
            if eng == "gpsimd":
                e.dma_start(out=out, in_=in_, **k).then_inc(s_ldg, 16)

        def ST(out, in_, **k):
            n["st"] += 16
            if eng == "sync":
                e.dma_start(out=out, in_=in_, **k).then_inc(s_st, 16)

        def W(which, tgt, val=None):
            if eng == tgt:
                e.wait_ge(SEMS[which], n[which] if val is None else val)

        def MM(fn):
            n["mm"] += 1
            if eng == "tensor":
                fn().then_inc(s_mm, 1)

        def MME(fn):
            if eng == "tensor":
                fn()

        def SC(fn):
            n["sc"] += 1
            if eng == "scalar":
                fn().then_inc(s_sc, 1)

        def SCE(fn):
            if eng == "scalar":
                fn()

        def VE(fn):
            n["ve"] += 1
            if eng == "vector":
                fn().then_inc(s_ve, 1)

        def VEE(fn):
            if eng == "vector":
                fn()

        def SG(fn):
            n["sg"] += 16
            if eng == "gpsimd":
                fn().then_inc(s_sg, 16)

        def VEc(fn):
            W("ve", "vector")
            VE(fn)

        def SCc(fn):
            W("sc", "scalar")
            SC(fn)

        def BAR(*engines):
            for tgt in engines:
                for k, sem in SEMS.items():
                    if PRODUCER[k] != tgt and n[k] > 0:
                        if eng == tgt:
                            e.wait_ge(sem, n[k])

        LD(strie, tri_in)
        LD(sallone, allone_in)
        LD(sident, identb_in)
        LD(scomp, comp_in)
        LD(riota, iota8w_in)
        LD(sones64, ones64_in)
        LD(sones128, ones128_in)
        LD(cmaxs, cmaxneg_in)
        LD(bqkv8, bqkvc_in)
        for i in range(8):
            LDG(wqkv[i], wqkvT_in[128 * i:128 * (i + 1), :])
            LDG(wproj[i], wprojT_in[128 * i:128 * (i + 1), :])
            LDG(wrsb[i], wr_in[128 * i:128 * (i + 1), :])
        VE(lambda: e.memset(expb[0], 0.0))
        W("ve", "gpsimd")
        for zi in range(NEXP * BPC * CAP // 128):
            LDG(buf_out[128 * zi:128 * (zi + 1), :], expb[0])
        LD(spind, posind_in)
        W("ld", "vector")
        VE(lambda: e.reduce_sum(spc, spind, AX.X))
        W("ve", "sync")
        ST(pcflip.rearrange("a b -> b a"), spc)
        W("st", "sync")
        LD(spcr, pcflip)
        W("ld", "tensor")
        MM(lambda: e.matmul(pb_bc[:, 0:8], sones128, spcr, start=True, stop=True))
        W("mm", "scalar")
        W("ld", "scalar")
        SC(lambda: e.copy(cmaxs[:, 8:16], pb_bc[:, 0:8]))
        for dst, row_in in ((vbias, bvrow_in), (pbias, bprow_in)):
            for cc in range(2):
                W("sc", "sync")
                LD(brow, row_in[:, 512 * cc:512 * (cc + 1)])
                W("ld", "tensor")
                W("sc", "tensor")
                MM(lambda: e.matmul(pb_bc, sones128, brow, start=True, stop=True))
                W("mm", "scalar")
                SC(lambda dst=dst, cc=cc: e.copy(dst[:, 512 * cc:512 * (cc + 1)],
                                                 pb_bc))

        def PH_ph0(b, lnscr):
            BAR("sync", "scalar", "vector")
            for tt in range(NT):
                W("ve", "sync")
                LD(sx, x_in[b, 128 * tt:128 * (tt + 1), :])
                W("ld", "scalar")
                W("ve", "scalar")
                SC(lambda: e.activation(sq_t, sx, AF.Square, accum_out=L["ssumsq"]))
                W("ld", "vector")
                VE(lambda: e.reduce_sum(L["ssum"], sx, AX.X))
                W("sc", "vector")
                VEc(lambda: e.tensor_scalar(L["smean"], L["ssum"], 1.0 / C, None, ALU.mult))
                VEc(lambda: e.tensor_scalar(L["svar"], L["ssumsq"], 1.0 / C, None, ALU.mult))
                VEc(lambda: e.tensor_mul(L["sm2"], L["smean"], L["smean"]))
                VEc(lambda: e.tensor_sub(L["svar"], L["svar"], L["sm2"]))
                VEc(lambda: e.tensor_scalar(L["svar"], L["svar"], 1e-5, None, ALU.add))
                W("ve", "scalar")
                SC(lambda: e.activation(L["ssqrt"], L["svar"], AF.Sqrt))
                W("sc", "vector")
                VEc(lambda: e.reciprocal(L["srstd"], L["ssqrt"]))
                W("st", "vector")
                VEc(lambda: e.tensor_scalar(sln, sx, L["smean"], L["srstd"],
                                           ALU.subtract, ALU.mult))
                W("ve", "sync")
                ST(lnscr[128 * tt:128 * (tt + 1), :], sln)
        def PH_ph1(b, lnscr):
            W("st", "sync")
            W("mm", "sync")
            for ci in range(8):
                LD(xlnT[ci], lnscr[:, 128 * ci:128 * (ci + 1)], transpose=True)
            W("ld", "sync")
        def PH_ph2a(b, lnscr):
            BAR("tensor", "vector")
            for tt in range(NT):
                VE(lambda tt=tt: e.memset(
                    v1[tt].rearrange("p (h w) -> p h w", w=65)[:, :, 64:65], 1.0))
            for tt in range(NT):
                for vc in range(2):
                    pb = pb_v[vc]
                    if tt > 0:
                        W("ve", "tensor", val=n["ve"] - 1)
                    for ci in range(7):
                        MME(lambda pb=pb, ci=ci, tt=tt, vc=vc: e.matmul(
                            pb, xlnT[ci][:, 128 * tt:128 * (tt + 1)],
                            wqkv[ci][:, 2 * C + 512 * vc:2 * C + 512 * (vc + 1)],
                            start=(ci == 0), stop=False))
                    MM(lambda pb=pb, tt=tt, vc=vc: e.matmul(
                        pb, xlnT[7][:, 128 * tt:128 * (tt + 1)],
                        wqkv[7][:, 2 * C + 512 * vc:2 * C + 512 * (vc + 1)],
                        start=False, stop=True))
                    W("mm", "vector")
                    VE(lambda tt=tt, vc=vc, pb=pb: e.tensor_add(
                        v1[tt].rearrange("p (h w) -> p h w", w=65)
                        [:, 8 * vc:8 * (vc + 1), 0:64],
                        pb.rearrange("p (h w) -> p h w", w=64),
                        vbias[:, 512 * vc:512 * (vc + 1)]
                        .rearrange("p (h w) -> p h w", w=64)))
        def PH_ph2b(b, lnscr):
            proj_hist = []
            pending_norm = [None]
            last_norm_ve = [0]
            for hp in range(NHEAD // 2 if L1_MAXPH >= 2 else 0):
                h0 = 2 * hp
                if pending_norm[0]:
                    while pending_norm[0]:
                        pending_norm[0].pop(0)()
                    pending_norm[0] = None
                for which, dst in ((0, qt), (1, ktl)):
                    for tc in range(4):
                        pb = pb_o[tc % 2]
                        if which == 0 and tc == 0:
                            W("ve", "tensor")
                        if len(proj_hist) >= 2:
                            W("ve", "tensor", val=proj_hist[-2])
                        for ci in range(7):
                            MME(lambda pb=pb, ci=ci, h0=h0, tc=tc, which=which:
                                e.matmul(
                                    pb,
                                    wqkv[ci][:, C * which + 64 * h0:
                                             C * which + 64 * (h0 + 2)],
                                    xlnT[ci][:, 512 * tc:512 * (tc + 1)],
                                    start=(ci == 0), stop=False))
                        MM(lambda pb=pb, h0=h0, tc=tc, which=which: e.matmul(
                            pb,
                            wqkv[7][:, C * which + 64 * h0:C * which + 64 * (h0 + 2)],
                            xlnT[7][:, 512 * tc:512 * (tc + 1)],
                            start=False, stop=True))
                        W("mm", "vector")
                        VE(lambda dst=dst, tc=tc, pb=pb, h0=h0, which=which:
                           e.tensor_scalar(dst[:, 512 * tc:512 * (tc + 1)], pb,
                                           bqkv8[:, 8 * which + h0 // 2:
                                                 8 * which + h0 // 2 + 1],
                                           0.125 if which == 0 else 1.0,
                                           ALU.add, ALU.mult))
                        proj_hist.append(n["ve"])
                if h0 < 8 and L1_SCOREMODE >= 2:
                    W("mm", "sync")
                    W("ve", "sync")
                    LD(sphi2, phiT_in[64 * h0:64 * (h0 + 2), :])
                    LD(spsi2, psiT_in[64 * h0:64 * (h0 + 2), :])
                for hh in range(2 if L1_SCOREMODE > 0 else 0):
                    h = h0 + hh
                    ext = (h < 8) and L1_SCOREMODE >= 2
                    prow = 64 * hh
                    for qc in range(4):
                        pb_av = pb_avs[qc % 2]
                        nk = 4 * qc + 4
                        groups = [list(range(g, min(g + 2, nk)))
                                  for g in range(0, nk, 2)]
                        exp_hist = {}
                        ve_hist = {}
                        has_mask = {}

                        def emit_av(g, groups=groups, nk=nk, pb_av=pb_av,
                                    exp_hist=exp_hist, ve_hist=ve_hist,
                                    has_mask=has_mask, h=h):
                            grp = groups[g]
                            if g == 0 and last_norm_ve[0]:
                                W("ve", "tensor", val=last_norm_ve[0])
                            if has_mask[g]:
                                W("ve", "tensor", val=ve_hist[g])
                            else:
                                W("sc", "tensor", val=exp_hist[g])
                            for j, kti in enumerate(grp):
                                last = (kti == nk - 1)
                                fn = (lambda g=g, j=j, kti=kti, h=h, last=last,
                                      pb_av=pb_av: e.matmul(
                                          pb_av[0:65, :],
                                          v1[kti][:, 65 * h:65 * (h + 1)],
                                          expb[g % 2][:, 512 * j:
                                                      512 * (j + 1)],
                                          start=(kti == 0), stop=last))
                                if last:
                                    MM(fn)
                                else:
                                    MME(fn)

                        for gi, grp in enumerate(groups):
                            pbs = pbig[gi % 2]
                            if gi >= 2:
                                W("sc", "tensor", val=exp_hist[gi - 2])
                            else:
                                W("sc", "tensor")
                            W("ld", "tensor")
                            W("ve", "tensor", val=proj_hist[-1])
                            for j, kti in enumerate(grp):
                                is_last = (j == len(grp) - 1)
                                fn1 = (lambda pbs=pbs, j=j, kti=kti, prow=prow,
                                       qc=qc: e.matmul(
                                           pbs[:, 512 * j:512 * (j + 1)],
                                           ktl[prow:prow + 64,
                                               128 * kti:128 * (kti + 1)],
                                           qt[prow:prow + 64,
                                              512 * qc:512 * (qc + 1)],
                                           start=True, stop=not ext))
                                fn2 = (lambda pbs=pbs, j=j, kti=kti, qc=qc,
                                       prow=prow: e.matmul(
                                           pbs[:, 512 * j:512 * (j + 1)],
                                           spsi2[prow:prow + 64,
                                                 128 * kti:128 * (kti + 1)],
                                           sphi2[prow:prow + 64,
                                                 512 * qc:512 * (qc + 1)],
                                           start=False, stop=True)) if ext else None
                                if not is_last:
                                    MME(fn1)
                                    if ext:
                                        MME(fn2)
                                else:
                                    if ext:
                                        MME(fn1)
                                        MM(fn2)
                                    else:
                                        MM(fn1)
                            W("mm", "scalar")
                            wid = 512 * len(grp)
                            if b == 0 and hp == 0 and hh == 0 and qc == 0 \
                                    and gi == 0:
                                W("ldg", "scalar")
                            SC(lambda gi=gi, pbs=pbs, h=h, wid=wid: e.activation(
                                expb[gi % 2][:, 0:wid], pbs[:, 0:wid], AF.Exp,
                                bias=cmaxs[:, h:h + 1]))
                            exp_hist[gi] = n["sc"]
                            ndiag = sum(1 for kti in grp if kti >= 4 * qc)
                            has_mask[gi] = ndiag > 0
                            if ndiag:
                                W("sc", "vector")
                                for j, kti in enumerate(grp):
                                    if kti >= 4 * qc:
                                        VE(lambda gi=gi, j=j, kti=kti, qc=qc:
                                           e.tensor_mul(
                                               expb[gi % 2][:, 512 * j:
                                                            512 * (j + 1)],
                                               expb[gi % 2][:, 512 * j:
                                                            512 * (j + 1)],
                                               smask[kti - 4 * qc]))
                                ve_hist[gi] = n["ve"]
                            if pending_norm[0]:
                                if gi == 0:
                                    pending_norm[0].pop(0)()
                                elif gi == 1:
                                    while pending_norm[0]:
                                        pending_norm[0].pop(0)()
                                    pending_norm[0] = None
                            if gi >= 1:
                                emit_av(gi - 1)
                        emit_av(len(groups) - 1)
                        av_mm_val = n["mm"]

                        def mk_norm(pb_av=pb_av, h=h, qc=qc,
                                    av_mm_val=av_mm_val):
                            def goA():
                                W("mm", "vector", val=av_mm_val)
                                VE(lambda: e.reciprocal(srecip,
                                                        pb_av[64:65, :]))
                            def goB():
                                W("ve", "tensor")
                                MM(lambda: e.matmul(pb_bc[0:64, :], sones64,
                                                    srecip,
                                                    start=True, stop=True))
                                W("mm", "scalar")
                                SC(lambda: e.copy(sbcast, pb_bc[0:64, :]))
                                W("sc", "vector")
                                VE(lambda: e.tensor_mul(
                                    catT[h // 2][64 * (h % 2):
                                                 64 * (h % 2) + 64,
                                                 512 * qc:512 * (qc + 1)],
                                    pb_av[0:64, :], sbcast))
                                last_norm_ve[0] = n["ve"]
                            return [goA, goB]
                        pending_norm[0] = mk_norm()
            if pending_norm[0]:
                while pending_norm[0]:
                    pending_norm[0].pop(0)()
                pending_norm[0] = None
                W("ve", "tensor")
        def PH_ph2c(b, lnscr):
            BAR("tensor", "sync", "vector")
            sxb = [sx2, sphi2.bitcast(F32)]
            load_val = {}
            W("ve", "sync", val=n["ve"])
            LD(sxb[0], x_in[b, 0:128, :])
            load_val[0] = n["ld"]
            W("mm", "gpsimd")
            LDG(sxb[1], x_in[b, 128:256, :])
            load_val[1] = n["ldg"]
            for tt in range(NT if L1_MAXPH >= 3 else 0):
                buf = sxb[tt % 2]
                for cc in range(2):
                    pb = pb_o[cc]
                    if tt > 0:
                        W("ve", "tensor", val=n["ve"] - 3)
                    for ci in range(7):
                        MME(lambda pb=pb, ci=ci, tt=tt, cc=cc: e.matmul(
                            pb, catT[ci][:, 128 * tt:128 * (tt + 1)],
                            wproj[ci][:, 512 * cc:512 * (cc + 1)],
                            start=(ci == 0), stop=False))
                    MM(lambda pb=pb, tt=tt, cc=cc: e.matmul(
                        pb, catT[7][:, 128 * tt:128 * (tt + 1)],
                        wproj[7][:, 512 * cc:512 * (cc + 1)],
                        start=False, stop=True))
                    W("mm", "vector")
                    if cc == 0:
                        W("ld" if tt % 2 == 0 else "ldg", "vector",
                          val=load_val[tt])
                    VE(lambda cc=cc, pb=pb: e.tensor_add(
                        sres[:, 512 * cc:512 * (cc + 1)], pb,
                        pbias[:, 512 * cc:512 * (cc + 1)]))
                    VEc(lambda cc=cc, buf=buf: e.tensor_add(
                        sres[:, 512 * cc:512 * (cc + 1)],
                        sres[:, 512 * cc:512 * (cc + 1)],
                        buf[:, 512 * cc:512 * (cc + 1)]))
                W("ve", "sync")
                ST(resid2_out[b, 128 * tt:128 * (tt + 1), :], sres)
                W("st", "vector")
                if tt + 2 < NT:
                    if tt % 2 == 0:
                        W("ve", "sync")
                        LD(sxb[0], x_in[b, 128 * (tt + 2):128 * (tt + 3), :])
                        load_val[tt + 2] = n["ld"]
                    else:
                        W("ve", "gpsimd")
                        LDG(sxb[1], x_in[b, 128 * (tt + 2):128 * (tt + 3), :])
                        load_val[tt + 2] = n["ldg"]
        def PH_ph3(b, lnscr):
            BAR("sync", "scalar", "vector")
            for tt in range(NT):
                W("ve", "sync")
                LD(sx, resid2_out[b, 128 * tt:128 * (tt + 1), :])
                W("ld", "scalar")
                W("ve", "scalar")
                SC(lambda: e.activation(sq_t, sx, AF.Square, accum_out=L["ssumsq"]))
                W("ld", "vector")
                VE(lambda: e.reduce_sum(L["ssum"], sx, AX.X))
                W("sc", "vector")
                VEc(lambda: e.tensor_scalar(L["smean"], L["ssum"], 1.0 / C, None, ALU.mult))
                VEc(lambda: e.tensor_scalar(L["svar"], L["ssumsq"], 1.0 / C, None, ALU.mult))
                VEc(lambda: e.tensor_mul(L["sm2"], L["smean"], L["smean"]))
                VEc(lambda: e.tensor_sub(L["svar"], L["svar"], L["sm2"]))
                VEc(lambda: e.tensor_scalar(L["svar"], L["svar"], 1e-5, None, ALU.add))
                W("ve", "scalar")
                SC(lambda: e.activation(L["ssqrt"], L["svar"], AF.Sqrt))
                W("sc", "vector")
                VEc(lambda: e.reciprocal(L["srstd"], L["ssqrt"]))
                W("st", "vector")
                VEc(lambda: e.tensor_scalar(sln, sx, L["smean"], L["srstd"],
                                           ALU.subtract, ALU.mult))
                W("ve", "sync")
                ST(lnscr[128 * tt:128 * (tt + 1), :], sln)
                trdsts = [pb_bc.bitcast(BF16)[:, 0:128],
                          pb_av_t.bitcast(BF16)[:, 0:128]]
                W("ve", "tensor")
                cp_hist = []
                for ci in range(8):
                    trd = trdsts[ci % 2]
                    if ci >= 2:
                        W("ve", "tensor", val=cp_hist[ci - 2])
                    MM(lambda trd=trd, ci=ci: e.transpose(
                        trd, sln[:, 128 * ci:128 * (ci + 1)], sident))
                    W("mm", "vector")
                    VE(lambda trd=trd, ci=ci: e.tensor_copy(
                        expb[0][:, 128 * ci:128 * (ci + 1)], trd))
                    cp_hist.append(n["ve"])
                W("ve", "tensor")
                for ci in range(7):
                    MME(lambda ci=ci, tt=tt: e.matmul(
                        pb_s[tt % 2][:, 0:8],
                        expb[0][:, 128 * ci:128 * (ci + 1)],
                        wrsb[ci], start=(ci == 0), stop=False))
                MM(lambda tt=tt: e.matmul(
                    pb_s[tt % 2][:, 0:8], expb[0][:, 896:1024],
                    wrsb[7], start=False, stop=True))
                W("mm", "scalar")
                SC(lambda tt=tt: e.copy(rlog[:, 8 * tt:8 * (tt + 1)],
                                        pb_s[tt % 2][:, 0:8]))
        def PH_ph4(b, lnscr):
            W("st", "sync")
            W("mm", "sync")
            for ci in range(8):
                LD(xlnT[ci], lnscr[:, 128 * ci:128 * (ci + 1)], transpose=True)
            W("ld", "sync")
        def PH_ph5(b, lnscr):
            BAR("tensor", "vector", "scalar", "gpsimd", "sync")
            NTc = T // 128
            r3 = lambda ap: ap.rearrange("p (t e) -> p t e", e=8)
            col3 = lambda ap: ap.unsqueeze(2)
            bc8 = lambda ap: ap.unsqueeze(2).broadcast_to([128, NTc, 8])
            W("sc", "vector")
            VE(lambda: e.tensor_reduce(col3(rm1), r3(rlog), AX.X, ALU.max))
            VEc(lambda: e.tensor_tensor(r3(rmsk), r3(rlog), bc8(rm1),
                                        ALU.is_equal))
            VEc(lambda: e.scalar_tensor_tensor(rxs, rmsk, -BIG, rlog,
                                               ALU.mult, ALU.add))
            VEc(lambda: e.tensor_reduce(col3(rm2), r3(rxs), AX.X, ALU.max))
            VEc(lambda: e.tensor_scalar(rxs, riota, BIG, None, ALU.subtract))
            VEc(lambda: e.tensor_mul(rxs, rmsk, rxs))
            VEc(lambda: e.tensor_scalar(rxs, rxs, BIG, None, ALU.add))
            VEc(lambda: e.tensor_reduce(col3(re1), r3(rxs), AX.X, ALU.min))
            VEc(lambda: e.tensor_tensor(r3(rmsk), r3(rlog), bc8(rm2),
                                        ALU.is_equal))
            VEc(lambda: e.tensor_scalar(rxs, riota, BIG, None, ALU.subtract))
            VEc(lambda: e.tensor_mul(rxs, rmsk, rxs))
            VEc(lambda: e.tensor_scalar(rxs, rxs, BIG, None, ALU.add))
            VEc(lambda: e.tensor_reduce(col3(re2), r3(rxs), AX.X, ALU.min))
            VEc(lambda: e.tensor_tensor(r3(rxs), r3(rlog), bc8(rm1),
                                        ALU.subtract))
            W("ve", "scalar")
            SCc(lambda: e.activation(rxs, rxs, AF.Exp))
            W("sc", "vector")
            VEc(lambda: e.tensor_reduce(col3(rz_), r3(rxs), AX.X, ALU.add))
            VEc(lambda: e.reciprocal(rp1, rz_))
            VEc(lambda: e.tensor_sub(rm2, rm2, rm1))
            W("ve", "scalar")
            SCc(lambda: e.activation(rm2, rm2, AF.Exp))
            W("sc", "vector")
            VEc(lambda: e.tensor_mul(rp2, rm2, rp1))
            VEc(lambda: e.tensor_tensor(r3(roh0), r3(riota), bc8(re1),
                                        ALU.is_equal))
            VEc(lambda: e.tensor_tensor(r3(roh1), r3(riota), bc8(re2),
                                        ALU.is_equal))
            VEc(lambda: e.tensor_add(rohs, roh0, roh1))
            W("ve", "tensor")
            sc_hist = []
            for tt in range(NTc):
                pbr = pb_s[tt % 2]
                if len(sc_hist) >= 2:
                    W("sc", "tensor", val=sc_hist[-2])
                for s_ in range(tt):
                    MME(lambda s_=s_, pbr=pbr: e.matmul(
                        pbr[:, 0:8], sallone, rohs[:, 8 * s_:8 * (s_ + 1)],
                        start=(s_ == 0), stop=False))
                MM(lambda tt=tt, pbr=pbr: e.matmul(
                    pbr[:, 0:8], strie, rohs[:, 8 * tt:8 * (tt + 1)],
                    start=(tt == 0), stop=True))
                W("mm", "scalar")
                SC(lambda tt=tt, pbr=pbr: e.copy(rcum[:, 8 * tt:8 * (tt + 1)],
                                                 pbr[:, 0:8]))
                sc_hist.append(n["sc"])
            W("sc", "vector")
            for k in range(2):
                oh = (roh0, roh1)[k]
                ee = (re1, re2)[k]
                pp_ = (rp1, rp2)[k]
                VEc(lambda oh=oh: e.tensor_mul(rxs, oh, rcum))
                VEc(lambda k=k: e.tensor_reduce(col3(rpos[k]), r3(rxs), AX.X,
                                                ALU.add))
                VEc(lambda k=k: e.tensor_scalar(rkeep[k], rpos[k], float(CAP),
                                                None, ALU.is_lt))
                VEc(lambda k=k, ee=ee: e.scalar_tensor_tensor(
                    ridxf[k], ee, float(BPC * CAP), rpos[k], ALU.mult, ALU.add))
                VEc(lambda k=k, b=b: e.tensor_scalar(
                    ridxf[k], ridxf[k], float(b * CAP), None, ALU.add))
                VEc(lambda k=k: e.tensor_mul(ridxf[k], ridxf[k], rkeep[k]))
                VEc(lambda k=k: e.tensor_scalar(rm1, rkeep[k], -1.0, None,
                                                ALU.mult))
                VEc(lambda: e.tensor_scalar(rm1, rm1, 1.0, None, ALU.add))
                VEc(lambda k=k: e.scalar_tensor_tensor(
                    ridxf[k], rm1, BIG, ridxf[k], ALU.mult, ALU.add))
                W("sg", "vector")
                VEc(lambda k=k: e.tensor_copy(ridx[k], ridxf[k]))
                W("st", "vector")
                VEc(lambda k=k, pp_=pp_: e.tensor_mul(rgate[k], pp_, rkeep[k]))
            W("ldg", "gpsimd")
            for tt in range(NTc):
                p5 = tt % 2
                stage = (sln2, sln2b)[p5]
                gk = "g50" if p5 == 0 else "g51"
                lk = "l50" if p5 == 0 else "l51"
                if tt >= 2:
                    W(gk, "sync")
                W("ve", "sync", val=n["ve"])
                n[lk] += 16
                if eng == "sync":
                    e.dma_start(out=stage,
                                in_=lnscr[128 * tt:128 * (tt + 1), :]) \
                        .then_inc(lp5[p5], 16)
                W(lk, "gpsimd")
                if tt == 0:
                    W("ve", "gpsimd")
                for k in range(2):
                    n[gk] += 16
                    if eng == "gpsimd":
                        e.indirect_dma_start(
                            buf_out[:, :],
                            IndirectOffsetOnAxis(ridx[k][:, tt:tt + 1], 0),
                            stage, None, bounds_check=breg,
                            oob_is_err=False).then_inc(sgp[p5], 16)
            if eng == "gpsimd":
                e.wait_ge(sgp[0], n["g50"])
                e.wait_ge(sgp[1], n["g51"])
            W("ve", "gpsimd")
            for k in range(2):
                SG(lambda k=k, b=b: e.dma_start(
                    out=gidx_out[b, k, :].rearrange("(t p) -> p t", p=128),
                    in_=ridx[k], allow_slow_non_contiguous=True))
            W("ve", "sync")
            for k in range(2):
                ST(gate_out[b, k, :].rearrange("(t p) -> p t", p=128),
                   rgate[k], allow_slow_non_contiguous=True)

        PH_ph0(0, lnscr2b[0])
        PH_ph1(0, lnscr2b[0])
        PH_ph2a(0, lnscr2b[0])
        PH_ph2b(0, lnscr2b[0])
        PH_ph0(1, lnscr2b[1])
        PH_ph2c(0, lnscr2b[0])
        PH_ph3(0, lnscr2b[0])
        PH_ph1(1, lnscr2b[1])
        PH_ph5(0, lnscr2b[0])
        PH_ph2a(1, lnscr2b[1])
        PH_ph2b(1, lnscr2b[1])
        PH_ph2c(1, lnscr2b[1])
        PH_ph3(1, lnscr2b[1])
        PH_ph5(1, lnscr2b[1])

        BAR("sync", "gpsimd", "tensor", "scalar", "vector")
        return n

    with nc.Block() as block:
        @block.sync
        def _(e):
            emit("sync", e)

        @block.gpsimd
        def _(e):
            emit("gpsimd", e)

        @block.tensor
        def _(e):
            emit("tensor", e)

        @block.scalar
        def _(e):
            emit("scalar", e)

        @block.vector
        def _(e):
            emit("vector", e)

    return nc



def build_launch3():
    nc = bass.Bass("TRN2", target_bir_lowering=False, debug=False,
                   num_devices=NCORES, name="launch3")
    NT = T // 128
    NS = NEXP * BPC * CAP

    r2_in = nc.dram_tensor("resid2", [BPC, T, C], F32, kind="ExternalInput").ap()
    yall_in = nc.dram_tensor("yall", [NS, C], BF16, kind="ExternalInput").ap()
    gidx_in = nc.dram_tensor("gidx", [BPC, 2, T], I32, kind="ExternalInput").ap()
    gate_in = nc.dram_tensor("gates", [BPC, 2, T], F32, kind="ExternalInput").ap()
    w_in = {w: nc.dram_tensor("w" + w, [C, C], F32, kind="ExternalInput").ap()
            for w in "qkvo"}
    b_in = {w: nc.dram_tensor("b" + w, [1, C], F32, kind="ExternalInput").ap()
            for w in "qkvo"}

    r4_out = nc.dram_tensor("resid4", [BPC, T, C], F32, kind="ExternalOutput").ap()
    xln4_out = nc.dram_tensor("xln4", [BPC, T, C], BF16, kind="ExternalOutput").ap()
    amax_out = nc.dram_tensor("amax1", [1, 1], F32, kind="ExternalOutput").ap()

    r3scr = nc.dram_tensor("r3scr", [BPC, T, C], F32, kind="Internal").ap()
    lnscr = nc.dram_tensor("lnscr3", [T, C], BF16, kind="Internal").ap()
    qkvscr = nc.dram_tensor("qkvscr", [T, C], BF16, kind="Internal").ap()
    flipscr = nc.dram_tensor("flipscr", [1, 128], F32, kind="Internal").ap()
    ones128_in = nc.inline_tensor(np.ones((1, 128), np.float32), "ones128c3").ap()

    A = nc.alloc_sbuf_tensor
    xlnT = [A(f"xlnT{i}", [128, T], BF16).ap() for i in range(8)]
    qkvT = [A(f"qkvT{i}", [128, T], BF16).ap() for i in range(8)]
    wsb = {w: [A(f"w_{w}{i}", [128, C], BF16).ap() for i in range(8)] for w in "qkvo"}
    bbc = {w: A(f"bbc_{w}", [128, C], BF16).ap() for w in "qkvo"}
    brow = A("brow3", [1, 512], F32).ap()
    sones128 = A("sones128_3", [1, 128], F32).ap()
    sx = [A(f"sx3_{i}", [128, C], F32).ap() for i in range(2)]
    sres = [A(f"sres3_{i}", [128, C], F32).ap() for i in range(2)]
    qk = [A(f"qk3_{i}", [128, 2 * C], F32).ap() for i in range(2)]
    vf = [A(f"vf3_{i}", [128, C], F32).ap() for i in range(2)]
    coeff = [A(f"coeff3_{i}", [128, 16], F32).ap() for i in range(2)]
    qkvsb = [A(f"qkvsb{i}", [128, C], BF16).ap() for i in range(2)]
    y0 = [A(f"y0sb{i}", [128, C], BF16).ap() for i in range(2)]
    y1 = [A(f"y1sb{i}", [128, C], BF16).ap() for i in range(2)]
    gi0 = [A(f"gi0_{i}", [128, 1], I32).ap() for i in range(2)]
    gi1 = [A(f"gi1_{i}", [128, 1], I32).ap() for i in range(2)]
    gt0 = [A(f"gt0_{i}", [128, 1], F32).ap() for i in range(2)]
    gt1 = [A(f"gt1_{i}", [128, 1], F32).ap() for i in range(2)]
    lncols = [A(f"l3cols{i}", [128, 8], F32).ap() for i in range(2)]
    LNK = ["ssum", "ssumsq", "smean", "sm2", "svar", "ssqrt", "srstd"]
    L2 = [{k: lncols[i][:, j:j + 1] for j, k in enumerate(LNK)}
          for i in range(2)]
    sln = [A(f"sln3_{i}", [128, C], BF16).ap() for i in range(2)]
    samax = A("samax", [128, 1], F32).ap()
    stmp1 = A("stmp1_3", [128, 1], F32).ap()
    sflip = A("sflip", [1, 128], F32).ap()
    sam1 = A("sam1", [1, 1], F32).ap()

    P = nc.alloc_psum_tensor
    pb = [P(f"pb3_{i}", [128, 512], F32).ap() for i in range(4)]
    pb_bc = P("pb3_bc", [128, 512], F32).ap()

    s_ld = nc.alloc_semaphore("s_ld")
    s_ldg = nc.alloc_semaphore("s_ldg")
    s_mm = nc.alloc_semaphore("s_mm")
    s_sc = nc.alloc_semaphore("s_sc")
    s_ve = nc.alloc_semaphore("s_ve")
    s_st = nc.alloc_semaphore("s_st")
    s_sg = nc.alloc_semaphore("s_sg")
    sp3 = [nc.alloc_semaphore("sp3_0"), nc.alloc_semaphore("sp3_1")]
    SEMS = {"ld": s_ld, "ldg": s_ldg, "mm": s_mm, "sc": s_sc, "ve": s_ve,
            "st": s_st, "sg": s_sg, "sp0": sp3[0], "sp1": sp3[1]}
    PRODUCER = {"ld": "sync", "st": "sync", "ldg": "gpsimd", "sg": "gpsimd",
                "mm": "tensor", "sc": "scalar", "ve": "vector",
                "sp0": "sync", "sp1": "sync"}

    def emit(eng, e):
        n = dict(ld=0, ldg=0, mm=0, sc=0, ve=0, st=0, sg=0, sp0=0, sp1=0)
        breg = None
        if eng == "gpsimd":
            breg = e.alloc_register("bcreg3")
            e.reg_mov(breg, NS - 1)

        def STP(p, out, in_):
            key = "sp0" if p == 0 else "sp1"
            n[key] += 16
            if eng == "sync":
                e.dma_start(out=out, in_=in_).then_inc(sp3[p], 16)

        def LD(out, in_, **k):
            n["ld"] += 16
            if eng == "sync":
                e.dma_start(out=out, in_=in_, **k).then_inc(s_ld, 16)

        def LDG(out, in_, **k):
            n["ldg"] += 16
            if eng == "gpsimd":
                e.dma_start(out=out, in_=in_, **k).then_inc(s_ldg, 16)

        def ST(out, in_):
            n["st"] += 16
            if eng == "sync":
                e.dma_start(out=out, in_=in_).then_inc(s_st, 16)

        def W(which, tgt, val=None):
            if eng == tgt:
                e.wait_ge(SEMS[which], n[which] if val is None else val)

        def MM(fn):
            n["mm"] += 1
            if eng == "tensor":
                fn().then_inc(s_mm, 1)

        def MME(fn):
            if eng == "tensor":
                fn()

        def SC(fn):
            n["sc"] += 1
            if eng == "scalar":
                fn().then_inc(s_sc, 1)

        def VE(fn):
            n["ve"] += 1
            if eng == "vector":
                fn().then_inc(s_ve, 1)

        def SG(fn):
            n["sg"] += 16
            if eng == "gpsimd":
                fn().then_inc(s_sg, 16)

        def VEc(fn):
            W("ve", "vector")
            VE(fn)

        def SCc(fn):
            W("sc", "scalar")
            SC(fn)

        def BAR(*engines):
            for tgt in engines:
                for k, sem in SEMS.items():
                    if PRODUCER[k] != tgt and n[k] > 0:
                        if eng == tgt:
                            e.wait_ge(sem, n[k])

        def LN_CHAIN(xin, Lc, dst_bf, extra_amax=False):
            W("ld", "scalar")
            W("ve", "scalar")
            SC(lambda: e.activation(sq_scr, xin, AF.Square,
                                    accum_out=Lc["ssumsq"]))
            W("ld", "vector")
            VE(lambda: e.reduce_sum(Lc["ssum"], xin, AX.X))
            W("sc", "vector")
            VEc(lambda: e.tensor_scalar(Lc["smean"], Lc["ssum"], 1.0 / C, None,
                                        ALU.mult))
            VEc(lambda: e.tensor_scalar(Lc["svar"], Lc["ssumsq"], 1.0 / C, None,
                                        ALU.mult))
            VEc(lambda: e.tensor_mul(Lc["sm2"], Lc["smean"], Lc["smean"]))
            VEc(lambda: e.tensor_sub(Lc["svar"], Lc["svar"], Lc["sm2"]))
            VEc(lambda: e.tensor_scalar(Lc["svar"], Lc["svar"], 1e-5, None,
                                        ALU.add))
            W("ve", "scalar")
            SC(lambda: e.activation(Lc["ssqrt"], Lc["svar"], AF.Sqrt))
            W("sc", "vector")
            VEc(lambda: e.reciprocal(Lc["srstd"], Lc["ssqrt"]))
            W("sp0", "vector")
            W("sp1", "vector")
            VEc(lambda: e.tensor_scalar(dst_bf, xin, Lc["smean"], Lc["srstd"],
                                        ALU.subtract, ALU.mult))
            if extra_amax:
                VEc(lambda: e.tensor_reduce(stmp1, dst_bf, AX.X, ALU.max,
                                            apply_absolute_value=True))
                VEc(lambda: e.tensor_max(samax, samax, stmp1))

        sq_scr = qk[0][:, 0:C]

        LD(sones128, ones128_in)
        for w in "qkvo":
            for i in range(8):
                LDG(wsb[w][i], w_in[w][128 * i:128 * (i + 1), :])
        for w in "qkvo":
            for cc in range(2):
                W("sc", "sync")
                LD(brow, b_in[w][:, 512 * cc:512 * (cc + 1)])
                W("ld", "tensor")
                W("sc", "tensor")
                MM(lambda: e.matmul(pb_bc, sones128, brow, start=True, stop=True))
                W("mm", "scalar")
                SC(lambda w=w, cc=cc: e.copy(bbc[w][:, 512 * cc:512 * (cc + 1)],
                                             pb_bc))
        W("ld", "vector")
        VE(lambda: e.memset(samax, 0.0))

        for b in range(BPC):
            BAR("sync", "scalar", "vector", "gpsimd")
            ve_hist = []
            sg_hist = []
            for tt in range(NT):
                p = tt % 2
                if tt >= 2:
                    W("ve", "sync", val=ve_hist[-2])
                    W("sg", "sync", val=sg_hist[-2])
                    W("sp0" if p == 0 else "sp1", "sync")
                LD(sx[p], r2_in[b, 128 * tt:128 * (tt + 1), :])
                LD(gi0[p], gidx_in[b, 0, 128 * tt:128 * (tt + 1)]
                   .rearrange("(p one) -> p one", one=1))
                LD(gi1[p], gidx_in[b, 1, 128 * tt:128 * (tt + 1)]
                   .rearrange("(p one) -> p one", one=1))
                LD(gt0[p], gate_in[b, 0, 128 * tt:128 * (tt + 1)]
                   .rearrange("(p one) -> p one", one=1))
                LD(gt1[p], gate_in[b, 1, 128 * tt:128 * (tt + 1)]
                   .rearrange("(p one) -> p one", one=1))
                if tt >= 2:
                    W("sg", "vector", val=sg_hist[-2])
                VE(lambda p=p: e.memset(y0[p], 0.0))
                VE(lambda p=p: e.memset(y1[p], 0.0))
                W("ve", "gpsimd")
                W("ld", "gpsimd")
                SG(lambda p=p: e.indirect_dma_start(
                    y0[p], None, yall_in[:, :], IndirectOffsetOnAxis(gi0[p], 0),
                    bounds_check=breg, oob_is_err=False))
                SG(lambda p=p: e.indirect_dma_start(
                    y1[p], None, yall_in[:, :], IndirectOffsetOnAxis(gi1[p], 0),
                    bounds_check=breg, oob_is_err=False))
                sg_hist.append(n["sg"])
                W("ld", "vector")
                W("sg", "vector")
                VEc(lambda p=p: e.scalar_tensor_tensor(sx[p], y0[p], gt0[p],
                                                       sx[p], ALU.mult, ALU.add))
                VEc(lambda p=p: e.scalar_tensor_tensor(sx[p], y1[p], gt1[p],
                                                       sx[p], ALU.mult, ALU.add))
                ve_hist.append(n["ve"])
                W("ve", "sync")
                STP(p, r3scr[b, 128 * tt:128 * (tt + 1), :], sx[p])
            if eng == "sync":
                e.wait_ge(sp3[0], n["sp0"])
                e.wait_ge(sp3[1], n["sp1"])
            ve_hist = []
            for tt in range(NT):
                p = tt % 2
                if tt >= 2:
                    W("ve", "sync", val=ve_hist[-2])
                    W("sp0" if p == 0 else "sp1", "sync")
                LD(sx[p], r3scr[b, 128 * tt:128 * (tt + 1), :])
                LN_CHAIN(sx[p], L2[p], sln[p])
                ve_hist.append(n["ve"])
                W("ve", "sync")
                STP(p, lnscr[128 * tt:128 * (tt + 1), :], sln[p])

            W("st", "sync")
            if eng == "sync":
                e.wait_ge(sp3[0], n["sp0"])
                e.wait_ge(sp3[1], n["sp1"])
            W("mm", "sync")
            for ci in range(8):
                LD(xlnT[ci], lnscr[:, 128 * ci:128 * (ci + 1)], transpose=True)
            W("ld", "sync")

            BAR("tensor", "vector", "scalar")
            ve_hist = []
            for tt in range(NT):
                p = tt % 2
                dsts = [(qk[p][:, 0:C], "q"), (qk[p][:, C:2 * C], "k"),
                        (vf[p], "v")]
                for wi, (dstf, w) in enumerate(dsts):
                    for cc in range(2):
                        g = tt * 6 + wi * 2 + cc
                        bank = pb[g % 3]
                        if g >= 3:
                            W("ve", "tensor", val=n["ve"] - 2)
                        for ci in range(7):
                            MME(lambda bank=bank, w=w, ci=ci, tt=tt, cc=cc:
                                e.matmul(
                                    bank, xlnT[ci][:, 128 * tt:128 * (tt + 1)],
                                    wsb[w][ci][:, 512 * cc:512 * (cc + 1)],
                                    start=(ci == 0), stop=False))
                        MM(lambda bank=bank, w=w, tt=tt, cc=cc: e.matmul(
                            bank, xlnT[7][:, 128 * tt:128 * (tt + 1)],
                            wsb[w][7][:, 512 * cc:512 * (cc + 1)],
                            start=False, stop=True))
                        W("mm", "vector")
                        VE(lambda dstf=dstf, w=w, cc=cc, bank=bank: e.tensor_add(
                            dstf[:, 512 * cc:512 * (cc + 1)], bank,
                            bbc[w][:, 512 * cc:512 * (cc + 1)]))
                W("ve", "scalar")
                SCc(lambda p=p: e.activation(qk[p], qk[p], AF.Square))
                SCc(lambda p=p: e.activation(qk[p], qk[p], AF.Exp, scale=-0.5))
                W("sc", "vector")
                VEc(lambda p=p: e.tensor_mul(qk[p][:, 0:C], qk[p][:, 0:C],
                                             qk[p][:, C:2 * C]))
                VEc(lambda p=p: e.tensor_reduce(
                    coeff[p].unsqueeze(2),
                    qk[p][:, 0:C].rearrange("p (h w) -> p h w", w=64),
                    AX.X, ALU.add))
                W("st", "vector")
                VEc(lambda p=p: e.tensor_mul(
                    qkvsb[p].rearrange("p (h w) -> p h w", w=64),
                    vf[p].rearrange("p (h w) -> p h w", w=64),
                    coeff[p].unsqueeze(2).broadcast_to([128, 16, 64])))
                ve_hist.append(n["ve"])
                W("ve", "sync")
                ST(qkvscr[128 * tt:128 * (tt + 1), :], qkvsb[p])

            W("st", "sync")
            W("mm", "sync")
            for ci in range(8):
                LD(qkvT[ci], qkvscr[:, 128 * ci:128 * (ci + 1)], transpose=True)
            W("ld", "sync")

            BAR("tensor", "sync", "vector", "scalar")
            ve_hist = []
            for tt in range(NT):
                p = tt % 2
                if tt >= 2:
                    W("ve", "sync", val=ve_hist[-2])
                    W("sp0" if p == 0 else "sp1", "sync")
                LD(sx[p], r3scr[b, 128 * tt:128 * (tt + 1), :])
                for cc in range(2):
                    g = tt * 2 + cc
                    bank = pb[g % 3]
                    if g >= 3:
                        W("ve", "tensor", val=n["ve"] - 3)
                    for ci in range(7):
                        MME(lambda bank=bank, ci=ci, tt=tt, cc=cc: e.matmul(
                            bank, qkvT[ci][:, 128 * tt:128 * (tt + 1)],
                            wsb["o"][ci][:, 512 * cc:512 * (cc + 1)],
                            start=(ci == 0), stop=False))
                    MM(lambda bank=bank, tt=tt, cc=cc: e.matmul(
                        bank, qkvT[7][:, 128 * tt:128 * (tt + 1)],
                        wsb["o"][7][:, 512 * cc:512 * (cc + 1)],
                        start=False, stop=True))
                    W("mm", "vector")
                    W("ld", "vector")
                    VE(lambda cc=cc, bank=bank, p=p: e.tensor_add(
                        sres[p][:, 512 * cc:512 * (cc + 1)], bank,
                        bbc["o"][:, 512 * cc:512 * (cc + 1)]))
                    VEc(lambda cc=cc, p=p: e.tensor_add(
                        sres[p][:, 512 * cc:512 * (cc + 1)],
                        sres[p][:, 512 * cc:512 * (cc + 1)],
                        sx[p][:, 512 * cc:512 * (cc + 1)]))
                W("ve", "sync")
                STP(p, r4_out[b, 128 * tt:128 * (tt + 1), :], sres[p])
                W("ve", "scalar")
                W("sp0" if p == 0 else "sp1", "scalar")
                SC(lambda p=p: e.activation(sx[p], sres[p], AF.Square,
                                            accum_out=L2[p]["ssumsq"]))
                VEc(lambda p=p: e.reduce_sum(L2[p]["ssum"], sres[p], AX.X))
                W("sc", "vector")
                VEc(lambda p=p: e.tensor_scalar(L2[p]["smean"], L2[p]["ssum"],
                                                1.0 / C, None, ALU.mult))
                VEc(lambda p=p: e.tensor_scalar(L2[p]["svar"], L2[p]["ssumsq"],
                                                1.0 / C, None, ALU.mult))
                VEc(lambda p=p: e.tensor_mul(L2[p]["sm2"], L2[p]["smean"],
                                             L2[p]["smean"]))
                VEc(lambda p=p: e.tensor_sub(L2[p]["svar"], L2[p]["svar"],
                                             L2[p]["sm2"]))
                VEc(lambda p=p: e.tensor_scalar(L2[p]["svar"], L2[p]["svar"],
                                                1e-5, None, ALU.add))
                W("ve", "scalar")
                SC(lambda p=p: e.activation(L2[p]["ssqrt"], L2[p]["svar"],
                                            AF.Sqrt))
                W("sc", "vector")
                VEc(lambda p=p: e.reciprocal(L2[p]["srstd"], L2[p]["ssqrt"]))
                W("sp0" if p == 0 else "sp1", "vector")
                VEc(lambda p=p: e.tensor_scalar(sln[p], sres[p], L2[p]["smean"],
                                                L2[p]["srstd"],
                                                ALU.subtract, ALU.mult))
                VEc(lambda p=p: e.tensor_reduce(stmp1, sln[p], AX.X, ALU.max,
                                                apply_absolute_value=True))
                VEc(lambda: e.tensor_max(samax, samax, stmp1))
                ve_hist.append(n["ve"])
                W("ve", "sync")
                STP(p, xln4_out[b, 128 * tt:128 * (tt + 1), :], sln[p])

        BAR("sync", "vector")
        ST(flipscr.rearrange("a b -> b a"), samax)
        W("st", "sync")
        LD(sflip, flipscr)
        W("ld", "vector")
        VEc(lambda: e.tensor_reduce(sam1, sflip, AX.X, ALU.max))
        W("ve", "sync")
        ST(amax_out, sam1)
        BAR("sync", "gpsimd", "tensor", "scalar", "vector")
        return n

    with nc.Block() as block:
        @block.sync
        def _(e):
            emit("sync", e)

        @block.gpsimd
        def _(e):
            emit("gpsimd", e)

        @block.tensor
        def _(e):
            emit("tensor", e)

        @block.scalar
        def _(e):
            emit("scalar", e)

        @block.vector
        def _(e):
            emit("vector", e)

    return nc



def _qdq_emit_helpers(nc, e, eng, n, SEMS, PRODUCER):
    h = {}

    def LD(out, in_, **k):
        n["ld"] += 16
        if eng == "sync":
            e.dma_start(out=out, in_=in_, **k).then_inc(SEMS["ld"], 16)

    def LDG(out, in_, **k):
        n["ldg"] += 16
        if eng == "gpsimd":
            e.dma_start(out=out, in_=in_, **k).then_inc(SEMS["ldg"], 16)

    def ST(out, in_):
        n["st"] += 16
        if eng == "sync":
            e.dma_start(out=out, in_=in_).then_inc(SEMS["st"], 16)

    def STG(out, in_):
        n["sg"] += 16
        if eng == "gpsimd":
            e.dma_start(out=out, in_=in_).then_inc(SEMS["sg"], 16)

    def W(which, tgt, val=None):
        if eng == tgt:
            e.wait_ge(SEMS[which], n[which] if val is None else val)

    def MM(fn):
        n["mm"] += 1
        if eng == "tensor":
            fn().then_inc(SEMS["mm"], 1)

    def MME(fn):
        if eng == "tensor":
            fn()

    def SC(fn):
        n["sc"] += 1
        if eng == "scalar":
            fn().then_inc(SEMS["sc"], 1)

    def VE(fn):
        n["ve"] += 1
        if eng == "vector":
            fn().then_inc(SEMS["ve"], 1)

    def VEc(fn):
        W("ve", "vector")
        VE(fn)

    def SCc(fn):
        W("sc", "scalar")
        SC(fn)

    def BAR(*engines):
        for tgt in engines:
            for k, sem in SEMS.items():
                if PRODUCER[k] != tgt and n[k] > 0:
                    if eng == tgt:
                        e.wait_ge(sem, n[k])

    h.update(LD=LD, LDG=LDG, ST=ST, STG=STG, W=W, MM=MM, MME=MME, SC=SC,
             VE=VE, VEc=VEc, SCc=SCc, BAR=BAR)
    return h


def _mk_sems(nc):
    SEMS = {k: nc.alloc_semaphore("s_" + k)
            for k in ["ld", "ldg", "mm", "sc", "ve", "st", "sg"]}
    PRODUCER = {"ld": "sync", "st": "sync", "ldg": "gpsimd", "sg": "gpsimd",
                "mm": "tensor", "sc": "scalar", "ve": "vector"}
    return SEMS, PRODUCER


def _run_block(nc, emit):
    with nc.Block() as block:
        @block.sync
        def _(e):
            emit("sync", e)

        @block.gpsimd
        def _(e):
            emit("gpsimd", e)

        @block.tensor
        def _(e):
            emit("tensor", e)

        @block.scalar
        def _(e):
            emit("scalar", e)

        @block.vector
        def _(e):
            emit("vector", e)
    return nc


def build_launch4():
    nc = bass.Bass("TRN2", target_bir_lowering=False, debug=False,
                   num_devices=NCORES, name="launch4")
    NT = TPC // 128
    IT = L1I // 128

    xln4_in = nc.dram_tensor("xln4", [TPC, C], BF16, kind="ExternalInput").ap()
    am_in = nc.dram_tensor("amax1all", [1, 8], F32, kind="ExternalInput").ap()
    wup_in = nc.dram_tensor("wup", [L1I, C], F32, kind="ExternalInput").ap()
    wg_in = nc.dram_tensor("wgate", [L1I, C], F32, kind="ExternalInput").ap()
    up_out = nc.dram_tensor("up", [L1I, TPC], BF16, kind="ExternalOutput").ap()
    gate_out = nc.dram_tensor("gate", [L1I, TPC], BF16, kind="ExternalOutput").ap()
    am2_out = nc.dram_tensor("amax2", [1, 2], F32, kind="ExternalOutput").ap()

    xqscr = nc.dram_tensor("xqscr", [TPC, C], BF16, kind="Internal").ap()
    wdscr = nc.dram_tensor("wdscr", [2, L1I, C], BF16, kind="Internal").ap()
    flipscr = nc.dram_tensor("flip4", [2, 128], F32, kind="Internal").ap()
    ones128_in = nc.inline_tensor(np.ones((1, 128), np.float32), "ones4").ap()

    A = nc.alloc_sbuf_tensor
    sam = A("sam4", [1, 8], F32).ap()
    sm1 = A("sm1", [1, 1], F32).ap()
    ssc = A("ssc4", [1, 2], F32).ap()
    sscb = A("sscb4", [128, 2], F32).ap()
    sones = A("sones4", [1, 128], F32).ap()
    sx = A("sx4", [128, C], BF16).ap()
    sxq = A("sxq", [128, C], F32).ap()
    sxqb = A("sxqb", [128, C], BF16).ap()
    sw = A("sw4", [128, C], F32).ap()
    swa = A("swa", [128, 32], F32).ap()
    sws = A("sws", [128, 32], F32).ap()
    swr = A("swr", [128, 32], F32).ap()
    swq = A("swq", [128, C], F32).ap()
    swqb = A("swqb", [128, C], BF16).ap()
    xqT = [A(f"xqT{i}", [128, TPC], BF16).ap() for i in range(8)]
    wT = [A(f"wT4_{i}", [128, L1I], BF16).ap() for i in range(8)]
    outb = [A(f"outb4_{i}", [128, 512], BF16).ap() for i in range(2)]
    samax = [A(f"samax4_{i}", [128, 1], F32).ap() for i in range(2)]
    stmp1 = A("stmp14", [128, 1], F32).ap()
    sflip = A("sflip4", [1, 128], F32).ap()
    sam2 = A("sam2o", [1, 2], F32).ap()

    P = nc.alloc_psum_tensor
    pb = [P(f"pb4_{i}", [128, 512], F32).ap() for i in range(2)]
    pb_bc = P("pb4_bc", [128, 2], F32).ap()

    SEMS, PRODUCER = _mk_sems(nc)
    sp = [nc.alloc_semaphore("sp4_0"), nc.alloc_semaphore("sp4_1")]

    def emit(eng, e):
        n = dict(ld=0, ldg=0, mm=0, sc=0, ve=0, st=0, sg=0)
        spn = [0, 0]
        H = _qdq_emit_helpers(nc, e, eng, n, SEMS, PRODUCER)
        LD, LDG, ST, STG, W = H["LD"], H["LDG"], H["ST"], H["STG"], H["W"]
        MM, MME, SC, VE, VEc, SCc, BAR = (H["MM"], H["MME"], H["SC"], H["VE"],
                                          H["VEc"], H["SCc"], H["BAR"])

        def STP(p, out, in_):
            spn[p] += 16
            if eng == "sync":
                e.dma_start(out=out, in_=in_).then_inc(sp[p], 16)

        LD(sam, am_in)
        LD(sones, ones128_in)
        W("ld", "vector")
        VE(lambda: e.tensor_reduce(sm1, sam, AX.X, ALU.max))
        VEc(lambda: e.tensor_scalar(ssc[:, 1:2], sm1, 1.0 / 32767.0, None,
                                    ALU.mult))
        VEc(lambda: e.tensor_scalar(ssc[:, 1:2], ssc[:, 1:2], 1e-12, None,
                                    ALU.max))
        VEc(lambda: e.reciprocal(ssc[:, 0:1], ssc[:, 1:2]))
        W("ve", "tensor")
        MM(lambda: e.matmul(pb_bc, sones, ssc, start=True, stop=True))
        W("mm", "scalar")
        SC(lambda: e.copy(sscb, pb_bc))

        for tt in range(NT):
            W("ve", "sync")
            LD(sx, xln4_in[128 * tt:128 * (tt + 1), :])
            W("ld", "vector")
            W("sc", "vector")
            VEc(lambda: e.tensor_scalar(sxq, sx, sscb[:, 0:1], ROUND_C,
                                        ALU.mult, ALU.add))
            W("st", "vector")
            VEc(lambda: e.tensor_scalar(sxqb, sxq, ROUND_C, sscb[:, 1:2],
                                        ALU.subtract, ALU.mult))
            W("ve", "sync")
            ST(xqscr[128 * tt:128 * (tt + 1), :], sxqb)

        for wi, w_in_ in enumerate((wup_in, wg_in)):
            for it in range(IT):
                W("ve", "sync")
                LD(sw, w_in_[128 * it:128 * (it + 1), :])
                W("ld", "vector")
                VEc(lambda: e.tensor_reduce(
                    swa.rearrange("p (a one) -> p a one", one=1),
                    sw.rearrange("p (a w) -> p a w", w=32), AX.X, ALU.max,
                    apply_absolute_value=True))
                VEc(lambda: e.tensor_scalar(sws, swa, 1.0 / 7.0, None, ALU.mult))
                VEc(lambda: e.tensor_scalar(sws, sws, 1e-12, None, ALU.max))
                VEc(lambda: e.reciprocal(swr, sws))
                VEc(lambda: e.tensor_mul(
                    swq.rearrange("p (a w) -> p a w", w=32),
                    sw.rearrange("p (a w) -> p a w", w=32),
                    swr.rearrange("p (a one) -> p a one", one=1)
                    .broadcast_to([128, 32, 32])))
                VEc(lambda: e.tensor_scalar(swq, swq, ROUND_C, ROUND_C,
                                            ALU.add, ALU.subtract))
                W("st", "vector")
                VEc(lambda: e.tensor_mul(
                    swqb.rearrange("p (a w) -> p a w", w=32),
                    swq.rearrange("p (a w) -> p a w", w=32),
                    sws.rearrange("p (a one) -> p a one", one=1)
                    .broadcast_to([128, 32, 32])))
                W("ve", "sync")
                ST(wdscr[wi, 128 * it:128 * (it + 1), :], swqb)

        W("st", "sync")
        for ci in range(8):
            LD(xqT[ci], xqscr[:, 128 * ci:128 * (ci + 1)], transpose=True)
        W("ld", "sync")

        for wi in range(2):
            dst = (up_out, gate_out)[wi]
            W("mm", "sync")
            for ci in range(8):
                LD(wT[ci], wdscr[wi, :, 128 * ci:128 * (ci + 1)], transpose=True)
            W("ld", "sync")
            BAR("tensor", "vector")
            if eng == "vector":
                e.memset(samax[wi], 0.0)
            evac_hist = []
            for it in range(IT):
                for tc in range(TPC // 512):
                    g = it * (TPC // 512) + tc
                    bank = pb[g % 2]
                    if g >= 2:
                        W("sc", "tensor", val=evac_hist[-2])
                    for ci in range(7):
                        MME(lambda bank=bank, ci=ci, it=it, tc=tc: e.matmul(
                            bank, wT[ci][:, 128 * it:128 * (it + 1)],
                            xqT[ci][:, 512 * tc:512 * (tc + 1)],
                            start=(ci == 0), stop=False))
                    MM(lambda bank=bank, it=it, tc=tc: e.matmul(
                        bank, wT[7][:, 128 * it:128 * (it + 1)],
                        xqT[7][:, 512 * tc:512 * (tc + 1)],
                        start=False, stop=True))
                    W("mm", "scalar")
                    if g >= 2:
                        if eng == "scalar":
                            e.wait_ge(sp[g % 2], spn[g % 2])
                    SC(lambda g=g, bank=bank: e.copy(outb[g % 2], bank))
                    evac_hist.append(n["sc"])
                    W("sc", "vector")
                    VEc(lambda g=g: e.tensor_reduce(stmp1, outb[g % 2], AX.X,
                                                    ALU.max,
                                                    apply_absolute_value=True))
                    VEc(lambda wi=wi: e.tensor_max(samax[wi], samax[wi], stmp1))
                    W("sc", "sync")
                    W("ve", "sync")
                    STP(g % 2, dst[128 * it:128 * (it + 1),
                                   512 * tc:512 * (tc + 1)], outb[g % 2])
            if eng == "sync":
                e.wait_ge(sp[0], spn[0])
                e.wait_ge(sp[1], spn[1])
            W("ve", "sync")
            ST(flipscr[wi:wi + 1, :].rearrange("a b -> b a"), samax[wi])
        W("st", "sync")
        LD(sflip, flipscr[0:1, :])
        W("ld", "vector")
        VEc(lambda: e.tensor_reduce(sam2[:, 0:1], sflip, AX.X, ALU.max))
        W("ve", "sync")
        LD(sflip, flipscr[1:2, :])
        W("ld", "vector")
        VEc(lambda: e.tensor_reduce(sam2[:, 1:2], sflip, AX.X, ALU.max))
        W("ve", "sync")
        ST(am2_out, sam2)
        BAR("sync", "gpsimd", "tensor", "scalar", "vector")
        return n

    return _run_block(nc, emit)


def build_launch5():
    nc = bass.Bass("TRN2", target_bir_lowering=False, debug=False,
                   num_devices=NCORES, name="launch5")
    WID = TPC

    gate_in = nc.dram_tensor("gate", [L1I, TPC], BF16, kind="ExternalInput").ap()
    am_in = nc.dram_tensor("amax2all", [8, 2], F32, kind="ExternalInput").ap()
    g2_out = nc.dram_tensor("g2pre", [L1I, TPC], BF16, kind="ExternalOutput").ap()
    am3_out = nc.dram_tensor("amax3", [1, 1], F32, kind="ExternalOutput").ap()
    flipscr = nc.dram_tensor("flip5", [1, 128], F32, kind="Internal").ap()
    mflip = nc.dram_tensor("mflip5", [1, 8], F32, kind="Internal").ap()
    ones128_in = nc.inline_tensor(np.ones((1, 128), np.float32), "ones5").ap()

    A = nc.alloc_sbuf_tensor
    sam = A("sam5", [8, 2], F32).ap()
    sm = A("sm5", [8, 1], F32).ap()
    smf = A("smf5", [1, 8], F32).ap()
    ssc = A("ssc5", [1, 2], F32).ap()
    sscb = A("sscb5", [128, 2], F32).ap()
    sones = A("sones5", [1, 128], F32).ap()
    sg_ = [A(f"sg5_{i}", [128, WID], BF16).ap() for i in range(2)]
    sq = [A(f"sq5_{i}", [128, WID], F32).ap() for i in range(2)]
    sgq = [A(f"sgq5_{i}", [128, WID], F32).ap() for i in range(2)]
    ssig = [A(f"ssig5_{i}", [128, WID], F32).ap() for i in range(2)]
    so = [A(f"so5_{i}", [128, WID], BF16).ap() for i in range(2)]
    samax = A("samax5", [128, 1], F32).ap()
    stmp1 = A("stmp15", [128, 1], F32).ap()
    sflip = A("sflip5", [1, 128], F32).ap()
    sam3 = A("sam3o", [1, 1], F32).ap()

    P = nc.alloc_psum_tensor
    pb_bc = P("pb5_bc", [128, 2], F32).ap()

    SEMS, PRODUCER = _mk_sems(nc)
    lp = [nc.alloc_semaphore("lp0"), nc.alloc_semaphore("lp1")]
    sp = [nc.alloc_semaphore("sp0"), nc.alloc_semaphore("sp1")]

    def emit(eng, e):
        n = dict(ld=0, ldg=0, mm=0, sc=0, ve=0, st=0, sg=0)
        lpn = [0, 0]
        spn = [0, 0]
        H = _qdq_emit_helpers(nc, e, eng, n, SEMS, PRODUCER)
        LD, LDG, ST, STG, W = H["LD"], H["LDG"], H["ST"], H["STG"], H["W"]
        MM, MME, SC, VE, VEc, SCc, BAR = (H["MM"], H["MME"], H["SC"], H["VE"],
                                          H["VEc"], H["SCc"], H["BAR"])

        def LDP(p, out, in_):
            lpn[p] += 16
            if eng == "sync":
                e.dma_start(out=out, in_=in_).then_inc(lp[p], 16)

        def STP(p, out, in_):
            spn[p] += 16
            if eng == "sync":
                e.dma_start(out=out, in_=in_).then_inc(sp[p], 16)

        def WP(sems, cnt, p, tgt):
            if eng == tgt:
                e.wait_ge(sems[p], cnt[p])

        LD(sam, am_in)
        LD(sones, ones128_in)
        W("ld", "vector")
        VE(lambda: e.tensor_copy(sm, sam[:, 1:2]))
        W("ve", "sync")
        ST(mflip.rearrange("a b -> b a"), sm)
        W("st", "sync")
        LD(smf, mflip)
        W("ld", "vector")
        VEc(lambda: e.tensor_reduce(ssc[:, 1:2], smf, AX.X, ALU.max))
        VEc(lambda: e.tensor_scalar(ssc[:, 1:2], ssc[:, 1:2], 1.0 / 32767.0,
                                    None, ALU.mult))
        VEc(lambda: e.tensor_scalar(ssc[:, 1:2], ssc[:, 1:2], 1e-12, None,
                                    ALU.max))
        VEc(lambda: e.reciprocal(ssc[:, 0:1], ssc[:, 1:2]))
        W("ve", "tensor")
        MM(lambda: e.matmul(pb_bc, sones, ssc, start=True, stop=True))
        W("mm", "scalar")
        SC(lambda: e.copy(sscb, pb_bc))
        if eng == "vector":
            e.memset(samax, 0.0)

        ve_hist = []
        for it in range(L1I // 128):
            p = it % 2
            if it >= 2:
                W("ve", "sync", val=ve_hist[-2][0])
                WP(sp, spn, p, "sync")
            LDP(p, sg_[p], gate_in[128 * it:128 * (it + 1), :])
            WP(lp, lpn, p, "vector")
            W("sc", "vector")
            VEc(lambda p=p: e.tensor_scalar(sq[p], sg_[p], sscb[:, 0:1],
                                            ROUND_C, ALU.mult, ALU.add))
            VEc(lambda p=p: e.tensor_scalar(sgq[p], sq[p], ROUND_C, sscb[:, 1:2],
                                            ALU.subtract, ALU.mult))
            W("ve", "scalar")
            SCc(lambda p=p: e.activation(ssig[p], sgq[p], AF.Sigmoid))
            W("sc", "vector")
            VEc(lambda p=p: e.tensor_scalar(ssig[p], ssig[p], 65536.0, ROUND_C,
                                            ALU.mult, ALU.add))
            VEc(lambda p=p: e.tensor_scalar(ssig[p], ssig[p], ROUND_C, 65535.0,
                                            ALU.subtract, ALU.min))
            VEc(lambda p=p: e.scalar_tensor_tensor(
                so[p], ssig[p], 1.0 / 65536.0, sgq[p], ALU.mult, ALU.mult))
            VEc(lambda p=p: e.tensor_reduce(stmp1, so[p], AX.X, ALU.max,
                                            apply_absolute_value=True))
            VEc(lambda: e.tensor_max(samax, samax, stmp1))
            ve_hist.append((n["ve"], None))
            W("ve", "sync")
            STP(p, g2_out[128 * it:128 * (it + 1), :], so[p])
        W("ve", "sync")
        ST(flipscr.rearrange("a b -> b a"), samax)
        W("st", "sync")
        LD(sflip, flipscr)
        W("ld", "vector")
        VEc(lambda: e.tensor_reduce(sam3, sflip, AX.X, ALU.max))
        W("ve", "sync")
        ST(am3_out, sam3)
        if eng == "sync":
            e.wait_ge(sp[0], spn[0])
            e.wait_ge(sp[1], spn[1])
        BAR("sync", "gpsimd", "tensor", "scalar", "vector")
        return n

    return _run_block(nc, emit)


def build_launch6():
    nc = bass.Bass("TRN2", target_bir_lowering=False, debug=False,
                   num_devices=NCORES, name="launch6")
    WID = TPC
    up_in = nc.dram_tensor("up", [L1I, TPC], BF16, kind="ExternalInput").ap()
    g2_in = nc.dram_tensor("g2pre", [L1I, TPC], BF16, kind="ExternalInput").ap()
    am2_in = nc.dram_tensor("amax2all", [8, 2], F32, kind="ExternalInput").ap()
    am3_in = nc.dram_tensor("amax3all", [1, 8], F32, kind="ExternalInput").ap()
    t_out = nc.dram_tensor("tprod", [L1I, TPC], BF16, kind="ExternalOutput").ap()
    am4_out = nc.dram_tensor("amax4", [1, 1], F32, kind="ExternalOutput").ap()
    flipscr = nc.dram_tensor("flip6", [1, 128], F32, kind="Internal").ap()
    mflip = nc.dram_tensor("mflip6", [1, 8], F32, kind="Internal").ap()
    ones128_in = nc.inline_tensor(np.ones((1, 128), np.float32), "ones6").ap()

    A = nc.alloc_sbuf_tensor
    sam2 = A("sam26", [8, 2], F32).ap()
    sam3 = A("sam36", [1, 8], F32).ap()
    sm = A("sm6", [8, 1], F32).ap()
    smf = A("smf6", [1, 8], F32).ap()
    ssc = A("ssc6", [1, 4], F32).ap()
    sscb = A("sscb6", [128, 4], F32).ap()
    sones = A("sones6", [1, 128], F32).ap()
    sup = [A(f"sup6_{i}", [128, WID], BF16).ap() for i in range(2)]
    sg2 = [A(f"sg26_{i}", [128, WID], BF16).ap() for i in range(2)]
    sq = [A(f"sq6_{i}", [128, WID], F32).ap() for i in range(2)]
    suq = [A(f"suq6_{i}", [128, WID], F32).ap() for i in range(2)]
    sgq = [A(f"sgq6_{i}", [128, WID], F32).ap() for i in range(2)]
    so = [A(f"so6_{i}", [128, WID], BF16).ap() for i in range(2)]
    samax = A("samax6", [128, 1], F32).ap()
    stmp1 = A("stmp16", [128, 1], F32).ap()
    sflip = A("sflip6", [1, 128], F32).ap()
    sam4 = A("sam4o", [1, 1], F32).ap()

    P = nc.alloc_psum_tensor
    pb_bc = P("pb6_bc", [128, 4], F32).ap()

    SEMS, PRODUCER = _mk_sems(nc)
    lp = [nc.alloc_semaphore("lp0"), nc.alloc_semaphore("lp1")]
    sp = [nc.alloc_semaphore("sp0"), nc.alloc_semaphore("sp1")]

    def emit(eng, e):
        n = dict(ld=0, ldg=0, mm=0, sc=0, ve=0, st=0, sg=0)
        lpn = [0, 0]
        spn = [0, 0]
        H = _qdq_emit_helpers(nc, e, eng, n, SEMS, PRODUCER)
        LD, LDG, ST, STG, W = H["LD"], H["LDG"], H["ST"], H["STG"], H["W"]
        MM, MME, SC, VE, VEc, SCc, BAR = (H["MM"], H["MME"], H["SC"], H["VE"],
                                          H["VEc"], H["SCc"], H["BAR"])

        def LDP(p, out, in_):
            lpn[p] += 16
            if eng == "sync":
                e.dma_start(out=out, in_=in_).then_inc(lp[p], 16)

        def STP(p, out, in_):
            spn[p] += 16
            if eng == "sync":
                e.dma_start(out=out, in_=in_).then_inc(sp[p], 16)

        def WP(sems, cnt, p, tgt):
            if eng == tgt:
                e.wait_ge(sems[p], cnt[p])

        LD(sam2, am2_in)
        LD(sam3, am3_in)
        LD(sones, ones128_in)
        W("ld", "vector")
        VE(lambda: e.tensor_copy(sm, sam2[:, 0:1]))
        W("ve", "sync")
        ST(mflip.rearrange("a b -> b a"), sm)
        W("st", "sync")
        LD(smf, mflip)
        W("ld", "vector")
        VEc(lambda: e.tensor_reduce(ssc[:, 1:2], smf, AX.X, ALU.max))
        VEc(lambda: e.tensor_scalar(ssc[:, 1:2], ssc[:, 1:2], 1.0 / 32767.0,
                                    None, ALU.mult))
        VEc(lambda: e.tensor_scalar(ssc[:, 1:2], ssc[:, 1:2], 1e-12, None,
                                    ALU.max))
        VEc(lambda: e.reciprocal(ssc[:, 0:1], ssc[:, 1:2]))
        VEc(lambda: e.tensor_reduce(ssc[:, 3:4], sam3, AX.X, ALU.max))
        VEc(lambda: e.tensor_scalar(ssc[:, 3:4], ssc[:, 3:4], 1.0 / 32767.0,
                                    None, ALU.mult))
        VEc(lambda: e.tensor_scalar(ssc[:, 3:4], ssc[:, 3:4], 1e-12, None,
                                    ALU.max))
        VEc(lambda: e.reciprocal(ssc[:, 2:3], ssc[:, 3:4]))
        W("ve", "tensor")
        MM(lambda: e.matmul(pb_bc, sones, ssc, start=True, stop=True))
        W("mm", "scalar")
        SC(lambda: e.copy(sscb, pb_bc))
        if eng == "vector":
            e.memset(samax, 0.0)

        ve_hist = []
        for it in range(L1I // 128):
            p = it % 2
            if it >= 2:
                W("ve", "sync", val=ve_hist[-2])
                WP(sp, spn, p, "sync")
            LDP(p, sup[p], up_in[128 * it:128 * (it + 1), :])
            LDP(p, sg2[p], g2_in[128 * it:128 * (it + 1), :])
            WP(lp, lpn, p, "vector")
            W("sc", "vector")
            VEc(lambda p=p: e.tensor_scalar(sq[p], sup[p], sscb[:, 0:1],
                                            ROUND_C, ALU.mult, ALU.add))
            VEc(lambda p=p: e.tensor_scalar(suq[p], sq[p], ROUND_C, sscb[:, 1:2],
                                            ALU.subtract, ALU.mult))
            VEc(lambda p=p: e.tensor_scalar(sq[p], sg2[p], sscb[:, 2:3],
                                            ROUND_C, ALU.mult, ALU.add))
            VEc(lambda p=p: e.tensor_scalar(sgq[p], sq[p], ROUND_C, sscb[:, 3:4],
                                            ALU.subtract, ALU.mult))
            VEc(lambda p=p: e.tensor_mul(so[p], sgq[p], suq[p]))
            VEc(lambda p=p: e.tensor_reduce(stmp1, so[p], AX.X, ALU.max,
                                            apply_absolute_value=True))
            VEc(lambda: e.tensor_max(samax, samax, stmp1))
            ve_hist.append(n["ve"])
            W("ve", "sync")
            STP(p, t_out[128 * it:128 * (it + 1), :], so[p])
        W("ve", "sync")
        ST(flipscr.rearrange("a b -> b a"), samax)
        W("st", "sync")
        LD(sflip, flipscr)
        W("ld", "vector")
        VEc(lambda: e.tensor_reduce(sam4, sflip, AX.X, ALU.max))
        W("ve", "sync")
        ST(am4_out, sam4)
        if eng == "sync":
            e.wait_ge(sp[0], spn[0])
            e.wait_ge(sp[1], spn[1])
        BAR("sync", "gpsimd", "tensor", "scalar", "vector")
        return n

    return _run_block(nc, emit)


def build_launch7():
    nc = bass.Bass("TRN2", target_bir_lowering=False, debug=False,
                   num_devices=NCORES, name="launch7")
    IT = L1I // 128
    TB = 1024
    NB = TPC // TB

    t_in = nc.dram_tensor("tprod", [L1I, TPC], BF16, kind="ExternalInput").ap()
    am4_in = nc.dram_tensor("amax4all", [1, 8], F32, kind="ExternalInput").ap()
    wd_in = nc.dram_tensor("wdn", [C, L1I], F32, kind="ExternalInput").ap()
    r4_in = nc.dram_tensor("resid4", [TPC, C], F32, kind="ExternalInput").ap()
    out_out = nc.dram_tensor("out", [TPC, C], F32, kind="ExternalOutput").ap()
    wdscr = nc.dram_tensor("wdscr7", [C, L1I], BF16, kind="Internal").ap()
    ones128_in = nc.inline_tensor(np.ones((1, 128), np.float32), "ones7").ap()

    A = nc.alloc_sbuf_tensor
    sam = A("sam7", [1, 8], F32).ap()
    ssc = A("ssc7", [1, 2], F32).ap()
    sscb = A("sscb7", [128, 2], F32).ap()
    sones = A("sones7", [1, 128], F32).ap()
    sw = [A(f"sw7_{i}", [128, L1I], F32).ap() for i in range(2)]
    swa = A("swa7", [128, 96], F32).ap()
    sws = [A(f"sws7_{i}", [128, 96], F32).ap() for i in range(2)]
    swr = A("swr7", [128, 96], F32).ap()
    swq = A("swq7", [128, L1I], F32).ap()
    swqb = [A(f"swqb7_{i}", [128, L1I], BF16).ap() for i in range(2)]
    wT = [A(f"wT7_{i}", [128, C], BF16).ap() for i in range(IT)]
    stin = [A(f"stin7_{i}", [128, TB], BF16).ap() for i in range(2)]
    sq = [A(f"sq7_{i}", [128, TB], F32).ap() for i in range(2)]
    obuf = [A(f"obuf7_{i}", [128, TB], BF16).ap() for i in range(IT)]
    sres = [A(f"sres7_{i}", [128, C], F32).ap() for i in range(2)]
    souts = [A(f"souts{i}", [128, C], F32).ap() for i in range(2)]

    P = nc.alloc_psum_tensor
    pb = [P(f"pb7_{i}", [128, 512], F32).ap() for i in range(4)]
    pb_bc = P("pb7_bc", [128, 2], F32).ap()

    SEMS, PRODUCER = _mk_sems(nc)
    lp = [nc.alloc_semaphore("lp0"), nc.alloc_semaphore("lp1")]
    sp = [nc.alloc_semaphore("sp0"), nc.alloc_semaphore("sp1")]

    def emit(eng, e):
        n = dict(ld=0, ldg=0, mm=0, sc=0, ve=0, st=0, sg=0)
        lpn = [0, 0]
        spn = [0, 0]
        H = _qdq_emit_helpers(nc, e, eng, n, SEMS, PRODUCER)
        LD, LDG, ST, STG, W = H["LD"], H["LDG"], H["ST"], H["STG"], H["W"]
        MM, MME, SC, VE, VEc, SCc, BAR = (H["MM"], H["MME"], H["SC"], H["VE"],
                                          H["VEc"], H["SCc"], H["BAR"])

        def LDP(p, out, in_):
            lpn[p] += 16
            if eng == "sync":
                e.dma_start(out=out, in_=in_).then_inc(lp[p], 16)

        def STP(p, out, in_):
            spn[p] += 16
            if eng == "sync":
                e.dma_start(out=out, in_=in_).then_inc(sp[p], 16)

        def WP(sems, cnt, p, tgt, val=None):
            if eng == tgt:
                e.wait_ge(sems[p], cnt[p] if val is None else val)

        LD(sam, am4_in)
        LD(sones, ones128_in)
        W("ld", "vector")
        VE(lambda: e.tensor_reduce(ssc[:, 1:2], sam, AX.X, ALU.max))
        VEc(lambda: e.tensor_scalar(ssc[:, 1:2], ssc[:, 1:2], 1.0 / 32767.0,
                                    None, ALU.mult))
        VEc(lambda: e.tensor_scalar(ssc[:, 1:2], ssc[:, 1:2], 1e-12, None,
                                    ALU.max))
        VEc(lambda: e.reciprocal(ssc[:, 0:1], ssc[:, 1:2]))
        W("ve", "tensor")
        MM(lambda: e.matmul(pb_bc, sones, ssc, start=True, stop=True))
        W("mm", "scalar")
        SC(lambda: e.copy(sscb, pb_bc))

        for ct in range(C // 128):
            p = ct % 2
            if ct >= 2:
                WP(sp, spn, p, "sync")
            LDP(p, sw[p], wd_in[128 * ct:128 * (ct + 1), :])
            WP(lp, lpn, p, "vector")
            VEc(lambda p=p: e.tensor_reduce(
                swa.rearrange("p (a one) -> p a one", one=1),
                sw[p].rearrange("p (a w) -> p a w", w=32), AX.X, ALU.max,
                apply_absolute_value=True))
            VEc(lambda p=p: e.tensor_scalar(sws[p], swa, 1.0 / 7.0, None, ALU.mult))
            VEc(lambda p=p: e.tensor_scalar(sws[p], sws[p], 1e-12, None, ALU.max))
            VEc(lambda p=p: e.reciprocal(swr, sws[p]))
            VEc(lambda p=p: e.tensor_mul(
                swq.rearrange("p (a w) -> p a w", w=32),
                sw[p].rearrange("p (a w) -> p a w", w=32),
                swr.rearrange("p (a one) -> p a one", one=1)
                .broadcast_to([128, 96, 32])))
            VEc(lambda: e.tensor_scalar(swq, swq, ROUND_C, ROUND_C,
                                        ALU.add, ALU.subtract))
            VEc(lambda p=p: e.tensor_mul(
                swqb[p].rearrange("p (a w) -> p a w", w=32),
                swq.rearrange("p (a w) -> p a w", w=32),
                sws[p].rearrange("p (a one) -> p a one", one=1)
                .broadcast_to([128, 96, 32])))
            W("ve", "sync")
            STP(p, wdscr[128 * ct:128 * (ct + 1), :], swqb[p])
        if eng == "sync":
            e.wait_ge(sp[0], spn[0])
            e.wait_ge(sp[1], spn[1])
        for it in range(IT):
            LD(wT[it], wdscr[:, 128 * it:128 * (it + 1)], transpose=True)
        W("ld", "sync")

        BAR("tensor", "vector", "scalar", "sync")
        qdq_ve = []
        for tb in range(NB):
            for it in range(IT):
                p = it % 2
                g = tb * IT + it
                if g >= 2:
                    W("ve", "sync", val=qdq_ve[g - 2])
                LDP(p, stin[p], t_in[128 * it:128 * (it + 1),
                                     TB * tb:TB * (tb + 1)])
                WP(lp, lpn, p, "vector")
                if tb > 0 and it == 0:
                    W("mm", "vector")
                VEc(lambda p=p: e.tensor_scalar(sq[p], stin[p], sscb[:, 0:1],
                                                ROUND_C, ALU.mult, ALU.add))
                VEc(lambda p=p, it=it: e.tensor_scalar(
                    obuf[it], sq[p], ROUND_C, sscb[:, 1:2],
                    ALU.subtract, ALU.mult))
                qdq_ve.append(n["ve"])
            W("ve", "tensor")
            for ts_ in range(TB // 128):
                tt = tb * (TB // 128) + ts_
                pr = tt % 2
                W("ve", "sync", val=n["ve"])
                if tt >= 2:
                    WP(sp, spn, pr, "sync")
                LDP(pr, sres[pr], r4_in[128 * tt:128 * (tt + 1), :])
                for cc in range(2):
                    bank = pb[cc]
                    if tt * 2 + cc >= 2:
                        W("ve", "tensor", val=n["ve"] - 1)
                    for it in range(IT - 1):
                        MME(lambda bank=bank, it=it, ts_=ts_, cc=cc: e.matmul(
                            bank, obuf[it][:, 128 * ts_:128 * (ts_ + 1)],
                            wT[it][:, 512 * cc:512 * (cc + 1)],
                            start=(it == 0), stop=False))
                    MM(lambda bank=bank, ts_=ts_, cc=cc: e.matmul(
                        bank, obuf[IT - 1][:, 128 * ts_:128 * (ts_ + 1)],
                        wT[IT - 1][:, 512 * cc:512 * (cc + 1)],
                        start=False, stop=True))
                    W("mm", "vector")
                    WP(lp, lpn, pr, "vector")
                    WP(sp, spn, pr, "vector")
                    VE(lambda cc=cc, bank=bank, pr=pr: e.tensor_add(
                        souts[pr][:, 512 * cc:512 * (cc + 1)], bank,
                        sres[pr][:, 512 * cc:512 * (cc + 1)]))
                W("ve", "sync")
                STP(pr, out_out[128 * tt:128 * (tt + 1), :], souts[pr])
        if eng == "sync":
            e.wait_ge(sp[0], spn[0])
            e.wait_ge(sp[1], spn[1])
        BAR("sync", "gpsimd", "tensor", "scalar", "vector")
        return n

    return _run_block(nc, emit)



_BUILDERS = {
    "l1": build_launch1, "l2": build_expert_ffn, "l3": build_launch3,
    "l4": build_launch4, "l5": build_launch5, "l6": build_launch6,
    "l7": build_launch7,
}


def _prog(name):
    if name not in _PROG_CACHE:
        _PROG_CACHE[name] = _BUILDERS[name]()
    return _PROG_CACHE[name]


def kernel(L0_attn_x, L0_attn_w_qkv, L0_attn_b_qkv, L0_attn_w_proj,
           L0_attn_b_proj, L0_attn_pos_independent, L0_ffn_Wr, L0_ffn_W1,
           L0_ffn_b1, L0_ffn_W2, L0_ffn_b2, L1_attn_Wq, L1_attn_bq,
           L1_attn_Wk, L1_attn_bk, L1_attn_Wv, L1_attn_bv, L1_attn_Wo,
           L1_attn_bo, L1_ffn_w_gate, L1_ffn_w_up, L1_ffn_w_down):
    x = np.ascontiguousarray(np.asarray(L0_attn_x, dtype=np.float32))
    w_qkv = np.asarray(L0_attn_w_qkv, np.float32)
    b_qkv = np.asarray(L0_attn_b_qkv, np.float32)
    w_proj = np.asarray(L0_attn_w_proj, np.float32)
    b_proj = np.asarray(L0_attn_b_proj, np.float32)
    Wr = np.ascontiguousarray(np.asarray(L0_ffn_Wr, np.float32))
    W1 = np.asarray(L0_ffn_W1, np.float32)
    b1 = np.asarray(L0_ffn_b1, np.float32)
    W2 = np.asarray(L0_ffn_W2, np.float32)
    b2 = np.asarray(L0_ffn_b2, np.float32)

    phi, psi, cmax = _l0_host_tables()
    cmaxneg = np.zeros((128, 16), np.float32)
    cmaxneg[:, :8] = -cmax[:8]

    in1 = []
    for c in range(NCORES):
        in1.append({
            "x": x[BPC * c:BPC * (c + 1)],
            "wqkvT": np.ascontiguousarray(w_qkv.T),
            "bqkvc": np.ascontiguousarray(b_qkv.reshape(24, 128).T),
            "bvrow": np.ascontiguousarray(b_qkv[2 * C:3 * C][None, :]),
            "wprojT": np.ascontiguousarray(w_proj.T),
            "bprow": np.ascontiguousarray(b_proj[None, :]),
            "phiT": phi.astype(ml_dtypes.bfloat16),
            "psiT": psi.astype(ml_dtypes.bfloat16), "cmaxneg": cmaxneg,
            "wr": Wr,
            "posind": np.ascontiguousarray(
                np.asarray(L0_attn_pos_independent,
                           np.float32)[:256].reshape(8, 64)),
        })
    r1 = run_bass_kernel_spmd(_prog("l1"), in1, core_ids=list(range(NCORES)))
    r1 = r1.results

    in2 = []
    for e in range(NCORES):
        buf_e = np.concatenate(
            [r1[c]["bufpart"][e * BPC * CAP:(e + 1) * BPC * CAP]
             for c in range(NCORES)], axis=0)
        in2.append({
            "buf": buf_e,
            "w1": np.ascontiguousarray(W1[e]),
            "b1c": np.ascontiguousarray(b1[e].reshape(FFH // 128, 128).T),
            "w2": np.ascontiguousarray(W2[e]),
            "b2": np.ascontiguousarray(b2[e][None, :]),
        })
    r2 = run_bass_kernel_spmd(_prog("l2"), in2, core_ids=list(range(NCORES)))
    r2 = r2.results

    in3 = []
    for c in range(NCORES):
        yall = np.concatenate(
            [r2[e]["y"][BPC * CAP * c:BPC * CAP * (c + 1)]
             for e in range(NCORES)], axis=0)
        in3.append({
            "resid2": r1[c]["resid2"],
            "yall": yall,
            "gidx": r1[c]["gidx"],
            "gates": r1[c]["gates"],
            "wq": np.ascontiguousarray(np.asarray(L1_attn_Wq, np.float32)),
            "wk": np.ascontiguousarray(np.asarray(L1_attn_Wk, np.float32)),
            "wv": np.ascontiguousarray(np.asarray(L1_attn_Wv, np.float32)),
            "wo": np.ascontiguousarray(np.asarray(L1_attn_Wo, np.float32)),
            "bq": np.asarray(L1_attn_bq, np.float32)[None, :],
            "bk": np.asarray(L1_attn_bk, np.float32)[None, :],
            "bv": np.asarray(L1_attn_bv, np.float32)[None, :],
            "bo": np.asarray(L1_attn_bo, np.float32)[None, :],
        })
    r3 = run_bass_kernel_spmd(_prog("l3"), in3, core_ids=list(range(NCORES)))
    r3 = r3.results

    am1all = np.concatenate([r3[c]["amax1"] for c in range(NCORES)],
                            axis=1)
    in4 = []
    for c in range(NCORES):
        in4.append({
            "xln4": r3[c]["xln4"].reshape(TPC, C),
            "amax1all": am1all,
            "wup": np.ascontiguousarray(np.asarray(L1_ffn_w_up, np.float32)),
            "wgate": np.ascontiguousarray(np.asarray(L1_ffn_w_gate, np.float32)),
        })
    r4 = run_bass_kernel_spmd(_prog("l4"), in4, core_ids=list(range(NCORES)))
    r4 = r4.results

    am2all = np.concatenate([r4[c]["amax2"] for c in range(NCORES)],
                            axis=0)
    in5 = [{"gate": r4[c]["gate"], "amax2all": am2all} for c in range(NCORES)]
    r5 = run_bass_kernel_spmd(_prog("l5"), in5, core_ids=list(range(NCORES)))
    r5 = r5.results

    am3all = np.concatenate([r5[c]["amax3"] for c in range(NCORES)], axis=1)
    in6 = [{"up": r4[c]["up"], "g2pre": r5[c]["g2pre"], "amax2all": am2all,
            "amax3all": am3all} for c in range(NCORES)]
    r6 = run_bass_kernel_spmd(_prog("l6"), in6, core_ids=list(range(NCORES)))
    r6 = r6.results

    am4all = np.concatenate([r6[c]["amax4"] for c in range(NCORES)], axis=1)
    in7 = []
    for c in range(NCORES):
        in7.append({
            "tprod": r6[c]["tprod"],
            "amax4all": am4all,
            "wdn": np.ascontiguousarray(np.asarray(L1_ffn_w_down, np.float32)),
            "resid4": r3[c]["resid4"].reshape(TPC, C),
        })
    r7 = run_bass_kernel_spmd(_prog("l7"), in7, core_ids=list(range(NCORES)))
    r7 = r7.results

    out = np.concatenate(
        [r7[c]["out"].reshape(BPC, T, C) for c in range(NCORES)], axis=0)
    return out
